# version 01
# baseline (speedup 1.0000x reference)
import math
from contextlib import ExitStack

import numpy as np
import ml_dtypes
import concourse.bass as bass
import concourse.mybir as mybir
from concourse.bass_utils import run_bass_kernel_spmd

F32 = mybir.dt.float32
BF16 = mybir.dt.bfloat16
AF = mybir.ActivationFunctionType
ALU = mybir.AluOpType
AX = mybir.AxisListType

BC = 4
S = 2048
T = BC * S
DM = 1024
NTT = T // 128
I_FF = 4096
NE = 4

BF = ml_dtypes.bfloat16


def host_tables():
    t = {}
    H, D = 16, 64
    a = 1.0 / 10000.0 ** np.linspace(0.0, 1.0, D // 2)
    angle = np.repeat(a, 2)
    decay = np.log(1.0 - 2.0 ** (-5.0 - np.arange(H, dtype=np.float64)))
    idx = np.arange(S, dtype=np.float64)
    sin = np.sin(idx[:, None] * angle[None, :])
    cos = np.cos(idx[:, None] * angle[None, :])
    sinp = sin.copy()
    sinp[:, 0::2] *= -1.0
    qdec = np.exp(idx[:, None] * decay[None, :])
    ed = np.exp(decay)[None, :]
    rowsum = (1.0 - np.exp((idx[:, None] + 1.0) * decay[None, :])) / (1.0 - ed)
    qdec = qdec / np.sqrt(rowsum)
    kdec = np.exp(-idx[:, None] * decay[None, :]) * (DM ** -0.5)
    cosT = np.tile(cos, (1, H))
    sinT = np.tile(sinp, (1, H))
    qdE = np.repeat(qdec, D, axis=1)
    kdE = np.repeat(kdec, D, axis=1)
    t["l0_qA"] = (cosT * qdE).astype(BF)
    t["l0_qB"] = (sinT * qdE).astype(BF)
    t["l0_kA"] = (cosT * kdE).astype(BF)
    t["l0_kB"] = (sinT * kdE).astype(BF)
    D1 = 128
    inv = 1.0 / 10000.0 ** (np.arange(0, D1, 2, dtype=np.float64) / D1)
    tt = np.arange(S, dtype=np.float64)
    freqs = np.repeat(tt[:, None] * inv[None, :], 2, axis=-1)
    c1 = np.cos(freqs)
    s1 = np.sin(freqs)
    s1[:, 0::2] *= -1.0
    t["l1_cA"] = np.ascontiguousarray(c1.reshape(256, 1024)).astype(BF)
    t["l1_cB"] = np.ascontiguousarray(s1.reshape(256, 1024)).astype(BF)
    tri = np.tril(np.ones((128, 128), np.float32))
    t["tril"] = tri.astype(BF)
    t["trilT"] = np.ascontiguousarray(tri.T).astype(BF)
    return t


TABLES = host_tables()

_SECTION = {"t": "tensor", "v": "vector", "s": "scalar", "g": "gpsimd", "q": "sync"}
_OWN_SEM = {"t": "t", "v": "v", "s": "s", "g": "g", "q": "q"}


class Sched:
    def __init__(self, nc, ctx, block):
        self.nc = nc
        self.block = block
        self.sem = {}
        self.count = {}
        self.waited = {}
        self.pending_barrier = {}
        for name in ["t", "v", "s", "g", "q", "qd", "sd"]:
            self.sem[name] = ctx.enter_context(nc.semaphore("sem_" + name))
            self.count[name] = 0

    def barrier(self):
        toks = [(n, self.count[n]) for n in self.sem if self.count[n] > 0]
        for eng in _SECTION:
            self.pending_barrier[eng] = toks

    def _emit_waits(self, q, eng, deps):
        alldeps = list(deps)
        if eng in self.pending_barrier:
            alldeps += self.pending_barrier.pop(eng)
        own = _OWN_SEM[eng]
        best = {}
        for d in alldeps:
            if d is None:
                continue
            sname, val = d
            if val <= 0:
                continue
            if sname == own:
                val = min(val, self.count[own])
            if val <= 0:
                continue
            best[sname] = max(best.get(sname, 0), val)
        for sname, val in best.items():
            key = (eng, sname)
            if self.waited.get(key, 0) >= val:
                continue
            q.wait_ge(self.sem[sname], val)
            self.waited[key] = val

    def run(self, eng, fn, deps=(), serialize=False):
        sec = getattr(self.block, _SECTION[eng])
        own = _OWN_SEM[eng]
        prev = self.count[own]

        @sec
        def _(q):
            if serialize and prev > 0:
                q.wait_ge(self.sem[own], prev)
            self._emit_waits(q, eng, deps)
            last = fn(q)
            last.then_inc(self.sem[own], 1)
        self.count[own] += 1
        return (own, self.count[own])

    def dma(self, queue_eng, sem_name, fn, deps=()):
        sec = getattr(self.block, _SECTION[queue_eng])

        @sec
        def _(q):
            self._emit_waits(q, queue_eng, deps)
            for ins in fn(q):
                ins.then_inc(self.sem[sem_name], 16)
                self.count[sem_name] += 16
        return (sem_name, self.count[sem_name])


def emit_ln(sc, G, xv, ybufs, eps, deps):
    stat, scr = G["stat"], G["scr_b"]

    def f1(q):
        q.activation(scr[:, :], xv, AF.Identity, accum_out=stat[:, 0:1])
        return q.activation(scr[:, :], xv, AF.Square, accum_out=stat[:, 1:2])
    t1 = sc.run("s", f1, deps=deps)

    def f2(q):
        q.tensor_scalar_mul(stat[:, 2:3], stat[:, 0:1], 1.0 / DM)
        return q.tensor_scalar_mul(stat[:, 3:4], stat[:, 1:2], 1.0 / DM)
    t2 = sc.run("v", f2, deps=[t1])

    t3 = sc.run("s", lambda q: q.activation(stat[:, 4:5], stat[:, 2:3], AF.Square),
                deps=[t2])

    t4 = sc.run("v", lambda q: q.scalar_tensor_tensor(
        stat[:, 3:4], stat[:, 3:4], float(eps), stat[:, 4:5],
        ALU.add, ALU.subtract), deps=[t3, t2])

    t5 = sc.run("s", lambda q: q.activation(stat[:, 4:5], stat[:, 3:4], AF.Sqrt),
                deps=[t4])

    t6 = sc.run("v", lambda q: q.reciprocal(stat[:, 5:6], stat[:, 4:5]), deps=[t5])

    def f7(q):
        last = None
        for buf in ybufs:
            last = q.tensor_scalar(buf, xv, stat[:, 2:3], stat[:, 5:6],
                                   ALU.subtract, ALU.mult)
        return last
    return sc.run("v", f7, deps=[t6])


def emit_l0ffn(sc, G, src, dst):
    nc, w = G["nc"], G["w"]
    tag = G["tag"]

    def scope():
        return ExitStack()

    with scope() as cx:
        def lsb(name, shape, dt=F32):
            return cx.enter_context(nc.sbuf_tensor(f"{tag}_{name}", shape, dt))

        def lps(name, shape, dt=F32):
            return cx.enter_context(nc.psum_tensor(f"{tag}_{name}", shape, dt))

        xt = lsb("xt", [128, 2 * DM])
        yb = lsb("yb", [128, DM], BF16)
        yf32 = lsb("yf32", [128, DM])
        ylo = lsb("ylo", [128, DM], BF16)
        yfm = lsb("yfm", [128, 2 * DM], BF16)
        ylofm = lsb("ylofm", [128, DM], BF16)
        wrf = lsb("wrf", [128, 8 * 4])
        wrhi = lsb("wrhi", [128, 8 * 4], BF16)
        wrlo = lsb("wrlo", [128, 8 * 4], BF16)
        wrlof = lsb("wrlof", [128, 8 * 4])
        lgs = lsb("lgs", [128, 4])
        mtmp = lsb("mtmp", [128, 2])
        tp = lps("tp", [128, 2 * DM], BF16)
        lg = lps("lg", [128, 4])

        t_wr = sc.dma("q", "qd", lambda q: [
            q.dma_start(out=wrf[:, :].rearrange("p (cc e) -> p cc e", cc=8),
                        in_=w["l0f_Wr"].rearrange("(cc p) e -> p cc e", p=128))])

        t_wr1 = sc.run("g", lambda q: q.tensor_copy(wrhi[:, :], wrf[:, :]),
                       deps=[t_wr])
        t_wr2 = sc.run("g", lambda q: q.tensor_sub(wrlof[:, :], wrf[:, :], wrhi[:, :]),
                       deps=[t_wr1])
        t_wrc = sc.run("g", lambda q: q.tensor_copy(wrlo[:, :], wrlof[:, :]),
                       deps=[t_wr2])

        slot = {}
        t_st = None
        for tt in range(NTT):
            par = tt % 2
            xv = xt[:, par * DM:(par + 1) * DM]
            yfmv = yfm[:, par * DM:(par + 1) * DM]

            t_x = sc.dma("q", "qd", lambda q, tt=tt, xv=xv: [
                q.dma_start(out=xv, in_=src[tt * 128:(tt + 1) * 128, :])],
                deps=[slot.get(("x", par))])

            t_ln = emit_ln(sc, G, xv, [yb[:, :], yf32[:, :]], 1e-5,
                           deps=[t_x, slot.get("yb_c")])
            slot[("x", par)] = t_ln

            t_lo = sc.run("v", lambda q: q.tensor_sub(ylo[:, :], yf32[:, :], yb[:, :]),
                          deps=[t_ln])


            def f_tr(q):
                for cc in range(8):
                    q.transpose(tp[:, cc * 128:(cc + 1) * 128],
                                yb[:, cc * 128:(cc + 1) * 128], G["ident_b"][:, :])
                ins = None
                for cc in range(8):
                    ins = q.transpose(tp[:, DM + cc * 128:DM + (cc + 1) * 128],
                                      ylo[:, cc * 128:(cc + 1) * 128], G["ident_b"][:, :])
                return ins
            t_tr = sc.run("t", f_tr, deps=[t_lo, slot.get("tp_ev")])
            slot["yb_c"] = t_tr
            if G.get("debug") and tt == 47:
                t_tap = sc.dma("g", "sd", lambda q, xv=xv: [
                    q.dma_start(out=G["yb_dbg"][0], in_=yb[:, :]),
                    q.dma_start(out=G["yb_dbg"][1], in_=yf32[:, :]),
                    q.dma_start(out=G["yb_dbg"][2], in_=xv),
                    q.dma_start(out=G["yb_dbg"][3][:, 0:8], in_=G["stat"][:, :])], deps=[t_tr])
                slot["yb_c"] = t_tap
                slot[("x", tt % 2)] = t_tap

            def f_ev(q, yfmv=yfmv):
                q.tensor_copy(yfmv, tp[:, 0:DM])
                return q.tensor_copy(ylofm[:, :], tp[:, DM:2 * DM])
            t_ev = sc.run("v", f_ev,
                          deps=[t_tr, slot.get(("yfm", par)), slot.get("ylofm_c")])
            slot["tp_ev"] = t_ev

            def f_rt(q, yfmv=yfmv):
                prods = [(yfmv, wrhi), (yfmv, wrlo), (ylofm[:, :], wrhi)]
                ins = None
                n = 0
                for (lh, rh) in prods:
                    for cc in range(8):
                        n += 1
                        ins = q.matmul(lg[:, :], lh[:, cc * 128:(cc + 1) * 128],
                                       rh[:, cc * 4:(cc + 1) * 4],
                                       start=(n == 1), stop=(n == 24))
                return ins
            t_rt = sc.run("t", f_rt, deps=[t_ev, t_wrc, slot.get("lg_ev")])
            slot["ylofm_c"] = t_rt

            t_cp = sc.run("v", lambda q: q.tensor_copy(lgs[:, :], lg[:, :]),
                          deps=[t_rt])
            t_mx = sc.run("v", lambda q: q.reduce_max(
                G["stat"][:, 6:7], lgs[:, :], axis=AX.X),
                deps=[t_cp], serialize=True)

            def f_mk(q, tt=tt):
                ins = None
                for e in range(4):
                    ins = q.tensor_tensor(
                        G["masks"][:, tt * 4 + e:tt * 4 + e + 1],
                        lgs[:, e:e + 1], G["stat"][:, 6:7], ALU.is_ge)
                return ins
            t_mk = sc.run("v", f_mk, deps=[t_mx], serialize=True)
            slot["lg_ev"] = t_mk
            if G.get("debug"):
                slot["lg_ev"] = sc.dma("s", "sd", lambda q, tt=tt: [
                    q.dma_start(out=G["lgs_dbg"][tt], in_=lgs[:, :])], deps=[t_mk])

            t_st = sc.dma("s", "sd", lambda q, tt=tt, yfmv=yfmv: [
                q.dma_start(out=G["yfm"][tt].rearrange("cc p t -> p cc t"),
                            in_=yfmv.rearrange("p (cc t) -> p cc t", cc=8))],
                deps=[t_ev])
            slot[("yfm", par)] = t_st

    sc.barrier()

    with scope() as cx:
        def lsb(name, shape, dt=F32):
            return cx.enter_context(nc.sbuf_tensor(f"{tag}_{name}", shape, dt))

        def lps(name, shape, dt=F32):
            return cx.enter_context(nc.psum_tensor(f"{tag}_{name}", shape, dt))

        w1f = lsb("w1f", [128, 8 * 512])
        w1b = lsb("w1b", [128, NE * 8 * 512], BF16)
        b1f = lsb("b1f", [1, NE * 512])
        b1b = lsb("b1b", [1, NE * 512], BF16)
        yfs = lsb("yfs", [128, 2 * DM], BF16)
        hsb = lsb("hsb", [128, 2 * 2048], BF16)
        isb = lsb("isb", [128, 2 * 512], BF16)
        hp = lps("hp", [128, 2 * 2048])

        aslot = {}
        t_is = None
        for ic in range(8):
            w_ready = []
            for e in range(NE):
                t_w = sc.dma("q", "qd", lambda q, e=e, ic=ic: [
                    q.dma_start(out=w1f[:, :].rearrange("p (cc i) -> p cc i", cc=8),
                                in_=w["l0f_W1"][e, :, ic * 512:(ic + 1) * 512]
                                .rearrange("(cc p) i -> p cc i", p=128))],
                    deps=[aslot.get("w1f_c")])

                def f_c(q, e=e):
                    return q.tensor_copy(w1b[:, e * 4096:(e + 1) * 4096], w1f[:, :])
                t_c = sc.run("g", f_c, deps=[t_w, aslot.get("w1b_c")])
                aslot["w1f_c"] = t_c
                w_ready.append(t_c)

            t_b1 = sc.dma("q", "qd", lambda q, ic=ic: [
                q.dma_start(out=b1f[:, :].rearrange("o (e i) -> o e i", e=NE),
                            in_=w["l0f_b1"][:, ic * 512:(ic + 1) * 512].unsqueeze(0))],
                deps=[aslot.get("b1_c")])
            t_b1c = sc.run("g", lambda q: q.tensor_copy(b1b[:, :], b1f[:, :]),
                           deps=[t_b1])
            aslot["b1_c"] = t_b1c
            w_ready.append(t_b1c)

            for tt in range(NTT):
                par = tt % 2
                yv = yfs[:, par * DM:(par + 1) * DM]
                hv = hp[:, par * 2048:(par + 1) * 2048]
                hbv = hsb[:, par * 2048:(par + 1) * 2048]
                iv = isb[:, par * 512:(par + 1) * 512]

                t_y = sc.dma("q", "qd", lambda q, tt=tt, yv=yv: [
                    q.dma_start(out=yv.rearrange("p (cc t) -> p cc t", cc=8),
                                in_=G["yfm"][tt].rearrange("cc p t -> p cc t"))],
                    deps=[aslot.get(("yfs", par))])

                def f_h(q, yv=yv, hv=hv):
                    ins = None
                    for e in range(NE):
                        ins = q.matmul(hv[:, e * 512:(e + 1) * 512], G["ones1"][:, :],
                                       b1b[:, e * 512:(e + 1) * 512],
                                       start=True, stop=False)
                        for cc in range(8):
                            ins = q.matmul(
                                hv[:, e * 512:(e + 1) * 512],
                                yv[:, cc * 128:(cc + 1) * 128],
                                w1b[:, e * 4096 + cc * 512:e * 4096 + (cc + 1) * 512],
                                start=False, stop=(cc == 7))
                    return ins
                t_h = sc.run("t", f_h, deps=[t_y] + w_ready + [aslot.get(("hp", par))])
                aslot[("yfs", par)] = t_h
                if tt == NTT - 1:
                    aslot["w1b_c"] = t_h

                t_g = sc.run("s", lambda q, hv=hv, hbv=hbv:
                             q.activation(hbv, hv, AF.Gelu),
                             deps=[t_h, aslot.get(("hsb", par))])
                aslot[("hp", par)] = t_g

                def f_cb(q, tt=tt, hbv=hbv, iv=iv):
                    q.tensor_scalar(iv, hbv[:, 0:512],
                                    G["masks"][:, tt * 4:tt * 4 + 1], None, ALU.mult)
                    ins = None
                    for e in range(1, NE):
                        ins = q.scalar_tensor_tensor(
                            iv, hbv[:, e * 512:(e + 1) * 512],
                            G["masks"][:, tt * 4 + e:tt * 4 + e + 1],
                            iv, ALU.mult, ALU.add)
                    return ins
                t_cb = sc.run("v", f_cb, deps=[t_g, aslot.get(("isb", par))])
                aslot[("hsb", par)] = t_cb

                t_is = sc.dma("s", "sd", lambda q, ic=ic, tt=tt, iv=iv: [
                    q.dma_start(out=G["interm"][ic, tt], in_=iv)], deps=[t_cb])
                aslot[("isb", par)] = t_is

    sc.barrier()

    with scope() as cx:
        def lsb(name, shape, dt=F32):
            return cx.enter_context(nc.sbuf_tensor(f"{tag}_{name}", shape, dt))

        def lps(name, shape, dt=F32):
            return cx.enter_context(nc.psum_tensor(f"{tag}_{name}", shape, dt))

        wof = lsb("wof", [128, 4 * DM])
        wob = lsb("wob", [128, 32 * DM], BF16)
        bo_b = lsb("bo_b", [128, DM])
        gam_b = lsb("gam_b", [128, DM])
        bet_b = lsb("bet_b", [128, DM])
        row_f = lsb("row_f", [1, DM])
        row_b = lsb("row_b", [1, DM], BF16)
        ib = lsb("ib", [128, 2 * 4096], BF16)
        itf = lsb("itf", [128, 4096], BF16)
        zsb = lsb("zsb", [128, DM])
        ztmp = lsb("ztmp", [128, DM])
        xt2 = lsb("xt2", [128, 2 * DM])
        y2 = lsb("y2", [128, DM])
        xo = lsb("xo", [128, 2 * DM])
        tpb = lps("tpb", [128, 2048], BF16)
        zp = lps("zp", [128, 2 * DM])

        bslot = {}
        bc_ps_t = cx.enter_context(nc.psum_tensor(f"{tag}_bcps", [128, DM], F32))
        t_prev = None
        for name, dstbuf in [("l0f_bo", bo_b), ("l0f_g", gam_b), ("l0f_b", bet_b)]:
            t_r = sc.dma("q", "qd", lambda q, name=name: [
                q.dma_start(out=row_f[:, :], in_=w[name].unsqueeze(0))],
                deps=[t_prev])
            t_c = sc.run("v", lambda q: q.tensor_copy(row_b[:, :], row_f[:, :]),
                         deps=[t_r, t_prev])

            def f_bc(q):
                ins = None
                for nch in range(DM // 512):
                    ins = q.matmul(bc_ps_t[:, nch * 512:(nch + 1) * 512],
                                   G["ones1"][:, :],
                                   row_b[:, nch * 512:(nch + 1) * 512],
                                   start=True, stop=True)
                return ins
            t_bc = sc.run("t", f_bc, deps=[t_c, t_prev])
            t_prev = sc.run("v", lambda q, dstbuf=dstbuf: q.tensor_copy(
                dstbuf[:, :], bc_ps_t[:, :]), deps=[t_bc])
        t_bcast = t_prev
        wo_casts = [t_bcast]
        for gq in range(8):
            t_wo = sc.dma("q", "qd", lambda q, gq=gq: [
                q.dma_start(out=wof[:, :].rearrange("p (cc d) -> p cc d", cc=4),
                            in_=w["l0f_Wo"][gq * 512:(gq + 1) * 512, :]
                            .rearrange("(cc p) d -> p cc d", p=128))],
                deps=[bslot.get("wof_c")])
            t_woc = sc.run("g", lambda q, gq=gq: q.tensor_copy(
                wob[:, gq * 4 * DM:(gq + 1) * 4 * DM], wof[:, :]), deps=[t_wo])
            bslot["wof_c"] = t_woc
            wo_casts.append(t_woc)

        t_xs = None
        for tt in range(NTT):
            par = tt % 2
            ibv = ib[:, par * 4096:(par + 1) * 4096]
            xv = xt2[:, par * DM:(par + 1) * DM]
            zv = zp[:, par * DM:(par + 1) * DM]
            xov = xo[:, par * DM:(par + 1) * DM]

            t_i = sc.dma("q", "qd", lambda q, tt=tt, ibv=ibv, xv=xv: [
                q.dma_start(out=ibv.rearrange("p (ic i) -> p ic i", ic=8),
                            in_=G["interm"][:, tt].rearrange("ic p i -> p ic i")),
                q.dma_start(out=xv, in_=src[tt * 128:(tt + 1) * 128, :])],
                deps=[bslot.get(("ib", par)), bslot.get(("x2", par))])

            tr_toks = []
            for rnd in range(2):
                def f_tr(q, rnd=rnd, ibv=ibv):
                    ins = None
                    for j in range(16):
                        ci = rnd * 16 + j
                        ins = q.transpose(tpb[:, j * 128:(j + 1) * 128],
                                          ibv[:, ci * 128:(ci + 1) * 128],
                                          G["ident_b"][:, :])
                    return ins
                t_tr = sc.run("t", f_tr, deps=[t_i, bslot.get("tpb_ev")])

                def f_te(q, rnd=rnd):
                    return q.tensor_copy(itf[:, rnd * 2048:(rnd + 1) * 2048], tpb[:, :])
                t_te = sc.run("v", f_te, deps=[t_tr, bslot.get("itf_c")])
                bslot["tpb_ev"] = t_te
                tr_toks.append(t_te)

            def f_z(q, zv=zv):
                ins = None
                for ci in range(32):
                    for dch in range(2):
                        ins = q.matmul(
                            zv[:, dch * 512:(dch + 1) * 512],
                            itf[:, ci * 128:(ci + 1) * 128],
                            wob[:, ci * DM + dch * 512:ci * DM + (dch + 1) * 512],
                            start=(ci == 0), stop=(ci == 31))
                return ins
            t_z = sc.run("t", f_z, deps=tr_toks + wo_casts + [bslot.get(("zp", par))])
            bslot[("ib", par)] = t_z
            bslot["itf_c"] = t_z

            t_y2 = emit_ln(sc, G, xv, [y2[:, :]], 1e-5,
                           deps=[t_i, bslot.get("y2_c")])

            t_ze = sc.run("s", lambda q, zv=zv: q.copy(zsb[:, :], zv),
                          deps=[t_z, bslot.get("zsb_c")])
            bslot[("zp", par)] = t_ze

            def f_za(q):
                q.tensor_add(zsb[:, :], zsb[:, :], bo_b[:, :])
                return q.tensor_add(zsb[:, :], zsb[:, :], y2[:, :])
            t_za = sc.run("g", f_za, deps=[t_ze, t_y2])
            bslot["y2_c"] = t_za
            if G.get("debug"):
                t_za = sc.dma("s", "sd", lambda q, tt=tt: [
                    q.dma_start(out=G["zsb_dbg"][tt], in_=zsb[:, :]),
                    q.dma_start(out=G["y2_dbg"][tt], in_=y2[:, :])], deps=[t_za])

            t_iln = emit_ln(sc, G, zsb[:, :], [ztmp[:, :]], 1e-12,
                            deps=[t_za, bslot.get("ztmp_c")])
            bslot["zsb_c"] = t_iln

            def f_gb(q):
                q.tensor_mul(ztmp[:, :], ztmp[:, :], gam_b[:, :])
                return q.tensor_add(ztmp[:, :], ztmp[:, :], bet_b[:, :])
            t_gb = sc.run("g", f_gb, deps=[t_iln])

            t_res = sc.run("v", lambda q, xov=xov, xv=xv: q.scalar_tensor_tensor(
                xov, ztmp[:, :], 0.5, xv, ALU.mult, ALU.add),
                deps=[t_gb, bslot.get(("xo", par))])
            bslot["ztmp_c"] = t_res
            bslot[("x2", par)] = t_res

            t_xs = sc.dma("s", "sd", lambda q, tt=tt, xov=xov: [
                q.dma_start(out=dst[tt * 128:(tt + 1) * 128, :], in_=xov)],
                deps=[t_res])
            bslot[("xo", par)] = t_xs

    sc.barrier()



def emit_l0attn(sc, G, src, dst):
    nc, w = G["nc"], G["w"]
    tag = G["tag"]
    H, HD = 16, 64

    with ExitStack() as cx:
        def lsb(name, shape, dt=F32):
            return cx.enter_context(nc.sbuf_tensor(f"{tag}_{name}", shape, dt))

        def lps(name, shape, dt=F32):
            return cx.enter_context(nc.psum_tensor(f"{tag}_{name}", shape, dt))

        yfm = lsb("yfm", [128, 8 * S], BF16)
        qrf = lsb("qrf", [128, 8 * S], BF16)
        krf = lsb("krf", [128, 8 * S], BF16)
        vtm = lsb("vtm", [128, 16 * DM], BF16)
        outall = lsb("outall", [128, 16 * DM], BF16)
        wsl = lsb("wsl", [128, 8 * DM], BF16)
        wf = lsb("wf", [128, DM])
        xt = lsb("xt", [128, DM])
        yb = lsb("yb", [128, DM], BF16)
        rtm = lsb("rtm", [128, 2 * DM], BF16)
        tabs = lsb("tabs", [128, 2 * DM], BF16)
        asb = lsb("asb", [128, 2048], BF16)
        dens = lsb("dens", [128, 16])
        rden = lsb("rden", [128, 16])
        gt = lsb("gt", [128, DM], BF16)
        gsum = lsb("gsum", [128, 16])
        grst = lsb("grst", [128, 16])
        otmp = lsb("otmp", [128, DM], BF16)
        pq = lps("pq", [128, 2048])
        pk = lps("pk", [128, 2048])

        def bank_chunks(lo, hi):
            out = []
            while lo < hi:
                nxt = min((lo // 512 + 1) * 512, hi)
                out.append((lo, nxt))
                lo = nxt
            return out

        slot = {}
        for b in range(BC):
            base = b * S

            def load_w(nm, dep_consumed):
                toks = []
                dep = dep_consumed
                for cc in range(8):
                    t_l = sc.dma("q", "qd", lambda q, nm=nm, cc=cc: [
                        q.dma_start(out=wf[:, :],
                                    in_=w[nm][cc * 128:(cc + 1) * 128, :])],
                        deps=[dep, slot.get("wf_c")])
                    t_c = sc.run("g", lambda q, cc=cc: q.tensor_copy(
                        wsl[:, cc * DM:(cc + 1) * DM], wf[:, :]), deps=[t_l])
                    slot["wf_c"] = t_c
                    toks.append(t_c)
                return toks

            for ts_ in range(16):
                t_x = sc.dma("q", "qd", lambda q, ts_=ts_: [
                    q.dma_start(out=xt[:, :],
                                in_=src[base + ts_ * 128:base + (ts_ + 1) * 128, :])],
                    deps=[slot.get("x_c")])
                t_ln = emit_ln(sc, G, xt[:, :], [yb[:, :]], 1e-5,
                               deps=[t_x, slot.get("yb_c")])
                slot["x_c"] = t_ln

                tp = pk[:, 0:512].bitcast(BF16)

                def f_tr(q, tp=tp):
                    ins = None
                    for cc in range(8):
                        ins = q.transpose(tp[:, cc * 128:(cc + 1) * 128],
                                          yb[:, cc * 128:(cc + 1) * 128],
                                          G["ident_b"][:, :])
                    return ins
                t_tr = sc.run("t", f_tr, deps=[t_ln, slot.get("tp_ev")])
                slot["yb_c"] = t_tr

                def f_te(q, ts_=ts_, tp=tp):
                    ov = yfm[:, :].rearrange("p (cc t) -> p cc t", t=S)[
                        :, :, ts_ * 128:(ts_ + 1) * 128]
                    return q.tensor_copy(ov, tp.rearrange("p (cc t) -> p cc t", t=128))
                t_te = sc.run("v", f_te, deps=[t_tr, slot.get("yfm_c")])
                slot["tp_ev"] = t_te
            yfm_done = sc.count["v"]
            t_yfm = ("v", yfm_done)

            for (wnm, dstf, ta, tb) in [("l0a_Wq", qrf, "l0_qA", "l0_qB"),
                                        ("l0a_Wk", krf, "l0_kA", "l0_kB")]:
                wtoks = load_w(wnm, slot.get("wsl_c"))
                for ts_ in range(16):
                    par = ts_ % 2
                    pv = pq[:, par * 1024:(par + 1) * 1024]

                    def f_mm(q, ts_=ts_, pv=pv):
                        ins = None
                        for cc in range(8):
                            lh = yfm[:, cc * S + ts_ * 128:cc * S + (ts_ + 1) * 128]
                            for n2 in range(2):
                                ins = q.matmul(pv[:, n2 * 512:(n2 + 1) * 512], lh,
                                               wsl[:, cc * DM + n2 * 512:cc * DM + (n2 + 1) * 512],
                                               start=(cc == 0), stop=(cc == 7))
                        return ins
                    t_mm = sc.run("t", f_mm,
                                  deps=[t_yfm] + wtoks + [slot.get(("pp", par))])

                    t_tab = sc.dma("q", "qd", lambda q, ts_=ts_, ta=ta, tb=tb: [
                        q.dma_start(out=tabs[:, 0:DM],
                                    in_=w[ta][ts_ * 128:(ts_ + 1) * 128, :]),
                        q.dma_start(out=tabs[:, DM:2 * DM],
                                    in_=w[tb][ts_ * 128:(ts_ + 1) * 128, :])],
                        deps=[slot.get("tab_c")])

                    def f_r1(q, pv=pv):
                        q.tensor_mul(rtm[:, 0:DM], pv, tabs[:, 0:DM])
                        pvv = pv.rearrange("p (a c) -> p a c", c=2)
                        bvv = tabs[:, DM:2 * DM].rearrange("p (a c) -> p a c", c=2)
                        mvv = rtm[:, DM:2 * DM].rearrange("p (a c) -> p a c", c=2)
                        q.tensor_mul(mvv[:, :, 0:1], pvv[:, :, 1:2], bvv[:, :, 0:1])
                        return q.tensor_mul(mvv[:, :, 1:2], pvv[:, :, 0:1], bvv[:, :, 1:2])
                    t_r1 = sc.run("v", f_r1, deps=[t_mm, t_tab, slot.get("rtm_c")])
                    slot["tab_c"] = t_r1
                    slot[("pp", par)] = t_r1

                    t_r2 = sc.run("v", lambda q: q.tensor_add(
                        yb[:, :], rtm[:, 0:DM], rtm[:, DM:2 * DM]),
                        deps=[t_r1, slot.get("yb_c")])
                    slot["rtm_c"] = t_r2

                    tp = pk[:, 0:512].bitcast(BF16)

                    def f_tr2(q, tp=tp):
                        ins = None
                        for cc in range(8):
                            ins = q.transpose(tp[:, cc * 128:(cc + 1) * 128],
                                              yb[:, cc * 128:(cc + 1) * 128],
                                              G["ident_b"][:, :])
                        return ins
                    t_tr2 = sc.run("t", f_tr2, deps=[t_r2, slot.get("tp_ev")])
                    slot["yb_c"] = t_tr2

                    def f_te2(q, ts_=ts_, tp=tp, dstf=dstf):
                        ov = dstf[:, :].rearrange("p (cc t) -> p cc t", t=S)[
                            :, :, ts_ * 128:(ts_ + 1) * 128]
                        return q.tensor_copy(ov, tp.rearrange("p (cc t) -> p cc t", t=128))
                    t_te2 = sc.run("v", f_te2, deps=[t_tr2, slot.get(("qk_c", wnm))])
                    slot["tp_ev"] = t_te2
                slot["wsl_c"] = t_mm
            qk_done = ("v", sc.count["v"])

            wtoks = load_w("l0a_Wv", slot.get("wsl_c"))
            for ts_ in range(16):
                par = ts_ % 2
                pv = pq[:, par * 1024:(par + 1) * 1024]

                def f_mmv(q, ts_=ts_, pv=pv):
                    ins = None
                    for cc in range(8):
                        lh = yfm[:, cc * S + ts_ * 128:cc * S + (ts_ + 1) * 128]
                        for n2 in range(2):
                            ins = q.matmul(pv[:, n2 * 512:(n2 + 1) * 512], lh,
                                           wsl[:, cc * DM + n2 * 512:cc * DM + (n2 + 1) * 512],
                                           start=(cc == 0), stop=(cc == 7))
                    return ins
                t_mmv = sc.run("t", f_mmv,
                               deps=[t_yfm] + wtoks + [slot.get(("pp", par))])
                t_ev = sc.run("v", lambda q, ts_=ts_, pv=pv: q.tensor_copy(
                    vtm[:, ts_ * DM:(ts_ + 1) * DM], pv),
                    deps=[t_mmv, slot.get("vtm_c")])
                slot[("pp", par)] = t_ev
                slot["wsl_c2"] = t_mmv
            v_done = ("v", sc.count["v"])
            slot["wsl_c"] = slot["wsl_c2"]

            wg_toks = load_w("l0a_Wg", slot.get("wsl_c"))

            head_last = None
            for h in range(H):
                hh, hp = h // 2, (h % 2) * 64
                qrh = qrf[hp:hp + 64, hh * S:(hh + 1) * S]
                krh = krf[hp:hp + 64, hh * S:(hh + 1) * S]

                a_toks = []
                for qt in range(16):
                    ps = pq if qt % 2 == 0 else pk
                    krange = (qt + 1) * 128

                    def f_a(q, qt=qt, ps=ps, krange=krange):
                        lh = qrh[:, qt * 128:(qt + 1) * 128]
                        ins = None
                        for (c0, c1) in bank_chunks(0, krange):
                            ins = q.matmul(ps[:, c0:c1], lh, krh[:, c0:c1],
                                           start=True, stop=True)
                        return ins
                    depk = [qk_done, slot.get(("psA", qt % 2))]
                    if qt < 2 and h == 0:
                        depk += [v_done, slot.get("comb_ps"), slot.get("tp_ev")]
                    t_a = sc.run("t", f_a, deps=depk)

                    t_dm = sc.run("v", lambda q, qt=qt, ps=ps: q.tensor_mul(
                        ps[:, qt * 128:(qt + 1) * 128],
                        ps[:, qt * 128:(qt + 1) * 128], G["tril"][:, :]),
                        deps=[t_a])

                    t_ab = sc.run("s", lambda q, qt=qt, ps=ps, krange=krange:
                                  q.activation(asb[:, 0:krange], ps[:, 0:krange],
                                               AF.Abs, accum_out=dens[:, qt:qt + 1]),
                                  deps=[t_dm, slot.get("asb_c"), slot.get("dens_c")])
                    slot[("psA", qt % 2)] = t_ab
                    slot["asb_c"] = t_ab
                    a_toks.append(t_ab)

                t_cl = sc.run("v", lambda q: q.tensor_scalar(
                    rden[:, :], dens[:, :], 1.0, 50000.0, ALU.max, ALU.min),
                    deps=a_toks)
                t_rc = sc.run("v", lambda q: q.reciprocal(rden[:, :], rden[:, :]),
                              deps=[t_cl])
                slot["dens_c"] = t_rc

                slot["atp0"] = slot.get(("psA", 0))
                slot["atp1"] = slot.get(("psA", 1))
                vv_last = None
                for qh in range(2):
                    for kt in range((qh + 1) * 8):
                        par = kt % 2
                        atp = pq[:, par * 1024:(par + 1) * 1024]
                        qlo = max(qh * 1024, kt * 128)
                        qhi = (qh + 1) * 1024
                        off = qlo - qh * 1024

                        def f_at(q, kt=kt, atp=atp, qlo=qlo, qhi=qhi, off=off):
                            lh = krh[:, kt * 128:(kt + 1) * 128]
                            ins = None
                            for (c0, c1) in bank_chunks(off, 1024):
                                ins = q.matmul(atp[:, c0:c1], lh,
                                               qrh[:, qh * 1024 + c0:qh * 1024 + c1],
                                               start=True, stop=True)
                            return ins
                        dep_at = [slot.get(("atB", par)), slot.get("atp%d" % par)]
                        slot["atp%d" % par] = None
                        t_at = sc.run("t", f_at, deps=dep_at)

                        if kt >= qh * 8:
                            t_at = sc.run("v", lambda q, kt=kt, atp=atp, off=off:
                                          q.tensor_mul(atp[:, off:off + 128],
                                                       atp[:, off:off + 128],
                                                       G["trilT"][:, :]),
                                          deps=[t_at])

                        t_ae = sc.run("s", lambda q, atp=atp, off=off, par=par:
                                      q.copy(asb[:, par * 1024 + off:(par + 1) * 1024],
                                             atp[:, off:1024]),
                                      deps=[t_at, slot.get(("asbB", par)), slot.get("asb_c")])
                        slot["asb_c"] = None
                        slot[("atB", par)] = t_ae

                        def f_vv(q, kt=kt, qh=qh, par=par, h=h):
                            ins = None
                            for i in range(max(kt, qh * 8), qh * 8 + 8):
                                ins = q.matmul(
                                    pk[0:64, i * 128:(i + 1) * 128],
                                    vtm[:, kt * DM + h * HD:kt * DM + (h + 1) * HD],
                                    asb[:, par * 1024 + i * 128 - qh * 1024:
                                        par * 1024 + (i + 1) * 128 - qh * 1024],
                                    start=False, stop=(kt == i), skip_group_check=True)
                            return ins
                        dep_vv = [t_ae]
                        if kt == 0 and qh == 0:
                            def f_z(q):
                                ins = None
                                for bk in range(4):
                                    ins = q.matmul(pk[0:64, bk * 512:(bk + 1) * 512],
                                                   G["ones1"][:, 0:64],
                                                   G["zrow"][:, :],
                                                   start=True, stop=True)
                                return ins
                            t_z = sc.run("t", f_z, deps=[slot.get("otB")] + a_toks[-2:])
                            dep_vv.append(t_z)
                        t_vv = sc.run("t", f_vv, deps=dep_vv)
                        slot[("asbB", par)] = t_vv
                        vv_last = t_vv

                t_oe = sc.run("v", lambda q: q.tensor_copy(asb[0:64, :], pk[0:64, :]),
                              deps=[vv_last, slot.get("otsb_c")])
                slot["otB"] = t_oe

                tr_grp = None
                for grp in range(4):
                    tpo = pq[:, grp * 512:grp * 512 + 128].bitcast(BF16)

                    def f_ot(q, grp=grp, tpo=tpo):
                        ins = None
                        for j in range(4):
                            i = grp * 4 + j
                            ins = q.transpose(tpo[:, j * 64:(j + 1) * 64],
                                              asb[0:64, i * 128:(i + 1) * 128],
                                              G["ident_b"][0:64, 0:64])
                        return ins
                    t_ot = sc.run("t", f_ot, deps=[t_oe, slot.get(("otT", grp))])

                    def f_os(q, grp=grp, tpo=tpo, h=h):
                        ins = None
                        for j in range(4):
                            i = grp * 4 + j
                            ins = q.tensor_scalar(
                                outall[:, i * DM + h * HD:i * DM + (h + 1) * HD],
                                tpo[:, j * 64:(j + 1) * 64],
                                rden[:, i:i + 1], None, ALU.mult)
                        return ins
                    t_os = sc.run("v", f_os, deps=[t_ot, t_rc, slot.get("outall_c")])
                    slot[("otT", grp)] = t_os
                    tr_grp = t_os
                slot["otsb_c"] = tr_grp
                slot[("psA", 0)] = tr_grp
                slot[("psA", 1)] = t_oe
                slot["asb_c"] = tr_grp
                head_last = tr_grp
            heads_done = head_last

            for i in range(16):
                t_x = sc.dma("q", "qd", lambda q, i=i: [
                    q.dma_start(out=xt[:, :],
                                in_=src[base + i * 128:base + (i + 1) * 128, :])],
                    deps=[slot.get("x_c"), heads_done])
                t_ln = emit_ln(sc, G, xt[:, :], [yb[:, :]], 1e-5,
                               deps=[t_x, slot.get("yb_c")])
                slot["x_c"] = t_ln

                tp = pk[:, 0:512].bitcast(BF16)

                def f_tr3(q, tp=tp):
                    ins = None
                    for cc in range(8):
                        ins = q.transpose(tp[:, cc * 128:(cc + 1) * 128],
                                          yb[:, cc * 128:(cc + 1) * 128],
                                          G["ident_b"][:, :])
                    return ins
                t_tr3 = sc.run("t", f_tr3, deps=[t_ln, slot.get("tp_ev"), heads_done])
                slot["yb_c"] = t_tr3

                yfmt = rtm[:, 0:DM]
                t_te3 = sc.run("v", lambda q, tp=tp: q.tensor_copy(
                    rtm[:, 0:DM].rearrange("p (cc t) -> p cc t", t=128),
                    tp.rearrange("p (cc t) -> p cc t", t=128)),
                    deps=[t_tr3, slot.get("rtm3_c")])
                slot["tp_ev"] = t_te3

                def f_gm(q):
                    ins = None
                    for cc in range(8):
                        for n2 in range(2):
                            ins = q.matmul(pq[:, 1024 + n2 * 512:1024 + (n2 + 1) * 512],
                                           rtm[:, cc * 128:(cc + 1) * 128],
                                           wsl[:, cc * DM + n2 * 512:cc * DM + (n2 + 1) * 512],
                                           start=(cc == 0), stop=(cc == 7))
                    return ins
                t_gm = sc.run("t", f_gm, deps=[t_te3] + wg_toks + [slot.get("gps_c")])
                slot["rtm3_c"] = t_gm

                t_sq = sc.run("s", lambda q, i=i: q.activation(
                    gt[:, :], outall[:, i * DM:(i + 1) * DM], AF.Square),
                    deps=[heads_done, slot.get("gt_c")])
                t_rd = sc.run("v", lambda q: q.tensor_reduce(
                    gsum[:, :], gt[:, :].rearrange("p (hh d) -> p hh d", d=HD),
                    axis=AX.X, op=ALU.add), deps=[t_sq])
                t_s1 = sc.run("v", lambda q: q.tensor_scalar(
                    grst[:, :], gsum[:, :], 1.0 / HD, 1e-6, ALU.mult, ALU.add),
                    deps=[t_rd])
                t_s2 = sc.run("s", lambda q: q.activation(
                    grst[:, :], grst[:, :], AF.Sqrt), deps=[t_s1])
                t_s3 = sc.run("v", lambda q: q.reciprocal(grst[:, :], grst[:, :]),
                              deps=[t_s2])

                t_si = sc.run("s", lambda q: q.activation(
                    gt[:, :], pq[:, 1024:2048], AF.Silu), deps=[t_gm, t_rd])
                slot["gps_c"] = t_si

                def f_ap(q, i=i):
                    ins = None
                    for h2 in range(H):
                        ins = q.tensor_scalar(
                            otmp[:, h2 * HD:(h2 + 1) * HD],
                            outall[:, i * DM + h2 * HD:i * DM + (h2 + 1) * HD],
                            grst[:, h2:h2 + 1], None, ALU.mult)
                    return ins
                t_ap = sc.run("v", f_ap, deps=[t_s3, slot.get("otmp_c")])

                t_mu2 = sc.run("v", lambda q: q.tensor_mul(otmp[:, :], otmp[:, :],
                                                           gt[:, :]),
                               deps=[t_ap, t_si])
                t_gt = t_mu2
                slot["gt_c"] = t_mu2

                t_fin = sc.run("v", lambda q: q.tensor_add(xt[:, :], otmp[:, :],
                                                           xt[:, :]),
                               deps=[t_mu2])
                slot["otmp_c"] = t_fin
                slot["outall_c"] = t_ap

                t_st = sc.dma("s", "sd", lambda q, i=i: [
                    q.dma_start(out=dst[base + i * 128:base + (i + 1) * 128, :],
                                in_=xt[:, :])], deps=[t_fin])
                slot["x_c"] = t_st
                slot["comb_ps"] = t_si
            slot["yfm_c"] = slot.get("rtm3_c")
            slot["rtm_c"] = slot.get("rtm3_c")
            slot["vtm_c"] = vv_last
            slot[("qk_c", "l0a_Wq")] = vv_last
            slot[("qk_c", "l0a_Wk")] = vv_last
            slot[("pp", 0)] = slot.get("gps_c")
            slot[("pp", 1)] = slot.get("gps_c")

    sc.barrier()



def emit_l1ffn(sc, G, src, dst):
    nc, w = G["nc"], G["w"]
    tag = G["tag"]
    TW = 1024

    with ExitStack() as cx:
        def lsb(name, shape, dt=F32):
            return cx.enter_context(nc.sbuf_tensor(f"{tag}_{name}", shape, dt))

        def lps(name, shape, dt=F32):
            return cx.enter_context(nc.psum_tensor(f"{tag}_{name}", shape, dt))

        wf = lsb("wf", [128, 4 * DM])
        wb = {}
        t_wc_all = []
        wslot = {}
        for wi, nm in enumerate(["l1f_WrT", "l1f_WkT", "l1f_WvT"]):
            wb[nm] = lsb(f"wb{wi}", [128, 8 * DM], BF16)
            for half in range(2):
                t_wl = sc.dma("q", "qd", lambda q, nm=nm, half=half: [
                    q.dma_start(out=wf[:, :].rearrange("p (cc d) -> p cc d", cc=4),
                                in_=w[nm][half * 512:(half + 1) * 512, :]
                                .rearrange("(cc p) d -> p cc d", p=128))],
                    deps=[wslot.get("wf_c")])
                t_wc = sc.run("g", lambda q, nm=nm, half=half: q.tensor_copy(
                    wb[nm][:, half * 4 * DM:(half + 1) * 4 * DM], wf[:, :]),
                    deps=[t_wl])
                wslot["wf_c"] = t_wc
                t_wc_all.append(t_wc)

        muf = lsb("muf", [128, 8 * 2])
        mur = lsb("mur", [128, 8 * 4])
        t_mu = sc.dma("q", "qd", lambda q: [
            q.dma_start(out=muf[:, 0:8], in_=w["l1f_mur"].rearrange("(cc p) -> p cc", p=128),
                        allow_slow_non_contiguous=True),
            q.dma_start(out=muf[:, 8:16], in_=w["l1f_muk"].rearrange("(cc p) -> p cc", p=128),
                        allow_slow_non_contiguous=True)])

        def f_mu(q):
            q.tensor_copy(mur[:, 0:8], muf[:, 0:8])
            q.tensor_scalar(mur[:, 8:16], muf[:, 0:8], -1.0, 1.0, ALU.mult, ALU.add)
            q.tensor_copy(mur[:, 16:24], muf[:, 8:16])
            return q.tensor_scalar(mur[:, 24:32], muf[:, 8:16], -1.0, 1.0, ALU.mult, ALU.add)
        t_muc = sc.run("v", f_mu, deps=[t_mu])

        yfm = lsb("yfm", [128, 8 * (S + 1)], BF16)
        xt = lsb("xt", [128, 2 * DM])
        yb = lsb("yb", [128, DM], BF16)
        xr = lsb("xr", [128, 8 * TW], BF16)
        xk = lsb("xk", [128, 8 * TW], BF16)
        t1 = lsb("t1", [128, 2 * TW])
        rr = lsb("rr", [128, 8 * TW], BF16)
        ka = lsb("ka", [128, 8 * TW], BF16)
        of = lsb("of", [128, 8 * TW], BF16)
        xo = lsb("xo", [128, 2 * DM])
        pa = lps("pa", [128, 2 * DM])
        pt = lps("pt", [128, DM], BF16)

        slot = {}
        for b in range(BC):
            base = b * S
            t_z = sc.run("g", lambda q: q.memset(
                yfm[:, :].rearrange("p (cc t) -> p cc t", t=S + 1)[:, :, 0:1], 0.0),
                deps=[slot.get("yfm_c")])

            tr_last = None
            for ts_ in range(16):
                par = ts_ % 2
                xv = xt[:, par * DM:(par + 1) * DM]
                t_x = sc.dma("q", "qd", lambda q, ts_=ts_, xv=xv: [
                    q.dma_start(out=xv, in_=src[base + ts_ * 128:base + (ts_ + 1) * 128, :])],
                    deps=[slot.get(("x", par))])
                t_ln = emit_ln(sc, G, xv, [yb[:, :]], 1e-5,
                               deps=[t_x, slot.get("yb_c")])
                slot[("x", par)] = t_ln

                def f_tr(q):
                    ins = None
                    for cc in range(8):
                        ins = q.transpose(pt[:, cc * 128:(cc + 1) * 128],
                                          yb[:, cc * 128:(cc + 1) * 128],
                                          G["ident_b"][:, :])
                    return ins
                t_tr = sc.run("t", f_tr, deps=[t_ln, slot.get("pt_ev")])
                slot["yb_c"] = t_tr

                def f_te(q, ts_=ts_):
                    ov = yfm[:, :].rearrange("p (cc t) -> p cc t", t=S + 1)[
                        :, :, 1 + ts_ * 128:1 + (ts_ + 1) * 128]
                    return q.tensor_copy(ov, pt[:, :].rearrange("p (cc t) -> p cc t", t=128))
                t_te = sc.run("v", f_te, deps=[t_tr, t_z])
                slot["pt_ev"] = t_te
                tr_last = t_te

            out_done = []
            for hh in range(2):
                yv = yfm[:, :].rearrange("p (cc t) -> p cc t", t=S + 1)
                t_xr = None
                for cc in range(8):
                    ys = yv[:, cc, 1 + hh * TW:1 + (hh + 1) * TW]
                    yp = yv[:, cc, hh * TW:(hh + 1) * TW]

                    def f_a(q, ys=ys, cc=cc):
                        q.tensor_scalar(t1[:, 0:TW], ys, mur[:, cc:cc + 1],
                                        None, ALU.mult)
                        return q.tensor_scalar(t1[:, TW:2 * TW], ys,
                                               mur[:, 16 + cc:16 + cc + 1],
                                               None, ALU.mult)
                    t_a = sc.run("v", f_a,
                                 deps=[tr_last, t_muc, slot.get("xr_c")] if cc == 0
                                 else [t_a0])

                    def f_b(q, yp=yp, cc=cc):
                        q.scalar_tensor_tensor(xr[:, cc * TW:(cc + 1) * TW], yp,
                                               mur[:, 8 + cc:8 + cc + 1], t1[:, 0:TW],
                                               ALU.mult, ALU.add)
                        return q.scalar_tensor_tensor(xk[:, cc * TW:(cc + 1) * TW], yp,
                                                      mur[:, 24 + cc:24 + cc + 1],
                                                      t1[:, TW:2 * TW],
                                                      ALU.mult, ALU.add)
                    t_xr = sc.run("v", f_b, deps=[t_a])
                    t_a0 = t_xr

                rk_done = []
                for (xin_, wnm, obuf, act) in [(xr, "l1f_WrT", rr, "sig"),
                                               (xk, "l1f_WkT", ka, "relu2")]:
                    for dd in range(8):
                        par = dd % 2
                        pv = pa[:, par * DM:(par + 1) * DM]

                        def f_mm(q, xin_=xin_, wnm=wnm, dd=dd, pv=pv):
                            ins = None
                            for cc in range(8):
                                for nc2 in range(2):
                                    ins = q.matmul(
                                        pv[:, nc2 * 512:(nc2 + 1) * 512],
                                        wb[wnm][:, cc * DM + dd * 128:cc * DM + (dd + 1) * 128],
                                        xin_[:, cc * TW + nc2 * 512:cc * TW + (nc2 + 1) * 512],
                                        start=(cc == 0), stop=(cc == 7))
                            return ins
                        t_mm = sc.run("t", f_mm,
                                      deps=[t_xr] + t_wc_all + [slot.get(("pa", par))])

                        if act == "sig":
                            t_ev = sc.run("s", lambda q, dd=dd, pv=pv: q.activation(
                                rr[:, dd * TW:(dd + 1) * TW], pv, AF.Sigmoid),
                                deps=[t_mm, slot.get("rr_c")])
                        else:
                            t_rl = sc.run("s", lambda q, dd=dd, pv=pv: q.activation(
                                G["scr_b"][:, :], pv, AF.Relu),
                                deps=[t_mm, slot.get("ka_c")])
                            t_ev = sc.run("s", lambda q, dd=dd: q.activation(
                                ka[:, dd * TW:(dd + 1) * TW], G["scr_b"][:, :],
                                AF.Square), deps=[t_rl], serialize=True)
                        slot[("pa", par)] = t_ev
                        rk_done.append(t_ev)

                slot["xr_c"] = rk_done[-1]

                o_done = []
                for dd in range(8):
                    par = dd % 2
                    pv = pa[:, par * DM:(par + 1) * DM]

                    def f_vm(q, dd=dd, pv=pv):
                        ins = None
                        for cc in range(8):
                            for nc2 in range(2):
                                ins = q.matmul(
                                    pv[:, nc2 * 512:(nc2 + 1) * 512],
                                    wb["l1f_WvT"][:, cc * DM + dd * 128:cc * DM + (dd + 1) * 128],
                                    ka[:, cc * TW + nc2 * 512:cc * TW + (nc2 + 1) * 512],
                                    start=(cc == 0), stop=(cc == 7))
                        return ins
                    t_vm = sc.run("t", f_vm, deps=rk_done + [slot.get(("pa", par))])

                    t_om = sc.run("v", lambda q, dd=dd, pv=pv: q.tensor_mul(
                        of[:, dd * TW:(dd + 1) * TW], rr[:, dd * TW:(dd + 1) * TW], pv),
                        deps=[t_vm, slot.get("of_c")])
                    slot[("pa", par)] = t_om
                    o_done.append(t_om)
                slot["rr_c"] = o_done[-1]
                slot["ka_c"] = o_done[-1]

                for ts2 in range(8):
                    gt = hh * 8 + ts2
                    par = gt % 2
                    xv = xt[:, par * DM:(par + 1) * DM]
                    xov = xo[:, par * DM:(par + 1) * DM]
                    t_x2 = sc.dma("q", "qd", lambda q, gt=gt, xv=xv: [
                        q.dma_start(out=xv,
                                    in_=src[base + gt * 128:base + (gt + 1) * 128, :])],
                        deps=[slot.get(("x", par))])

                    def f_tb(q, ts2=ts2):
                        ins = None
                        for dd in range(8):
                            ins = q.transpose(
                                pt[:, dd * 128:(dd + 1) * 128],
                                of[:, dd * TW + ts2 * 128:dd * TW + (ts2 + 1) * 128],
                                G["ident_b"][:, :])
                        return ins
                    t_tb = sc.run("t", f_tb, deps=o_done + [slot.get("pt_ev")])

                    t_rs = sc.run("v", lambda q, xv=xv, xov=xov: q.scalar_tensor_tensor(
                        xov, pt[:, :], 0.5, xv, ALU.mult, ALU.add),
                        deps=[t_tb, t_x2, slot.get(("xo", par))])
                    slot["pt_ev"] = t_rs
                    slot[("x", par)] = t_rs
                    if ts2 == 7:
                        slot["of_c"] = t_rs

                    t_xs = sc.dma("s", "sd", lambda q, gt=gt, xov=xov: [
                        q.dma_start(out=dst[base + gt * 128:base + (gt + 1) * 128, :],
                                    in_=xov)], deps=[t_rs])
                    slot[("xo", par)] = t_xs
                    out_done.append(t_xs)
            slot["yfm_c"] = out_done[-1]

    sc.barrier()



def emit_l1attn(sc, G, src, dst):
    nc, w = G["nc"], G["w"]
    tag = G["tag"]
    NH, HD1 = 8, 128

    with ExitStack() as cx:
        def lsb(name, shape, dt=F32):
            return cx.enter_context(nc.sbuf_tensor(f"{tag}_{name}", shape, dt))

        def lps(name, shape, dt=F32):
            return cx.enter_context(nc.psum_tensor(f"{tag}_{name}", shape, dt))

        yfm = lsb("yfm", [128, 8 * S], BF16)
        outall = lsb("outall", [128, 16 * DM], BF16)
        wsl = lsb("wsl", [128, 8 * DM], BF16)
        wf = lsb("wf", [128, DM])
        xt = lsb("xt", [128, DM])
        yb = lsb("yb", [128, DM], BF16)
        rtm = lsb("rtm", [128, 2 * DM], BF16)
        qro = lsb("qro", [128, 2 * DM], BF16)
        l1tb = lsb("l1tb", [128, 4 * DM], BF16)
        qTs = lsb("qTs", [128, 2048], BF16)
        kTs = lsb("kTs", [128, 2048], BF16)
        vh = lsb("vh", [128, 2048], BF16)
        ptsb = lsb("ptsb", [128, 2 * 1024], BF16)
        osb = lsb("osb", [128, 1024], BF16)
        denr = lsb("denr", [1, 1024])
        denc = lsb("denc", [128, 8])
        rdn = lsb("rdn", [128, 8])
        pq = lps("pq", [128, 2048])
        pk = lps("pk", [128, 2048])

        t_tb = sc.dma("q", "qd", lambda q: [
            q.dma_start(out=l1tb[:, 0:DM], in_=w["l1_cA"][0:128, :]),
            q.dma_start(out=l1tb[:, DM:2 * DM], in_=w["l1_cA"][128:256, :]),
            q.dma_start(out=l1tb[:, 2 * DM:3 * DM], in_=w["l1_cB"][0:128, :]),
            q.dma_start(out=l1tb[:, 3 * DM:4 * DM], in_=w["l1_cB"][128:256, :])])

        slot = {}
        for b in range(BC):
            base = b * S

            def load_w(nm, dep_consumed):
                toks = []
                for cc in range(8):
                    t_l = sc.dma("q", "qd", lambda q, nm=nm, cc=cc: [
                        q.dma_start(out=wf[:, :],
                                    in_=w[nm][cc * 128:(cc + 1) * 128, :])],
                        deps=[dep_consumed, slot.get("wf_c")])
                    t_c = sc.run("g", lambda q, cc=cc: q.tensor_copy(
                        wsl[:, cc * DM:(cc + 1) * DM], wf[:, :]), deps=[t_l])
                    slot["wf_c"] = t_c
                    toks.append(t_c)
                return toks

            for ts_ in range(16):
                t_x = sc.dma("q", "qd", lambda q, ts_=ts_: [
                    q.dma_start(out=xt[:, :],
                                in_=src[base + ts_ * 128:base + (ts_ + 1) * 128, :])],
                    deps=[slot.get("x_c")])
                t_ln = emit_ln(sc, G, xt[:, :], [yb[:, :]], 1e-5,
                               deps=[t_x, slot.get("yb_c")])
                slot["x_c"] = t_ln

                tp = pk[:, 0:512].bitcast(BF16)

                def f_tr(q, tp=tp):
                    ins = None
                    for cc in range(8):
                        ins = q.transpose(tp[:, cc * 128:(cc + 1) * 128],
                                          yb[:, cc * 128:(cc + 1) * 128],
                                          G["ident_b"][:, :])
                    return ins
                t_tr = sc.run("t", f_tr, deps=[t_ln, slot.get("tp_ev")])
                slot["yb_c"] = t_tr

                def f_te(q, ts_=ts_, tp=tp):
                    ov = yfm[:, :].rearrange("p (cc t) -> p cc t", t=S)[
                        :, :, ts_ * 128:(ts_ + 1) * 128]
                    return q.tensor_copy(ov, tp.rearrange("p (cc t) -> p cc t", t=128))
                t_te = sc.run("v", f_te, deps=[t_tr, slot.get("yfm_c")])
                slot["tp_ev"] = t_te
            t_yfm = ("v", sc.count["v"])

            for wi, (wnm, qi, rope) in enumerate([("l1a_W1T", 0, True),
                                                  ("l1a_W2T", 1, True),
                                                  ("l1a_W3T", 2, False)]):
                wtoks = load_w(wnm, slot.get("wsl_c"))
                for ts_ in range(16):
                    par = ts_ % 2
                    pv = pq[:, par * 1024:(par + 1) * 1024]
                    qv = qro[:, par * DM:(par + 1) * DM]

                    def f_mm(q, ts_=ts_, pv=pv):
                        ins = None
                        for cc in range(8):
                            lh = yfm[:, cc * S + ts_ * 128:cc * S + (ts_ + 1) * 128]
                            for n2 in range(2):
                                ins = q.matmul(
                                    pv[:, n2 * 512:(n2 + 1) * 512], lh,
                                    wsl[:, cc * DM + n2 * 512:cc * DM + (n2 + 1) * 512],
                                    start=(cc == 0), stop=(cc == 7))
                        return ins
                    t_mm = sc.run("t", f_mm,
                                  deps=[t_yfm] + wtoks + [slot.get(("pp", par))])

                    if rope:
                        tb = ts_ % 2
                        cav = l1tb[:, tb * DM:(tb + 1) * DM]
                        cbv = l1tb[:, (2 + tb) * DM:(3 + tb) * DM]

                        def f_r1(q, pv=pv, cav=cav, cbv=cbv):
                            q.tensor_mul(rtm[:, 0:DM], pv, cav)
                            pvv = pv.rearrange("p (a c) -> p a c", c=2)
                            bvv = cbv.rearrange("p (a c) -> p a c", c=2)
                            mvv = rtm[:, DM:2 * DM].rearrange("p (a c) -> p a c", c=2)
                            q.tensor_mul(mvv[:, :, 0:1], pvv[:, :, 1:2], bvv[:, :, 0:1])
                            return q.tensor_mul(mvv[:, :, 1:2], pvv[:, :, 0:1],
                                                bvv[:, :, 1:2])
                        t_r1 = sc.run("v", f_r1, deps=[t_mm, t_tb, slot.get("rtm_c")])
                        t_r2 = sc.run("v", lambda q, qv=qv: q.tensor_add(
                            qv, rtm[:, 0:DM], rtm[:, DM:2 * DM]),
                            deps=[t_r1, slot.get(("qro", par))])
                        slot["rtm_c"] = t_r2
                        slot[("pp", par)] = t_r1
                        t_out = t_r2
                    else:
                        t_out = sc.run("v", lambda q, pv=pv, qv=qv: q.tensor_copy(qv, pv),
                                       deps=[t_mm, slot.get(("qro", par))])
                        slot[("pp", par)] = t_out

                    t_sto = sc.dma("s", "sd", lambda q, ts_=ts_, qi=qi, qv=qv: [
                        q.dma_start(out=G["qkv"][qi, ts_ * 128:(ts_ + 1) * 128, :],
                                    in_=qv)], deps=[t_out])
                    slot[("qro", par)] = t_sto
                slot["wsl_c"] = t_mm
            qkv_done = ("sd", sc.count["sd"])

            wo_toks = load_w("l1a_WoT", slot.get("wsl_c"))

            heads_done = None
            for h in range(NH):
                t_qT = sc.dma("q", "qd", lambda q, h=h: [
                    q.dma_start_transpose(
                        out=qTs[:, :],
                        in_=G["qkv"][0, 256 * h:256 * (h + 1), :]
                        .rearrange("u (w d) -> (u w) d", d=128))],
                    deps=[qkv_done, slot.get("qT_c")])
                t_kT = sc.dma("q", "qd", lambda q, h=h: [
                    q.dma_start_transpose(
                        out=kTs[:, :],
                        in_=G["qkv"][1, 256 * h:256 * (h + 1), :]
                        .rearrange("u (w d) -> (u w) d", d=128))],
                    deps=[qkv_done, slot.get("kT_c")])
                t_v = sc.dma("q", "qd", lambda q, h=h: [
                    q.dma_start(out=vh[:, :].rearrange("p (tt d) -> p tt d", tt=16),
                                in_=G["qkv"][2, 256 * h:256 * (h + 1), :]
                                .rearrange("(tt u2) (w d) -> (u2 w) tt d",
                                           tt=16, u2=16, w=8))],
                    deps=[qkv_done, slot.get("vh_c")])

                for qh in range(2):
                    vv_last = None
                    for kt in range(16):
                        par = kt % 2
                        stp = pq[:, par * 1024:(par + 1) * 1024]

                        def f_st(q, kt=kt, qh=qh, stp=stp):
                            lh = kTs[:, kt * 128:(kt + 1) * 128]
                            ins = None
                            for n2 in range(2):
                                ins = q.matmul(
                                    stp[:, n2 * 512:(n2 + 1) * 512], lh,
                                    qTs[:, qh * 1024 + n2 * 512:qh * 1024 + (n2 + 1) * 512],
                                    start=True, stop=True)
                            return ins
                        dep_st = [t_qT, t_kT, slot.get(("stB", par))]
                        if kt < 2 and qh == 0 and h == 0:
                            dep_st += [slot.get("pq_free"), slot.get("tp_ev"),
                                       slot.get(("pp", 0)), slot.get(("pp", 1))]
                        t_st = sc.run("t", f_st, deps=dep_st)

                        t_ex = sc.run("s", lambda q, stp=stp, par=par: q.activation(
                            ptsb[:, par * 1024:(par + 1) * 1024], stp, AF.Exp,
                            bias=G["c20"][:, :], scale=float(1.0 / math.sqrt(8.0))),
                            deps=[t_st, slot.get(("ptsb", par))])
                        slot[("stB", par)] = t_ex

                        def f_dn(q, kt=kt, par=par):
                            ins = None
                            for n2 in range(2):
                                ins = q.matmul(
                                    pk[0:1, 1024 + n2 * 512:1024 + (n2 + 1) * 512],
                                    G["onec"][:, :],
                                    ptsb[:, par * 1024 + n2 * 512:par * 1024 + (n2 + 1) * 512],
                                    start=(kt == 0), stop=(kt == 15))
                            return ins
                        t_dn = sc.run("t", f_dn, deps=[t_ex, slot.get("dn_c")])

                        def f_vv(q, kt=kt, par=par):
                            ins = None
                            for n2 in range(2):
                                ins = q.matmul(
                                    pk[:, n2 * 512:(n2 + 1) * 512],
                                    vh[:, kt * 128:(kt + 1) * 128],
                                    ptsb[:, par * 1024 + n2 * 512:par * 1024 + (n2 + 1) * 512],
                                    start=(kt == 0), stop=(kt == 15))
                            return ins
                        t_vv = sc.run("t", f_vv, deps=[t_v, t_ex, slot.get("ot_c")])
                        slot[("ptsb", par)] = t_vv
                        vv_last = t_vv

                    t_dc = sc.run("v", lambda q: q.tensor_copy(denr[:, :],
                                                               pk[0:1, 1024:2048]),
                                  deps=[t_dn, slot.get("denr_c")])
                    slot["dn_c"] = t_dc

                    def f_dt(q):
                        ins = None
                        for j in range(8):
                            ins = q.transpose(pq[:, 1536 + j:1536 + j + 1],
                                              denr[:, j * 128:(j + 1) * 128],
                                              G["ident_f"][0:1, 0:1])
                        return ins
                    t_dt = sc.run("t", f_dt, deps=[t_dc, vv_last])
                    t_dc2 = sc.run("v", lambda q: q.tensor_copy(denc[:, :],
                                                                pq[:, 1536:1544]),
                                   deps=[t_dt])
                    t_rd = sc.run("v", lambda q: q.reciprocal(rdn[:, :], denc[:, :]),
                                  deps=[t_dc2])
                    slot["denr_c"] = t_rd

                    t_oe = sc.run("v", lambda q: q.tensor_copy(osb[:, :],
                                                               pk[:, 0:1024]),
                                  deps=[vv_last, slot.get("osb_c")])
                    slot["ot_c"] = t_oe
                    if G.get("debug") and b == BC - 1 and h == 0 and qh == 0:
                        t_oe = sc.dma("g", "sd", lambda q: [
                            q.dma_start(out=G["osb_dbg"][:, :], in_=osb[:, :]),
                            q.dma_start(out=G["pt_dbg"][:, :], in_=ptsb[:, :])],
                            deps=[t_oe])
                        slot["ot_c"] = t_oe

                    tr_last = None
                    for grp in range(2):
                        base_c = 1024 if grp == 0 else 1792
                        tpo = pq[:, base_c:base_c + 256].bitcast(BF16)

                        def f_ot(q, grp=grp, tpo=tpo):
                            ins = None
                            for j in range(4):
                                i = grp * 4 + j
                                ins = q.transpose(tpo[:, j * 128:(j + 1) * 128],
                                                  osb[:, i * 128:(i + 1) * 128],
                                                  G["ident_b"][:, :])
                            return ins
                        t_ot = sc.run("t", f_ot, deps=[t_oe, slot.get(("otT", grp)),
                                                       t_dt, t_dc2])

                        def f_os(q, grp=grp, tpo=tpo, h=h, qh=qh):
                            ins = None
                            for j in range(4):
                                i = grp * 4 + j
                                tile = qh * 8 + i
                                ins = q.tensor_scalar(
                                    outall[:, tile * DM + h * HD1:tile * DM + (h + 1) * HD1],
                                    tpo[:, j * 128:(j + 1) * 128],
                                    rdn[:, i:i + 1], None, ALU.mult)
                            return ins
                        t_os = sc.run("v", f_os, deps=[t_ot, t_rd, slot.get("outall_c")])
                        slot[("otT", grp)] = t_os
                        tr_last = t_os
                    slot["osb_c"] = tr_last
                    slot[("stB", 1)] = tr_last
                    heads_done = tr_last
                slot["qT_c"] = vv_last
                slot["kT_c"] = vv_last
                slot["vh_c"] = vv_last
            slot["pq_free"] = heads_done

            if G.get("debug") and b == BC - 1:
                heads_done = sc.dma("g", "sd", lambda q: [
                    q.dma_start(out=G["oa_dbg"][:, :], in_=outall[:, :])],
                    deps=[heads_done])
            for sp in range(16):
                half = sp // 8
                spc = sp % 8
                par = sp % 2
                fp = pk[:, par * 1024:(par + 1) * 1024]

                def f_f(q, half=half, spc=spc, fp=fp):
                    ins = None
                    for jc in range(8):
                        tile = half * 8 + jc
                        for n2 in range(2):
                            ins = q.matmul(
                                fp[:, n2 * 512:(n2 + 1) * 512],
                                wsl[:, jc * DM + spc * 128:jc * DM + (spc + 1) * 128],
                                outall[:, tile * DM + n2 * 512:tile * DM + (n2 + 1) * 512],
                                start=(jc == 0), stop=(jc == 7))
                    return ins
                dep_f = [heads_done] + wo_toks + [slot.get(("fp", par))]
                if sp < 2:
                    dep_f += [slot.get("ot_c"), slot.get("dn_c")]
                t_f = sc.run("t", f_f, deps=dep_f)

                t_x2 = sc.dma("q", "qd", lambda q, sp=sp: [
                    q.dma_start(out=xt[:, :],
                                in_=src[base + sp * 128:base + (sp + 1) * 128, :])],
                    deps=[slot.get("x_c")])

                t_rs = sc.run("v", lambda q, fp=fp: q.tensor_tensor(
                    xt[:, :], fp, xt[:, :], ALU.add),
                    deps=[t_f, t_x2])
                slot[("fp", par)] = t_rs

                t_st2 = sc.dma("s", "sd", lambda q, sp=sp: [
                    q.dma_start(out=dst[base + sp * 128:base + (sp + 1) * 128, :],
                                in_=xt[:, :])], deps=[t_rs])
                slot["x_c"] = t_st2
            slot["yfm_c"] = slot.get("wsl_c")
            slot["outall_c"] = t_f
            slot["wsl_c"] = t_f
            slot["tp_ev"] = slot.get("tp_ev")

    sc.barrier()


def build_program(phases=(0, 1, 2, 3, 4, 5), debug=False):
    nc = bass.Bass("TRN2", target_bir_lowering=False)
    ap = lambda h: h.ap()

    xin = ap(nc.dram_tensor("xin", [T, DM], F32, kind="ExternalInput"))
    xout = ap(nc.dram_tensor("xout", [T, DM], F32, kind="ExternalOutput"))
    w = {}
    for name, shape in [
        ("l0f_Wr", [DM, 4]), ("l0f_W1", [NE, DM, I_FF]), ("l0f_b1", [NE, I_FF]),
        ("l0f_Wo", [I_FF, DM]), ("l0f_bo", [DM]), ("l0f_g", [DM]), ("l0f_b", [DM]),
        ("l0a_Wq", [DM, DM]), ("l0a_Wk", [DM, DM]), ("l0a_Wv", [DM, DM]),
        ("l0a_Wg", [DM, DM]),
        ("l1a_W1T", [DM, DM]), ("l1a_W2T", [DM, DM]), ("l1a_W3T", [DM, DM]),
        ("l1a_WoT", [DM, DM]),
        ("l1f_WrT", [DM, DM]), ("l1f_WkT", [DM, DM]), ("l1f_WvT", [DM, DM]),
        ("l1f_mur", [DM]), ("l1f_muk", [DM]),
    ]:
        w[name] = ap(nc.dram_tensor(name, shape, F32, kind="ExternalInput"))
    for name, arr in TABLES.items():
        w[name] = ap(nc.dram_tensor(name, list(arr.shape), BF16, kind="ExternalInput"))

    dk = "ExternalOutput" if debug else "Internal"
    xs = ap(nc.dram_tensor("xs", [T, DM], F32, kind="Internal"))
    yfm_hbm = ap(nc.dram_tensor("yfm_hbm", [NTT, 8, 128, 128], BF16, kind=dk))
    interm_hbm = ap(nc.dram_tensor("interm_hbm", [8, NTT, 128, 512], BF16,
                                   kind=dk))
    qkv_hbm = ap(nc.dram_tensor("qkv_hbm", [3, S, DM], BF16, kind=dk))
    masks_dbg = ap(nc.dram_tensor("masks_dbg", [128, NTT * 4], F32, kind=dk)) if debug else None
    lgs_dbg = ap(nc.dram_tensor("lgs_dbg", [NTT, 128, 4], F32, kind=dk)) if debug else None
    zsb_dbg = ap(nc.dram_tensor("zsb_dbg", [NTT, 128, DM], F32, kind=dk)) if debug else None
    oa_dbg = ap(nc.dram_tensor("oa_dbg", [128, 16 * DM], F32, kind=dk)) if debug else None
    osb_dbg = ap(nc.dram_tensor("osb_dbg", [128, 1024], F32, kind=dk)) if debug else None
    pt_dbg = ap(nc.dram_tensor("pt_dbg", [128, 2048], F32, kind=dk)) if debug else None
    yb_dbg = ap(nc.dram_tensor("yb_dbg", [4, 128, DM], F32, kind=dk)) if debug else None
    y2_dbg = ap(nc.dram_tensor("y2_dbg", [NTT, 128, DM], F32, kind=dk)) if debug else None

    phases = sorted(phases)

    with ExitStack() as ctx:
        block = ctx.enter_context(nc.Block())
        sc = Sched(nc, ctx, block)

        G = {"nc": nc, "w": w, "yfm": yfm_hbm, "interm": interm_hbm, "qkv": qkv_hbm,
             "lgs_dbg": lgs_dbg, "zsb_dbg": zsb_dbg, "y2_dbg": y2_dbg,
             "yb_dbg": yb_dbg, "oa_dbg": oa_dbg, "osb_dbg": osb_dbg,
             "pt_dbg": pt_dbg, "debug": debug}
        for nm, shape, dt in [
            ("ident_b", [128, 128], BF16), ("ident_f", [128, 128], F32),
            ("ones1", [1, 128], BF16), ("zrow", [1, 512], BF16),
            ("onec", [128, 1], BF16), ("c20", [128, 1], F32),
            ("masks", [128, NTT * 4], F32),
            ("tril_s", [128, 128], BF16),
            ("trilT_s", [128, 128], BF16), ("stat", [128, 8], F32),
            ("scr_b", [128, DM], BF16),
        ]:
            G[nm.replace("_s", "") if nm.endswith("_s") else nm] = ctx.enter_context(nc.sbuf_tensor(nm, shape, dt))

        t_tri = sc.dma("q", "qd", lambda q: [
            q.dma_start(out=G["tril"][:, :], in_=w["tril"][:, :]),
            q.dma_start(out=G["trilT"][:, :], in_=w["trilT"][:, :])])

        def f_init(q):
            q.memset(G["ones1"][:, :], 1.0)
            q.memset(G["zrow"][:, :], 0.0)
            q.memset(G["onec"][:, :], 1.0)
            q.memset(G["c20"][:, :], -20.0)
            q.memset(G["ident_b"][:, :], 0.0)
            q.affine_select(out=G["ident_b"][:, :], in_=G["ident_b"][:, :],
                            compare_op=ALU.not_equal, fill=1.0, base=0,
                            pattern=[[-1, 128]], channel_multiplier=1)
            q.memset(G["ident_f"][:, :], 0.0)
            return q.affine_select(out=G["ident_f"][:, :], in_=G["ident_f"][:, :],
                                   compare_op=ALU.not_equal, fill=1.0, base=0,
                                   pattern=[[-1, 128]], channel_multiplier=1)
        sc.run("g", f_init, deps=[t_tri])

        sc.barrier()

        emitters = {0: emit_l0ffn, 1: emit_l0attn, 2: emit_l0ffn,
                    3: emit_l1ffn, 4: emit_l1attn, 5: emit_l1ffn}
        for ph in phases:
            src = xin if ph == phases[0] else xs
            dst = xout if ph == phases[-1] else xs
            G["tag"] = f"p{ph}"
            emitters[ph](sc, G, src, dst)

        if debug:
            sc.dma("s", "sd", lambda q: [
                q.dma_start(out=masks_dbg[:, :], in_=G["masks"][:, :])])

        @block.sync
        def _(q):
            q.wait_ge(sc.sem["sd"], sc.count["sd"])
            q.wait_ge(sc.sem["qd"], sc.count["qd"])
    return nc


def make_inmap(core, inputs):
    x = np.asarray(inputs["L0_attn_x"])
    xsh = np.ascontiguousarray(x.reshape(32, S, DM)[core * BC:(core + 1) * BC]
                               .reshape(T, DM))
    m = {"xin": xsh,
         "l0f_Wr": inputs["L0_ffn_W_router"], "l0f_W1": inputs["L0_ffn_W1"],
         "l0f_b1": inputs["L0_ffn_b1"], "l0f_Wo": inputs["L0_ffn_Wo"],
         "l0f_bo": inputs["L0_ffn_bo"], "l0f_g": inputs["L0_ffn_ln_gamma"],
         "l0f_b": inputs["L0_ffn_ln_beta"],
         "l0a_Wq": inputs["L0_attn_Wq"], "l0a_Wk": inputs["L0_attn_Wk"],
         "l0a_Wv": inputs["L0_attn_Wv"], "l0a_Wg": inputs["L0_attn_Wg"],
         "l1a_W1T": np.asarray(inputs["L1_attn_W1"]).T,
         "l1a_W2T": np.asarray(inputs["L1_attn_W2"]).T,
         "l1a_W3T": np.asarray(inputs["L1_attn_W3"]).T,
         "l1a_WoT": np.asarray(inputs["L1_attn_Wo"]).T,
         "l1f_WrT": np.asarray(inputs["L1_ffn_W_r"]).T,
         "l1f_WkT": np.asarray(inputs["L1_ffn_W_k"]).T,
         "l1f_WvT": np.asarray(inputs["L1_ffn_W_v"]).T,
         "l1f_mur": inputs["L1_ffn_mu_r"], "l1f_muk": inputs["L1_ffn_mu_k"]}
    for k, v in TABLES.items():
        m[k] = v
    return {k: np.ascontiguousarray(np.asarray(v)) for k, v in m.items()}


def kernel(**inputs):
    nc = build_program(phases=(0, 1, 2, 3, 4, 5))
    in_maps = [make_inmap(core, inputs) for core in range(8)]
    res = run_bass_kernel_spmd(nc, in_maps, core_ids=list(range(8)))
    out = np.concatenate([np.asarray(r["xout"]).reshape(BC, S, DM)
                          for r in res.results], axis=0)
    return out.astype(np.float32)


# version 02
# speedup vs baseline: 1.1456x; 1.1456x over previous
import math
from contextlib import ExitStack

import numpy as np
import ml_dtypes
import concourse.bass as bass
import concourse.mybir as mybir
from concourse.bass_utils import run_bass_kernel_spmd

F32 = mybir.dt.float32
BF16 = mybir.dt.bfloat16
AF = mybir.ActivationFunctionType
ALU = mybir.AluOpType
AX = mybir.AxisListType

BC = 4
S = 2048
T = BC * S
DM = 1024
NTT = T // 128
I_FF = 4096
NE = 4

BF = ml_dtypes.bfloat16


def host_tables():
    t = {}
    H, D = 16, 64
    a = 1.0 / 10000.0 ** np.linspace(0.0, 1.0, D // 2)
    angle = np.repeat(a, 2)
    decay = np.log(1.0 - 2.0 ** (-5.0 - np.arange(H, dtype=np.float64)))
    idx = np.arange(S, dtype=np.float64)
    sin = np.sin(idx[:, None] * angle[None, :])
    cos = np.cos(idx[:, None] * angle[None, :])
    sinp = sin.copy()
    sinp[:, 0::2] *= -1.0
    qdec = np.exp(idx[:, None] * decay[None, :])
    ed = np.exp(decay)[None, :]
    rowsum = (1.0 - np.exp((idx[:, None] + 1.0) * decay[None, :])) / (1.0 - ed)
    qdec = qdec / np.sqrt(rowsum)
    kdec = np.exp(-idx[:, None] * decay[None, :]) * (DM ** -0.5)
    cosT = np.tile(cos, (1, H))
    sinT = np.tile(sinp, (1, H))
    qdE = np.repeat(qdec, D, axis=1)
    kdE = np.repeat(kdec, D, axis=1)
    t["l0_qA"] = (cosT * qdE).astype(BF)
    t["l0_qB"] = (sinT * qdE).astype(BF)
    t["l0_kA"] = (cosT * kdE).astype(BF)
    t["l0_kB"] = (sinT * kdE).astype(BF)
    D1 = 128
    inv = 1.0 / 10000.0 ** (np.arange(0, D1, 2, dtype=np.float64) / D1)
    tt = np.arange(S, dtype=np.float64)
    freqs = np.repeat(tt[:, None] * inv[None, :], 2, axis=-1)
    c1 = np.cos(freqs)
    s1 = np.sin(freqs)
    s1[:, 0::2] *= -1.0
    t["l1_cA"] = np.ascontiguousarray(c1.reshape(256, 1024)).astype(BF)
    t["l1_cB"] = np.ascontiguousarray(s1.reshape(256, 1024)).astype(BF)
    tri = np.tril(np.ones((128, 128), np.float32))
    t["tril"] = tri.astype(BF)
    t["trilT"] = np.ascontiguousarray(tri.T).astype(BF)
    return t


TABLES = host_tables()
PHASE_MARKS = []

_SECTION = {"t": "tensor", "v": "vector", "s": "scalar", "g": "gpsimd", "q": "sync"}
_OWN_SEM = {"t": "t", "v": "v", "s": "s", "g": "g", "q": "q"}


class Sched:
    def __init__(self, nc, ctx, block):
        self.nc = nc
        self.block = block
        self.sem = {}
        self.count = {}
        self.waited = {}
        self.pending_barrier = {}
        for name in ["t", "v", "s", "g", "q", "qd", "sd"]:
            self.sem[name] = ctx.enter_context(nc.semaphore("sem_" + name))
            self.count[name] = 0

    def barrier(self):
        toks = [(n, self.count[n]) for n in self.sem if self.count[n] > 0]
        for eng in _SECTION:
            self.pending_barrier[eng] = toks

    def _emit_waits(self, q, eng, deps):
        alldeps = list(deps)
        if eng in self.pending_barrier:
            alldeps += self.pending_barrier.pop(eng)
        own = _OWN_SEM[eng]
        best = {}
        for d in alldeps:
            if d is None:
                continue
            sname, val = d
            if val <= 0:
                continue
            if sname == own:
                val = min(val, self.count[own])
            if val <= 0:
                continue
            best[sname] = max(best.get(sname, 0), val)
        for sname, val in best.items():
            key = (eng, sname)
            if self.waited.get(key, 0) >= val:
                continue
            q.wait_ge(self.sem[sname], val)
            self.waited[key] = val

    def run(self, eng, fn, deps=(), serialize=False):
        sec = getattr(self.block, _SECTION[eng])
        own = _OWN_SEM[eng]
        prev = self.count[own]

        @sec
        def _(q):
            if serialize and prev > 0:
                q.wait_ge(self.sem[own], prev)
            self._emit_waits(q, eng, deps)
            last = fn(q)
            last.then_inc(self.sem[own], 1)
        self.count[own] += 1
        return (own, self.count[own])

    def dma(self, queue_eng, sem_name, fn, deps=()):
        sec = getattr(self.block, _SECTION[queue_eng])

        @sec
        def _(q):
            self._emit_waits(q, queue_eng, deps)
            for ins in fn(q):
                ins.then_inc(self.sem[sem_name], 16)
                self.count[sem_name] += 16
        return (sem_name, self.count[sem_name])


def emit_ln(sc, G, xv, ybufs, eps, deps):
    stat, scr = G["stat"], G["scr_b"]

    def f1(q):
        q.activation(scr[:, :], xv, AF.Identity, accum_out=stat[:, 0:1])
        return q.activation(scr[:, :], xv, AF.Square, accum_out=stat[:, 1:2])
    t1 = sc.run("s", f1, deps=deps)

    def f2(q):
        q.tensor_scalar_mul(stat[:, 2:3], stat[:, 0:1], 1.0 / DM)
        return q.tensor_scalar_mul(stat[:, 3:4], stat[:, 1:2], 1.0 / DM)
    t2 = sc.run("v", f2, deps=[t1])

    t3 = sc.run("s", lambda q: q.activation(stat[:, 4:5], stat[:, 2:3], AF.Square),
                deps=[t2])

    t4 = sc.run("v", lambda q: q.scalar_tensor_tensor(
        stat[:, 3:4], stat[:, 3:4], float(eps), stat[:, 4:5],
        ALU.add, ALU.subtract), deps=[t3, t2])

    t5 = sc.run("s", lambda q: q.activation(stat[:, 4:5], stat[:, 3:4], AF.Sqrt),
                deps=[t4])

    t6 = sc.run("v", lambda q: q.reciprocal(stat[:, 5:6], stat[:, 4:5]), deps=[t5])

    def f7(q):
        last = None
        for buf in ybufs:
            last = q.tensor_scalar(buf, xv, stat[:, 2:3], stat[:, 5:6],
                                   ALU.subtract, ALU.mult)
        return last
    return sc.run("v", f7, deps=[t6])


def emit_l0ffn(sc, G, src, dst):
    nc, w = G["nc"], G["w"]
    tag = G["tag"]

    def scope():
        return ExitStack()

    with scope() as cx:
        def lsb(name, shape, dt=F32):
            return cx.enter_context(nc.sbuf_tensor(f"{tag}_{name}", shape, dt))

        def lps(name, shape, dt=F32):
            return cx.enter_context(nc.psum_tensor(f"{tag}_{name}", shape, dt))

        xt = lsb("xt", [128, 2 * DM])
        yb = lsb("yb", [128, DM], BF16)
        yf32 = lsb("yf32", [128, DM])
        ylo = lsb("ylo", [128, DM], BF16)
        yfm = lsb("yfm", [128, 2 * DM], BF16)
        ylofm = lsb("ylofm", [128, DM], BF16)
        wrf = lsb("wrf", [128, 8 * 4])
        wrhi = lsb("wrhi", [128, 8 * 4], BF16)
        wrlo = lsb("wrlo", [128, 8 * 4], BF16)
        wrlof = lsb("wrlof", [128, 8 * 4])
        lgs = lsb("lgs", [128, 4])
        mtmp = lsb("mtmp", [128, 2])
        tp = lps("tp", [128, 2 * DM], BF16)
        lg = lps("lg", [128, 4])

        t_wr = sc.dma("q", "qd", lambda q: [
            q.dma_start(out=wrf[:, :].rearrange("p (cc e) -> p cc e", cc=8),
                        in_=w["l0f_Wr"].rearrange("(cc p) e -> p cc e", p=128))])

        t_wr1 = sc.run("g", lambda q: q.tensor_copy(wrhi[:, :], wrf[:, :]),
                       deps=[t_wr])
        t_wr2 = sc.run("g", lambda q: q.tensor_sub(wrlof[:, :], wrf[:, :], wrhi[:, :]),
                       deps=[t_wr1])
        t_wrc = sc.run("g", lambda q: q.tensor_copy(wrlo[:, :], wrlof[:, :]),
                       deps=[t_wr2])

        slot = {}
        t_st = None
        for tt in range(NTT):
            par = tt % 2
            xv = xt[:, par * DM:(par + 1) * DM]
            yfmv = yfm[:, par * DM:(par + 1) * DM]

            t_x = sc.dma("q", "qd", lambda q, tt=tt, xv=xv: [
                q.dma_start(out=xv, in_=src[tt * 128:(tt + 1) * 128, :])],
                deps=[slot.get(("x", par))])

            t_ln = emit_ln(sc, G, xv, [yb[:, :], yf32[:, :]], 1e-5,
                           deps=[t_x, slot.get("yb_c")])
            slot[("x", par)] = t_ln

            t_lo = sc.run("v", lambda q: q.tensor_sub(ylo[:, :], yf32[:, :], yb[:, :]),
                          deps=[t_ln])


            def f_tr(q):
                for cc in range(8):
                    q.transpose(tp[:, cc * 128:(cc + 1) * 128],
                                yb[:, cc * 128:(cc + 1) * 128], G["ident_b"][:, :])
                ins = None
                for cc in range(8):
                    ins = q.transpose(tp[:, DM + cc * 128:DM + (cc + 1) * 128],
                                      ylo[:, cc * 128:(cc + 1) * 128], G["ident_b"][:, :])
                return ins
            t_tr = sc.run("t", f_tr, deps=[t_lo, slot.get("tp_ev")])
            slot["yb_c"] = t_tr
            if G.get("debug") and tt == 47:
                t_tap = sc.dma("g", "sd", lambda q, xv=xv: [
                    q.dma_start(out=G["yb_dbg"][0], in_=yb[:, :]),
                    q.dma_start(out=G["yb_dbg"][1], in_=yf32[:, :]),
                    q.dma_start(out=G["yb_dbg"][2], in_=xv),
                    q.dma_start(out=G["yb_dbg"][3][:, 0:8], in_=G["stat"][:, :])], deps=[t_tr])
                slot["yb_c"] = t_tap
                slot[("x", tt % 2)] = t_tap

            def f_ev(q, yfmv=yfmv):
                q.tensor_copy(yfmv, tp[:, 0:DM])
                return q.tensor_copy(ylofm[:, :], tp[:, DM:2 * DM])
            t_ev = sc.run("v", f_ev,
                          deps=[t_tr, slot.get(("yfm", par)), slot.get("ylofm_c")])
            slot["tp_ev"] = t_ev

            def f_rt(q, yfmv=yfmv):
                prods = [(yfmv, wrhi), (yfmv, wrlo), (ylofm[:, :], wrhi)]
                ins = None
                n = 0
                for (lh, rh) in prods:
                    for cc in range(8):
                        n += 1
                        ins = q.matmul(lg[:, :], lh[:, cc * 128:(cc + 1) * 128],
                                       rh[:, cc * 4:(cc + 1) * 4],
                                       start=(n == 1), stop=(n == 24))
                return ins
            t_rt = sc.run("t", f_rt, deps=[t_ev, t_wrc, slot.get("lg_ev")])
            slot["ylofm_c"] = t_rt

            t_cp = sc.run("v", lambda q: q.tensor_copy(lgs[:, :], lg[:, :]),
                          deps=[t_rt])
            t_mx = sc.run("v", lambda q: q.reduce_max(
                G["stat"][:, 6:7], lgs[:, :], axis=AX.X),
                deps=[t_cp], serialize=True)

            def f_mk(q, tt=tt):
                ins = None
                for e in range(4):
                    ins = q.tensor_tensor(
                        G["masks"][:, tt * 4 + e:tt * 4 + e + 1],
                        lgs[:, e:e + 1], G["stat"][:, 6:7], ALU.is_ge)
                return ins
            t_mk = sc.run("v", f_mk, deps=[t_mx], serialize=True)
            slot["lg_ev"] = t_mk
            if G.get("debug"):
                slot["lg_ev"] = sc.dma("s", "sd", lambda q, tt=tt: [
                    q.dma_start(out=G["lgs_dbg"][tt], in_=lgs[:, :])], deps=[t_mk])

            t_st = sc.dma("s", "sd", lambda q, tt=tt, yfmv=yfmv: [
                q.dma_start(out=G["yfm"][tt].rearrange("cc p t -> p cc t"),
                            in_=yfmv.rearrange("p (cc t) -> p cc t", cc=8))],
                deps=[t_ev])
            slot[("yfm", par)] = t_st

    sc.barrier()

    with scope() as cx:
        def lsb(name, shape, dt=F32):
            return cx.enter_context(nc.sbuf_tensor(f"{tag}_{name}", shape, dt))

        def lps(name, shape, dt=F32):
            return cx.enter_context(nc.psum_tensor(f"{tag}_{name}", shape, dt))

        w1f = lsb("w1f", [128, 8 * 512])
        w1b = lsb("w1b", [128, NE * 8 * 512], BF16)
        b1f = lsb("b1f", [1, NE * 512])
        b1b = lsb("b1b", [1, NE * 512], BF16)
        yfs = lsb("yfs", [128, 2 * DM], BF16)
        hsb = lsb("hsb", [128, 2 * 2048], BF16)
        isb = lsb("isb", [128, 2 * 512], BF16)
        hp = lps("hp", [128, 2 * 2048])

        aslot = {}
        t_is = None
        for ic in range(8):
            w_ready = []
            for e in range(NE):
                t_w = sc.dma("q", "qd", lambda q, e=e, ic=ic: [
                    q.dma_start(out=w1f[:, :].rearrange("p (cc i) -> p cc i", cc=8),
                                in_=w["l0f_W1"][e, :, ic * 512:(ic + 1) * 512]
                                .rearrange("(cc p) i -> p cc i", p=128))],
                    deps=[aslot.get("w1f_c")])

                def f_c(q, e=e):
                    return q.tensor_copy(w1b[:, e * 4096:(e + 1) * 4096], w1f[:, :])
                t_c = sc.run("g", f_c, deps=[t_w, aslot.get("w1b_c")])
                aslot["w1f_c"] = t_c
                w_ready.append(t_c)

            t_b1 = sc.dma("q", "qd", lambda q, ic=ic: [
                q.dma_start(out=b1f[:, :].rearrange("o (e i) -> o e i", e=NE),
                            in_=w["l0f_b1"][:, ic * 512:(ic + 1) * 512].unsqueeze(0))],
                deps=[aslot.get("b1_c")])
            t_b1c = sc.run("g", lambda q: q.tensor_copy(b1b[:, :], b1f[:, :]),
                           deps=[t_b1])
            aslot["b1_c"] = t_b1c
            w_ready.append(t_b1c)

            for tt in range(NTT):
                par = tt % 2
                yv = yfs[:, par * DM:(par + 1) * DM]
                hv = hp[:, par * 2048:(par + 1) * 2048]
                hbv = hsb[:, par * 2048:(par + 1) * 2048]
                iv = isb[:, par * 512:(par + 1) * 512]

                t_y = sc.dma("q", "qd", lambda q, tt=tt, yv=yv: [
                    q.dma_start(out=yv.rearrange("p (cc t) -> p cc t", cc=8),
                                in_=G["yfm"][tt].rearrange("cc p t -> p cc t"))],
                    deps=[aslot.get(("yfs", par))])

                def f_h(q, yv=yv, hv=hv):
                    ins = None
                    for e in range(NE):
                        ins = q.matmul(hv[:, e * 512:(e + 1) * 512], G["ones1"][:, :],
                                       b1b[:, e * 512:(e + 1) * 512],
                                       start=True, stop=False)
                        for cc in range(8):
                            ins = q.matmul(
                                hv[:, e * 512:(e + 1) * 512],
                                yv[:, cc * 128:(cc + 1) * 128],
                                w1b[:, e * 4096 + cc * 512:e * 4096 + (cc + 1) * 512],
                                start=False, stop=(cc == 7))
                    return ins
                t_h = sc.run("t", f_h, deps=[t_y] + w_ready + [aslot.get(("hp", par))])
                aslot[("yfs", par)] = t_h
                if tt == NTT - 1:
                    aslot["w1b_c"] = t_h

                t_g = sc.run("s", lambda q, hv=hv, hbv=hbv:
                             q.activation(hbv, hv, AF.Gelu),
                             deps=[t_h, aslot.get(("hsb", par))])
                aslot[("hp", par)] = t_g

                def f_cb(q, tt=tt, hbv=hbv, iv=iv):
                    q.tensor_scalar(iv, hbv[:, 0:512],
                                    G["masks"][:, tt * 4:tt * 4 + 1], None, ALU.mult)
                    ins = None
                    for e in range(1, NE):
                        ins = q.scalar_tensor_tensor(
                            iv, hbv[:, e * 512:(e + 1) * 512],
                            G["masks"][:, tt * 4 + e:tt * 4 + e + 1],
                            iv, ALU.mult, ALU.add)
                    return ins
                t_cb = sc.run("v", f_cb, deps=[t_g, aslot.get(("isb", par))])
                aslot[("hsb", par)] = t_cb

                t_is = sc.dma("s", "sd", lambda q, ic=ic, tt=tt, iv=iv: [
                    q.dma_start(out=G["interm"][ic, tt], in_=iv)], deps=[t_cb])
                aslot[("isb", par)] = t_is

    sc.barrier()

    with scope() as cx:
        def lsb(name, shape, dt=F32):
            return cx.enter_context(nc.sbuf_tensor(f"{tag}_{name}", shape, dt))

        def lps(name, shape, dt=F32):
            return cx.enter_context(nc.psum_tensor(f"{tag}_{name}", shape, dt))

        wof = lsb("wof", [128, 4 * DM])
        wob = lsb("wob", [128, 32 * DM], BF16)
        bo_b = lsb("bo_b", [128, DM])
        gam_b = lsb("gam_b", [128, DM])
        bet_b = lsb("bet_b", [128, DM])
        row_f = lsb("row_f", [1, DM])
        row_b = lsb("row_b", [1, DM], BF16)
        ib = lsb("ib", [128, 2 * 4096], BF16)
        itf = lsb("itf", [128, 4096], BF16)
        zsb = lsb("zsb", [128, DM])
        ztmp = lsb("ztmp", [128, DM])
        xt2 = lsb("xt2", [128, 2 * DM])
        y2 = lsb("y2", [128, DM])
        xo = lsb("xo", [128, 2 * DM])
        tpb = lps("tpb", [128, 2048], BF16)
        zp = lps("zp", [128, 2 * DM])

        bslot = {}
        bc_ps_t = cx.enter_context(nc.psum_tensor(f"{tag}_bcps", [128, DM], F32))
        t_prev = None
        for name, dstbuf in [("l0f_bo", bo_b), ("l0f_g", gam_b), ("l0f_b", bet_b)]:
            t_r = sc.dma("q", "qd", lambda q, name=name: [
                q.dma_start(out=row_f[:, :], in_=w[name].unsqueeze(0))],
                deps=[t_prev])
            t_c = sc.run("v", lambda q: q.tensor_copy(row_b[:, :], row_f[:, :]),
                         deps=[t_r, t_prev])

            def f_bc(q):
                ins = None
                for nch in range(DM // 512):
                    ins = q.matmul(bc_ps_t[:, nch * 512:(nch + 1) * 512],
                                   G["ones1"][:, :],
                                   row_b[:, nch * 512:(nch + 1) * 512],
                                   start=True, stop=True)
                return ins
            t_bc = sc.run("t", f_bc, deps=[t_c, t_prev])
            t_prev = sc.run("v", lambda q, dstbuf=dstbuf: q.tensor_copy(
                dstbuf[:, :], bc_ps_t[:, :]), deps=[t_bc])
        t_bcast = t_prev
        wo_casts = [t_bcast]
        for gq in range(8):
            t_wo = sc.dma("q", "qd", lambda q, gq=gq: [
                q.dma_start(out=wof[:, :].rearrange("p (cc d) -> p cc d", cc=4),
                            in_=w["l0f_Wo"][gq * 512:(gq + 1) * 512, :]
                            .rearrange("(cc p) d -> p cc d", p=128))],
                deps=[bslot.get("wof_c")])
            t_woc = sc.run("g", lambda q, gq=gq: q.tensor_copy(
                wob[:, gq * 4 * DM:(gq + 1) * 4 * DM], wof[:, :]), deps=[t_wo])
            bslot["wof_c"] = t_woc
            wo_casts.append(t_woc)

        t_xs = None
        for tt in range(NTT):
            par = tt % 2
            ibv = ib[:, par * 4096:(par + 1) * 4096]
            xv = xt2[:, par * DM:(par + 1) * DM]
            zv = zp[:, par * DM:(par + 1) * DM]
            xov = xo[:, par * DM:(par + 1) * DM]

            t_i = sc.dma("q", "qd", lambda q, tt=tt, ibv=ibv, xv=xv: [
                q.dma_start(out=ibv.rearrange("p (ic i) -> p ic i", ic=8),
                            in_=G["interm"][:, tt].rearrange("ic p i -> p ic i")),
                q.dma_start(out=xv, in_=src[tt * 128:(tt + 1) * 128, :])],
                deps=[bslot.get(("ib", par)), bslot.get(("x2", par))])

            tr_toks = []
            for rnd in range(2):
                def f_tr(q, rnd=rnd, ibv=ibv):
                    ins = None
                    for j in range(16):
                        ci = rnd * 16 + j
                        ins = q.transpose(tpb[:, j * 128:(j + 1) * 128],
                                          ibv[:, ci * 128:(ci + 1) * 128],
                                          G["ident_b"][:, :])
                    return ins
                t_tr = sc.run("t", f_tr, deps=[t_i, bslot.get("tpb_ev")])

                def f_te(q, rnd=rnd):
                    return q.tensor_copy(itf[:, rnd * 2048:(rnd + 1) * 2048], tpb[:, :])
                t_te = sc.run("v", f_te, deps=[t_tr, bslot.get("itf_c")])
                bslot["tpb_ev"] = t_te
                tr_toks.append(t_te)

            def f_z(q, zv=zv):
                ins = None
                for ci in range(32):
                    for dch in range(2):
                        ins = q.matmul(
                            zv[:, dch * 512:(dch + 1) * 512],
                            itf[:, ci * 128:(ci + 1) * 128],
                            wob[:, ci * DM + dch * 512:ci * DM + (dch + 1) * 512],
                            start=(ci == 0), stop=(ci == 31))
                return ins
            t_z = sc.run("t", f_z, deps=tr_toks + wo_casts + [bslot.get(("zp", par))])
            bslot[("ib", par)] = t_z
            bslot["itf_c"] = t_z

            t_y2 = emit_ln(sc, G, xv, [y2[:, :]], 1e-5,
                           deps=[t_i, bslot.get("y2_c")])

            t_ze = sc.run("s", lambda q, zv=zv: q.copy(zsb[:, :], zv),
                          deps=[t_z, bslot.get("zsb_c")])
            bslot[("zp", par)] = t_ze

            def f_za(q):
                q.tensor_add(zsb[:, :], zsb[:, :], bo_b[:, :])
                return q.tensor_add(zsb[:, :], zsb[:, :], y2[:, :])
            t_za = sc.run("g", f_za, deps=[t_ze, t_y2])
            bslot["y2_c"] = t_za
            if G.get("debug"):
                t_za = sc.dma("s", "sd", lambda q, tt=tt: [
                    q.dma_start(out=G["zsb_dbg"][tt], in_=zsb[:, :]),
                    q.dma_start(out=G["y2_dbg"][tt], in_=y2[:, :])], deps=[t_za])

            t_iln = emit_ln(sc, G, zsb[:, :], [ztmp[:, :]], 1e-12,
                            deps=[t_za, bslot.get("ztmp_c")])
            bslot["zsb_c"] = t_iln

            def f_gb(q):
                q.tensor_mul(ztmp[:, :], ztmp[:, :], gam_b[:, :])
                return q.tensor_add(ztmp[:, :], ztmp[:, :], bet_b[:, :])
            t_gb = sc.run("g", f_gb, deps=[t_iln])

            t_res = sc.run("v", lambda q, xov=xov, xv=xv: q.scalar_tensor_tensor(
                xov, ztmp[:, :], 0.5, xv, ALU.mult, ALU.add),
                deps=[t_gb, bslot.get(("xo", par))])
            bslot["ztmp_c"] = t_res
            bslot[("x2", par)] = t_res

            t_xs = sc.dma("s", "sd", lambda q, tt=tt, xov=xov: [
                q.dma_start(out=dst[tt * 128:(tt + 1) * 128, :], in_=xov)],
                deps=[t_res])
            bslot[("xo", par)] = t_xs

    sc.barrier()



def emit_l0attn(sc, G, src, dst):
    nc, w = G["nc"], G["w"]
    tag = G["tag"]
    H, HD = 16, 64

    with ExitStack() as cx:
        def lsb(name, shape, dt=F32):
            return cx.enter_context(nc.sbuf_tensor(f"{tag}_{name}", shape, dt))

        def lps(name, shape, dt=F32):
            return cx.enter_context(nc.psum_tensor(f"{tag}_{name}", shape, dt))

        yfm = lsb("yfm", [128, 8 * S], BF16)
        qrf = lsb("qrf", [128, 8 * S], BF16)
        krf = lsb("krf", [128, 8 * S], BF16)
        vtm = lsb("vtm", [128, 16 * DM], BF16)
        outall = lsb("outall", [128, 16 * DM], BF16)
        wsl = lsb("wsl", [128, 8 * DM], BF16)
        wf = lsb("wf", [128, DM // 2])
        xt = lsb("xt", [128, DM])
        yb = lsb("yb", [128, DM], BF16)
        rtm = lsb("rtm", [128, 2 * DM], BF16)
        tabs = lsb("tabs", [128, 2 * DM], BF16)
        asb = lsb("asb", [128, 2048], BF16)
        dens = lsb("dens", [128, 16])
        rden = lsb("rden", [128, 16])
        dsb = lsb("dsb", [128, 256], BF16)
        dtmp = lsb("dtmp", [128, 16])
        gt = lsb("gt", [128, DM], BF16)
        gsum = lsb("gsum", [128, 16])
        grst = lsb("grst", [128, 16])
        otmp = lsb("otmp", [128, DM], BF16)
        pq = lps("pq", [128, 2048])
        pk = lps("pk", [128, 2048])

        def bank_chunks(lo, hi):
            out = []
            while lo < hi:
                nxt = min((lo // 512 + 1) * 512, hi)
                out.append((lo, nxt))
                lo = nxt
            return out

        slot = {}
        for b in range(BC):
            base = b * S

            def load_w(nm, dep_consumed):
                toks = []
                dep = dep_consumed
                for cc in range(16):
                    t_l = sc.dma("q", "qd", lambda q, nm=nm, cc=cc: [
                        q.dma_start(out=wf[:, :],
                                    in_=w[nm][(cc // 2) * 128:(cc // 2 + 1) * 128,
                                              (cc % 2) * 512:(cc % 2 + 1) * 512])],
                        deps=[dep, slot.get("wf_c")])
                    t_c = sc.run("g", lambda q, cc=cc: q.tensor_copy(
                        wsl[:, cc * 512:(cc + 1) * 512], wf[:, :]), deps=[t_l])
                    slot["wf_c"] = t_c
                    toks.append(t_c)
                return toks

            for ts_ in range(16):
                t_x = sc.dma("q", "qd", lambda q, ts_=ts_: [
                    q.dma_start(out=xt[:, :],
                                in_=src[base + ts_ * 128:base + (ts_ + 1) * 128, :])],
                    deps=[slot.get("x_c")])
                t_ln = emit_ln(sc, G, xt[:, :], [yb[:, :]], 1e-5,
                               deps=[t_x, slot.get("yb_c")])
                slot["x_c"] = t_ln

                tp = pk[:, 0:512].bitcast(BF16)

                def f_tr(q, tp=tp):
                    ins = None
                    for cc in range(8):
                        ins = q.transpose(tp[:, cc * 128:(cc + 1) * 128],
                                          yb[:, cc * 128:(cc + 1) * 128],
                                          G["ident_b"][:, :])
                    return ins
                t_tr = sc.run("t", f_tr, deps=[t_ln, slot.get("tp_ev")])
                slot["yb_c"] = t_tr

                def f_te(q, ts_=ts_, tp=tp):
                    ov = yfm[:, :].rearrange("p (cc t) -> p cc t", t=S)[
                        :, :, ts_ * 128:(ts_ + 1) * 128]
                    return q.tensor_copy(ov, tp.rearrange("p (cc t) -> p cc t", t=128))
                t_te = sc.run("v", f_te, deps=[t_tr, slot.get("yfm_c")])
                slot["tp_ev"] = t_te
            yfm_done = sc.count["v"]
            t_yfm = ("v", yfm_done)

            for (wnm, dstf, ta, tb) in [("l0a_Wq", qrf, "l0_qA", "l0_qB"),
                                        ("l0a_Wk", krf, "l0_kA", "l0_kB")]:
                wtoks = load_w(wnm, slot.get("wsl_c"))
                for ts_ in range(16):
                    par = ts_ % 2
                    pv = pq[:, par * 1024:(par + 1) * 1024]

                    def f_mm(q, ts_=ts_, pv=pv):
                        ins = None
                        for cc in range(8):
                            lh = yfm[:, cc * S + ts_ * 128:cc * S + (ts_ + 1) * 128]
                            for n2 in range(2):
                                ins = q.matmul(pv[:, n2 * 512:(n2 + 1) * 512], lh,
                                               wsl[:, cc * DM + n2 * 512:cc * DM + (n2 + 1) * 512],
                                               start=(cc == 0), stop=(cc == 7))
                        return ins
                    t_mm = sc.run("t", f_mm,
                                  deps=[t_yfm] + wtoks + [slot.get(("pp", par))])

                    t_tab = sc.dma("q", "qd", lambda q, ts_=ts_, ta=ta, tb=tb: [
                        q.dma_start(out=tabs[:, 0:DM],
                                    in_=w[ta][ts_ * 128:(ts_ + 1) * 128, :]),
                        q.dma_start(out=tabs[:, DM:2 * DM],
                                    in_=w[tb][ts_ * 128:(ts_ + 1) * 128, :])],
                        deps=[slot.get("tab_c")])

                    def f_r1(q, pv=pv):
                        q.tensor_mul(rtm[:, 0:DM], pv, tabs[:, 0:DM])
                        pvv = pv.rearrange("p (a c) -> p a c", c=2)
                        bvv = tabs[:, DM:2 * DM].rearrange("p (a c) -> p a c", c=2)
                        mvv = rtm[:, DM:2 * DM].rearrange("p (a c) -> p a c", c=2)
                        q.tensor_mul(mvv[:, :, 0:1], pvv[:, :, 1:2], bvv[:, :, 0:1])
                        return q.tensor_mul(mvv[:, :, 1:2], pvv[:, :, 0:1], bvv[:, :, 1:2])
                    t_r1 = sc.run("v", f_r1, deps=[t_mm, t_tab, slot.get("rtm_c")])
                    slot["tab_c"] = t_r1
                    slot[("pp", par)] = t_r1

                    t_r2 = sc.run("v", lambda q: q.tensor_add(
                        yb[:, :], rtm[:, 0:DM], rtm[:, DM:2 * DM]),
                        deps=[t_r1, slot.get("yb_c")])
                    slot["rtm_c"] = t_r2

                    tp = pk[:, 0:512].bitcast(BF16)

                    def f_tr2(q, tp=tp):
                        ins = None
                        for cc in range(8):
                            ins = q.transpose(tp[:, cc * 128:(cc + 1) * 128],
                                              yb[:, cc * 128:(cc + 1) * 128],
                                              G["ident_b"][:, :])
                        return ins
                    t_tr2 = sc.run("t", f_tr2, deps=[t_r2, slot.get("tp_ev")])
                    slot["yb_c"] = t_tr2

                    def f_te2(q, ts_=ts_, tp=tp, dstf=dstf):
                        ov = dstf[:, :].rearrange("p (cc t) -> p cc t", t=S)[
                            :, :, ts_ * 128:(ts_ + 1) * 128]
                        return q.tensor_copy(ov, tp.rearrange("p (cc t) -> p cc t", t=128))
                    t_te2 = sc.run("v", f_te2, deps=[t_tr2, slot.get(("qk_c", wnm))])
                    slot["tp_ev"] = t_te2
                slot["wsl_c"] = t_mm
            qk_done = ("v", sc.count["v"])

            wtoks = load_w("l0a_Wv", slot.get("wsl_c"))
            for ts_ in range(16):
                par = ts_ % 2
                pv = pq[:, par * 1024:(par + 1) * 1024]

                def f_mmv(q, ts_=ts_, pv=pv):
                    ins = None
                    for cc in range(8):
                        lh = yfm[:, cc * S + ts_ * 128:cc * S + (ts_ + 1) * 128]
                        for n2 in range(2):
                            ins = q.matmul(pv[:, n2 * 512:(n2 + 1) * 512], lh,
                                           wsl[:, cc * DM + n2 * 512:cc * DM + (n2 + 1) * 512],
                                           start=(cc == 0), stop=(cc == 7))
                    return ins
                t_mmv = sc.run("t", f_mmv,
                               deps=[t_yfm] + wtoks + [slot.get(("pp", par))])
                t_ev = sc.run("v", lambda q, ts_=ts_, pv=pv: q.tensor_copy(
                    vtm[:, ts_ * DM:(ts_ + 1) * DM], pv),
                    deps=[t_mmv, slot.get("vtm_c")])
                slot[("pp", par)] = t_ev
                slot["wsl_c2"] = t_mmv
            v_done = ("v", sc.count["v"])
            slot["wsl_c"] = slot["wsl_c2"]

            wg_toks = load_w("l0a_Wg", slot.get("wsl_c"))

            head_last = None
            for h in range(H):
                hh, hp = h // 2, (h % 2) * 64
                qrh = qrf[hp:hp + 64, hh * S:(hh + 1) * S]
                krh = krf[hp:hp + 64, hh * S:(hh + 1) * S]

                a_toks = []
                gp_toks = []
                for qt in range(16):
                    ps = pq if qt % 2 == 0 else pk
                    krange = (qt + 1) * 128
                    dpar = qt % 2

                    def f_a(q, qt=qt, ps=ps, krange=krange):
                        lh = qrh[:, qt * 128:(qt + 1) * 128]
                        ins = None
                        for (c0, c1) in bank_chunks(0, krange):
                            ins = q.matmul(ps[:, c0:c1], lh, krh[:, c0:c1],
                                           start=True, stop=True)
                        return ins
                    depk = [qk_done, slot.get(("psA", qt % 2))]
                    if qt < 2 and h == 0:
                        depk += [v_done, slot.get("comb_ps"), slot.get("tp_ev")]
                    t_a = sc.run("t", f_a, deps=depk)

                    def f_ab(q, qt=qt, ps=ps):
                        if qt > 0:
                            q.activation(asb[:, 0:qt * 128], ps[:, 0:qt * 128],
                                         AF.Abs, accum_out=dens[:, qt:qt + 1])
                        return q.copy(dsb[:, (qt % 2) * 128:(qt % 2) * 128 + 128],
                                      ps[:, qt * 128:(qt + 1) * 128])
                    t_ab = sc.run("s", f_ab,
                                  deps=[t_a, slot.get("asb_c"), slot.get("dens_c"),
                                        slot.get(("dsb", dpar))])
                    slot[("psA", qt % 2)] = t_ab
                    slot["asb_c"] = t_ab
                    a_toks.append(t_ab)

                    dv = dsb[:, dpar * 128:dpar * 128 + 128]
                    t_g1 = sc.run("g", lambda q, dv=dv: q.tensor_mul(
                        dv, dv, G["tril"][:, :]), deps=[t_ab])
                    t_g2 = sc.run("v", lambda q, qt=qt, dv=dv: q.tensor_reduce(
                        dtmp[:, qt:qt + 1], dv, axis=AX.X, op=ALU.add,
                        apply_absolute_value=True), deps=[t_g1])
                    slot[("dsb", dpar)] = t_g2
                    if qt == 0:
                        t_g3 = sc.run("g", lambda q, qt=qt: q.tensor_copy(
                            dens[:, qt:qt + 1], dtmp[:, qt:qt + 1]), deps=[t_g2])
                    else:
                        t_g3 = sc.run("g", lambda q, qt=qt: q.tensor_tensor(
                            dens[:, qt:qt + 1], dens[:, qt:qt + 1],
                            dtmp[:, qt:qt + 1], ALU.add), deps=[t_g2, t_ab])
                    gp_toks.append(t_g3)

                t_cl = sc.run("v", lambda q: q.tensor_scalar(
                    rden[:, :], dens[:, :], 1.0, 50000.0, ALU.max, ALU.min),
                    deps=gp_toks[-2:] + a_toks[-1:])
                t_rc = sc.run("v", lambda q: q.reciprocal(rden[:, :], rden[:, :]),
                              deps=[t_cl])
                slot["dens_c"] = t_rc

                slot["atp0"] = slot.get(("psA", 0))
                slot["atp1"] = slot.get(("psA", 1))
                vv_last = None
                for qh in range(2):
                    nkt = (qh + 1) * 8
                    ae_tok = {}
                    gp_tok = {}
                    for step in range(nkt + 1):
                        if step < nkt:
                            kt = step
                            par = kt % 2
                            atp = pq[:, par * 1024:(par + 1) * 1024]
                            qlo = max(qh * 1024, kt * 128)
                            off = qlo - qh * 1024

                            def f_at(q, kt=kt, atp=atp, off=off, qh=qh):
                                lh = krh[:, kt * 128:(kt + 1) * 128]
                                ins = None
                                for (c0, c1) in bank_chunks(off, 1024):
                                    ins = q.matmul(atp[:, c0:c1], lh,
                                                   qrh[:, qh * 1024 + c0:qh * 1024 + c1],
                                                   start=True, stop=True)
                                return ins
                            dep_at = [slot.get(("atB", par)), slot.get("atp%d" % par)]
                            slot["atp%d" % par] = None
                            t_at = sc.run("t", f_at, deps=dep_at)

                            eng = "s" if kt % 2 == 0 else "v"
                            t_ae = sc.run(eng, lambda q, atp=atp, off=off, par=par:
                                          q.copy(asb[:, par * 1024 + off:(par + 1) * 1024],
                                                 atp[:, off:1024])
                                          if eng == "s" else
                                          q.tensor_copy(asb[:, par * 1024 + off:(par + 1) * 1024],
                                                        atp[:, off:1024]),
                                          deps=[t_at, slot.get(("asbB", par)),
                                                slot.get("asb_c")])
                            slot["asb_c"] = None
                            slot[("atB", par)] = t_ae
                            ae_tok[kt] = t_ae
                            if kt >= qh * 8:
                                dg0 = par * 1024 + kt * 128 - qh * 1024
                                t_gd = sc.run("g", lambda q, dg0=dg0: q.tensor_mul(
                                    asb[:, dg0:dg0 + 128], asb[:, dg0:dg0 + 128],
                                    G["trilT"][:, :]), deps=[t_ae])
                                gp_tok[kt] = t_gd
                        if step >= 1:
                            kt = step - 1
                            par = kt % 2

                            def f_vv(q, kt=kt, qh=qh, par=par, h=h):
                                ins = None
                                for i in range(max(kt + 1, qh * 8), qh * 8 + 8):
                                    ins = q.matmul(
                                        pk[0:64, i * 128:(i + 1) * 128],
                                        vtm[:, kt * DM + h * HD:kt * DM + (h + 1) * HD],
                                        asb[:, par * 1024 + i * 128 - qh * 1024:
                                            par * 1024 + (i + 1) * 128 - qh * 1024],
                                        start=False, stop=False, skip_group_check=True)
                                return ins
                            dep_vv = [ae_tok[kt]]
                            if kt == 0 and qh == 0:
                                def f_z(q):
                                    ins = None
                                    for bk in range(4):
                                        ins = q.matmul(pk[0:64, bk * 512:(bk + 1) * 512],
                                                       G["ones1"][:, 0:64],
                                                       G["zrow"][:, :],
                                                       start=True, stop=True)
                                    return ins
                                t_z = sc.run("t", f_z, deps=[slot.get("otB")] + a_toks[-2:])
                                dep_vv.append(t_z)
                            t_vv = None
                            if max(kt + 1, qh * 8) < qh * 8 + 8:
                                t_vv = sc.run("t", f_vv, deps=dep_vv)
                            if kt >= qh * 8:
                                def f_vd(q, kt=kt, qh=qh, par=par, h=h):
                                    return q.matmul(
                                        pk[0:64, kt * 128:(kt + 1) * 128],
                                        vtm[:, kt * DM + h * HD:kt * DM + (h + 1) * HD],
                                        asb[:, par * 1024 + kt * 128 - qh * 1024:
                                            par * 1024 + (kt + 1) * 128 - qh * 1024],
                                        start=False, stop=True, skip_group_check=True)
                                t_vv = sc.run("t", f_vd, deps=dep_vv + [gp_tok[kt]])
                            slot[("asbB", par)] = t_vv
                            vv_last = t_vv

                t_oe = sc.run("v", lambda q: q.tensor_copy(asb[0:64, :], pk[0:64, :]),
                              deps=[vv_last, slot.get("otsb_c")])
                slot["otB"] = t_oe

                tr_grp = None
                for grp in range(4):
                    tpo = pq[:, grp * 512:grp * 512 + 128].bitcast(BF16)

                    def f_ot(q, grp=grp, tpo=tpo):
                        ins = None
                        for j in range(4):
                            i = grp * 4 + j
                            ins = q.transpose(tpo[:, j * 64:(j + 1) * 64],
                                              asb[0:64, i * 128:(i + 1) * 128],
                                              G["ident_b"][0:64, 0:64])
                        return ins
                    t_ot = sc.run("t", f_ot, deps=[t_oe, slot.get(("otT", grp))])

                    def f_os(q, grp=grp, tpo=tpo, h=h):
                        ins = None
                        for j in range(4):
                            i = grp * 4 + j
                            ins = q.tensor_scalar(
                                outall[:, i * DM + h * HD:i * DM + (h + 1) * HD],
                                tpo[:, j * 64:(j + 1) * 64],
                                rden[:, i:i + 1], None, ALU.mult)
                        return ins
                    t_os = sc.run("v", f_os, deps=[t_ot, t_rc, slot.get("outall_c")])
                    slot[("otT", grp)] = t_os
                    tr_grp = t_os
                slot["otsb_c"] = tr_grp
                slot[("psA", 0)] = tr_grp
                slot[("psA", 1)] = t_oe
                slot["asb_c"] = tr_grp
                head_last = tr_grp
            heads_done = head_last

            for i in range(16):
                t_x = sc.dma("q", "qd", lambda q, i=i: [
                    q.dma_start(out=xt[:, :],
                                in_=src[base + i * 128:base + (i + 1) * 128, :])],
                    deps=[slot.get("x_c"), heads_done])
                t_ln = emit_ln(sc, G, xt[:, :], [yb[:, :]], 1e-5,
                               deps=[t_x, slot.get("yb_c")])
                slot["x_c"] = t_ln

                tp = pk[:, 0:512].bitcast(BF16)

                def f_tr3(q, tp=tp):
                    ins = None
                    for cc in range(8):
                        ins = q.transpose(tp[:, cc * 128:(cc + 1) * 128],
                                          yb[:, cc * 128:(cc + 1) * 128],
                                          G["ident_b"][:, :])
                    return ins
                t_tr3 = sc.run("t", f_tr3, deps=[t_ln, slot.get("tp_ev"), heads_done])
                slot["yb_c"] = t_tr3

                yfmt = rtm[:, 0:DM]
                t_te3 = sc.run("v", lambda q, tp=tp: q.tensor_copy(
                    rtm[:, 0:DM].rearrange("p (cc t) -> p cc t", t=128),
                    tp.rearrange("p (cc t) -> p cc t", t=128)),
                    deps=[t_tr3, slot.get("rtm3_c")])
                slot["tp_ev"] = t_te3

                def f_gm(q):
                    ins = None
                    for cc in range(8):
                        for n2 in range(2):
                            ins = q.matmul(pq[:, 1024 + n2 * 512:1024 + (n2 + 1) * 512],
                                           rtm[:, cc * 128:(cc + 1) * 128],
                                           wsl[:, cc * DM + n2 * 512:cc * DM + (n2 + 1) * 512],
                                           start=(cc == 0), stop=(cc == 7))
                    return ins
                t_gm = sc.run("t", f_gm, deps=[t_te3] + wg_toks + [slot.get("gps_c")])
                slot["rtm3_c"] = t_gm

                t_sq = sc.run("s", lambda q, i=i: q.activation(
                    gt[:, :], outall[:, i * DM:(i + 1) * DM], AF.Square),
                    deps=[heads_done, slot.get("gt_c")])
                t_rd = sc.run("v", lambda q: q.tensor_reduce(
                    gsum[:, :], gt[:, :].rearrange("p (hh d) -> p hh d", d=HD),
                    axis=AX.X, op=ALU.add), deps=[t_sq])
                t_s1 = sc.run("v", lambda q: q.tensor_scalar(
                    grst[:, :], gsum[:, :], 1.0 / HD, 1e-6, ALU.mult, ALU.add),
                    deps=[t_rd])
                t_s2 = sc.run("s", lambda q: q.activation(
                    grst[:, :], grst[:, :], AF.Sqrt), deps=[t_s1])
                t_s3 = sc.run("v", lambda q: q.reciprocal(grst[:, :], grst[:, :]),
                              deps=[t_s2])

                t_si = sc.run("s", lambda q: q.activation(
                    gt[:, :], pq[:, 1024:2048], AF.Silu), deps=[t_gm, t_rd])
                slot["gps_c"] = t_si

                def f_ap(q, i=i):
                    ins = None
                    for h2 in range(H):
                        ins = q.tensor_scalar(
                            otmp[:, h2 * HD:(h2 + 1) * HD],
                            outall[:, i * DM + h2 * HD:i * DM + (h2 + 1) * HD],
                            grst[:, h2:h2 + 1], None, ALU.mult)
                    return ins
                t_ap = sc.run("v", f_ap, deps=[t_s3, slot.get("otmp_c")])

                t_mu2 = sc.run("v", lambda q: q.tensor_mul(otmp[:, :], otmp[:, :],
                                                           gt[:, :]),
                               deps=[t_ap, t_si])
                t_gt = t_mu2
                slot["gt_c"] = t_mu2

                t_fin = sc.run("v", lambda q: q.tensor_add(xt[:, :], otmp[:, :],
                                                           xt[:, :]),
                               deps=[t_mu2])
                slot["otmp_c"] = t_fin
                slot["outall_c"] = t_ap

                t_st = sc.dma("s", "sd", lambda q, i=i: [
                    q.dma_start(out=dst[base + i * 128:base + (i + 1) * 128, :],
                                in_=xt[:, :])], deps=[t_fin])
                slot["x_c"] = t_st
                slot["comb_ps"] = t_si
            slot["yfm_c"] = slot.get("rtm3_c")
            slot["rtm_c"] = slot.get("rtm3_c")
            slot["vtm_c"] = vv_last
            slot[("qk_c", "l0a_Wq")] = vv_last
            slot[("qk_c", "l0a_Wk")] = vv_last
            slot[("pp", 0)] = slot.get("gps_c")
            slot[("pp", 1)] = slot.get("gps_c")

    sc.barrier()



def emit_l1ffn(sc, G, src, dst):
    nc, w = G["nc"], G["w"]
    tag = G["tag"]
    TW = 1024

    with ExitStack() as cx:
        def lsb(name, shape, dt=F32):
            return cx.enter_context(nc.sbuf_tensor(f"{tag}_{name}", shape, dt))

        def lps(name, shape, dt=F32):
            return cx.enter_context(nc.psum_tensor(f"{tag}_{name}", shape, dt))

        wf = lsb("wf", [128, 4 * DM])
        wb = {}
        t_wc_all = []
        wslot = {}
        for wi, nm in enumerate(["l1f_WrT", "l1f_WkT", "l1f_WvT"]):
            wb[nm] = lsb(f"wb{wi}", [128, 8 * DM], BF16)
            for half in range(2):
                t_wl = sc.dma("q", "qd", lambda q, nm=nm, half=half: [
                    q.dma_start(out=wf[:, :].rearrange("p (cc d) -> p cc d", cc=4),
                                in_=w[nm][half * 512:(half + 1) * 512, :]
                                .rearrange("(cc p) d -> p cc d", p=128))],
                    deps=[wslot.get("wf_c")])
                t_wc = sc.run("g", lambda q, nm=nm, half=half: q.tensor_copy(
                    wb[nm][:, half * 4 * DM:(half + 1) * 4 * DM], wf[:, :]),
                    deps=[t_wl])
                wslot["wf_c"] = t_wc
                t_wc_all.append(t_wc)

        muf = lsb("muf", [128, 8 * 2])
        mur = lsb("mur", [128, 8 * 4])
        t_mu = sc.dma("q", "qd", lambda q: [
            q.dma_start(out=muf[:, 0:8], in_=w["l1f_mur"].rearrange("(cc p) -> p cc", p=128),
                        allow_slow_non_contiguous=True),
            q.dma_start(out=muf[:, 8:16], in_=w["l1f_muk"].rearrange("(cc p) -> p cc", p=128),
                        allow_slow_non_contiguous=True)])

        def f_mu(q):
            q.tensor_copy(mur[:, 0:8], muf[:, 0:8])
            q.tensor_scalar(mur[:, 8:16], muf[:, 0:8], -1.0, 1.0, ALU.mult, ALU.add)
            q.tensor_copy(mur[:, 16:24], muf[:, 8:16])
            return q.tensor_scalar(mur[:, 24:32], muf[:, 8:16], -1.0, 1.0, ALU.mult, ALU.add)
        t_muc = sc.run("v", f_mu, deps=[t_mu])

        yfm = lsb("yfm", [128, 8 * (S + 1)], BF16)
        xt = lsb("xt", [128, 2 * DM])
        yb = lsb("yb", [128, DM], BF16)
        xr = lsb("xr", [128, 8 * TW], BF16)
        xk = lsb("xk", [128, 8 * TW], BF16)
        t1 = lsb("t1", [128, 2 * TW])
        rr = lsb("rr", [128, 8 * TW], BF16)
        ka = lsb("ka", [128, 8 * TW], BF16)
        of = lsb("of", [128, 8 * TW], BF16)
        xo = lsb("xo", [128, 2 * DM])
        pa = lps("pa", [128, 2 * DM])
        pt = lps("pt", [128, DM], BF16)

        slot = {}
        for b in range(BC):
            base = b * S
            t_z = sc.run("g", lambda q: q.memset(
                yfm[:, :].rearrange("p (cc t) -> p cc t", t=S + 1)[:, :, 0:1], 0.0),
                deps=[slot.get("yfm_c")])

            tr_last = None
            for ts_ in range(16):
                par = ts_ % 2
                xv = xt[:, par * DM:(par + 1) * DM]
                t_x = sc.dma("q", "qd", lambda q, ts_=ts_, xv=xv: [
                    q.dma_start(out=xv, in_=src[base + ts_ * 128:base + (ts_ + 1) * 128, :])],
                    deps=[slot.get(("x", par))])
                t_ln = emit_ln(sc, G, xv, [yb[:, :]], 1e-5,
                               deps=[t_x, slot.get("yb_c")])
                slot[("x", par)] = t_ln

                def f_tr(q):
                    ins = None
                    for cc in range(8):
                        ins = q.transpose(pt[:, cc * 128:(cc + 1) * 128],
                                          yb[:, cc * 128:(cc + 1) * 128],
                                          G["ident_b"][:, :])
                    return ins
                t_tr = sc.run("t", f_tr, deps=[t_ln, slot.get("pt_ev")])
                slot["yb_c"] = t_tr

                def f_te(q, ts_=ts_):
                    ov = yfm[:, :].rearrange("p (cc t) -> p cc t", t=S + 1)[
                        :, :, 1 + ts_ * 128:1 + (ts_ + 1) * 128]
                    return q.tensor_copy(ov, pt[:, :].rearrange("p (cc t) -> p cc t", t=128))
                t_te = sc.run("v", f_te, deps=[t_tr, t_z])
                slot["pt_ev"] = t_te
                tr_last = t_te

            out_done = []
            for hh in range(2):
                yv = yfm[:, :].rearrange("p (cc t) -> p cc t", t=S + 1)
                t_xr = None
                for cc in range(8):
                    ys = yv[:, cc, 1 + hh * TW:1 + (hh + 1) * TW]
                    yp = yv[:, cc, hh * TW:(hh + 1) * TW]

                    def f_a(q, ys=ys, cc=cc):
                        q.tensor_scalar(t1[:, 0:TW], ys, mur[:, cc:cc + 1],
                                        None, ALU.mult)
                        return q.tensor_scalar(t1[:, TW:2 * TW], ys,
                                               mur[:, 16 + cc:16 + cc + 1],
                                               None, ALU.mult)
                    t_a = sc.run("v", f_a,
                                 deps=[tr_last, t_muc, slot.get("xr_c")] if cc == 0
                                 else [t_a0])

                    def f_b(q, yp=yp, cc=cc):
                        q.scalar_tensor_tensor(xr[:, cc * TW:(cc + 1) * TW], yp,
                                               mur[:, 8 + cc:8 + cc + 1], t1[:, 0:TW],
                                               ALU.mult, ALU.add)
                        return q.scalar_tensor_tensor(xk[:, cc * TW:(cc + 1) * TW], yp,
                                                      mur[:, 24 + cc:24 + cc + 1],
                                                      t1[:, TW:2 * TW],
                                                      ALU.mult, ALU.add)
                    t_xr = sc.run("v", f_b, deps=[t_a])
                    t_a0 = t_xr

                rk_done = []
                for (xin_, wnm, obuf, act) in [(xr, "l1f_WrT", rr, "sig"),
                                               (xk, "l1f_WkT", ka, "relu2")]:
                    for dd in range(8):
                        par = dd % 2
                        pv = pa[:, par * DM:(par + 1) * DM]

                        def f_mm(q, xin_=xin_, wnm=wnm, dd=dd, pv=pv):
                            ins = None
                            for cc in range(8):
                                for nc2 in range(2):
                                    ins = q.matmul(
                                        pv[:, nc2 * 512:(nc2 + 1) * 512],
                                        wb[wnm][:, cc * DM + dd * 128:cc * DM + (dd + 1) * 128],
                                        xin_[:, cc * TW + nc2 * 512:cc * TW + (nc2 + 1) * 512],
                                        start=(cc == 0), stop=(cc == 7))
                            return ins
                        t_mm = sc.run("t", f_mm,
                                      deps=[t_xr] + t_wc_all + [slot.get(("pa", par))])

                        if act == "sig":
                            t_ev = sc.run("s", lambda q, dd=dd, pv=pv: q.activation(
                                rr[:, dd * TW:(dd + 1) * TW], pv, AF.Sigmoid),
                                deps=[t_mm, slot.get("rr_c")])
                        else:
                            t_rl = sc.run("s", lambda q, dd=dd, pv=pv: q.activation(
                                G["scr_b"][:, :], pv, AF.Relu),
                                deps=[t_mm, slot.get("ka_c")])
                            t_ev = sc.run("s", lambda q, dd=dd: q.activation(
                                ka[:, dd * TW:(dd + 1) * TW], G["scr_b"][:, :],
                                AF.Square), deps=[t_rl], serialize=True)
                        slot[("pa", par)] = t_ev
                        rk_done.append(t_ev)

                slot["xr_c"] = rk_done[-1]

                o_done = []
                for dd in range(8):
                    par = dd % 2
                    pv = pa[:, par * DM:(par + 1) * DM]

                    def f_vm(q, dd=dd, pv=pv):
                        ins = None
                        for cc in range(8):
                            for nc2 in range(2):
                                ins = q.matmul(
                                    pv[:, nc2 * 512:(nc2 + 1) * 512],
                                    wb["l1f_WvT"][:, cc * DM + dd * 128:cc * DM + (dd + 1) * 128],
                                    ka[:, cc * TW + nc2 * 512:cc * TW + (nc2 + 1) * 512],
                                    start=(cc == 0), stop=(cc == 7))
                        return ins
                    t_vm = sc.run("t", f_vm, deps=rk_done + [slot.get(("pa", par))])

                    t_om = sc.run("v", lambda q, dd=dd, pv=pv: q.tensor_mul(
                        of[:, dd * TW:(dd + 1) * TW], rr[:, dd * TW:(dd + 1) * TW], pv),
                        deps=[t_vm, slot.get("of_c")])
                    slot[("pa", par)] = t_om
                    o_done.append(t_om)
                slot["rr_c"] = o_done[-1]
                slot["ka_c"] = o_done[-1]

                for ts2 in range(8):
                    gt = hh * 8 + ts2
                    par = gt % 2
                    xv = xt[:, par * DM:(par + 1) * DM]
                    xov = xo[:, par * DM:(par + 1) * DM]
                    t_x2 = sc.dma("q", "qd", lambda q, gt=gt, xv=xv: [
                        q.dma_start(out=xv,
                                    in_=src[base + gt * 128:base + (gt + 1) * 128, :])],
                        deps=[slot.get(("x", par))])

                    def f_tb(q, ts2=ts2):
                        ins = None
                        for dd in range(8):
                            ins = q.transpose(
                                pt[:, dd * 128:(dd + 1) * 128],
                                of[:, dd * TW + ts2 * 128:dd * TW + (ts2 + 1) * 128],
                                G["ident_b"][:, :])
                        return ins
                    t_tb = sc.run("t", f_tb, deps=o_done + [slot.get("pt_ev")])

                    t_rs = sc.run("v", lambda q, xv=xv, xov=xov: q.scalar_tensor_tensor(
                        xov, pt[:, :], 0.5, xv, ALU.mult, ALU.add),
                        deps=[t_tb, t_x2, slot.get(("xo", par))])
                    slot["pt_ev"] = t_rs
                    slot[("x", par)] = t_rs
                    if ts2 == 7:
                        slot["of_c"] = t_rs

                    t_xs = sc.dma("s", "sd", lambda q, gt=gt, xov=xov: [
                        q.dma_start(out=dst[base + gt * 128:base + (gt + 1) * 128, :],
                                    in_=xov)], deps=[t_rs])
                    slot[("xo", par)] = t_xs
                    out_done.append(t_xs)
            slot["yfm_c"] = out_done[-1]

    sc.barrier()



def emit_l1attn(sc, G, src, dst):
    nc, w = G["nc"], G["w"]
    tag = G["tag"]
    NH, HD1 = 8, 128

    with ExitStack() as cx:
        def lsb(name, shape, dt=F32):
            return cx.enter_context(nc.sbuf_tensor(f"{tag}_{name}", shape, dt))

        def lps(name, shape, dt=F32):
            return cx.enter_context(nc.psum_tensor(f"{tag}_{name}", shape, dt))

        yfm = lsb("yfm", [128, 8 * S], BF16)
        outall = lsb("outall", [128, 16 * DM], BF16)
        wsl = lsb("wsl", [128, 8 * DM], BF16)
        wf = lsb("wf", [128, DM])
        xt = lsb("xt", [128, DM])
        yb = lsb("yb", [128, DM], BF16)
        rtm = lsb("rtm", [128, 2 * DM], BF16)
        qro = lsb("qro", [128, 2 * DM], BF16)
        l1tb = lsb("l1tb", [128, 4 * DM], BF16)
        qTs = lsb("qTs", [128, 2048], BF16)
        kTs = lsb("kTs", [128, 2048], BF16)
        vh = lsb("vh", [128, 2048], BF16)
        ptsb = lsb("ptsb", [128, 2 * 1024], BF16)
        osb = lsb("osb", [128, 1024], BF16)
        denr = lsb("denr", [1, 1024])
        denc = lsb("denc", [128, 8])
        rdn = lsb("rdn", [128, 8])
        pq = lps("pq", [128, 2048])
        pk = lps("pk", [128, 2048])

        t_tb = sc.dma("q", "qd", lambda q: [
            q.dma_start(out=l1tb[:, 0:DM], in_=w["l1_cA"][0:128, :]),
            q.dma_start(out=l1tb[:, DM:2 * DM], in_=w["l1_cA"][128:256, :]),
            q.dma_start(out=l1tb[:, 2 * DM:3 * DM], in_=w["l1_cB"][0:128, :]),
            q.dma_start(out=l1tb[:, 3 * DM:4 * DM], in_=w["l1_cB"][128:256, :])])

        slot = {}
        for b in range(BC):
            base = b * S

            def load_w(nm, dep_consumed):
                toks = []
                for cc in range(8):
                    t_l = sc.dma("q", "qd", lambda q, nm=nm, cc=cc: [
                        q.dma_start(out=wf[:, :],
                                    in_=w[nm][cc * 128:(cc + 1) * 128, :])],
                        deps=[dep_consumed, slot.get("wf_c")])
                    t_c = sc.run("g", lambda q, cc=cc: q.tensor_copy(
                        wsl[:, cc * DM:(cc + 1) * DM], wf[:, :]), deps=[t_l])
                    slot["wf_c"] = t_c
                    toks.append(t_c)
                return toks

            for ts_ in range(16):
                t_x = sc.dma("q", "qd", lambda q, ts_=ts_: [
                    q.dma_start(out=xt[:, :],
                                in_=src[base + ts_ * 128:base + (ts_ + 1) * 128, :])],
                    deps=[slot.get("x_c")])
                t_ln = emit_ln(sc, G, xt[:, :], [yb[:, :]], 1e-5,
                               deps=[t_x, slot.get("yb_c")])
                slot["x_c"] = t_ln

                tp = pk[:, 0:512].bitcast(BF16)

                def f_tr(q, tp=tp):
                    ins = None
                    for cc in range(8):
                        ins = q.transpose(tp[:, cc * 128:(cc + 1) * 128],
                                          yb[:, cc * 128:(cc + 1) * 128],
                                          G["ident_b"][:, :])
                    return ins
                t_tr = sc.run("t", f_tr, deps=[t_ln, slot.get("tp_ev")])
                slot["yb_c"] = t_tr

                def f_te(q, ts_=ts_, tp=tp):
                    ov = yfm[:, :].rearrange("p (cc t) -> p cc t", t=S)[
                        :, :, ts_ * 128:(ts_ + 1) * 128]
                    return q.tensor_copy(ov, tp.rearrange("p (cc t) -> p cc t", t=128))
                t_te = sc.run("v", f_te, deps=[t_tr, slot.get("yfm_c")])
                slot["tp_ev"] = t_te
            t_yfm = ("v", sc.count["v"])

            for wi, (wnm, qi, rope) in enumerate([("l1a_W1T", 0, True),
                                                  ("l1a_W2T", 1, True),
                                                  ("l1a_W3T", 2, False)]):
                wtoks = load_w(wnm, slot.get("wsl_c"))
                for ts_ in range(16):
                    par = ts_ % 2
                    pv = pq[:, par * 1024:(par + 1) * 1024]
                    qv = qro[:, par * DM:(par + 1) * DM]

                    def f_mm(q, ts_=ts_, pv=pv):
                        ins = None
                        for cc in range(8):
                            lh = yfm[:, cc * S + ts_ * 128:cc * S + (ts_ + 1) * 128]
                            for n2 in range(2):
                                ins = q.matmul(
                                    pv[:, n2 * 512:(n2 + 1) * 512], lh,
                                    wsl[:, cc * DM + n2 * 512:cc * DM + (n2 + 1) * 512],
                                    start=(cc == 0), stop=(cc == 7))
                        return ins
                    t_mm = sc.run("t", f_mm,
                                  deps=[t_yfm] + wtoks + [slot.get(("pp", par))])

                    if rope:
                        tb = ts_ % 2
                        cav = l1tb[:, tb * DM:(tb + 1) * DM]
                        cbv = l1tb[:, (2 + tb) * DM:(3 + tb) * DM]

                        def f_r1(q, pv=pv, cav=cav, cbv=cbv):
                            q.tensor_mul(rtm[:, 0:DM], pv, cav)
                            pvv = pv.rearrange("p (a c) -> p a c", c=2)
                            bvv = cbv.rearrange("p (a c) -> p a c", c=2)
                            mvv = rtm[:, DM:2 * DM].rearrange("p (a c) -> p a c", c=2)
                            q.tensor_mul(mvv[:, :, 0:1], pvv[:, :, 1:2], bvv[:, :, 0:1])
                            return q.tensor_mul(mvv[:, :, 1:2], pvv[:, :, 0:1],
                                                bvv[:, :, 1:2])
                        t_r1 = sc.run("v", f_r1, deps=[t_mm, t_tb, slot.get("rtm_c")])
                        t_r2 = sc.run("v", lambda q, qv=qv: q.tensor_add(
                            qv, rtm[:, 0:DM], rtm[:, DM:2 * DM]),
                            deps=[t_r1, slot.get(("qro", par))])
                        slot["rtm_c"] = t_r2
                        slot[("pp", par)] = t_r1
                        t_out = t_r2
                    else:
                        t_out = sc.run("v", lambda q, pv=pv, qv=qv: q.tensor_copy(qv, pv),
                                       deps=[t_mm, slot.get(("qro", par))])
                        slot[("pp", par)] = t_out

                    t_sto = sc.dma("s", "sd", lambda q, ts_=ts_, qi=qi, qv=qv: [
                        q.dma_start(out=G["qkv"][qi, ts_ * 128:(ts_ + 1) * 128, :],
                                    in_=qv)], deps=[t_out])
                    slot[("qro", par)] = t_sto
                slot["wsl_c"] = t_mm
            qkv_done = ("sd", sc.count["sd"])

            wo_toks = load_w("l1a_WoT", slot.get("wsl_c"))

            heads_done = None
            for h in range(NH):
                t_qT = sc.dma("q", "qd", lambda q, h=h: [
                    q.dma_start_transpose(
                        out=qTs[:, :],
                        in_=G["qkv"][0, 256 * h:256 * (h + 1), :]
                        .rearrange("u (w d) -> (u w) d", d=128))],
                    deps=[qkv_done, slot.get("qT_c")])
                t_kT = sc.dma("q", "qd", lambda q, h=h: [
                    q.dma_start_transpose(
                        out=kTs[:, :],
                        in_=G["qkv"][1, 256 * h:256 * (h + 1), :]
                        .rearrange("u (w d) -> (u w) d", d=128))],
                    deps=[qkv_done, slot.get("kT_c")])
                t_v = sc.dma("q", "qd", lambda q, h=h: [
                    q.dma_start(out=vh[:, :].rearrange("p (tt d) -> p tt d", tt=16),
                                in_=G["qkv"][2, 256 * h:256 * (h + 1), :]
                                .rearrange("(tt u2) (w d) -> (u2 w) tt d",
                                           tt=16, u2=16, w=8))],
                    deps=[qkv_done, slot.get("vh_c")])

                for qh in range(2):
                    vv_last = None
                    ex_tok = {}
                    for step in range(17):
                        if step < 16:
                            kt = step
                            par = kt % 2
                            stp = pq[:, par * 1024:(par + 1) * 1024]

                            def f_st(q, kt=kt, qh=qh, stp=stp):
                                lh = kTs[:, kt * 128:(kt + 1) * 128]
                                ins = None
                                for n2 in range(2):
                                    ins = q.matmul(
                                        stp[:, n2 * 512:(n2 + 1) * 512], lh,
                                        qTs[:, qh * 1024 + n2 * 512:qh * 1024 + (n2 + 1) * 512],
                                        start=True, stop=True)
                                return ins
                            dep_st = [t_qT, t_kT, slot.get(("stB", par))]
                            if kt < 2 and qh == 0 and h == 0:
                                dep_st += [slot.get("pq_free"), slot.get("tp_ev"),
                                           slot.get(("pp", 0)), slot.get(("pp", 1))]
                            t_st = sc.run("t", f_st, deps=dep_st)

                            t_ex = sc.run("s", lambda q, stp=stp, par=par: q.activation(
                                ptsb[:, par * 1024:(par + 1) * 1024], stp, AF.Exp,
                                bias=G["c20"][:, :], scale=float(1.0 / math.sqrt(8.0))),
                                deps=[t_st, slot.get(("ptsb", par))])
                            slot[("stB", par)] = t_ex
                            ex_tok[kt] = t_ex
                        if step >= 1:
                            kt = step - 1
                            par = kt % 2

                            def f_dn(q, kt=kt, par=par):
                                ins = None
                                for n2 in range(2):
                                    ins = q.matmul(
                                        pk[0:1, 1024 + n2 * 512:1024 + (n2 + 1) * 512],
                                        G["onec"][:, :],
                                        ptsb[:, par * 1024 + n2 * 512:par * 1024 + (n2 + 1) * 512],
                                        start=(kt == 0), stop=(kt == 15))
                                return ins
                            t_dn = sc.run("t", f_dn, deps=[ex_tok[kt], slot.get("dn_c")])

                            def f_vv(q, kt=kt, par=par):
                                ins = None
                                for n2 in range(2):
                                    ins = q.matmul(
                                        pk[:, n2 * 512:(n2 + 1) * 512],
                                        vh[:, kt * 128:(kt + 1) * 128],
                                        ptsb[:, par * 1024 + n2 * 512:par * 1024 + (n2 + 1) * 512],
                                        start=(kt == 0), stop=(kt == 15))
                                return ins
                            t_vv = sc.run("t", f_vv, deps=[t_v, ex_tok[kt], slot.get("ot_c")])
                            slot[("ptsb", par)] = t_vv
                            vv_last = t_vv

                    t_dc = sc.run("v", lambda q: q.tensor_copy(denr[:, :],
                                                               pk[0:1, 1024:2048]),
                                  deps=[t_dn, slot.get("denr_c")])
                    slot["dn_c"] = t_dc

                    def f_dt(q):
                        ins = None
                        for j in range(8):
                            ins = q.transpose(pq[:, 1536 + j:1536 + j + 1],
                                              denr[:, j * 128:(j + 1) * 128],
                                              G["ident_f"][0:1, 0:1])
                        return ins
                    t_dt = sc.run("t", f_dt, deps=[t_dc, vv_last])
                    t_dc2 = sc.run("v", lambda q: q.tensor_copy(denc[:, :],
                                                                pq[:, 1536:1544]),
                                   deps=[t_dt])
                    t_rd = sc.run("v", lambda q: q.reciprocal(rdn[:, :], denc[:, :]),
                                  deps=[t_dc2])
                    slot["denr_c"] = t_rd

                    t_oe = sc.run("v", lambda q: q.tensor_copy(osb[:, :],
                                                               pk[:, 0:1024]),
                                  deps=[vv_last, slot.get("osb_c")])
                    slot["ot_c"] = t_oe
                    if G.get("debug") and b == BC - 1 and h == 0 and qh == 0:
                        t_oe = sc.dma("g", "sd", lambda q: [
                            q.dma_start(out=G["osb_dbg"][:, :], in_=osb[:, :]),
                            q.dma_start(out=G["pt_dbg"][:, :], in_=ptsb[:, :])],
                            deps=[t_oe])
                        slot["ot_c"] = t_oe

                    tr_last = None
                    for grp in range(2):
                        base_c = 1024 if grp == 0 else 1792
                        tpo = pq[:, base_c:base_c + 256].bitcast(BF16)

                        def f_ot(q, grp=grp, tpo=tpo):
                            ins = None
                            for j in range(4):
                                i = grp * 4 + j
                                ins = q.transpose(tpo[:, j * 128:(j + 1) * 128],
                                                  osb[:, i * 128:(i + 1) * 128],
                                                  G["ident_b"][:, :])
                            return ins
                        t_ot = sc.run("t", f_ot, deps=[t_oe, slot.get(("otT", grp)),
                                                       t_dt, t_dc2])

                        def f_os(q, grp=grp, tpo=tpo, h=h, qh=qh):
                            ins = None
                            for j in range(4):
                                i = grp * 4 + j
                                tile = qh * 8 + i
                                ins = q.tensor_scalar(
                                    outall[:, tile * DM + h * HD1:tile * DM + (h + 1) * HD1],
                                    tpo[:, j * 128:(j + 1) * 128],
                                    rdn[:, i:i + 1], None, ALU.mult)
                            return ins
                        t_os = sc.run("v", f_os, deps=[t_ot, t_rd, slot.get("outall_c")])
                        slot[("otT", grp)] = t_os
                        tr_last = t_os
                    slot["osb_c"] = tr_last
                    slot[("stB", 1)] = tr_last
                    heads_done = tr_last
                slot["qT_c"] = vv_last
                slot["kT_c"] = vv_last
                slot["vh_c"] = vv_last
            slot["pq_free"] = heads_done

            if G.get("debug") and b == BC - 1:
                heads_done = sc.dma("g", "sd", lambda q: [
                    q.dma_start(out=G["oa_dbg"][:, :], in_=outall[:, :])],
                    deps=[heads_done])
            for sp in range(16):
                half = sp // 8
                spc = sp % 8
                par = sp % 2
                fp = pk[:, par * 1024:(par + 1) * 1024]

                def f_f(q, half=half, spc=spc, fp=fp):
                    ins = None
                    for jc in range(8):
                        tile = half * 8 + jc
                        for n2 in range(2):
                            ins = q.matmul(
                                fp[:, n2 * 512:(n2 + 1) * 512],
                                wsl[:, jc * DM + spc * 128:jc * DM + (spc + 1) * 128],
                                outall[:, tile * DM + n2 * 512:tile * DM + (n2 + 1) * 512],
                                start=(jc == 0), stop=(jc == 7))
                    return ins
                dep_f = [heads_done] + wo_toks + [slot.get(("fp", par))]
                if sp < 2:
                    dep_f += [slot.get("ot_c"), slot.get("dn_c")]
                t_f = sc.run("t", f_f, deps=dep_f)

                t_x2 = sc.dma("q", "qd", lambda q, sp=sp: [
                    q.dma_start(out=xt[:, :],
                                in_=src[base + sp * 128:base + (sp + 1) * 128, :])],
                    deps=[slot.get("x_c")])

                t_rs = sc.run("v", lambda q, fp=fp: q.tensor_tensor(
                    xt[:, :], fp, xt[:, :], ALU.add),
                    deps=[t_f, t_x2])
                slot[("fp", par)] = t_rs

                t_st2 = sc.dma("s", "sd", lambda q, sp=sp: [
                    q.dma_start(out=dst[base + sp * 128:base + (sp + 1) * 128, :],
                                in_=xt[:, :])], deps=[t_rs])
                slot["x_c"] = t_st2
            slot["yfm_c"] = slot.get("wsl_c")
            slot["outall_c"] = t_f
            slot["wsl_c"] = t_f
            slot["tp_ev"] = slot.get("tp_ev")

    sc.barrier()


def build_program(phases=(0, 1, 2, 3, 4, 5), debug=False):
    nc = bass.Bass("TRN2", target_bir_lowering=False)
    ap = lambda h: h.ap()

    xin = ap(nc.dram_tensor("xin", [T, DM], F32, kind="ExternalInput"))
    xout = ap(nc.dram_tensor("xout", [T, DM], F32, kind="ExternalOutput"))
    w = {}
    for name, shape in [
        ("l0f_Wr", [DM, 4]), ("l0f_W1", [NE, DM, I_FF]), ("l0f_b1", [NE, I_FF]),
        ("l0f_Wo", [I_FF, DM]), ("l0f_bo", [DM]), ("l0f_g", [DM]), ("l0f_b", [DM]),
        ("l0a_Wq", [DM, DM]), ("l0a_Wk", [DM, DM]), ("l0a_Wv", [DM, DM]),
        ("l0a_Wg", [DM, DM]),
        ("l1a_W1T", [DM, DM]), ("l1a_W2T", [DM, DM]), ("l1a_W3T", [DM, DM]),
        ("l1a_WoT", [DM, DM]),
        ("l1f_WrT", [DM, DM]), ("l1f_WkT", [DM, DM]), ("l1f_WvT", [DM, DM]),
        ("l1f_mur", [DM]), ("l1f_muk", [DM]),
    ]:
        w[name] = ap(nc.dram_tensor(name, shape, F32, kind="ExternalInput"))
    for name, arr in TABLES.items():
        w[name] = ap(nc.dram_tensor(name, list(arr.shape), BF16, kind="ExternalInput"))

    dk = "ExternalOutput" if debug else "Internal"
    xs = ap(nc.dram_tensor("xs", [T, DM], F32, kind="Internal"))
    yfm_hbm = ap(nc.dram_tensor("yfm_hbm", [NTT, 8, 128, 128], BF16, kind=dk))
    interm_hbm = ap(nc.dram_tensor("interm_hbm", [8, NTT, 128, 512], BF16,
                                   kind=dk))
    qkv_hbm = ap(nc.dram_tensor("qkv_hbm", [3, S, DM], BF16, kind=dk))
    masks_dbg = ap(nc.dram_tensor("masks_dbg", [128, NTT * 4], F32, kind=dk)) if debug else None
    lgs_dbg = ap(nc.dram_tensor("lgs_dbg", [NTT, 128, 4], F32, kind=dk)) if debug else None
    zsb_dbg = ap(nc.dram_tensor("zsb_dbg", [NTT, 128, DM], F32, kind=dk)) if debug else None
    oa_dbg = ap(nc.dram_tensor("oa_dbg", [128, 16 * DM], F32, kind=dk)) if debug else None
    osb_dbg = ap(nc.dram_tensor("osb_dbg", [128, 1024], F32, kind=dk)) if debug else None
    pt_dbg = ap(nc.dram_tensor("pt_dbg", [128, 2048], F32, kind=dk)) if debug else None
    yb_dbg = ap(nc.dram_tensor("yb_dbg", [4, 128, DM], F32, kind=dk)) if debug else None
    y2_dbg = ap(nc.dram_tensor("y2_dbg", [NTT, 128, DM], F32, kind=dk)) if debug else None

    phases = sorted(phases)

    with ExitStack() as ctx:
        block = ctx.enter_context(nc.Block())
        sc = Sched(nc, ctx, block)

        G = {"nc": nc, "w": w, "yfm": yfm_hbm, "interm": interm_hbm, "qkv": qkv_hbm,
             "lgs_dbg": lgs_dbg, "zsb_dbg": zsb_dbg, "y2_dbg": y2_dbg,
             "yb_dbg": yb_dbg, "oa_dbg": oa_dbg, "osb_dbg": osb_dbg,
             "pt_dbg": pt_dbg, "debug": debug}
        for nm, shape, dt in [
            ("ident_b", [128, 128], BF16), ("ident_f", [128, 128], F32),
            ("ones1", [1, 128], BF16), ("zrow", [1, 512], BF16),
            ("onec", [128, 1], BF16), ("c20", [128, 1], F32),
            ("masks", [128, NTT * 4], F32),
            ("tril_s", [128, 128], BF16),
            ("trilT_s", [128, 128], BF16), ("stat", [128, 8], F32),
            ("scr_b", [128, DM], BF16),
        ]:
            G[nm.replace("_s", "") if nm.endswith("_s") else nm] = ctx.enter_context(nc.sbuf_tensor(nm, shape, dt))

        t_tri = sc.dma("q", "qd", lambda q: [
            q.dma_start(out=G["tril"][:, :], in_=w["tril"][:, :]),
            q.dma_start(out=G["trilT"][:, :], in_=w["trilT"][:, :])])

        def f_init(q):
            q.memset(G["ones1"][:, :], 1.0)
            q.memset(G["zrow"][:, :], 0.0)
            q.memset(G["onec"][:, :], 1.0)
            q.memset(G["c20"][:, :], -20.0)
            q.memset(G["ident_b"][:, :], 0.0)
            q.affine_select(out=G["ident_b"][:, :], in_=G["ident_b"][:, :],
                            compare_op=ALU.not_equal, fill=1.0, base=0,
                            pattern=[[-1, 128]], channel_multiplier=1)
            q.memset(G["ident_f"][:, :], 0.0)
            return q.affine_select(out=G["ident_f"][:, :], in_=G["ident_f"][:, :],
                                   compare_op=ALU.not_equal, fill=1.0, base=0,
                                   pattern=[[-1, 128]], channel_multiplier=1)
        sc.run("g", f_init, deps=[t_tri])

        sc.barrier()

        emitters = {0: emit_l0ffn, 1: emit_l0attn, 2: emit_l0ffn,
                    3: emit_l1ffn, 4: emit_l1attn, 5: emit_l1ffn}
        PHASE_MARKS.clear()
        for ph in phases:
            src = xin if ph == phases[0] else xs
            dst = xout if ph == phases[-1] else xs
            G["tag"] = f"p{ph}"
            PHASE_MARKS.append((ph, nc.next_id()))
            emitters[ph](sc, G, src, dst)
        PHASE_MARKS.append((-1, nc.next_id()))

        if debug:
            sc.dma("s", "sd", lambda q: [
                q.dma_start(out=masks_dbg[:, :], in_=G["masks"][:, :])])

        @block.sync
        def _(q):
            q.wait_ge(sc.sem["sd"], sc.count["sd"])
            q.wait_ge(sc.sem["qd"], sc.count["qd"])
    return nc


def make_inmap(core, inputs):
    x = np.asarray(inputs["L0_attn_x"])
    xsh = np.ascontiguousarray(x.reshape(32, S, DM)[core * BC:(core + 1) * BC]
                               .reshape(T, DM))
    m = {"xin": xsh,
         "l0f_Wr": inputs["L0_ffn_W_router"], "l0f_W1": inputs["L0_ffn_W1"],
         "l0f_b1": inputs["L0_ffn_b1"], "l0f_Wo": inputs["L0_ffn_Wo"],
         "l0f_bo": inputs["L0_ffn_bo"], "l0f_g": inputs["L0_ffn_ln_gamma"],
         "l0f_b": inputs["L0_ffn_ln_beta"],
         "l0a_Wq": inputs["L0_attn_Wq"], "l0a_Wk": inputs["L0_attn_Wk"],
         "l0a_Wv": inputs["L0_attn_Wv"], "l0a_Wg": inputs["L0_attn_Wg"],
         "l1a_W1T": np.asarray(inputs["L1_attn_W1"]).T,
         "l1a_W2T": np.asarray(inputs["L1_attn_W2"]).T,
         "l1a_W3T": np.asarray(inputs["L1_attn_W3"]).T,
         "l1a_WoT": np.asarray(inputs["L1_attn_Wo"]).T,
         "l1f_WrT": np.asarray(inputs["L1_ffn_W_r"]).T,
         "l1f_WkT": np.asarray(inputs["L1_ffn_W_k"]).T,
         "l1f_WvT": np.asarray(inputs["L1_ffn_W_v"]).T,
         "l1f_mur": inputs["L1_ffn_mu_r"], "l1f_muk": inputs["L1_ffn_mu_k"]}
    for k, v in TABLES.items():
        m[k] = v
    return {k: np.ascontiguousarray(np.asarray(v)) for k, v in m.items()}


def kernel(**inputs):
    nc = build_program(phases=(0, 1, 2, 3, 4, 5))
    in_maps = [make_inmap(core, inputs) for core in range(8)]
    res = run_bass_kernel_spmd(nc, in_maps, core_ids=list(range(8)))
    out = np.concatenate([np.asarray(r["xout"]).reshape(BC, S, DM)
                          for r in res.results], axis=0)
    return out.astype(np.float32)


# version 03
# speedup vs baseline: 1.1929x; 1.0412x over previous
import math
from contextlib import ExitStack

import numpy as np
import ml_dtypes
import concourse.bass as bass
import concourse.mybir as mybir
from concourse.bass_utils import run_bass_kernel_spmd

F32 = mybir.dt.float32
BF16 = mybir.dt.bfloat16
AF = mybir.ActivationFunctionType
ALU = mybir.AluOpType
AX = mybir.AxisListType

BC = 4
S = 2048
T = BC * S
DM = 1024
NTT = T // 128
I_FF = 4096
NE = 4

BF = ml_dtypes.bfloat16


def host_tables():
    t = {}
    H, D = 16, 64
    a = 1.0 / 10000.0 ** np.linspace(0.0, 1.0, D // 2)
    angle = np.repeat(a, 2)
    decay = np.log(1.0 - 2.0 ** (-5.0 - np.arange(H, dtype=np.float64)))
    idx = np.arange(S, dtype=np.float64)
    sin = np.sin(idx[:, None] * angle[None, :])
    cos = np.cos(idx[:, None] * angle[None, :])
    sinp = sin.copy()
    sinp[:, 0::2] *= -1.0
    qdec = np.exp(idx[:, None] * decay[None, :])
    ed = np.exp(decay)[None, :]
    rowsum = (1.0 - np.exp((idx[:, None] + 1.0) * decay[None, :])) / (1.0 - ed)
    qdec = qdec / np.sqrt(rowsum)
    kdec = np.exp(-idx[:, None] * decay[None, :]) * (DM ** -0.5)
    cosT = np.tile(cos, (1, H))
    sinT = np.tile(sinp, (1, H))
    qdE = np.repeat(qdec, D, axis=1)
    kdE = np.repeat(kdec, D, axis=1)
    t["l0_qA"] = (cosT * qdE).astype(BF)
    t["l0_qB"] = (sinT * qdE).astype(BF)
    t["l0_kA"] = (cosT * kdE).astype(BF)
    t["l0_kB"] = (sinT * kdE).astype(BF)
    D1 = 128
    inv = 1.0 / 10000.0 ** (np.arange(0, D1, 2, dtype=np.float64) / D1)
    tt = np.arange(S, dtype=np.float64)
    freqs = np.repeat(tt[:, None] * inv[None, :], 2, axis=-1)
    c1 = np.cos(freqs)
    s1 = np.sin(freqs)
    s1[:, 0::2] *= -1.0
    t["l1_cA"] = np.ascontiguousarray(c1.reshape(256, 1024)).astype(BF)
    t["l1_cB"] = np.ascontiguousarray(s1.reshape(256, 1024)).astype(BF)
    tri = np.tril(np.ones((128, 128), np.float32))
    t["tril"] = tri.astype(BF)
    t["trilT"] = np.ascontiguousarray(tri.T).astype(BF)
    return t


TABLES = host_tables()
PHASE_MARKS = []

_SECTION = {"t": "tensor", "v": "vector", "s": "scalar", "g": "gpsimd", "q": "sync"}
_OWN_SEM = {"t": "t", "v": "v", "s": "s", "g": "g", "q": "q"}


class Sched:
    def __init__(self, nc, ctx, block):
        self.nc = nc
        self.block = block
        self.sem = {}
        self.count = {}
        self.waited = {}
        self.pending_barrier = {}
        for name in ["t", "v", "s", "g", "q", "qd", "sd"]:
            self.sem[name] = ctx.enter_context(nc.semaphore("sem_" + name))
            self.count[name] = 0

    def barrier(self):
        toks = [(n, self.count[n]) for n in self.sem if self.count[n] > 0]
        for eng in _SECTION:
            self.pending_barrier[eng] = toks

    def _emit_waits(self, q, eng, deps):
        alldeps = list(deps)
        if eng in self.pending_barrier:
            alldeps += self.pending_barrier.pop(eng)
        own = _OWN_SEM[eng]
        best = {}
        for d in alldeps:
            if d is None:
                continue
            sname, val = d
            if val <= 0:
                continue
            if sname == own:
                val = min(val, self.count[own])
            if val <= 0:
                continue
            best[sname] = max(best.get(sname, 0), val)
        for sname, val in best.items():
            key = (eng, sname)
            if self.waited.get(key, 0) >= val:
                continue
            q.wait_ge(self.sem[sname], val)
            self.waited[key] = val

    def run(self, eng, fn, deps=(), serialize=False):
        sec = getattr(self.block, _SECTION[eng])
        own = _OWN_SEM[eng]
        prev = self.count[own]

        @sec
        def _(q):
            if serialize and prev > 0:
                q.wait_ge(self.sem[own], prev)
            self._emit_waits(q, eng, deps)
            last = fn(q)
            last.then_inc(self.sem[own], 1)
        self.count[own] += 1
        return (own, self.count[own])

    def dma(self, queue_eng, sem_name, fn, deps=()):
        sec = getattr(self.block, _SECTION[queue_eng])

        @sec
        def _(q):
            self._emit_waits(q, queue_eng, deps)
            for ins in fn(q):
                ins.then_inc(self.sem[sem_name], 16)
                self.count[sem_name] += 16
        return (sem_name, self.count[sem_name])


def emit_ln(sc, G, xv, ybufs, eps, deps):
    stat, scr = G["stat"], G["scr_b"]

    def f1(q):
        q.tensor_scalar(scr[:, :], xv, 0.0, None, ALU.add, ALU.add,
                        accum_out=stat[:, 0:1])
        return q.scalar_tensor_tensor(scr[:, :], xv, 1.0, xv,
                                      ALU.mult, ALU.mult,
                                      accum_out=stat[:, 1:2])
    t1 = sc.run("v", f1, deps=deps)

    def f2(q):
        q.tensor_scalar_mul(stat[:, 2:3], stat[:, 0:1], 1.0 / DM)
        return q.tensor_scalar_mul(stat[:, 3:4], stat[:, 1:2], 1.0 / DM)
    t2 = sc.run("v", f2, deps=[t1])

    t3 = sc.run("s", lambda q: q.activation(stat[:, 4:5], stat[:, 2:3], AF.Square),
                deps=[t2])

    t4 = sc.run("v", lambda q: q.scalar_tensor_tensor(
        stat[:, 3:4], stat[:, 3:4], float(eps), stat[:, 4:5],
        ALU.add, ALU.subtract), deps=[t3, t2])

    t5 = sc.run("s", lambda q: q.activation(stat[:, 4:5], stat[:, 3:4], AF.Sqrt),
                deps=[t4])

    t6 = sc.run("v", lambda q: q.reciprocal(stat[:, 5:6], stat[:, 4:5]), deps=[t5])

    def f7(q):
        last = None
        for buf in ybufs:
            last = q.tensor_scalar(buf, xv, stat[:, 2:3], stat[:, 5:6],
                                   ALU.subtract, ALU.mult)
        return last
    return sc.run("v", f7, deps=[t6])


def emit_l0ffn(sc, G, src, dst):
    nc, w = G["nc"], G["w"]
    tag = G["tag"]

    def scope():
        return ExitStack()

    with scope() as cx:
        def lsb(name, shape, dt=F32):
            return cx.enter_context(nc.sbuf_tensor(f"{tag}_{name}", shape, dt))

        def lps(name, shape, dt=F32):
            return cx.enter_context(nc.psum_tensor(f"{tag}_{name}", shape, dt))

        xt = lsb("xt", [128, 2 * DM])
        yb = lsb("yb", [128, DM], BF16)
        yf32 = lsb("yf32", [128, DM])
        ylo = lsb("ylo", [128, DM], BF16)
        yfm = lsb("yfm", [128, 2 * DM], BF16)
        ylofm = lsb("ylofm", [128, DM], BF16)
        wrf = lsb("wrf", [128, 8 * 4])
        wrhi = lsb("wrhi", [128, 8 * 4], BF16)
        wrlo = lsb("wrlo", [128, 8 * 4], BF16)
        wrlof = lsb("wrlof", [128, 8 * 4])
        lgs = lsb("lgs", [128, 4])
        mtmp = lsb("mtmp", [128, 2])
        tp = lps("tp", [128, 2 * DM], BF16)
        lg = lps("lg", [128, 4])

        t_wr = sc.dma("q", "qd", lambda q: [
            q.dma_start(out=wrf[:, :].rearrange("p (cc e) -> p cc e", cc=8),
                        in_=w["l0f_Wr"].rearrange("(cc p) e -> p cc e", p=128))])

        t_wr1 = sc.run("g", lambda q: q.tensor_copy(wrhi[:, :], wrf[:, :]),
                       deps=[t_wr])
        t_wr2 = sc.run("g", lambda q: q.tensor_sub(wrlof[:, :], wrf[:, :], wrhi[:, :]),
                       deps=[t_wr1])
        t_wrc = sc.run("g", lambda q: q.tensor_copy(wrlo[:, :], wrlof[:, :]),
                       deps=[t_wr2])

        slot = {}
        t_st = None
        for tt in range(NTT):
            par = tt % 2
            xv = xt[:, par * DM:(par + 1) * DM]
            yfmv = yfm[:, par * DM:(par + 1) * DM]

            t_x = sc.dma("q", "qd", lambda q, tt=tt, xv=xv: [
                q.dma_start(out=xv, in_=src[tt * 128:(tt + 1) * 128, :])],
                deps=[slot.get(("x", par))])

            t_ln = emit_ln(sc, G, xv, [yb[:, :], yf32[:, :]], 1e-5,
                           deps=[t_x, slot.get("yb_c")])
            slot[("x", par)] = t_ln

            t_lo = sc.run("v", lambda q: q.tensor_sub(ylo[:, :], yf32[:, :], yb[:, :]),
                          deps=[t_ln])


            def f_tr(q):
                for cc in range(8):
                    q.transpose(tp[:, cc * 128:(cc + 1) * 128],
                                yb[:, cc * 128:(cc + 1) * 128], G["ident_b"][:, :])
                ins = None
                for cc in range(8):
                    ins = q.transpose(tp[:, DM + cc * 128:DM + (cc + 1) * 128],
                                      ylo[:, cc * 128:(cc + 1) * 128], G["ident_b"][:, :])
                return ins
            t_tr = sc.run("t", f_tr, deps=[t_lo, slot.get("tp_ev")])
            slot["yb_c"] = t_tr
            if G.get("debug") and tt == 47:
                t_tap = sc.dma("g", "sd", lambda q, xv=xv: [
                    q.dma_start(out=G["yb_dbg"][0], in_=yb[:, :]),
                    q.dma_start(out=G["yb_dbg"][1], in_=yf32[:, :]),
                    q.dma_start(out=G["yb_dbg"][2], in_=xv),
                    q.dma_start(out=G["yb_dbg"][3][:, 0:8], in_=G["stat"][:, :])], deps=[t_tr])
                slot["yb_c"] = t_tap
                slot[("x", tt % 2)] = t_tap

            def f_ev(q, yfmv=yfmv):
                q.tensor_copy(yfmv, tp[:, 0:DM])
                return q.tensor_copy(ylofm[:, :], tp[:, DM:2 * DM])
            t_ev = sc.run("v", f_ev,
                          deps=[t_tr, slot.get(("yfm", par)), slot.get("ylofm_c")])
            slot["tp_ev"] = t_ev

            def f_rt(q, yfmv=yfmv):
                prods = [(yfmv, wrhi), (yfmv, wrlo), (ylofm[:, :], wrhi)]
                ins = None
                n = 0
                for (lh, rh) in prods:
                    for cc in range(8):
                        n += 1
                        ins = q.matmul(lg[:, :], lh[:, cc * 128:(cc + 1) * 128],
                                       rh[:, cc * 4:(cc + 1) * 4],
                                       start=(n == 1), stop=(n == 24))
                return ins
            t_rt = sc.run("t", f_rt, deps=[t_ev, t_wrc, slot.get("lg_ev")])
            slot["ylofm_c"] = t_rt

            t_cp = sc.run("v", lambda q: q.tensor_copy(lgs[:, :], lg[:, :]),
                          deps=[t_rt])
            t_mx = sc.run("v", lambda q: q.reduce_max(
                G["stat"][:, 6:7], lgs[:, :], axis=AX.X),
                deps=[t_cp], serialize=True)

            def f_mk(q, tt=tt):
                ins = None
                for e in range(4):
                    ins = q.tensor_tensor(
                        G["masks"][:, tt * 4 + e:tt * 4 + e + 1],
                        lgs[:, e:e + 1], G["stat"][:, 6:7], ALU.is_ge)
                return ins
            t_mk = sc.run("v", f_mk, deps=[t_mx], serialize=True)
            slot["lg_ev"] = t_mk
            if G.get("debug"):
                slot["lg_ev"] = sc.dma("s", "sd", lambda q, tt=tt: [
                    q.dma_start(out=G["lgs_dbg"][tt], in_=lgs[:, :])], deps=[t_mk])

            t_st = sc.dma("s", "sd", lambda q, tt=tt, yfmv=yfmv: [
                q.dma_start(out=G["yfm"][tt].rearrange("cc p t -> p cc t"),
                            in_=yfmv.rearrange("p (cc t) -> p cc t", cc=8))],
                deps=[t_ev])
            slot[("yfm", par)] = t_st

    sc.barrier()

    with scope() as cx:
        def lsb(name, shape, dt=F32):
            return cx.enter_context(nc.sbuf_tensor(f"{tag}_{name}", shape, dt))

        def lps(name, shape, dt=F32):
            return cx.enter_context(nc.psum_tensor(f"{tag}_{name}", shape, dt))

        w1f = lsb("w1f", [128, 8 * 512])
        w1b = lsb("w1b", [128, NE * 8 * 512], BF16)
        b1f = lsb("b1f", [1, NE * 512])
        b1b = lsb("b1b", [1, NE * 512], BF16)
        yfs = lsb("yfs", [128, 2 * DM], BF16)
        hsb = lsb("hsb", [128, 2 * 2048], BF16)
        isb = lsb("isb", [128, 2 * 512], BF16)
        hp = lps("hp", [128, 2 * 2048])

        aslot = {}
        t_is = None
        for ic in range(8):
            w_ready = []
            for e in range(NE):
                t_w = sc.dma("q", "qd", lambda q, e=e, ic=ic: [
                    q.dma_start(out=w1f[:, :].rearrange("p (cc i) -> p cc i", cc=8),
                                in_=w["l0f_W1"][e, :, ic * 512:(ic + 1) * 512]
                                .rearrange("(cc p) i -> p cc i", p=128))],
                    deps=[aslot.get("w1f_c")])

                def f_c(q, e=e):
                    return q.tensor_copy(w1b[:, e * 4096:(e + 1) * 4096], w1f[:, :])
                t_c = sc.run("g", f_c, deps=[t_w, aslot.get("w1b_c")])
                aslot["w1f_c"] = t_c
                w_ready.append(t_c)

            t_b1 = sc.dma("q", "qd", lambda q, ic=ic: [
                q.dma_start(out=b1f[:, :].rearrange("o (e i) -> o e i", e=NE),
                            in_=w["l0f_b1"][:, ic * 512:(ic + 1) * 512].unsqueeze(0))],
                deps=[aslot.get("b1_c")])
            t_b1c = sc.run("g", lambda q: q.tensor_copy(b1b[:, :], b1f[:, :]),
                           deps=[t_b1])
            aslot["b1_c"] = t_b1c
            w_ready.append(t_b1c)

            for tt in range(NTT):
                par = tt % 2
                yv = yfs[:, par * DM:(par + 1) * DM]
                hv = hp[:, par * 2048:(par + 1) * 2048]
                hbv = hsb[:, par * 2048:(par + 1) * 2048]
                iv = isb[:, par * 512:(par + 1) * 512]

                t_y = sc.dma("q", "qd", lambda q, tt=tt, yv=yv: [
                    q.dma_start(out=yv.rearrange("p (cc t) -> p cc t", cc=8),
                                in_=G["yfm"][tt].rearrange("cc p t -> p cc t"))],
                    deps=[aslot.get(("yfs", par))])

                def f_h(q, yv=yv, hv=hv):
                    ins = None
                    for e in range(NE):
                        ins = q.matmul(hv[:, e * 512:(e + 1) * 512], G["ones1"][:, :],
                                       b1b[:, e * 512:(e + 1) * 512],
                                       start=True, stop=False)
                        for cc in range(8):
                            ins = q.matmul(
                                hv[:, e * 512:(e + 1) * 512],
                                yv[:, cc * 128:(cc + 1) * 128],
                                w1b[:, e * 4096 + cc * 512:e * 4096 + (cc + 1) * 512],
                                start=False, stop=(cc == 7))
                    return ins
                t_h = sc.run("t", f_h, deps=[t_y] + w_ready + [aslot.get(("hp", par))])
                aslot[("yfs", par)] = t_h
                if tt == NTT - 1:
                    aslot["w1b_c"] = t_h

                t_g = sc.run("s", lambda q, hv=hv, hbv=hbv:
                             q.activation(hbv, hv, AF.Gelu),
                             deps=[t_h, aslot.get(("hsb", par))])
                aslot[("hp", par)] = t_g

                def f_cb(q, tt=tt, hbv=hbv, iv=iv):
                    q.tensor_scalar(iv, hbv[:, 0:512],
                                    G["masks"][:, tt * 4:tt * 4 + 1], None, ALU.mult)
                    ins = None
                    for e in range(1, NE):
                        ins = q.scalar_tensor_tensor(
                            iv, hbv[:, e * 512:(e + 1) * 512],
                            G["masks"][:, tt * 4 + e:tt * 4 + e + 1],
                            iv, ALU.mult, ALU.add)
                    return ins
                t_cb = sc.run("v", f_cb, deps=[t_g, aslot.get(("isb", par))])
                aslot[("hsb", par)] = t_cb

                t_is = sc.dma("s", "sd", lambda q, ic=ic, tt=tt, iv=iv: [
                    q.dma_start(out=G["interm"][ic, tt], in_=iv)], deps=[t_cb])
                aslot[("isb", par)] = t_is

    sc.barrier()

    with scope() as cx:
        def lsb(name, shape, dt=F32):
            return cx.enter_context(nc.sbuf_tensor(f"{tag}_{name}", shape, dt))

        def lps(name, shape, dt=F32):
            return cx.enter_context(nc.psum_tensor(f"{tag}_{name}", shape, dt))

        wof = lsb("wof", [128, 4 * DM])
        wob = lsb("wob", [128, 32 * DM], BF16)
        bo_b = lsb("bo_b", [128, DM])
        gam_b = lsb("gam_b", [128, DM])
        bet_b = lsb("bet_b", [128, DM])
        row_f = lsb("row_f", [1, DM])
        row_b = lsb("row_b", [1, DM], BF16)
        ib = lsb("ib", [128, 2 * 4096], BF16)
        itf = lsb("itf", [128, 4096], BF16)
        zsb = lsb("zsb", [128, DM])
        ztmp = lsb("ztmp", [128, DM])
        xt2 = lsb("xt2", [128, 2 * DM])
        y2 = lsb("y2", [128, DM])
        xo = lsb("xo", [128, 2 * DM])
        tpb = lps("tpb", [128, 2048], BF16)
        zp = lps("zp", [128, 2 * DM])

        bslot = {}
        bc_ps_t = cx.enter_context(nc.psum_tensor(f"{tag}_bcps", [128, DM], F32))
        t_prev = None
        for name, dstbuf in [("l0f_bo", bo_b), ("l0f_g", gam_b), ("l0f_b", bet_b)]:
            t_r = sc.dma("q", "qd", lambda q, name=name: [
                q.dma_start(out=row_f[:, :], in_=w[name].unsqueeze(0))],
                deps=[t_prev])
            t_c = sc.run("v", lambda q: q.tensor_copy(row_b[:, :], row_f[:, :]),
                         deps=[t_r, t_prev])

            def f_bc(q):
                ins = None
                for nch in range(DM // 512):
                    ins = q.matmul(bc_ps_t[:, nch * 512:(nch + 1) * 512],
                                   G["ones1"][:, :],
                                   row_b[:, nch * 512:(nch + 1) * 512],
                                   start=True, stop=True)
                return ins
            t_bc = sc.run("t", f_bc, deps=[t_c, t_prev])
            t_prev = sc.run("v", lambda q, dstbuf=dstbuf: q.tensor_copy(
                dstbuf[:, :], bc_ps_t[:, :]), deps=[t_bc])
        t_bcast = t_prev
        wo_casts = [t_bcast]
        for gq in range(8):
            t_wo = sc.dma("q", "qd", lambda q, gq=gq: [
                q.dma_start(out=wof[:, :].rearrange("p (cc d) -> p cc d", cc=4),
                            in_=w["l0f_Wo"][gq * 512:(gq + 1) * 512, :]
                            .rearrange("(cc p) d -> p cc d", p=128))],
                deps=[bslot.get("wof_c")])
            t_woc = sc.run("g", lambda q, gq=gq: q.tensor_copy(
                wob[:, gq * 4 * DM:(gq + 1) * 4 * DM], wof[:, :]), deps=[t_wo])
            bslot["wof_c"] = t_woc
            wo_casts.append(t_woc)

        t_xs = None
        for tt in range(NTT):
            par = tt % 2
            ibv = ib[:, par * 4096:(par + 1) * 4096]
            xv = xt2[:, par * DM:(par + 1) * DM]
            zv = zp[:, par * DM:(par + 1) * DM]
            xov = xo[:, par * DM:(par + 1) * DM]

            t_i = sc.dma("q", "qd", lambda q, tt=tt, ibv=ibv, xv=xv: [
                q.dma_start(out=ibv.rearrange("p (ic i) -> p ic i", ic=8),
                            in_=G["interm"][:, tt].rearrange("ic p i -> p ic i")),
                q.dma_start(out=xv, in_=src[tt * 128:(tt + 1) * 128, :])],
                deps=[bslot.get(("ib", par)), bslot.get(("x2", par))])

            tr_toks = []
            for rnd in range(2):
                def f_tr(q, rnd=rnd, ibv=ibv):
                    ins = None
                    for j in range(16):
                        ci = rnd * 16 + j
                        ins = q.transpose(tpb[:, j * 128:(j + 1) * 128],
                                          ibv[:, ci * 128:(ci + 1) * 128],
                                          G["ident_b"][:, :])
                    return ins
                t_tr = sc.run("t", f_tr, deps=[t_i, bslot.get("tpb_ev")])

                def f_te(q, rnd=rnd):
                    return q.tensor_copy(itf[:, rnd * 2048:(rnd + 1) * 2048], tpb[:, :])
                t_te = sc.run("v", f_te, deps=[t_tr, bslot.get("itf_c")])
                bslot["tpb_ev"] = t_te
                tr_toks.append(t_te)

            def f_z(q, zv=zv):
                ins = None
                for ci in range(32):
                    for dch in range(2):
                        ins = q.matmul(
                            zv[:, dch * 512:(dch + 1) * 512],
                            itf[:, ci * 128:(ci + 1) * 128],
                            wob[:, ci * DM + dch * 512:ci * DM + (dch + 1) * 512],
                            start=(ci == 0), stop=(ci == 31))
                return ins
            t_z = sc.run("t", f_z, deps=tr_toks + wo_casts + [bslot.get(("zp", par))])
            bslot[("ib", par)] = t_z
            bslot["itf_c"] = t_z

            t_y2 = emit_ln(sc, G, xv, [y2[:, :]], 1e-5,
                           deps=[t_i, bslot.get("y2_c")])

            t_ze = sc.run("s", lambda q, zv=zv: q.copy(zsb[:, :], zv),
                          deps=[t_z, bslot.get("zsb_c")])
            bslot[("zp", par)] = t_ze

            def f_za(q):
                q.tensor_add(zsb[:, :], zsb[:, :], bo_b[:, :])
                return q.tensor_add(zsb[:, :], zsb[:, :], y2[:, :])
            t_za = sc.run("g", f_za, deps=[t_ze, t_y2])
            bslot["y2_c"] = t_za
            if G.get("debug"):
                t_za = sc.dma("s", "sd", lambda q, tt=tt: [
                    q.dma_start(out=G["zsb_dbg"][tt], in_=zsb[:, :]),
                    q.dma_start(out=G["y2_dbg"][tt], in_=y2[:, :])], deps=[t_za])

            t_iln = emit_ln(sc, G, zsb[:, :], [ztmp[:, :]], 1e-12,
                            deps=[t_za, bslot.get("ztmp_c")])
            bslot["zsb_c"] = t_iln

            def f_gb(q):
                q.tensor_mul(ztmp[:, :], ztmp[:, :], gam_b[:, :])
                return q.tensor_add(ztmp[:, :], ztmp[:, :], bet_b[:, :])
            t_gb = sc.run("g", f_gb, deps=[t_iln])

            t_res = sc.run("v", lambda q, xov=xov, xv=xv: q.scalar_tensor_tensor(
                xov, ztmp[:, :], 0.5, xv, ALU.mult, ALU.add),
                deps=[t_gb, bslot.get(("xo", par))])
            bslot["ztmp_c"] = t_res
            bslot[("x2", par)] = t_res

            t_xs = sc.dma("s", "sd", lambda q, tt=tt, xov=xov: [
                q.dma_start(out=dst[tt * 128:(tt + 1) * 128, :], in_=xov)],
                deps=[t_res])
            bslot[("xo", par)] = t_xs

    sc.barrier()



def emit_l0attn(sc, G, src, dst):
    nc, w = G["nc"], G["w"]
    tag = G["tag"]
    H, HD = 16, 64

    with ExitStack() as cx:
        def lsb(name, shape, dt=F32):
            return cx.enter_context(nc.sbuf_tensor(f"{tag}_{name}", shape, dt))

        def lps(name, shape, dt=F32):
            return cx.enter_context(nc.psum_tensor(f"{tag}_{name}", shape, dt))

        yfm = lsb("yfm", [128, 8 * S], BF16)
        qrf = lsb("qrf", [128, 8 * S], BF16)
        krf = lsb("krf", [128, 8 * S], BF16)
        vtm = lsb("vtm", [128, 16 * DM], BF16)
        outall = lsb("outall", [128, 16 * DM], BF16)
        wsl = lsb("wsl", [128, 8 * DM], BF16)
        wf = lsb("wf", [128, DM // 2])
        xt = lsb("xt", [128, DM])
        yb = lsb("yb", [128, DM], BF16)
        rtm = lsb("rtm", [128, 2 * DM], BF16)
        tabs = lsb("tabs", [128, 2 * DM], BF16)
        asb = lsb("asb", [128, 2048], BF16)
        dens = lsb("dens", [128, 16])
        rden = lsb("rden", [128, 16])
        dsb = lsb("dsb", [128, 256], BF16)
        dtmp = lsb("dtmp", [128, 16])
        otmp = lsb("otmp", [128, DM], BF16)
        pq = lps("pq", [128, 2048])
        pk = lps("pk", [128, 2048])

        def bank_chunks(lo, hi):
            out = []
            while lo < hi:
                nxt = min((lo // 512 + 1) * 512, hi)
                out.append((lo, nxt))
                lo = nxt
            return out

        slot = {}
        for b in range(BC):
            base = b * S
            PHASE_MARKS.append((f"p1s1_b{b}", nc.next_id()))

            def load_w(nm, dep_consumed):
                toks = []
                dep = dep_consumed
                for cc in range(16):
                    t_l = sc.dma("q", "qd", lambda q, nm=nm, cc=cc: [
                        q.dma_start(out=wf[:, :],
                                    in_=w[nm][(cc // 2) * 128:(cc // 2 + 1) * 128,
                                              (cc % 2) * 512:(cc % 2 + 1) * 512])],
                        deps=[dep, slot.get("wf_c")])
                    t_c = sc.run("g", lambda q, cc=cc: q.tensor_copy(
                        wsl[:, cc * 512:(cc + 1) * 512], wf[:, :]), deps=[t_l])
                    slot["wf_c"] = t_c
                    toks.append(t_c)
                return toks

            for ts_ in range(16):
                t_x = sc.dma("q", "qd", lambda q, ts_=ts_: [
                    q.dma_start(out=xt[:, :],
                                in_=src[base + ts_ * 128:base + (ts_ + 1) * 128, :])],
                    deps=[slot.get("x_c")])
                t_ln = emit_ln(sc, G, xt[:, :], [yb[:, :]], 1e-5,
                               deps=[t_x, slot.get("yb_c")])
                slot["x_c"] = t_ln

                tp = pk[:, 0:512].bitcast(BF16)

                def f_tr(q, tp=tp):
                    ins = None
                    for cc in range(8):
                        ins = q.transpose(tp[:, cc * 128:(cc + 1) * 128],
                                          yb[:, cc * 128:(cc + 1) * 128],
                                          G["ident_b"][:, :])
                    return ins
                t_tr = sc.run("t", f_tr, deps=[t_ln, slot.get("tp_ev")])
                slot["yb_c"] = t_tr

                def f_te(q, ts_=ts_, tp=tp):
                    ov = yfm[:, :].rearrange("p (cc t) -> p cc t", t=S)[
                        :, :, ts_ * 128:(ts_ + 1) * 128]
                    return q.tensor_copy(ov, tp.rearrange("p (cc t) -> p cc t", t=128))
                t_te = sc.run("v", f_te, deps=[t_tr, slot.get("yfm_c")])
                slot["tp_ev"] = t_te
            yfm_done = sc.count["v"]
            t_yfm = ("v", yfm_done)

            for (wnm, dstf, ta, tb) in [("l0a_Wq", qrf, "l0_qA", "l0_qB"),
                                        ("l0a_Wk", krf, "l0_kA", "l0_kB")]:
                wtoks = load_w(wnm, slot.get("wsl_c"))
                for ts_ in range(16):
                    par = ts_ % 2
                    pv = pq[:, par * 1024:(par + 1) * 1024]

                    def f_mm(q, ts_=ts_, pv=pv):
                        ins = None
                        for cc in range(8):
                            lh = yfm[:, cc * S + ts_ * 128:cc * S + (ts_ + 1) * 128]
                            for n2 in range(2):
                                ins = q.matmul(pv[:, n2 * 512:(n2 + 1) * 512], lh,
                                               wsl[:, cc * DM + n2 * 512:cc * DM + (n2 + 1) * 512],
                                               start=(cc == 0), stop=(cc == 7))
                        return ins
                    t_mm = sc.run("t", f_mm,
                                  deps=[t_yfm] + wtoks + [slot.get(("pp", par))])

                    t_tab = sc.dma("q", "qd", lambda q, ts_=ts_, ta=ta, tb=tb: [
                        q.dma_start(out=tabs[:, 0:DM],
                                    in_=w[ta][ts_ * 128:(ts_ + 1) * 128, :]),
                        q.dma_start(out=tabs[:, DM:2 * DM],
                                    in_=w[tb][ts_ * 128:(ts_ + 1) * 128, :])],
                        deps=[slot.get("tab_c")])

                    def f_r1(q, pv=pv):
                        q.tensor_mul(rtm[:, 0:DM], pv, tabs[:, 0:DM])
                        pvv = pv.rearrange("p (a c) -> p a c", c=2)
                        bvv = tabs[:, DM:2 * DM].rearrange("p (a c) -> p a c", c=2)
                        mvv = rtm[:, DM:2 * DM].rearrange("p (a c) -> p a c", c=2)
                        q.tensor_mul(mvv[:, :, 0:1], pvv[:, :, 1:2], bvv[:, :, 0:1])
                        return q.tensor_mul(mvv[:, :, 1:2], pvv[:, :, 0:1], bvv[:, :, 1:2])
                    t_r1 = sc.run("v", f_r1, deps=[t_mm, t_tab, slot.get("rtm_c")])
                    slot["tab_c"] = t_r1
                    slot[("pp", par)] = t_r1

                    t_r2 = sc.run("v", lambda q: q.tensor_add(
                        yb[:, :], rtm[:, 0:DM], rtm[:, DM:2 * DM]),
                        deps=[t_r1, slot.get("yb_c")])
                    slot["rtm_c"] = t_r2

                    tp = pk[:, 0:512].bitcast(BF16)

                    def f_tr2(q, tp=tp):
                        ins = None
                        for cc in range(8):
                            ins = q.transpose(tp[:, cc * 128:(cc + 1) * 128],
                                              yb[:, cc * 128:(cc + 1) * 128],
                                              G["ident_b"][:, :])
                        return ins
                    t_tr2 = sc.run("t", f_tr2, deps=[t_r2, slot.get("tp_ev")])
                    slot["yb_c"] = t_tr2

                    def f_te2(q, ts_=ts_, tp=tp, dstf=dstf):
                        ov = dstf[:, :].rearrange("p (cc t) -> p cc t", t=S)[
                            :, :, ts_ * 128:(ts_ + 1) * 128]
                        return q.tensor_copy(ov, tp.rearrange("p (cc t) -> p cc t", t=128))
                    t_te2 = sc.run("v", f_te2, deps=[t_tr2, slot.get(("qk_c", wnm))])
                    slot["tp_ev"] = t_te2
                slot["wsl_c"] = t_mm
            qk_done = ("v", sc.count["v"])

            wtoks = load_w("l0a_Wv", slot.get("wsl_c"))
            for ts_ in range(16):
                par = ts_ % 2
                pv = pq[:, par * 1024:(par + 1) * 1024]

                def f_mmv(q, ts_=ts_, pv=pv):
                    ins = None
                    for cc in range(8):
                        lh = yfm[:, cc * S + ts_ * 128:cc * S + (ts_ + 1) * 128]
                        for n2 in range(2):
                            ins = q.matmul(pv[:, n2 * 512:(n2 + 1) * 512], lh,
                                           wsl[:, cc * DM + n2 * 512:cc * DM + (n2 + 1) * 512],
                                           start=(cc == 0), stop=(cc == 7))
                    return ins
                t_mmv = sc.run("t", f_mmv,
                               deps=[t_yfm] + wtoks + [slot.get(("pp", par))])
                t_ev = sc.run("v", lambda q, ts_=ts_, pv=pv: q.tensor_copy(
                    vtm[:, ts_ * DM:(ts_ + 1) * DM], pv),
                    deps=[t_mmv, slot.get("vtm_c")])
                slot[("pp", par)] = t_ev
                slot["wsl_c2"] = t_mmv
            v_done = ("v", sc.count["v"])
            slot["wsl_c"] = slot["wsl_c2"]

            wg_toks = load_w("l0a_Wg", slot.get("wsl_c"))
            yv3 = yfm[:, :].rearrange("p (cc t) -> p cc t", t=S)
            for i in range(16):
                par = i % 2
                pv = pq[:, par * 1024:(par + 1) * 1024]

                def f_gm(q, i=i, pv=pv):
                    ins = None
                    for cc in range(8):
                        lh = yfm[:, cc * S + i * 128:cc * S + (i + 1) * 128]
                        for n2 in range(2):
                            ins = q.matmul(pv[:, n2 * 512:(n2 + 1) * 512], lh,
                                           wsl[:, cc * DM + n2 * 512:cc * DM + (n2 + 1) * 512],
                                           start=(cc == 0), stop=(cc == 7))
                    return ins
                t_gm = sc.run("t", f_gm,
                              deps=[t_yfm] + wg_toks + [slot.get(("pp", par))])

                def f_si(q, i=i, pv=pv):
                    ov = yv3[:, :, i * 128:(i + 1) * 128]
                    return q.activation(
                        ov.rearrange("p cc t -> p (cc t)") if False else ov,
                        pv.rearrange("p (cc t) -> p cc t", t=128), AF.Silu)
                t_si = sc.run("s", f_si, deps=[t_gm])
                slot[("pp", par)] = t_si
            gsilu_done = ("s", sc.count["s"])

            PHASE_MARKS.append((f"p1s2_b{b}", nc.next_id()))
            head_last = None
            for h in range(H):
                hh, hp = h // 2, (h % 2) * 64
                qrh = qrf[hp:hp + 64, hh * S:(hh + 1) * S]
                krh = krf[hp:hp + 64, hh * S:(hh + 1) * S]

                a_toks = []
                gp_toks = []
                for qt in range(16):
                    ps = pq if qt % 2 == 0 else pk
                    krange = (qt + 1) * 128
                    dpar = qt % 2

                    def f_a(q, qt=qt, ps=ps, krange=krange):
                        lh = qrh[:, qt * 128:(qt + 1) * 128]
                        ins = None
                        for (c0, c1) in bank_chunks(0, krange):
                            ins = q.matmul(ps[:, c0:c1], lh, krh[:, c0:c1],
                                           start=True, stop=True)
                        return ins
                    depk = [qk_done, slot.get(("psA", qt % 2)),
                            slot.get(("psAv", qt % 2))]
                    if qt < 2 and h == 0:
                        depk += [v_done, gsilu_done, slot.get(("pp", 0)),
                                 slot.get(("pp", 1)), slot.get("tp_ev")]
                    t_a = sc.run("t", f_a, deps=depk)

                    if qt > 0:
                        t_abv = sc.run("s", lambda q, qt=qt, ps=ps: q.activation(
                            asb[:, 0:qt * 128], ps[:, 0:qt * 128],
                            AF.Abs, accum_out=dens[:, qt:qt + 1]),
                            deps=[t_a, slot.get("asb_c"), slot.get("dens_c")])
                    else:
                        t_abv = None
                    t_ab = sc.run("s", lambda q, qt=qt, ps=ps: q.copy(
                        dsb[:, (qt % 2) * 128:(qt % 2) * 128 + 128],
                        ps[:, qt * 128:(qt + 1) * 128]),
                        deps=[t_a, slot.get(("dsb", dpar))])
                    slot[("psA", qt % 2)] = t_ab
                    slot[("psAv", qt % 2)] = t_abv
                    slot["asb_c"] = t_abv
                    a_toks.append(t_ab)
                    if t_abv is not None:
                        a_toks.append(t_abv)

                    dv = dsb[:, dpar * 128:dpar * 128 + 128]
                    t_g1 = sc.run("g", lambda q, dv=dv: q.tensor_mul(
                        dv, dv, G["tril"][:, :]), deps=[t_ab])
                    t_g2 = sc.run("v", lambda q, qt=qt, dv=dv: q.tensor_reduce(
                        dtmp[:, qt:qt + 1], dv, axis=AX.X, op=ALU.add,
                        apply_absolute_value=True), deps=[t_g1])
                    slot[("dsb", dpar)] = t_g2
                    if qt == 0:
                        t_g3 = sc.run("g", lambda q, qt=qt: q.tensor_copy(
                            dens[:, qt:qt + 1], dtmp[:, qt:qt + 1]), deps=[t_g2])
                    else:
                        t_g3 = sc.run("g", lambda q, qt=qt: q.tensor_tensor(
                            dens[:, qt:qt + 1], dens[:, qt:qt + 1],
                            dtmp[:, qt:qt + 1], ALU.add), deps=[t_g2, t_abv])
                    gp_toks.append(t_g3)

                t_cl = sc.run("v", lambda q: q.tensor_scalar(
                    rden[:, :], dens[:, :], 1.0, 50000.0, ALU.max, ALU.min),
                    deps=gp_toks[-2:] + a_toks[-1:])
                t_rc = sc.run("v", lambda q: q.reciprocal(rden[:, :], rden[:, :]),
                              deps=[t_cl])
                slot["dens_c"] = t_rc

                slot["atp0"] = slot.get(("psA", 0))
                slot["atp1"] = slot.get(("psA", 1))
                vv_last = None
                for qh in range(2):
                    nkt = (qh + 1) * 8
                    ae_tok = {}
                    gp_tok = {}
                    for step in range(nkt + 1):
                        if step < nkt:
                            kt = step
                            par = kt % 2
                            atp = pq[:, par * 1024:(par + 1) * 1024]
                            qlo = max(qh * 1024, kt * 128)
                            off = qlo - qh * 1024

                            def f_at(q, kt=kt, atp=atp, off=off, qh=qh):
                                lh = krh[:, kt * 128:(kt + 1) * 128]
                                ins = None
                                for (c0, c1) in bank_chunks(off, 1024):
                                    ins = q.matmul(atp[:, c0:c1], lh,
                                                   qrh[:, qh * 1024 + c0:qh * 1024 + c1],
                                                   start=True, stop=True)
                                return ins
                            dep_at = [slot.get(("atB", par)), slot.get("atp%d" % par)]
                            slot["atp%d" % par] = None
                            t_at = sc.run("t", f_at, deps=dep_at)

                            eng = "s" if kt % 2 == 0 else "v"
                            t_ae = sc.run(eng, lambda q, atp=atp, off=off, par=par:
                                          q.copy(asb[:, par * 1024 + off:(par + 1) * 1024],
                                                 atp[:, off:1024])
                                          if eng == "s" else
                                          q.tensor_copy(asb[:, par * 1024 + off:(par + 1) * 1024],
                                                        atp[:, off:1024]),
                                          deps=[t_at, slot.get(("asbB", par)),
                                                slot.get("asb_c")])
                            slot["asb_c"] = None
                            slot[("atB", par)] = t_ae
                            ae_tok[kt] = t_ae
                            if kt >= qh * 8:
                                dg0 = par * 1024 + kt * 128 - qh * 1024
                                t_gd = sc.run("g", lambda q, dg0=dg0: q.tensor_mul(
                                    asb[:, dg0:dg0 + 128], asb[:, dg0:dg0 + 128],
                                    G["trilT"][:, :]), deps=[t_ae])
                                gp_tok[kt] = t_gd
                        if step >= 1:
                            kt = step - 1
                            par = kt % 2

                            def f_vv(q, kt=kt, qh=qh, par=par, h=h):
                                ins = None
                                for i in range(max(kt + 1, qh * 8), qh * 8 + 8):
                                    ins = q.matmul(
                                        pk[0:64, i * 128:(i + 1) * 128],
                                        vtm[:, kt * DM + h * HD:kt * DM + (h + 1) * HD],
                                        asb[:, par * 1024 + i * 128 - qh * 1024:
                                            par * 1024 + (i + 1) * 128 - qh * 1024],
                                        start=False, stop=False, skip_group_check=True)
                                return ins
                            dep_vv = [ae_tok[kt]]
                            if kt == 0 and qh == 0:
                                def f_z(q):
                                    ins = None
                                    for bk in range(4):
                                        ins = q.matmul(pk[0:64, bk * 512:(bk + 1) * 512],
                                                       G["ones1"][:, 0:64],
                                                       G["zrow"][:, :],
                                                       start=True, stop=True)
                                    return ins
                                t_z = sc.run("t", f_z, deps=[slot.get("otB")] + a_toks[-2:])
                                dep_vv.append(t_z)
                            t_vv = None
                            if max(kt + 1, qh * 8) < qh * 8 + 8:
                                t_vv = sc.run("t", f_vv, deps=dep_vv)
                            if kt >= qh * 8:
                                def f_vd(q, kt=kt, qh=qh, par=par, h=h):
                                    return q.matmul(
                                        pk[0:64, kt * 128:(kt + 1) * 128],
                                        vtm[:, kt * DM + h * HD:kt * DM + (h + 1) * HD],
                                        asb[:, par * 1024 + kt * 128 - qh * 1024:
                                            par * 1024 + (kt + 1) * 128 - qh * 1024],
                                        start=False, stop=True, skip_group_check=True)
                                t_vv = sc.run("t", f_vd, deps=dep_vv + [gp_tok[kt]])
                            slot[("asbB", par)] = t_vv
                            vv_last = t_vv

                t_oe = sc.run("v", lambda q: q.tensor_copy(asb[0:64, :], pk[0:64, :]),
                              deps=[vv_last, slot.get("otsb_c")])
                slot["otB"] = t_oe

                tr_grp = None
                for grp in range(4):
                    tpo = pq[:, grp * 512:grp * 512 + 128].bitcast(BF16)

                    def f_ot(q, grp=grp, tpo=tpo):
                        ins = None
                        for j in range(4):
                            i = grp * 4 + j
                            ins = q.transpose(tpo[:, j * 64:(j + 1) * 64],
                                              asb[0:64, i * 128:(i + 1) * 128],
                                              G["ident_b"][0:64, 0:64])
                        return ins
                    t_ot = sc.run("t", f_ot, deps=[t_oe, slot.get(("otT", grp))])

                    def f_os(q, grp=grp, tpo=tpo, h=h):
                        ins = None
                        for j in range(4):
                            i = grp * 4 + j
                            ins = q.tensor_scalar(
                                outall[:, i * DM + h * HD:i * DM + (h + 1) * HD],
                                tpo[:, j * 64:(j + 1) * 64],
                                rden[:, i:i + 1], None, ALU.mult)
                        return ins
                    t_os = sc.run("v", f_os, deps=[t_ot, t_rc, slot.get("outall_c")])
                    slot[("otT", grp)] = t_os
                    tr_grp = t_os
                slot["otsb_c"] = tr_grp
                slot[("psA", 0)] = tr_grp
                slot[("psA", 1)] = t_oe
                slot["asb_c"] = tr_grp
                head_last = tr_grp
            heads_done = head_last

            PHASE_MARKS.append((f"p1s3_b{b}", nc.next_id()))
            gsumA = asb[:, 0:1024].bitcast(F32)
            ptm = tabs[:, :].bitcast(F32)
            sq_toks = []
            last_rd = None
            for i in range(16):
                sbuf_t = rtm[:, (i % 2) * DM:(i % 2 + 1) * DM]
                t_sq = sc.run("s", lambda q, i=i, sbuf_t=sbuf_t: q.activation(
                    sbuf_t, outall[:, i * DM:(i + 1) * DM], AF.Square),
                    deps=[heads_done, slot.get(("gnsq", i % 2)), slot.get("asb_gn"),
                          slot.get("rtm_c")])
                t_rd = sc.run("v", lambda q, i=i, sbuf_t=sbuf_t: q.tensor_reduce(
                    gsumA[:, i * 16:(i + 1) * 16],
                    sbuf_t.rearrange("p (hh d) -> p hh d", d=HD),
                    axis=AX.X, op=ALU.add), deps=[t_sq])
                slot[("gnsq", i % 2)] = t_rd
                last_rd = t_rd
            t_s1 = sc.run("v", lambda q: q.tensor_scalar(
                gsumA[:, 256:512], gsumA[:, 0:256], 1.0 / HD, 1e-6,
                ALU.mult, ALU.add), deps=[last_rd])
            t_s2 = sc.run("s", lambda q: q.activation(
                gsumA[:, 256:512], gsumA[:, 256:512], AF.Sqrt), deps=[t_s1])
            t_s3 = sc.run("v", lambda q: q.reciprocal(
                gsumA[:, 256:512], gsumA[:, 256:512]), deps=[t_s2])
            grstA = gsumA[:, 256:512]

            st_last = None
            for i in range(16):
                t_x = sc.dma("q", "qd", lambda q, i=i: [
                    q.dma_start(out=xt[:, :],
                                in_=src[base + i * 128:base + (i + 1) * 128, :])],
                    deps=[slot.get("x_c"), heads_done])

                def f_ap(q, i=i):
                    ins = None
                    for h2 in range(H):
                        ins = q.tensor_scalar(
                            otmp[:, h2 * HD:(h2 + 1) * HD],
                            outall[:, i * DM + h2 * HD:i * DM + (h2 + 1) * HD],
                            grstA[:, i * 16 + h2:i * 16 + h2 + 1], None, ALU.mult)
                    return ins
                t_ap = sc.run("v", f_ap, deps=[t_s3, slot.get("otmp_c")])

                gview = yv3[:, :, i * 128:(i + 1) * 128]
                t_mu2 = sc.run("v", lambda q, gview=gview: q.tensor_mul(
                    otmp[:, :].rearrange("p (cc t) -> p cc t", t=128),
                    otmp[:, :].rearrange("p (cc t) -> p cc t", t=128), gview),
                    deps=[t_ap, gsilu_done])

                t_fin = sc.run("v", lambda q: q.tensor_add(ptm, otmp[:, :],
                                                           xt[:, :]),
                               deps=[t_mu2, t_x, slot.get("ptm_st"),
                                     slot.get("tab_fin")])
                slot["otmp_c"] = t_fin
                slot["outall_c"] = t_ap
                slot["x_c"] = t_fin

                t_st = sc.dma("s", "sd", lambda q, i=i: [
                    q.dma_start(out=dst[base + i * 128:base + (i + 1) * 128, :],
                                in_=ptm)], deps=[t_fin])
                slot["ptm_st"] = t_st
                st_last = t_st
            slot["comb_ps"] = gsilu_done
            slot["asb_gn"] = t_s3
            slot["asb_c"] = slot.get("outall_c")
            slot["rtm_c"] = last_rd
            slot["tab_c"] = st_last
            slot["gps_c"] = gsilu_done
            slot["yfm_c"] = st_last
            slot["rtm_c"] = slot.get("rtm_c")
            slot["vtm_c"] = vv_last
            slot[("qk_c", "l0a_Wq")] = vv_last
            slot[("qk_c", "l0a_Wk")] = vv_last
            slot[("pp", 0)] = slot.get("gps_c")
            slot[("pp", 1)] = slot.get("gps_c")

    sc.barrier()



def emit_l1ffn(sc, G, src, dst):
    nc, w = G["nc"], G["w"]
    tag = G["tag"]
    TW = 1024

    with ExitStack() as cx:
        def lsb(name, shape, dt=F32):
            return cx.enter_context(nc.sbuf_tensor(f"{tag}_{name}", shape, dt))

        def lps(name, shape, dt=F32):
            return cx.enter_context(nc.psum_tensor(f"{tag}_{name}", shape, dt))

        wf = lsb("wf", [128, 4 * DM])
        wb = {}
        t_wc_all = []
        wslot = {}
        for wi, nm in enumerate(["l1f_WrT", "l1f_WkT", "l1f_WvT"]):
            wb[nm] = lsb(f"wb{wi}", [128, 8 * DM], BF16)
            for half in range(2):
                t_wl = sc.dma("q", "qd", lambda q, nm=nm, half=half: [
                    q.dma_start(out=wf[:, :].rearrange("p (cc d) -> p cc d", cc=4),
                                in_=w[nm][half * 512:(half + 1) * 512, :]
                                .rearrange("(cc p) d -> p cc d", p=128))],
                    deps=[wslot.get("wf_c")])
                t_wc = sc.run("g", lambda q, nm=nm, half=half: q.tensor_copy(
                    wb[nm][:, half * 4 * DM:(half + 1) * 4 * DM], wf[:, :]),
                    deps=[t_wl])
                wslot["wf_c"] = t_wc
                t_wc_all.append(t_wc)

        muf = lsb("muf", [128, 8 * 2])
        mur = lsb("mur", [128, 8 * 4])
        t_mu = sc.dma("q", "qd", lambda q: [
            q.dma_start(out=muf[:, 0:8], in_=w["l1f_mur"].rearrange("(cc p) -> p cc", p=128),
                        allow_slow_non_contiguous=True),
            q.dma_start(out=muf[:, 8:16], in_=w["l1f_muk"].rearrange("(cc p) -> p cc", p=128),
                        allow_slow_non_contiguous=True)])

        def f_mu(q):
            q.tensor_copy(mur[:, 0:8], muf[:, 0:8])
            q.tensor_scalar(mur[:, 8:16], muf[:, 0:8], -1.0, 1.0, ALU.mult, ALU.add)
            q.tensor_copy(mur[:, 16:24], muf[:, 8:16])
            return q.tensor_scalar(mur[:, 24:32], muf[:, 8:16], -1.0, 1.0, ALU.mult, ALU.add)
        t_muc = sc.run("v", f_mu, deps=[t_mu])

        yfm = lsb("yfm", [128, 8 * (S + 1)], BF16)
        xt = lsb("xt", [128, 2 * DM])
        yb = lsb("yb", [128, DM], BF16)
        xr = lsb("xr", [128, 8 * TW], BF16)
        xk = lsb("xk", [128, 8 * TW], BF16)
        t1 = lsb("t1", [128, 2 * TW])
        rr = lsb("rr", [128, 8 * TW], BF16)
        ka = lsb("ka", [128, 8 * TW], BF16)
        of = lsb("of", [128, 8 * TW], BF16)
        xo = lsb("xo", [128, 2 * DM])
        pa = lps("pa", [128, 2 * DM])
        pt = lps("pt", [128, DM], BF16)

        slot = {}
        for b in range(BC):
            base = b * S
            t_z = sc.run("g", lambda q: q.memset(
                yfm[:, :].rearrange("p (cc t) -> p cc t", t=S + 1)[:, :, 0:1], 0.0),
                deps=[slot.get("yfm_c")])

            tr_last = None
            for ts_ in range(16):
                par = ts_ % 2
                xv = xt[:, par * DM:(par + 1) * DM]
                t_x = sc.dma("q", "qd", lambda q, ts_=ts_, xv=xv: [
                    q.dma_start(out=xv, in_=src[base + ts_ * 128:base + (ts_ + 1) * 128, :])],
                    deps=[slot.get(("x", par))])
                t_ln = emit_ln(sc, G, xv, [yb[:, :]], 1e-5,
                               deps=[t_x, slot.get("yb_c")])
                slot[("x", par)] = t_ln

                def f_tr(q):
                    ins = None
                    for cc in range(8):
                        ins = q.transpose(pt[:, cc * 128:(cc + 1) * 128],
                                          yb[:, cc * 128:(cc + 1) * 128],
                                          G["ident_b"][:, :])
                    return ins
                t_tr = sc.run("t", f_tr, deps=[t_ln, slot.get("pt_ev")])
                slot["yb_c"] = t_tr

                def f_te(q, ts_=ts_):
                    ov = yfm[:, :].rearrange("p (cc t) -> p cc t", t=S + 1)[
                        :, :, 1 + ts_ * 128:1 + (ts_ + 1) * 128]
                    return q.tensor_copy(ov, pt[:, :].rearrange("p (cc t) -> p cc t", t=128))
                t_te = sc.run("v", f_te, deps=[t_tr, t_z])
                slot["pt_ev"] = t_te
                tr_last = t_te

            out_done = []
            for hh in range(2):
                yv = yfm[:, :].rearrange("p (cc t) -> p cc t", t=S + 1)
                t_xr = None
                for cc in range(8):
                    ys = yv[:, cc, 1 + hh * TW:1 + (hh + 1) * TW]
                    yp = yv[:, cc, hh * TW:(hh + 1) * TW]

                    def f_a(q, ys=ys, cc=cc):
                        q.tensor_scalar(t1[:, 0:TW], ys, mur[:, cc:cc + 1],
                                        None, ALU.mult)
                        return q.tensor_scalar(t1[:, TW:2 * TW], ys,
                                               mur[:, 16 + cc:16 + cc + 1],
                                               None, ALU.mult)
                    t_a = sc.run("v", f_a,
                                 deps=[tr_last, t_muc, slot.get("xr_c")] if cc == 0
                                 else [t_a0])

                    def f_b(q, yp=yp, cc=cc):
                        q.scalar_tensor_tensor(xr[:, cc * TW:(cc + 1) * TW], yp,
                                               mur[:, 8 + cc:8 + cc + 1], t1[:, 0:TW],
                                               ALU.mult, ALU.add)
                        return q.scalar_tensor_tensor(xk[:, cc * TW:(cc + 1) * TW], yp,
                                                      mur[:, 24 + cc:24 + cc + 1],
                                                      t1[:, TW:2 * TW],
                                                      ALU.mult, ALU.add)
                    t_xr = sc.run("v", f_b, deps=[t_a])
                    t_a0 = t_xr

                rk_done = []
                for (xin_, wnm, obuf, act) in [(xr, "l1f_WrT", rr, "sig"),
                                               (xk, "l1f_WkT", ka, "relu2")]:
                    for dd in range(8):
                        par = dd % 2
                        pv = pa[:, par * DM:(par + 1) * DM]

                        def f_mm(q, xin_=xin_, wnm=wnm, dd=dd, pv=pv):
                            ins = None
                            for cc in range(8):
                                for nc2 in range(2):
                                    ins = q.matmul(
                                        pv[:, nc2 * 512:(nc2 + 1) * 512],
                                        wb[wnm][:, cc * DM + dd * 128:cc * DM + (dd + 1) * 128],
                                        xin_[:, cc * TW + nc2 * 512:cc * TW + (nc2 + 1) * 512],
                                        start=(cc == 0), stop=(cc == 7))
                            return ins
                        t_mm = sc.run("t", f_mm,
                                      deps=[t_xr] + t_wc_all + [slot.get(("pa", par))])

                        if act == "sig":
                            t_ev = sc.run("s", lambda q, dd=dd, pv=pv: q.activation(
                                rr[:, dd * TW:(dd + 1) * TW], pv, AF.Sigmoid),
                                deps=[t_mm, slot.get("rr_c")])
                        else:
                            t_rl = sc.run("s", lambda q, dd=dd, pv=pv: q.activation(
                                G["scr_b"][:, :], pv, AF.Relu),
                                deps=[t_mm, slot.get("ka_c")])
                            t_ev = sc.run("s", lambda q, dd=dd: q.activation(
                                ka[:, dd * TW:(dd + 1) * TW], G["scr_b"][:, :],
                                AF.Square), deps=[t_rl], serialize=True)
                        slot[("pa", par)] = t_ev
                        rk_done.append(t_ev)

                slot["xr_c"] = rk_done[-1]

                o_done = []
                for dd in range(8):
                    par = dd % 2
                    pv = pa[:, par * DM:(par + 1) * DM]

                    def f_vm(q, dd=dd, pv=pv):
                        ins = None
                        for cc in range(8):
                            for nc2 in range(2):
                                ins = q.matmul(
                                    pv[:, nc2 * 512:(nc2 + 1) * 512],
                                    wb["l1f_WvT"][:, cc * DM + dd * 128:cc * DM + (dd + 1) * 128],
                                    ka[:, cc * TW + nc2 * 512:cc * TW + (nc2 + 1) * 512],
                                    start=(cc == 0), stop=(cc == 7))
                        return ins
                    t_vm = sc.run("t", f_vm, deps=rk_done + [slot.get(("pa", par))])

                    t_om = sc.run("v", lambda q, dd=dd, pv=pv: q.tensor_mul(
                        of[:, dd * TW:(dd + 1) * TW], rr[:, dd * TW:(dd + 1) * TW], pv),
                        deps=[t_vm, slot.get("of_c")])
                    slot[("pa", par)] = t_om
                    o_done.append(t_om)
                slot["rr_c"] = o_done[-1]
                slot["ka_c"] = o_done[-1]

                for ts2 in range(8):
                    gt = hh * 8 + ts2
                    par = gt % 2
                    xv = xt[:, par * DM:(par + 1) * DM]
                    xov = xo[:, par * DM:(par + 1) * DM]
                    t_x2 = sc.dma("q", "qd", lambda q, gt=gt, xv=xv: [
                        q.dma_start(out=xv,
                                    in_=src[base + gt * 128:base + (gt + 1) * 128, :])],
                        deps=[slot.get(("x", par))])

                    def f_tb(q, ts2=ts2):
                        ins = None
                        for dd in range(8):
                            ins = q.transpose(
                                pt[:, dd * 128:(dd + 1) * 128],
                                of[:, dd * TW + ts2 * 128:dd * TW + (ts2 + 1) * 128],
                                G["ident_b"][:, :])
                        return ins
                    t_tb = sc.run("t", f_tb, deps=o_done + [slot.get("pt_ev")])

                    t_rs = sc.run("v", lambda q, xv=xv, xov=xov: q.scalar_tensor_tensor(
                        xov, pt[:, :], 0.5, xv, ALU.mult, ALU.add),
                        deps=[t_tb, t_x2, slot.get(("xo", par))])
                    slot["pt_ev"] = t_rs
                    slot[("x", par)] = t_rs
                    if ts2 == 7:
                        slot["of_c"] = t_rs

                    t_xs = sc.dma("s", "sd", lambda q, gt=gt, xov=xov: [
                        q.dma_start(out=dst[base + gt * 128:base + (gt + 1) * 128, :],
                                    in_=xov)], deps=[t_rs])
                    slot[("xo", par)] = t_xs
                    out_done.append(t_xs)
            slot["yfm_c"] = out_done[-1]

    sc.barrier()



def emit_l1attn(sc, G, src, dst):
    nc, w = G["nc"], G["w"]
    tag = G["tag"]
    NH, HD1 = 8, 128

    with ExitStack() as cx:
        def lsb(name, shape, dt=F32):
            return cx.enter_context(nc.sbuf_tensor(f"{tag}_{name}", shape, dt))

        def lps(name, shape, dt=F32):
            return cx.enter_context(nc.psum_tensor(f"{tag}_{name}", shape, dt))

        yfm = lsb("yfm", [128, 8 * S], BF16)
        outall = lsb("outall", [128, 16 * DM], BF16)
        wsl = lsb("wsl", [128, 8 * DM], BF16)
        wf = lsb("wf", [128, DM])
        xt = lsb("xt", [128, DM])
        yb = lsb("yb", [128, DM], BF16)
        rtm = lsb("rtm", [128, 2 * DM], BF16)
        qro = lsb("qro", [128, 2 * DM], BF16)
        l1tb = lsb("l1tb", [128, 4 * DM], BF16)
        qTs = lsb("qTs", [128, 2048], BF16)
        kTs = lsb("kTs", [128, 2048], BF16)
        vh = lsb("vh", [128, 2048], BF16)
        ptsb = lsb("ptsb", [128, 2 * 1024], BF16)
        osb = lsb("osb", [128, 1024], BF16)
        denr = lsb("denr", [1, 1024])
        denc = lsb("denc", [128, 8])
        rdn = lsb("rdn", [128, 8])
        pq = lps("pq", [128, 2048])
        pk = lps("pk", [128, 2048])

        t_tb = sc.dma("q", "qd", lambda q: [
            q.dma_start(out=l1tb[:, 0:DM], in_=w["l1_cA"][0:128, :]),
            q.dma_start(out=l1tb[:, DM:2 * DM], in_=w["l1_cA"][128:256, :]),
            q.dma_start(out=l1tb[:, 2 * DM:3 * DM], in_=w["l1_cB"][0:128, :]),
            q.dma_start(out=l1tb[:, 3 * DM:4 * DM], in_=w["l1_cB"][128:256, :])])

        slot = {}
        for b in range(BC):
            base = b * S

            def load_w(nm, dep_consumed):
                toks = []
                for cc in range(8):
                    t_l = sc.dma("q", "qd", lambda q, nm=nm, cc=cc: [
                        q.dma_start(out=wf[:, :],
                                    in_=w[nm][cc * 128:(cc + 1) * 128, :])],
                        deps=[dep_consumed, slot.get("wf_c")])
                    t_c = sc.run("g", lambda q, cc=cc: q.tensor_copy(
                        wsl[:, cc * DM:(cc + 1) * DM], wf[:, :]), deps=[t_l])
                    slot["wf_c"] = t_c
                    toks.append(t_c)
                return toks

            for ts_ in range(16):
                t_x = sc.dma("q", "qd", lambda q, ts_=ts_: [
                    q.dma_start(out=xt[:, :],
                                in_=src[base + ts_ * 128:base + (ts_ + 1) * 128, :])],
                    deps=[slot.get("x_c")])
                t_ln = emit_ln(sc, G, xt[:, :], [yb[:, :]], 1e-5,
                               deps=[t_x, slot.get("yb_c")])
                slot["x_c"] = t_ln

                tp = pk[:, 0:512].bitcast(BF16)

                def f_tr(q, tp=tp):
                    ins = None
                    for cc in range(8):
                        ins = q.transpose(tp[:, cc * 128:(cc + 1) * 128],
                                          yb[:, cc * 128:(cc + 1) * 128],
                                          G["ident_b"][:, :])
                    return ins
                t_tr = sc.run("t", f_tr, deps=[t_ln, slot.get("tp_ev")])
                slot["yb_c"] = t_tr

                def f_te(q, ts_=ts_, tp=tp):
                    ov = yfm[:, :].rearrange("p (cc t) -> p cc t", t=S)[
                        :, :, ts_ * 128:(ts_ + 1) * 128]
                    return q.tensor_copy(ov, tp.rearrange("p (cc t) -> p cc t", t=128))
                t_te = sc.run("v", f_te, deps=[t_tr, slot.get("yfm_c")])
                slot["tp_ev"] = t_te
            t_yfm = ("v", sc.count["v"])

            for wi, (wnm, qi, rope) in enumerate([("l1a_W1T", 0, True),
                                                  ("l1a_W2T", 1, True),
                                                  ("l1a_W3T", 2, False)]):
                wtoks = load_w(wnm, slot.get("wsl_c"))
                for ts_ in range(16):
                    par = ts_ % 2
                    pv = pq[:, par * 1024:(par + 1) * 1024]
                    qv = qro[:, par * DM:(par + 1) * DM]

                    def f_mm(q, ts_=ts_, pv=pv):
                        ins = None
                        for cc in range(8):
                            lh = yfm[:, cc * S + ts_ * 128:cc * S + (ts_ + 1) * 128]
                            for n2 in range(2):
                                ins = q.matmul(
                                    pv[:, n2 * 512:(n2 + 1) * 512], lh,
                                    wsl[:, cc * DM + n2 * 512:cc * DM + (n2 + 1) * 512],
                                    start=(cc == 0), stop=(cc == 7))
                        return ins
                    t_mm = sc.run("t", f_mm,
                                  deps=[t_yfm] + wtoks + [slot.get(("pp", par))])

                    if rope:
                        tb = ts_ % 2
                        cav = l1tb[:, tb * DM:(tb + 1) * DM]
                        cbv = l1tb[:, (2 + tb) * DM:(3 + tb) * DM]

                        def f_r1(q, pv=pv, cav=cav, cbv=cbv):
                            q.tensor_mul(rtm[:, 0:DM], pv, cav)
                            pvv = pv.rearrange("p (a c) -> p a c", c=2)
                            bvv = cbv.rearrange("p (a c) -> p a c", c=2)
                            mvv = rtm[:, DM:2 * DM].rearrange("p (a c) -> p a c", c=2)
                            q.tensor_mul(mvv[:, :, 0:1], pvv[:, :, 1:2], bvv[:, :, 0:1])
                            return q.tensor_mul(mvv[:, :, 1:2], pvv[:, :, 0:1],
                                                bvv[:, :, 1:2])
                        t_r1 = sc.run("v", f_r1, deps=[t_mm, t_tb, slot.get("rtm_c")])
                        t_r2 = sc.run("v", lambda q, qv=qv: q.tensor_add(
                            qv, rtm[:, 0:DM], rtm[:, DM:2 * DM]),
                            deps=[t_r1, slot.get(("qro", par))])
                        slot["rtm_c"] = t_r2
                        slot[("pp", par)] = t_r1
                        t_out = t_r2
                    else:
                        t_out = sc.run("v", lambda q, pv=pv, qv=qv: q.tensor_copy(qv, pv),
                                       deps=[t_mm, slot.get(("qro", par))])
                        slot[("pp", par)] = t_out

                    t_sto = sc.dma("s", "sd", lambda q, ts_=ts_, qi=qi, qv=qv: [
                        q.dma_start(out=G["qkv"][qi, ts_ * 128:(ts_ + 1) * 128, :],
                                    in_=qv)], deps=[t_out])
                    slot[("qro", par)] = t_sto
                slot["wsl_c"] = t_mm
            qkv_done = ("sd", sc.count["sd"])

            wo_toks = load_w("l1a_WoT", slot.get("wsl_c"))

            heads_done = None
            for h in range(NH):
                t_qT = sc.dma("q", "qd", lambda q, h=h: [
                    q.dma_start_transpose(
                        out=qTs[:, :],
                        in_=G["qkv"][0, 256 * h:256 * (h + 1), :]
                        .rearrange("u (w d) -> (u w) d", d=128))],
                    deps=[qkv_done, slot.get("qT_c")])
                t_kT = sc.dma("q", "qd", lambda q, h=h: [
                    q.dma_start_transpose(
                        out=kTs[:, :],
                        in_=G["qkv"][1, 256 * h:256 * (h + 1), :]
                        .rearrange("u (w d) -> (u w) d", d=128))],
                    deps=[qkv_done, slot.get("kT_c")])
                t_v = sc.dma("q", "qd", lambda q, h=h: [
                    q.dma_start(out=vh[:, :].rearrange("p (tt d) -> p tt d", tt=16),
                                in_=G["qkv"][2, 256 * h:256 * (h + 1), :]
                                .rearrange("(tt u2) (w d) -> (u2 w) tt d",
                                           tt=16, u2=16, w=8))],
                    deps=[qkv_done, slot.get("vh_c")])

                for qh in range(2):
                    vv_last = None
                    ex_tok = {}
                    for step in range(17):
                        if step < 16:
                            kt = step
                            par = kt % 2
                            stp = pq[:, par * 1024:(par + 1) * 1024]

                            def f_st(q, kt=kt, qh=qh, stp=stp):
                                lh = kTs[:, kt * 128:(kt + 1) * 128]
                                ins = None
                                for n2 in range(2):
                                    ins = q.matmul(
                                        stp[:, n2 * 512:(n2 + 1) * 512], lh,
                                        qTs[:, qh * 1024 + n2 * 512:qh * 1024 + (n2 + 1) * 512],
                                        start=True, stop=True)
                                return ins
                            dep_st = [t_qT, t_kT, slot.get(("stB", par))]
                            if kt < 2 and qh == 0 and h == 0:
                                dep_st += [slot.get("pq_free"), slot.get("tp_ev"),
                                           slot.get(("pp", 0)), slot.get(("pp", 1))]
                            t_st = sc.run("t", f_st, deps=dep_st)

                            t_ex = sc.run("s", lambda q, stp=stp, par=par: q.activation(
                                ptsb[:, par * 1024:(par + 1) * 1024], stp, AF.Exp,
                                bias=G["c20"][:, :], scale=float(1.0 / math.sqrt(8.0))),
                                deps=[t_st, slot.get(("ptsb", par))])
                            slot[("stB", par)] = t_ex
                            ex_tok[kt] = t_ex
                        if step >= 1:
                            kt = step - 1
                            par = kt % 2

                            def f_dn(q, kt=kt, par=par):
                                ins = None
                                for n2 in range(2):
                                    ins = q.matmul(
                                        pk[0:1, 1024 + n2 * 512:1024 + (n2 + 1) * 512],
                                        G["onec"][:, :],
                                        ptsb[:, par * 1024 + n2 * 512:par * 1024 + (n2 + 1) * 512],
                                        start=(kt == 0), stop=(kt == 15))
                                return ins
                            t_dn = sc.run("t", f_dn, deps=[ex_tok[kt], slot.get("dn_c")])

                            def f_vv(q, kt=kt, par=par):
                                ins = None
                                for n2 in range(2):
                                    ins = q.matmul(
                                        pk[:, n2 * 512:(n2 + 1) * 512],
                                        vh[:, kt * 128:(kt + 1) * 128],
                                        ptsb[:, par * 1024 + n2 * 512:par * 1024 + (n2 + 1) * 512],
                                        start=(kt == 0), stop=(kt == 15))
                                return ins
                            t_vv = sc.run("t", f_vv, deps=[t_v, ex_tok[kt], slot.get("ot_c")])
                            slot[("ptsb", par)] = t_vv
                            vv_last = t_vv

                    t_dc = sc.run("v", lambda q: q.tensor_copy(denr[:, :],
                                                               pk[0:1, 1024:2048]),
                                  deps=[t_dn, slot.get("denr_c")])
                    slot["dn_c"] = t_dc

                    def f_dt(q):
                        ins = None
                        for j in range(8):
                            ins = q.transpose(pq[:, 1536 + j:1536 + j + 1],
                                              denr[:, j * 128:(j + 1) * 128],
                                              G["ident_f"][0:1, 0:1])
                        return ins
                    t_dt = sc.run("t", f_dt, deps=[t_dc, vv_last])
                    t_dc2 = sc.run("v", lambda q: q.tensor_copy(denc[:, :],
                                                                pq[:, 1536:1544]),
                                   deps=[t_dt])
                    t_rd = sc.run("v", lambda q: q.reciprocal(rdn[:, :], denc[:, :]),
                                  deps=[t_dc2])
                    slot["denr_c"] = t_rd

                    t_oe = sc.run("v", lambda q: q.tensor_copy(osb[:, :],
                                                               pk[:, 0:1024]),
                                  deps=[vv_last, slot.get("osb_c")])
                    slot["ot_c"] = t_oe
                    if G.get("debug") and b == BC - 1 and h == 0 and qh == 0:
                        t_oe = sc.dma("g", "sd", lambda q: [
                            q.dma_start(out=G["osb_dbg"][:, :], in_=osb[:, :]),
                            q.dma_start(out=G["pt_dbg"][:, :], in_=ptsb[:, :])],
                            deps=[t_oe])
                        slot["ot_c"] = t_oe

                    tr_last = None
                    for grp in range(2):
                        base_c = 1024 if grp == 0 else 1792
                        tpo = pq[:, base_c:base_c + 256].bitcast(BF16)

                        def f_ot(q, grp=grp, tpo=tpo):
                            ins = None
                            for j in range(4):
                                i = grp * 4 + j
                                ins = q.transpose(tpo[:, j * 128:(j + 1) * 128],
                                                  osb[:, i * 128:(i + 1) * 128],
                                                  G["ident_b"][:, :])
                            return ins
                        t_ot = sc.run("t", f_ot, deps=[t_oe, slot.get(("otT", grp)),
                                                       t_dt, t_dc2])

                        def f_os(q, grp=grp, tpo=tpo, h=h, qh=qh):
                            ins = None
                            for j in range(4):
                                i = grp * 4 + j
                                tile = qh * 8 + i
                                ins = q.tensor_scalar(
                                    outall[:, tile * DM + h * HD1:tile * DM + (h + 1) * HD1],
                                    tpo[:, j * 128:(j + 1) * 128],
                                    rdn[:, i:i + 1], None, ALU.mult)
                            return ins
                        t_os = sc.run("v", f_os, deps=[t_ot, t_rd, slot.get("outall_c")])
                        slot[("otT", grp)] = t_os
                        tr_last = t_os
                    slot["osb_c"] = tr_last
                    slot[("stB", 1)] = tr_last
                    heads_done = tr_last
                slot["qT_c"] = vv_last
                slot["kT_c"] = vv_last
                slot["vh_c"] = vv_last
            slot["pq_free"] = heads_done

            if G.get("debug") and b == BC - 1:
                heads_done = sc.dma("g", "sd", lambda q: [
                    q.dma_start(out=G["oa_dbg"][:, :], in_=outall[:, :])],
                    deps=[heads_done])
            for sp in range(16):
                half = sp // 8
                spc = sp % 8
                par = sp % 2
                fp = pk[:, par * 1024:(par + 1) * 1024]

                def f_f(q, half=half, spc=spc, fp=fp):
                    ins = None
                    for jc in range(8):
                        tile = half * 8 + jc
                        for n2 in range(2):
                            ins = q.matmul(
                                fp[:, n2 * 512:(n2 + 1) * 512],
                                wsl[:, jc * DM + spc * 128:jc * DM + (spc + 1) * 128],
                                outall[:, tile * DM + n2 * 512:tile * DM + (n2 + 1) * 512],
                                start=(jc == 0), stop=(jc == 7))
                    return ins
                dep_f = [heads_done] + wo_toks + [slot.get(("fp", par))]
                if sp < 2:
                    dep_f += [slot.get("ot_c"), slot.get("dn_c")]
                t_f = sc.run("t", f_f, deps=dep_f)

                t_x2 = sc.dma("q", "qd", lambda q, sp=sp: [
                    q.dma_start(out=xt[:, :],
                                in_=src[base + sp * 128:base + (sp + 1) * 128, :])],
                    deps=[slot.get("x_c")])

                t_rs = sc.run("v", lambda q, fp=fp: q.tensor_tensor(
                    xt[:, :], fp, xt[:, :], ALU.add),
                    deps=[t_f, t_x2])
                slot[("fp", par)] = t_rs

                t_st2 = sc.dma("s", "sd", lambda q, sp=sp: [
                    q.dma_start(out=dst[base + sp * 128:base + (sp + 1) * 128, :],
                                in_=xt[:, :])], deps=[t_rs])
                slot["x_c"] = t_st2
            slot["yfm_c"] = slot.get("wsl_c")
            slot["outall_c"] = t_f
            slot["wsl_c"] = t_f
            slot["tp_ev"] = slot.get("tp_ev")

    sc.barrier()


def build_program(phases=(0, 1, 2, 3, 4, 5), debug=False):
    nc = bass.Bass("TRN2", target_bir_lowering=False)
    ap = lambda h: h.ap()

    xin = ap(nc.dram_tensor("xin", [T, DM], F32, kind="ExternalInput"))
    xout = ap(nc.dram_tensor("xout", [T, DM], F32, kind="ExternalOutput"))
    w = {}
    for name, shape in [
        ("l0f_Wr", [DM, 4]), ("l0f_W1", [NE, DM, I_FF]), ("l0f_b1", [NE, I_FF]),
        ("l0f_Wo", [I_FF, DM]), ("l0f_bo", [DM]), ("l0f_g", [DM]), ("l0f_b", [DM]),
        ("l0a_Wq", [DM, DM]), ("l0a_Wk", [DM, DM]), ("l0a_Wv", [DM, DM]),
        ("l0a_Wg", [DM, DM]),
        ("l1a_W1T", [DM, DM]), ("l1a_W2T", [DM, DM]), ("l1a_W3T", [DM, DM]),
        ("l1a_WoT", [DM, DM]),
        ("l1f_WrT", [DM, DM]), ("l1f_WkT", [DM, DM]), ("l1f_WvT", [DM, DM]),
        ("l1f_mur", [DM]), ("l1f_muk", [DM]),
    ]:
        w[name] = ap(nc.dram_tensor(name, shape, F32, kind="ExternalInput"))
    for name, arr in TABLES.items():
        w[name] = ap(nc.dram_tensor(name, list(arr.shape), BF16, kind="ExternalInput"))

    dk = "ExternalOutput" if debug else "Internal"
    xs = ap(nc.dram_tensor("xs", [T, DM], F32, kind="Internal"))
    yfm_hbm = ap(nc.dram_tensor("yfm_hbm", [NTT, 8, 128, 128], BF16, kind=dk))
    interm_hbm = ap(nc.dram_tensor("interm_hbm", [8, NTT, 128, 512], BF16,
                                   kind=dk))
    qkv_hbm = ap(nc.dram_tensor("qkv_hbm", [3, S, DM], BF16, kind=dk))
    masks_dbg = ap(nc.dram_tensor("masks_dbg", [128, NTT * 4], F32, kind=dk)) if debug else None
    lgs_dbg = ap(nc.dram_tensor("lgs_dbg", [NTT, 128, 4], F32, kind=dk)) if debug else None
    zsb_dbg = ap(nc.dram_tensor("zsb_dbg", [NTT, 128, DM], F32, kind=dk)) if debug else None
    oa_dbg = ap(nc.dram_tensor("oa_dbg", [128, 16 * DM], F32, kind=dk)) if debug else None
    osb_dbg = ap(nc.dram_tensor("osb_dbg", [128, 1024], F32, kind=dk)) if debug else None
    pt_dbg = ap(nc.dram_tensor("pt_dbg", [128, 2048], F32, kind=dk)) if debug else None
    yb_dbg = ap(nc.dram_tensor("yb_dbg", [4, 128, DM], F32, kind=dk)) if debug else None
    y2_dbg = ap(nc.dram_tensor("y2_dbg", [NTT, 128, DM], F32, kind=dk)) if debug else None

    phases = sorted(phases)

    with ExitStack() as ctx:
        block = ctx.enter_context(nc.Block())
        sc = Sched(nc, ctx, block)

        G = {"nc": nc, "w": w, "yfm": yfm_hbm, "interm": interm_hbm, "qkv": qkv_hbm,
             "lgs_dbg": lgs_dbg, "zsb_dbg": zsb_dbg, "y2_dbg": y2_dbg,
             "yb_dbg": yb_dbg, "oa_dbg": oa_dbg, "osb_dbg": osb_dbg,
             "pt_dbg": pt_dbg, "debug": debug}
        for nm, shape, dt in [
            ("ident_b", [128, 128], BF16), ("ident_f", [128, 128], F32),
            ("ones1", [1, 128], BF16), ("zrow", [1, 512], BF16),
            ("onec", [128, 1], BF16), ("c20", [128, 1], F32),
            ("masks", [128, NTT * 4], F32),
            ("tril_s", [128, 128], BF16),
            ("trilT_s", [128, 128], BF16), ("stat", [128, 8], F32),
            ("scr_b", [128, DM], BF16),
        ]:
            G[nm.replace("_s", "") if nm.endswith("_s") else nm] = ctx.enter_context(nc.sbuf_tensor(nm, shape, dt))

        t_tri = sc.dma("q", "qd", lambda q: [
            q.dma_start(out=G["tril"][:, :], in_=w["tril"][:, :]),
            q.dma_start(out=G["trilT"][:, :], in_=w["trilT"][:, :])])

        def f_init(q):
            q.memset(G["ones1"][:, :], 1.0)
            q.memset(G["zrow"][:, :], 0.0)
            q.memset(G["onec"][:, :], 1.0)
            q.memset(G["c20"][:, :], -20.0)
            q.memset(G["ident_b"][:, :], 0.0)
            q.affine_select(out=G["ident_b"][:, :], in_=G["ident_b"][:, :],
                            compare_op=ALU.not_equal, fill=1.0, base=0,
                            pattern=[[-1, 128]], channel_multiplier=1)
            q.memset(G["ident_f"][:, :], 0.0)
            return q.affine_select(out=G["ident_f"][:, :], in_=G["ident_f"][:, :],
                                   compare_op=ALU.not_equal, fill=1.0, base=0,
                                   pattern=[[-1, 128]], channel_multiplier=1)
        sc.run("g", f_init, deps=[t_tri])

        sc.barrier()

        emitters = {0: emit_l0ffn, 1: emit_l0attn, 2: emit_l0ffn,
                    3: emit_l1ffn, 4: emit_l1attn, 5: emit_l1ffn}
        PHASE_MARKS.clear()
        for ph in phases:
            src = xin if ph == phases[0] else xs
            dst = xout if ph == phases[-1] else xs
            G["tag"] = f"p{ph}"
            PHASE_MARKS.append((ph, nc.next_id()))
            emitters[ph](sc, G, src, dst)
        PHASE_MARKS.append((-1, nc.next_id()))

        if debug:
            sc.dma("s", "sd", lambda q: [
                q.dma_start(out=masks_dbg[:, :], in_=G["masks"][:, :])])

        @block.sync
        def _(q):
            q.wait_ge(sc.sem["sd"], sc.count["sd"])
            q.wait_ge(sc.sem["qd"], sc.count["qd"])
    return nc


def make_inmap(core, inputs):
    x = np.asarray(inputs["L0_attn_x"])
    xsh = np.ascontiguousarray(x.reshape(32, S, DM)[core * BC:(core + 1) * BC]
                               .reshape(T, DM))
    m = {"xin": xsh,
         "l0f_Wr": inputs["L0_ffn_W_router"], "l0f_W1": inputs["L0_ffn_W1"],
         "l0f_b1": inputs["L0_ffn_b1"], "l0f_Wo": inputs["L0_ffn_Wo"],
         "l0f_bo": inputs["L0_ffn_bo"], "l0f_g": inputs["L0_ffn_ln_gamma"],
         "l0f_b": inputs["L0_ffn_ln_beta"],
         "l0a_Wq": inputs["L0_attn_Wq"], "l0a_Wk": inputs["L0_attn_Wk"],
         "l0a_Wv": inputs["L0_attn_Wv"], "l0a_Wg": inputs["L0_attn_Wg"],
         "l1a_W1T": np.asarray(inputs["L1_attn_W1"]).T,
         "l1a_W2T": np.asarray(inputs["L1_attn_W2"]).T,
         "l1a_W3T": np.asarray(inputs["L1_attn_W3"]).T,
         "l1a_WoT": np.asarray(inputs["L1_attn_Wo"]).T,
         "l1f_WrT": np.asarray(inputs["L1_ffn_W_r"]).T,
         "l1f_WkT": np.asarray(inputs["L1_ffn_W_k"]).T,
         "l1f_WvT": np.asarray(inputs["L1_ffn_W_v"]).T,
         "l1f_mur": inputs["L1_ffn_mu_r"], "l1f_muk": inputs["L1_ffn_mu_k"]}
    for k, v in TABLES.items():
        m[k] = v
    return {k: np.ascontiguousarray(np.asarray(v)) for k, v in m.items()}


def kernel(**inputs):
    nc = build_program(phases=(0, 1, 2, 3, 4, 5))
    in_maps = [make_inmap(core, inputs) for core in range(8)]
    res = run_bass_kernel_spmd(nc, in_maps, core_ids=list(range(8)))
    out = np.concatenate([np.asarray(r["xout"]).reshape(BC, S, DM)
                          for r in res.results], axis=0)
    return out.astype(np.float32)


# version 04
# speedup vs baseline: 1.2037x; 1.0091x over previous
import math
from contextlib import ExitStack

import numpy as np
import ml_dtypes
import concourse.bass as bass
import concourse.mybir as mybir
from concourse.bass_utils import run_bass_kernel_spmd

F32 = mybir.dt.float32
BF16 = mybir.dt.bfloat16
AF = mybir.ActivationFunctionType
ALU = mybir.AluOpType
AX = mybir.AxisListType

BC = 4
S = 2048
T = BC * S
DM = 1024
NTT = T // 128
I_FF = 4096
NE = 4

BF = ml_dtypes.bfloat16


def host_tables():
    t = {}
    H, D = 16, 64
    a = 1.0 / 10000.0 ** np.linspace(0.0, 1.0, D // 2)
    angle = np.repeat(a, 2)
    decay = np.log(1.0 - 2.0 ** (-5.0 - np.arange(H, dtype=np.float64)))
    idx = np.arange(S, dtype=np.float64)
    sin = np.sin(idx[:, None] * angle[None, :])
    cos = np.cos(idx[:, None] * angle[None, :])
    sinp = sin.copy()
    sinp[:, 0::2] *= -1.0
    qdec = np.exp(idx[:, None] * decay[None, :])
    ed = np.exp(decay)[None, :]
    rowsum = (1.0 - np.exp((idx[:, None] + 1.0) * decay[None, :])) / (1.0 - ed)
    qdec = qdec / np.sqrt(rowsum)
    kdec = np.exp(-idx[:, None] * decay[None, :]) * (DM ** -0.5)
    cosT = np.tile(cos, (1, H))
    sinT = np.tile(sinp, (1, H))
    qdE = np.repeat(qdec, D, axis=1)
    kdE = np.repeat(kdec, D, axis=1)
    t["l0_qA"] = (cosT * qdE).astype(BF)
    t["l0_qB"] = (sinT * qdE).astype(BF)
    t["l0_kA"] = (cosT * kdE).astype(BF)
    t["l0_kB"] = (sinT * kdE).astype(BF)
    D1 = 128
    inv = 1.0 / 10000.0 ** (np.arange(0, D1, 2, dtype=np.float64) / D1)
    tt = np.arange(S, dtype=np.float64)
    freqs = np.repeat(tt[:, None] * inv[None, :], 2, axis=-1)
    c1 = np.cos(freqs)
    s1 = np.sin(freqs)
    s1[:, 0::2] *= -1.0
    t["l1_cA"] = np.ascontiguousarray(c1.reshape(256, 1024)).astype(BF)
    t["l1_cB"] = np.ascontiguousarray(s1.reshape(256, 1024)).astype(BF)
    tri = np.tril(np.ones((128, 128), np.float32))
    t["tril"] = tri.astype(BF)
    t["trilT"] = np.ascontiguousarray(tri.T).astype(BF)
    return t


TABLES = host_tables()
PHASE_MARKS = []

_SECTION = {"t": "tensor", "v": "vector", "s": "scalar", "g": "gpsimd", "q": "sync"}
_OWN_SEM = {"t": "t", "v": "v", "s": "s", "g": "g", "q": "q"}


class Sched:
    def __init__(self, nc, ctx, block):
        self.nc = nc
        self.block = block
        self.sem = {}
        self.count = {}
        self.waited = {}
        self.pending_barrier = {}
        for name in ["t", "v", "s", "g", "q", "qd", "sd"]:
            self.sem[name] = ctx.enter_context(nc.semaphore("sem_" + name))
            self.count[name] = 0

    def barrier(self):
        toks = [(n, self.count[n]) for n in self.sem if self.count[n] > 0]
        for eng in _SECTION:
            self.pending_barrier[eng] = toks

    def _emit_waits(self, q, eng, deps):
        alldeps = list(deps)
        if eng in self.pending_barrier:
            alldeps += self.pending_barrier.pop(eng)
        own = _OWN_SEM[eng]
        best = {}
        for d in alldeps:
            if d is None:
                continue
            sname, val = d
            if val <= 0:
                continue
            if sname == own:
                val = min(val, self.count[own])
            if val <= 0:
                continue
            best[sname] = max(best.get(sname, 0), val)
        for sname, val in best.items():
            key = (eng, sname)
            if self.waited.get(key, 0) >= val:
                continue
            q.wait_ge(self.sem[sname], val)
            self.waited[key] = val

    def run(self, eng, fn, deps=(), serialize=False):
        sec = getattr(self.block, _SECTION[eng])
        own = _OWN_SEM[eng]
        prev = self.count[own]

        @sec
        def _(q):
            if serialize and prev > 0:
                q.wait_ge(self.sem[own], prev)
            self._emit_waits(q, eng, deps)
            last = fn(q)
            last.then_inc(self.sem[own], 1)
        self.count[own] += 1
        return (own, self.count[own])

    def dma(self, queue_eng, sem_name, fn, deps=()):
        sec = getattr(self.block, _SECTION[queue_eng])

        @sec
        def _(q):
            self._emit_waits(q, queue_eng, deps)
            for ins in fn(q):
                ins.then_inc(self.sem[sem_name], 16)
                self.count[sem_name] += 16
        return (sem_name, self.count[sem_name])


def emit_ln(sc, G, xv, ybufs, eps, deps):
    stat, scr = G["stat"], G["scr_b"]

    def f1(q):
        q.tensor_scalar(scr[:, :], xv, 0.0, None, ALU.add, ALU.add,
                        accum_out=stat[:, 0:1])
        return q.scalar_tensor_tensor(scr[:, :], xv, 1.0, xv,
                                      ALU.mult, ALU.mult,
                                      accum_out=stat[:, 1:2])
    t1 = sc.run("v", f1, deps=deps)

    def f2(q):
        q.tensor_scalar_mul(stat[:, 2:3], stat[:, 0:1], 1.0 / DM)
        return q.tensor_scalar_mul(stat[:, 3:4], stat[:, 1:2], 1.0 / DM)
    t2 = sc.run("v", f2, deps=[t1])

    t3 = sc.run("s", lambda q: q.activation(stat[:, 4:5], stat[:, 2:3], AF.Square),
                deps=[t2])

    t4 = sc.run("v", lambda q: q.scalar_tensor_tensor(
        stat[:, 3:4], stat[:, 3:4], float(eps), stat[:, 4:5],
        ALU.add, ALU.subtract), deps=[t3, t2])

    t5 = sc.run("s", lambda q: q.activation(stat[:, 4:5], stat[:, 3:4], AF.Sqrt),
                deps=[t4])

    t6 = sc.run("v", lambda q: q.reciprocal(stat[:, 5:6], stat[:, 4:5]), deps=[t5])

    def f7(q):
        last = None
        for buf in ybufs:
            last = q.tensor_scalar(buf, xv, stat[:, 2:3], stat[:, 5:6],
                                   ALU.subtract, ALU.mult)
        return last
    return sc.run("v", f7, deps=[t6])


def emit_l0ffn(sc, G, src, dst):
    nc, w = G["nc"], G["w"]
    tag = G["tag"]

    def scope():
        return ExitStack()

    with scope() as cx:
        def lsb(name, shape, dt=F32):
            return cx.enter_context(nc.sbuf_tensor(f"{tag}_{name}", shape, dt))

        def lps(name, shape, dt=F32):
            return cx.enter_context(nc.psum_tensor(f"{tag}_{name}", shape, dt))

        xt = lsb("xt", [128, 2 * DM])
        yb = lsb("yb", [128, DM], BF16)
        yf32 = lsb("yf32", [128, DM])
        ylo = lsb("ylo", [128, DM], BF16)
        yfm = lsb("yfm", [128, 2 * DM], BF16)
        ylofm = lsb("ylofm", [128, DM], BF16)
        wrf = lsb("wrf", [128, 8 * 4])
        wrhi = lsb("wrhi", [128, 8 * 4], BF16)
        wrlo = lsb("wrlo", [128, 8 * 4], BF16)
        wrlof = lsb("wrlof", [128, 8 * 4])
        lgs = lsb("lgs", [128, 4])
        mtmp = lsb("mtmp", [128, 2])
        tp = lps("tp", [128, 2 * DM], BF16)
        lg = lps("lg", [128, 4])

        t_wr = sc.dma("q", "qd", lambda q: [
            q.dma_start(out=wrf[:, :].rearrange("p (cc e) -> p cc e", cc=8),
                        in_=w["l0f_Wr"].rearrange("(cc p) e -> p cc e", p=128))])

        t_wr1 = sc.run("g", lambda q: q.tensor_copy(wrhi[:, :], wrf[:, :]),
                       deps=[t_wr])
        t_wr2 = sc.run("g", lambda q: q.tensor_sub(wrlof[:, :], wrf[:, :], wrhi[:, :]),
                       deps=[t_wr1])
        t_wrc = sc.run("g", lambda q: q.tensor_copy(wrlo[:, :], wrlof[:, :]),
                       deps=[t_wr2])

        slot = {}
        t_st = None
        for tt in range(NTT):
            par = tt % 2
            xv = xt[:, par * DM:(par + 1) * DM]
            yfmv = yfm[:, par * DM:(par + 1) * DM]

            t_x = sc.dma("q", "qd", lambda q, tt=tt, xv=xv: [
                q.dma_start(out=xv, in_=src[tt * 128:(tt + 1) * 128, :])],
                deps=[slot.get(("x", par))])

            t_ln = emit_ln(sc, G, xv, [yb[:, :], yf32[:, :]], 1e-5,
                           deps=[t_x, slot.get("yb_c")])
            slot[("x", par)] = t_ln

            t_lo = sc.run("v", lambda q: q.tensor_sub(ylo[:, :], yf32[:, :], yb[:, :]),
                          deps=[t_ln])


            def f_tr(q):
                for cc in range(8):
                    q.transpose(tp[:, cc * 128:(cc + 1) * 128],
                                yb[:, cc * 128:(cc + 1) * 128], G["ident_b"][:, :])
                ins = None
                for cc in range(8):
                    ins = q.transpose(tp[:, DM + cc * 128:DM + (cc + 1) * 128],
                                      ylo[:, cc * 128:(cc + 1) * 128], G["ident_b"][:, :])
                return ins
            t_tr = sc.run("t", f_tr, deps=[t_lo, slot.get("tp_ev")])
            slot["yb_c"] = t_tr
            if G.get("debug") and tt == 47:
                t_tap = sc.dma("g", "sd", lambda q, xv=xv: [
                    q.dma_start(out=G["yb_dbg"][0], in_=yb[:, :]),
                    q.dma_start(out=G["yb_dbg"][1], in_=yf32[:, :]),
                    q.dma_start(out=G["yb_dbg"][2], in_=xv),
                    q.dma_start(out=G["yb_dbg"][3][:, 0:8], in_=G["stat"][:, :])], deps=[t_tr])
                slot["yb_c"] = t_tap
                slot[("x", tt % 2)] = t_tap

            def f_ev(q, yfmv=yfmv):
                q.tensor_copy(yfmv, tp[:, 0:DM])
                return q.tensor_copy(ylofm[:, :], tp[:, DM:2 * DM])
            t_ev = sc.run("v", f_ev,
                          deps=[t_tr, slot.get(("yfm", par)), slot.get("ylofm_c")])
            slot["tp_ev"] = t_ev

            def f_rt(q, yfmv=yfmv):
                prods = [(yfmv, wrhi), (yfmv, wrlo), (ylofm[:, :], wrhi)]
                ins = None
                n = 0
                for (lh, rh) in prods:
                    for cc in range(8):
                        n += 1
                        ins = q.matmul(lg[:, :], lh[:, cc * 128:(cc + 1) * 128],
                                       rh[:, cc * 4:(cc + 1) * 4],
                                       start=(n == 1), stop=(n == 24))
                return ins
            t_rt = sc.run("t", f_rt, deps=[t_ev, t_wrc, slot.get("lg_ev")])
            slot["ylofm_c"] = t_rt

            t_cp = sc.run("v", lambda q: q.tensor_copy(lgs[:, :], lg[:, :]),
                          deps=[t_rt])
            t_mx = sc.run("v", lambda q: q.reduce_max(
                G["stat"][:, 6:7], lgs[:, :], axis=AX.X),
                deps=[t_cp], serialize=True)

            def f_mk(q, tt=tt):
                ins = None
                for e in range(4):
                    ins = q.tensor_tensor(
                        G["masks"][:, tt * 4 + e:tt * 4 + e + 1],
                        lgs[:, e:e + 1], G["stat"][:, 6:7], ALU.is_ge)
                return ins
            t_mk = sc.run("v", f_mk, deps=[t_mx], serialize=True)
            slot["lg_ev"] = t_mk
            if G.get("debug"):
                slot["lg_ev"] = sc.dma("s", "sd", lambda q, tt=tt: [
                    q.dma_start(out=G["lgs_dbg"][tt], in_=lgs[:, :])], deps=[t_mk])

            t_st = sc.dma("s", "sd", lambda q, tt=tt, yfmv=yfmv: [
                q.dma_start(out=G["yfm"][tt].rearrange("cc p t -> p cc t"),
                            in_=yfmv.rearrange("p (cc t) -> p cc t", cc=8))],
                deps=[t_ev])
            slot[("yfm", par)] = t_st

    sc.barrier()

    with scope() as cx:
        def lsb(name, shape, dt=F32):
            return cx.enter_context(nc.sbuf_tensor(f"{tag}_{name}", shape, dt))

        def lps(name, shape, dt=F32):
            return cx.enter_context(nc.psum_tensor(f"{tag}_{name}", shape, dt))

        w1f = lsb("w1f", [128, 8 * 512])
        w1b = lsb("w1b", [128, 2 * NE * 8 * 512], BF16)
        b1f = lsb("b1f", [1, NE * 512])
        b1b = lsb("b1b", [1, 2 * NE * 512], BF16)
        yfs = lsb("yfs", [128, 2 * DM], BF16)
        hsb = lsb("hsb", [128, 2 * 2048], BF16)
        isb = lsb("isb", [128, 2 * 512], BF16)
        hp = lps("hp", [128, 2 * 2048])

        aslot = {}
        t_is = None

        def stream_w1(ic):
            wp = (ic % 2) * NE * 4096
            bp = (ic % 2) * NE * 512
            toks = []
            for e in range(NE):
                t_w = sc.dma("q", "qd", lambda q, e=e, ic=ic: [
                    q.dma_start(out=w1f[:, :].rearrange("p (cc i) -> p cc i", cc=8),
                                in_=w["l0f_W1"][e, :, ic * 512:(ic + 1) * 512]
                                .rearrange("(cc p) i -> p cc i", p=128))],
                    deps=[aslot.get("w1f_c")])

                def f_c(q, e=e, wp=wp):
                    return q.tensor_copy(w1b[:, wp + e * 4096:wp + (e + 1) * 4096],
                                         w1f[:, :])
                t_c = sc.run("g", f_c, deps=[t_w, aslot.get(("w1b_c", ic % 2))])
                aslot["w1f_c"] = t_c
                toks.append(t_c)
            t_b1 = sc.dma("q", "qd", lambda q, ic=ic: [
                q.dma_start(out=b1f[:, :].rearrange("o (e i) -> o e i", e=NE),
                            in_=w["l0f_b1"][:, ic * 512:(ic + 1) * 512].unsqueeze(0))],
                deps=[aslot.get(("b1_c", ic % 2)), aslot.get("b1f_c")])
            t_b1c = sc.run("g", lambda q, bp=bp: q.tensor_copy(
                b1b[:, bp:bp + NE * 512], b1f[:, :]), deps=[t_b1])
            aslot["b1f_c"] = t_b1c
            toks.append(t_b1c)
            return toks

        w_next = stream_w1(0)
        for ic in range(8):
            w_ready = w_next
            wp = (ic % 2) * NE * 4096
            bp = (ic % 2) * NE * 512
            if ic < 7:
                w_next = stream_w1(ic + 1)

            for tt in range(NTT):
                par = tt % 2
                yv = yfs[:, par * DM:(par + 1) * DM]
                hv = hp[:, par * 2048:(par + 1) * 2048]
                hbv = hsb[:, par * 2048:(par + 1) * 2048]
                iv = isb[:, par * 512:(par + 1) * 512]

                t_y = sc.dma("q", "qd", lambda q, tt=tt, yv=yv: [
                    q.dma_start(out=yv.rearrange("p (cc t) -> p cc t", cc=8),
                                in_=G["yfm"][tt].rearrange("cc p t -> p cc t"))],
                    deps=[aslot.get(("yfs", par))])

                def f_h(q, yv=yv, hv=hv, wp=wp, bp=bp):
                    ins = None
                    for e in range(NE):
                        ins = q.matmul(hv[:, e * 512:(e + 1) * 512], G["ones1"][:, :],
                                       b1b[:, bp + e * 512:bp + (e + 1) * 512],
                                       start=True, stop=False)
                        for cc in range(8):
                            ins = q.matmul(
                                hv[:, e * 512:(e + 1) * 512],
                                yv[:, cc * 128:(cc + 1) * 128],
                                w1b[:, wp + e * 4096 + cc * 512:wp + e * 4096 + (cc + 1) * 512],
                                start=False, stop=(cc == 7))
                    return ins
                t_h = sc.run("t", f_h, deps=[t_y] + w_ready + [aslot.get(("hp", par))])
                aslot[("yfs", par)] = t_h
                if tt == NTT - 1:
                    aslot[("w1b_c", ic % 2)] = t_h
                    aslot[("b1_c", ic % 2)] = t_h

                t_g = sc.run("s", lambda q, hv=hv, hbv=hbv:
                             q.activation(hbv, hv, AF.Gelu),
                             deps=[t_h, aslot.get(("hsb", par))])
                aslot[("hp", par)] = t_g

                def f_cb(q, tt=tt, hbv=hbv, iv=iv):
                    q.tensor_scalar(iv, hbv[:, 0:512],
                                    G["masks"][:, tt * 4:tt * 4 + 1], None, ALU.mult)
                    ins = None
                    for e in range(1, NE):
                        ins = q.scalar_tensor_tensor(
                            iv, hbv[:, e * 512:(e + 1) * 512],
                            G["masks"][:, tt * 4 + e:tt * 4 + e + 1],
                            iv, ALU.mult, ALU.add)
                    return ins
                t_cb = sc.run("v", f_cb, deps=[t_g, aslot.get(("isb", par))])
                aslot[("hsb", par)] = t_cb

                t_is = sc.dma("s", "sd", lambda q, ic=ic, tt=tt, iv=iv: [
                    q.dma_start(out=G["interm"][ic, tt], in_=iv)], deps=[t_cb])
                aslot[("isb", par)] = t_is

    sc.barrier()

    with scope() as cx:
        def lsb(name, shape, dt=F32):
            return cx.enter_context(nc.sbuf_tensor(f"{tag}_{name}", shape, dt))

        def lps(name, shape, dt=F32):
            return cx.enter_context(nc.psum_tensor(f"{tag}_{name}", shape, dt))

        wof = lsb("wof", [128, 4 * DM])
        wob = lsb("wob", [128, 32 * DM], BF16)
        bo_b = lsb("bo_b", [128, DM])
        gam_b = lsb("gam_b", [128, DM])
        bet_b = lsb("bet_b", [128, DM])
        row_f = lsb("row_f", [1, DM])
        row_b = lsb("row_b", [1, DM], BF16)
        ib = lsb("ib", [128, 2 * 4096], BF16)
        itf = lsb("itf", [128, 4096], BF16)
        zsb = lsb("zsb", [128, DM])
        ztmp = lsb("ztmp", [128, DM])
        xt2 = lsb("xt2", [128, 2 * DM])
        y2 = lsb("y2", [128, DM])
        xo = lsb("xo", [128, 2 * DM])
        tpb = lps("tpb", [128, 2048], BF16)
        zp = lps("zp", [128, 2 * DM])

        bslot = {}
        bc_ps_t = cx.enter_context(nc.psum_tensor(f"{tag}_bcps", [128, DM], F32))
        t_prev = None
        for name, dstbuf in [("l0f_bo", bo_b), ("l0f_g", gam_b), ("l0f_b", bet_b)]:
            t_r = sc.dma("q", "qd", lambda q, name=name: [
                q.dma_start(out=row_f[:, :], in_=w[name].unsqueeze(0))],
                deps=[t_prev])
            t_c = sc.run("v", lambda q: q.tensor_copy(row_b[:, :], row_f[:, :]),
                         deps=[t_r, t_prev])

            def f_bc(q):
                ins = None
                for nch in range(DM // 512):
                    ins = q.matmul(bc_ps_t[:, nch * 512:(nch + 1) * 512],
                                   G["ones1"][:, :],
                                   row_b[:, nch * 512:(nch + 1) * 512],
                                   start=True, stop=True)
                return ins
            t_bc = sc.run("t", f_bc, deps=[t_c, t_prev])
            t_prev = sc.run("v", lambda q, dstbuf=dstbuf: q.tensor_copy(
                dstbuf[:, :], bc_ps_t[:, :]), deps=[t_bc])
        t_bcast = t_prev
        wo_casts = [t_bcast]
        for gq in range(8):
            t_wo = sc.dma("q", "qd", lambda q, gq=gq: [
                q.dma_start(out=wof[:, :].rearrange("p (cc d) -> p cc d", cc=4),
                            in_=w["l0f_Wo"][gq * 512:(gq + 1) * 512, :]
                            .rearrange("(cc p) d -> p cc d", p=128))],
                deps=[bslot.get("wof_c")])
            t_woc = sc.run("g", lambda q, gq=gq: q.tensor_copy(
                wob[:, gq * 4 * DM:(gq + 1) * 4 * DM], wof[:, :]), deps=[t_wo])
            bslot["wof_c"] = t_woc
            wo_casts.append(t_woc)

        t_xs = None
        for tt in range(NTT):
            par = tt % 2
            ibv = ib[:, par * 4096:(par + 1) * 4096]
            xv = xt2[:, par * DM:(par + 1) * DM]
            zv = zp[:, par * DM:(par + 1) * DM]
            xov = xo[:, par * DM:(par + 1) * DM]

            t_i = sc.dma("q", "qd", lambda q, tt=tt, ibv=ibv, xv=xv: [
                q.dma_start(out=ibv.rearrange("p (ic i) -> p ic i", ic=8),
                            in_=G["interm"][:, tt].rearrange("ic p i -> p ic i")),
                q.dma_start(out=xv, in_=src[tt * 128:(tt + 1) * 128, :])],
                deps=[bslot.get(("ib", par)), bslot.get(("x2", par))])

            tr_toks = []
            for rnd in range(2):
                def f_tr(q, rnd=rnd, ibv=ibv):
                    ins = None
                    for j in range(16):
                        ci = rnd * 16 + j
                        ins = q.transpose(tpb[:, j * 128:(j + 1) * 128],
                                          ibv[:, ci * 128:(ci + 1) * 128],
                                          G["ident_b"][:, :])
                    return ins
                t_tr = sc.run("t", f_tr, deps=[t_i, bslot.get("tpb_ev")])

                def f_te(q, rnd=rnd):
                    return q.tensor_copy(itf[:, rnd * 2048:(rnd + 1) * 2048], tpb[:, :])
                t_te = sc.run("v", f_te, deps=[t_tr, bslot.get("itf_c")])
                bslot["tpb_ev"] = t_te
                tr_toks.append(t_te)

            def f_z(q, zv=zv):
                ins = None
                for ci in range(32):
                    for dch in range(2):
                        ins = q.matmul(
                            zv[:, dch * 512:(dch + 1) * 512],
                            itf[:, ci * 128:(ci + 1) * 128],
                            wob[:, ci * DM + dch * 512:ci * DM + (dch + 1) * 512],
                            start=(ci == 0), stop=(ci == 31))
                return ins
            t_z = sc.run("t", f_z, deps=tr_toks + wo_casts + [bslot.get(("zp", par))])
            bslot[("ib", par)] = t_z
            bslot["itf_c"] = t_z

            t_y2 = emit_ln(sc, G, xv, [y2[:, :]], 1e-5,
                           deps=[t_i, bslot.get("y2_c")])

            t_ze = sc.run("s", lambda q, zv=zv: q.copy(zsb[:, :], zv),
                          deps=[t_z, bslot.get("zsb_c")])
            bslot[("zp", par)] = t_ze

            def f_za(q):
                q.tensor_add(zsb[:, :], zsb[:, :], bo_b[:, :])
                return q.tensor_add(zsb[:, :], zsb[:, :], y2[:, :])
            t_za = sc.run("g", f_za, deps=[t_ze, t_y2])
            bslot["y2_c"] = t_za
            if G.get("debug"):
                t_za = sc.dma("s", "sd", lambda q, tt=tt: [
                    q.dma_start(out=G["zsb_dbg"][tt], in_=zsb[:, :]),
                    q.dma_start(out=G["y2_dbg"][tt], in_=y2[:, :])], deps=[t_za])

            t_iln = emit_ln(sc, G, zsb[:, :], [ztmp[:, :]], 1e-12,
                            deps=[t_za, bslot.get("ztmp_c")])
            bslot["zsb_c"] = t_iln

            def f_gb(q):
                q.tensor_mul(ztmp[:, :], ztmp[:, :], gam_b[:, :])
                return q.tensor_add(ztmp[:, :], ztmp[:, :], bet_b[:, :])
            t_gb = sc.run("g", f_gb, deps=[t_iln])

            t_res = sc.run("v", lambda q, xov=xov, xv=xv: q.scalar_tensor_tensor(
                xov, ztmp[:, :], 0.5, xv, ALU.mult, ALU.add),
                deps=[t_gb, bslot.get(("xo", par))])
            bslot["ztmp_c"] = t_res
            bslot[("x2", par)] = t_res

            t_xs = sc.dma("s", "sd", lambda q, tt=tt, xov=xov: [
                q.dma_start(out=dst[tt * 128:(tt + 1) * 128, :], in_=xov)],
                deps=[t_res])
            bslot[("xo", par)] = t_xs

    sc.barrier()



def emit_l0attn(sc, G, src, dst):
    nc, w = G["nc"], G["w"]
    tag = G["tag"]
    H, HD = 16, 64

    with ExitStack() as cx:
        def lsb(name, shape, dt=F32):
            return cx.enter_context(nc.sbuf_tensor(f"{tag}_{name}", shape, dt))

        def lps(name, shape, dt=F32):
            return cx.enter_context(nc.psum_tensor(f"{tag}_{name}", shape, dt))

        yfm = lsb("yfm", [128, 8 * S], BF16)
        qrf = lsb("qrf", [128, 8 * S], BF16)
        krf = lsb("krf", [128, 8 * S], BF16)
        vtm = lsb("vtm", [128, 16 * DM], BF16)
        outall = lsb("outall", [128, 16 * DM], BF16)
        wsl = lsb("wsl", [128, 8 * DM], BF16)
        wf = lsb("wf", [128, DM // 2])
        xt = lsb("xt", [128, DM])
        yb = lsb("yb", [128, DM], BF16)
        rtm = lsb("rtm", [128, 2 * DM], BF16)
        tabs = lsb("tabs", [128, 2 * DM], BF16)
        asb = lsb("asb", [128, 2048], BF16)
        dens = lsb("dens", [128, 16])
        rden = lsb("rden", [128, 16])
        dsb = lsb("dsb", [128, 256], BF16)
        dtmp = lsb("dtmp", [128, 16])
        otmp = lsb("otmp", [128, DM], BF16)
        pq = lps("pq", [128, 2048])
        pk = lps("pk", [128, 2048])

        def bank_chunks(lo, hi):
            out = []
            while lo < hi:
                nxt = min((lo // 512 + 1) * 512, hi)
                out.append((lo, nxt))
                lo = nxt
            return out

        slot = {}
        for b in range(BC):
            base = b * S
            PHASE_MARKS.append((f"p1s1_b{b}", nc.next_id()))

            def load_w(nm, dep_consumed):
                toks = []
                dep = dep_consumed
                for cc in range(16):
                    t_l = sc.dma("q", "qd", lambda q, nm=nm, cc=cc: [
                        q.dma_start(out=wf[:, :],
                                    in_=w[nm][(cc // 2) * 128:(cc // 2 + 1) * 128,
                                              (cc % 2) * 512:(cc % 2 + 1) * 512])],
                        deps=[dep, slot.get("wf_c")])
                    t_c = sc.run("g", lambda q, cc=cc: q.tensor_copy(
                        wsl[:, cc * 512:(cc + 1) * 512], wf[:, :]), deps=[t_l])
                    slot["wf_c"] = t_c
                    toks.append(t_c)
                return toks

            for ts_ in range(16):
                t_x = sc.dma("q", "qd", lambda q, ts_=ts_: [
                    q.dma_start(out=xt[:, :],
                                in_=src[base + ts_ * 128:base + (ts_ + 1) * 128, :])],
                    deps=[slot.get("x_c")])
                t_ln = emit_ln(sc, G, xt[:, :], [yb[:, :]], 1e-5,
                               deps=[t_x, slot.get("yb_c")])
                slot["x_c"] = t_ln

                tp = pk[:, 0:512].bitcast(BF16)

                def f_tr(q, tp=tp):
                    ins = None
                    for cc in range(8):
                        ins = q.transpose(tp[:, cc * 128:(cc + 1) * 128],
                                          yb[:, cc * 128:(cc + 1) * 128],
                                          G["ident_b"][:, :])
                    return ins
                t_tr = sc.run("t", f_tr, deps=[t_ln, slot.get("tp_ev")])
                slot["yb_c"] = t_tr

                def f_te(q, ts_=ts_, tp=tp):
                    ov = yfm[:, :].rearrange("p (cc t) -> p cc t", t=S)[
                        :, :, ts_ * 128:(ts_ + 1) * 128]
                    return q.tensor_copy(ov, tp.rearrange("p (cc t) -> p cc t", t=128))
                t_te = sc.run("v", f_te, deps=[t_tr, slot.get("yfm_c")])
                slot["tp_ev"] = t_te
            yfm_done = sc.count["v"]
            t_yfm = ("v", yfm_done)

            for (wnm, dstf, ta, tb) in [("l0a_Wq", qrf, "l0_qA", "l0_qB"),
                                        ("l0a_Wk", krf, "l0_kA", "l0_kB")]:
                wtoks = load_w(wnm, slot.get("wsl_c"))
                for ts_ in range(16):
                    par = ts_ % 2
                    pv = pq[:, par * 1024:(par + 1) * 1024]

                    def f_mm(q, ts_=ts_, pv=pv):
                        ins = None
                        for cc in range(8):
                            lh = yfm[:, cc * S + ts_ * 128:cc * S + (ts_ + 1) * 128]
                            for n2 in range(2):
                                ins = q.matmul(pv[:, n2 * 512:(n2 + 1) * 512], lh,
                                               wsl[:, cc * DM + n2 * 512:cc * DM + (n2 + 1) * 512],
                                               start=(cc == 0), stop=(cc == 7))
                        return ins
                    t_mm = sc.run("t", f_mm,
                                  deps=[t_yfm] + wtoks + [slot.get(("pp", par))])

                    t_tab = sc.dma("q", "qd", lambda q, ts_=ts_, ta=ta, tb=tb: [
                        q.dma_start(out=tabs[:, 0:DM],
                                    in_=w[ta][ts_ * 128:(ts_ + 1) * 128, :]),
                        q.dma_start(out=tabs[:, DM:2 * DM],
                                    in_=w[tb][ts_ * 128:(ts_ + 1) * 128, :])],
                        deps=[slot.get("tab_c")])

                    def f_r1(q, pv=pv):
                        q.tensor_mul(rtm[:, 0:DM], pv, tabs[:, 0:DM])
                        pvv = pv.rearrange("p (a c) -> p a c", c=2)
                        bvv = tabs[:, DM:2 * DM].rearrange("p (a c) -> p a c", c=2)
                        mvv = rtm[:, DM:2 * DM].rearrange("p (a c) -> p a c", c=2)
                        q.tensor_mul(mvv[:, :, 0:1], pvv[:, :, 1:2], bvv[:, :, 0:1])
                        return q.tensor_mul(mvv[:, :, 1:2], pvv[:, :, 0:1], bvv[:, :, 1:2])
                    t_r1 = sc.run("v", f_r1, deps=[t_mm, t_tab, slot.get("rtm_c")])
                    slot["tab_c"] = t_r1
                    slot[("pp", par)] = t_r1

                    t_r2 = sc.run("v", lambda q: q.tensor_add(
                        yb[:, :], rtm[:, 0:DM], rtm[:, DM:2 * DM]),
                        deps=[t_r1, slot.get("yb_c")])
                    slot["rtm_c"] = t_r2

                    tp = pk[:, 0:512].bitcast(BF16)

                    def f_tr2(q, tp=tp):
                        ins = None
                        for cc in range(8):
                            ins = q.transpose(tp[:, cc * 128:(cc + 1) * 128],
                                              yb[:, cc * 128:(cc + 1) * 128],
                                              G["ident_b"][:, :])
                        return ins
                    t_tr2 = sc.run("t", f_tr2, deps=[t_r2, slot.get("tp_ev")])
                    slot["yb_c"] = t_tr2

                    def f_te2(q, ts_=ts_, tp=tp, dstf=dstf):
                        ov = dstf[:, :].rearrange("p (cc t) -> p cc t", t=S)[
                            :, :, ts_ * 128:(ts_ + 1) * 128]
                        return q.tensor_copy(ov, tp.rearrange("p (cc t) -> p cc t", t=128))
                    t_te2 = sc.run("v", f_te2, deps=[t_tr2, slot.get(("qk_c", wnm))])
                    slot["tp_ev"] = t_te2
                slot["wsl_c"] = t_mm
            qk_done = ("v", sc.count["v"])

            wtoks = load_w("l0a_Wv", slot.get("wsl_c"))
            for ts_ in range(16):
                par = ts_ % 2
                pv = pq[:, par * 1024:(par + 1) * 1024]

                def f_mmv(q, ts_=ts_, pv=pv):
                    ins = None
                    for cc in range(8):
                        lh = yfm[:, cc * S + ts_ * 128:cc * S + (ts_ + 1) * 128]
                        for n2 in range(2):
                            ins = q.matmul(pv[:, n2 * 512:(n2 + 1) * 512], lh,
                                           wsl[:, cc * DM + n2 * 512:cc * DM + (n2 + 1) * 512],
                                           start=(cc == 0), stop=(cc == 7))
                    return ins
                t_mmv = sc.run("t", f_mmv,
                               deps=[t_yfm] + wtoks + [slot.get(("pp", par))])
                t_ev = sc.run("v", lambda q, ts_=ts_, pv=pv: q.tensor_copy(
                    vtm[:, ts_ * DM:(ts_ + 1) * DM], pv),
                    deps=[t_mmv, slot.get("vtm_c")])
                slot[("pp", par)] = t_ev
                slot["wsl_c2"] = t_mmv
            v_done = ("v", sc.count["v"])
            slot["wsl_c"] = slot["wsl_c2"]

            wg_toks = load_w("l0a_Wg", slot.get("wsl_c"))
            yv3 = yfm[:, :].rearrange("p (cc t) -> p cc t", t=S)
            for i in range(16):
                par = i % 2
                pv = pq[:, par * 1024:(par + 1) * 1024]

                def f_gm(q, i=i, pv=pv):
                    ins = None
                    for cc in range(8):
                        lh = yfm[:, cc * S + i * 128:cc * S + (i + 1) * 128]
                        for n2 in range(2):
                            ins = q.matmul(pv[:, n2 * 512:(n2 + 1) * 512], lh,
                                           wsl[:, cc * DM + n2 * 512:cc * DM + (n2 + 1) * 512],
                                           start=(cc == 0), stop=(cc == 7))
                    return ins
                t_gm = sc.run("t", f_gm,
                              deps=[t_yfm] + wg_toks + [slot.get(("pp", par))])

                def f_si(q, i=i, pv=pv):
                    ov = yv3[:, :, i * 128:(i + 1) * 128]
                    return q.activation(
                        ov.rearrange("p cc t -> p (cc t)") if False else ov,
                        pv.rearrange("p (cc t) -> p cc t", t=128), AF.Silu)
                t_si = sc.run("s", f_si, deps=[t_gm])
                slot[("pp", par)] = t_si
            gsilu_done = ("s", sc.count["s"])

            PHASE_MARKS.append((f"p1s2_b{b}", nc.next_id()))
            head_last = None
            for h in range(H):
                hh, hp = h // 2, (h % 2) * 64
                qrh = qrf[hp:hp + 64, hh * S:(hh + 1) * S]
                krh = krf[hp:hp + 64, hh * S:(hh + 1) * S]

                a_toks = []
                gp_toks = []
                for qt in range(16):
                    ps = pq if qt % 2 == 0 else pk
                    krange = (qt + 1) * 128
                    dpar = qt % 2

                    def f_a(q, qt=qt, ps=ps, krange=krange):
                        lh = qrh[:, qt * 128:(qt + 1) * 128]
                        ins = None
                        for (c0, c1) in bank_chunks(0, krange):
                            ins = q.matmul(ps[:, c0:c1], lh, krh[:, c0:c1],
                                           start=True, stop=True)
                        return ins
                    depk = [qk_done, slot.get(("psA", qt % 2)),
                            slot.get(("psAv", qt % 2))]
                    if qt < 2 and h == 0:
                        depk += [v_done, gsilu_done, slot.get(("pp", 0)),
                                 slot.get(("pp", 1)), slot.get("tp_ev")]
                    t_a = sc.run("t", f_a, deps=depk)

                    if qt > 0:
                        t_abv = sc.run("s", lambda q, qt=qt, ps=ps: q.activation(
                            asb[:, 0:qt * 128], ps[:, 0:qt * 128],
                            AF.Abs, accum_out=dens[:, qt:qt + 1]),
                            deps=[t_a, slot.get("asb_c"), slot.get("dens_c")])
                    else:
                        t_abv = None
                    t_ab = sc.run("s", lambda q, qt=qt, ps=ps: q.copy(
                        dsb[:, (qt % 2) * 128:(qt % 2) * 128 + 128],
                        ps[:, qt * 128:(qt + 1) * 128]),
                        deps=[t_a, slot.get(("dsb", dpar))])
                    slot[("psA", qt % 2)] = t_ab
                    slot[("psAv", qt % 2)] = t_abv
                    slot["asb_c"] = t_abv
                    a_toks.append(t_ab)
                    if t_abv is not None:
                        a_toks.append(t_abv)

                    dv = dsb[:, dpar * 128:dpar * 128 + 128]
                    t_g1 = sc.run("g", lambda q, dv=dv: q.tensor_mul(
                        dv, dv, G["tril"][:, :]), deps=[t_ab])
                    t_g2 = sc.run("v", lambda q, qt=qt, dv=dv: q.tensor_reduce(
                        dtmp[:, qt:qt + 1], dv, axis=AX.X, op=ALU.add,
                        apply_absolute_value=True), deps=[t_g1])
                    slot[("dsb", dpar)] = t_g2
                    if qt == 0:
                        t_g3 = sc.run("g", lambda q, qt=qt: q.tensor_copy(
                            dens[:, qt:qt + 1], dtmp[:, qt:qt + 1]), deps=[t_g2])
                    else:
                        t_g3 = sc.run("g", lambda q, qt=qt: q.tensor_tensor(
                            dens[:, qt:qt + 1], dens[:, qt:qt + 1],
                            dtmp[:, qt:qt + 1], ALU.add), deps=[t_g2, t_abv])
                    gp_toks.append(t_g3)

                t_cl = sc.run("v", lambda q: q.tensor_scalar(
                    rden[:, :], dens[:, :], 1.0, 50000.0, ALU.max, ALU.min),
                    deps=gp_toks[-2:] + a_toks[-1:])
                t_rc = sc.run("v", lambda q: q.reciprocal(rden[:, :], rden[:, :]),
                              deps=[t_cl])
                slot["dens_c"] = t_rc

                slot["atp0"] = slot.get(("psA", 0))
                slot["atp1"] = slot.get(("psA", 1))
                vv_last = None
                for qh in range(2):
                    nkt = (qh + 1) * 8
                    ae_tok = {}
                    gp_tok = {}
                    for step in range(nkt + 1):
                        if step < nkt:
                            kt = step
                            par = kt % 2
                            atp = pq[:, par * 1024:(par + 1) * 1024]
                            qlo = max(qh * 1024, kt * 128)
                            off = qlo - qh * 1024

                            def f_at(q, kt=kt, atp=atp, off=off, qh=qh):
                                lh = krh[:, kt * 128:(kt + 1) * 128]
                                ins = None
                                for (c0, c1) in bank_chunks(off, 1024):
                                    ins = q.matmul(atp[:, c0:c1], lh,
                                                   qrh[:, qh * 1024 + c0:qh * 1024 + c1],
                                                   start=True, stop=True)
                                return ins
                            dep_at = [slot.get(("atB", par)), slot.get("atp%d" % par)]
                            slot["atp%d" % par] = None
                            t_at = sc.run("t", f_at, deps=dep_at)

                            eng = "s" if kt % 2 == 0 else "v"
                            t_ae = sc.run(eng, lambda q, atp=atp, off=off, par=par:
                                          q.copy(asb[:, par * 1024 + off:(par + 1) * 1024],
                                                 atp[:, off:1024])
                                          if eng == "s" else
                                          q.tensor_copy(asb[:, par * 1024 + off:(par + 1) * 1024],
                                                        atp[:, off:1024]),
                                          deps=[t_at, slot.get(("asbB", par)),
                                                slot.get("asb_c")])
                            slot["asb_c"] = None
                            slot[("atB", par)] = t_ae
                            ae_tok[kt] = t_ae
                            if kt >= qh * 8:
                                dg0 = par * 1024 + kt * 128 - qh * 1024
                                t_gd = sc.run("g", lambda q, dg0=dg0: q.tensor_mul(
                                    asb[:, dg0:dg0 + 128], asb[:, dg0:dg0 + 128],
                                    G["trilT"][:, :]), deps=[t_ae])
                                gp_tok[kt] = t_gd
                        if step >= 1:
                            kt = step - 1
                            par = kt % 2

                            def f_vv(q, kt=kt, qh=qh, par=par, h=h):
                                ins = None
                                for i in range(max(kt + 1, qh * 8), qh * 8 + 8):
                                    ins = q.matmul(
                                        pk[0:64, i * 128:(i + 1) * 128],
                                        vtm[:, kt * DM + h * HD:kt * DM + (h + 1) * HD],
                                        asb[:, par * 1024 + i * 128 - qh * 1024:
                                            par * 1024 + (i + 1) * 128 - qh * 1024],
                                        start=False, stop=False, skip_group_check=True)
                                return ins
                            dep_vv = [ae_tok[kt]]
                            if kt == 0 and qh == 0:
                                def f_z(q):
                                    ins = None
                                    for bk in range(4):
                                        ins = q.matmul(pk[0:64, bk * 512:(bk + 1) * 512],
                                                       G["ones1"][:, 0:64],
                                                       G["zrow"][:, :],
                                                       start=True, stop=True)
                                    return ins
                                t_z = sc.run("t", f_z, deps=[slot.get("otB")] + a_toks[-2:])
                                dep_vv.append(t_z)
                            t_vv = None
                            if max(kt + 1, qh * 8) < qh * 8 + 8:
                                t_vv = sc.run("t", f_vv, deps=dep_vv)
                            if kt >= qh * 8:
                                def f_vd(q, kt=kt, qh=qh, par=par, h=h):
                                    return q.matmul(
                                        pk[0:64, kt * 128:(kt + 1) * 128],
                                        vtm[:, kt * DM + h * HD:kt * DM + (h + 1) * HD],
                                        asb[:, par * 1024 + kt * 128 - qh * 1024:
                                            par * 1024 + (kt + 1) * 128 - qh * 1024],
                                        start=False, stop=True, skip_group_check=True)
                                t_vv = sc.run("t", f_vd, deps=dep_vv + [gp_tok[kt]])
                            slot[("asbB", par)] = t_vv
                            vv_last = t_vv

                t_oe = sc.run("v", lambda q: q.tensor_copy(asb[0:64, :], pk[0:64, :]),
                              deps=[vv_last, slot.get("otsb_c")])
                slot["otB"] = t_oe

                tr_grp = None
                for grp in range(4):
                    tpo = pq[:, grp * 512:grp * 512 + 128].bitcast(BF16)

                    def f_ot(q, grp=grp, tpo=tpo):
                        ins = None
                        for j in range(4):
                            i = grp * 4 + j
                            ins = q.transpose(tpo[:, j * 64:(j + 1) * 64],
                                              asb[0:64, i * 128:(i + 1) * 128],
                                              G["ident_b"][0:64, 0:64])
                        return ins
                    t_ot = sc.run("t", f_ot, deps=[t_oe, slot.get(("otT", grp))])

                    def f_os(q, grp=grp, tpo=tpo, h=h):
                        ins = None
                        for j in range(4):
                            i = grp * 4 + j
                            ins = q.tensor_scalar(
                                outall[:, i * DM + h * HD:i * DM + (h + 1) * HD],
                                tpo[:, j * 64:(j + 1) * 64],
                                rden[:, i:i + 1], None, ALU.mult)
                        return ins
                    t_os = sc.run("v", f_os, deps=[t_ot, t_rc, slot.get("outall_c")])
                    slot[("otT", grp)] = t_os
                    tr_grp = t_os
                slot["otsb_c"] = tr_grp
                slot[("psA", 0)] = tr_grp
                slot[("psA", 1)] = t_oe
                slot["asb_c"] = tr_grp
                head_last = tr_grp
            heads_done = head_last

            PHASE_MARKS.append((f"p1s3_b{b}", nc.next_id()))
            gsumA = asb[:, 0:1024].bitcast(F32)
            ptm = tabs[:, :].bitcast(F32)
            sq_toks = []
            last_rd = None
            for i in range(16):
                sbuf_t = rtm[:, (i % 2) * DM:(i % 2 + 1) * DM]
                t_sq = sc.run("s", lambda q, i=i, sbuf_t=sbuf_t: q.activation(
                    sbuf_t, outall[:, i * DM:(i + 1) * DM], AF.Square),
                    deps=[heads_done, slot.get(("gnsq", i % 2)), slot.get("asb_gn"),
                          slot.get("rtm_c")])
                t_rd = sc.run("v", lambda q, i=i, sbuf_t=sbuf_t: q.tensor_reduce(
                    gsumA[:, i * 16:(i + 1) * 16],
                    sbuf_t.rearrange("p (hh d) -> p hh d", d=HD),
                    axis=AX.X, op=ALU.add), deps=[t_sq])
                slot[("gnsq", i % 2)] = t_rd
                last_rd = t_rd
            t_s1 = sc.run("v", lambda q: q.tensor_scalar(
                gsumA[:, 256:512], gsumA[:, 0:256], 1.0 / HD, 1e-6,
                ALU.mult, ALU.add), deps=[last_rd])
            t_s2 = sc.run("s", lambda q: q.activation(
                gsumA[:, 256:512], gsumA[:, 256:512], AF.Sqrt), deps=[t_s1])
            t_s3 = sc.run("v", lambda q: q.reciprocal(
                gsumA[:, 256:512], gsumA[:, 256:512]), deps=[t_s2])
            grstA = gsumA[:, 256:512]

            st_last = None
            for i in range(16):
                t_x = sc.dma("q", "qd", lambda q, i=i: [
                    q.dma_start(out=xt[:, :],
                                in_=src[base + i * 128:base + (i + 1) * 128, :])],
                    deps=[slot.get("x_c"), heads_done])

                def f_ap(q, i=i):
                    ins = None
                    for h2 in range(H):
                        ins = q.tensor_scalar(
                            otmp[:, h2 * HD:(h2 + 1) * HD],
                            outall[:, i * DM + h2 * HD:i * DM + (h2 + 1) * HD],
                            grstA[:, i * 16 + h2:i * 16 + h2 + 1], None, ALU.mult)
                    return ins
                t_ap = sc.run("v", f_ap, deps=[t_s3, slot.get("otmp_c")])

                gview = yv3[:, :, i * 128:(i + 1) * 128]
                t_mu2 = sc.run("v", lambda q, gview=gview: q.tensor_mul(
                    otmp[:, :].rearrange("p (cc t) -> p cc t", t=128),
                    otmp[:, :].rearrange("p (cc t) -> p cc t", t=128), gview),
                    deps=[t_ap, gsilu_done])

                t_fin = sc.run("v", lambda q: q.tensor_add(ptm, otmp[:, :],
                                                           xt[:, :]),
                               deps=[t_mu2, t_x, slot.get("ptm_st"),
                                     slot.get("tab_fin")])
                slot["otmp_c"] = t_fin
                slot["outall_c"] = t_ap
                slot["x_c"] = t_fin

                t_st = sc.dma("s", "sd", lambda q, i=i: [
                    q.dma_start(out=dst[base + i * 128:base + (i + 1) * 128, :],
                                in_=ptm)], deps=[t_fin])
                slot["ptm_st"] = t_st
                st_last = t_st
            slot["comb_ps"] = gsilu_done
            slot["asb_gn"] = t_s3
            slot["asb_c"] = slot.get("outall_c")
            slot["rtm_c"] = last_rd
            slot["tab_c"] = st_last
            slot["gps_c"] = gsilu_done
            slot["yfm_c"] = st_last
            slot["rtm_c"] = slot.get("rtm_c")
            slot["vtm_c"] = vv_last
            slot[("qk_c", "l0a_Wq")] = vv_last
            slot[("qk_c", "l0a_Wk")] = vv_last
            slot[("pp", 0)] = slot.get("gps_c")
            slot[("pp", 1)] = slot.get("gps_c")

    sc.barrier()



def emit_l1ffn(sc, G, src, dst):
    nc, w = G["nc"], G["w"]
    tag = G["tag"]
    TW = 1024

    with ExitStack() as cx:
        def lsb(name, shape, dt=F32):
            return cx.enter_context(nc.sbuf_tensor(f"{tag}_{name}", shape, dt))

        def lps(name, shape, dt=F32):
            return cx.enter_context(nc.psum_tensor(f"{tag}_{name}", shape, dt))

        wf = lsb("wf", [128, 4 * DM])
        wb = {}
        t_wc_all = []
        wslot = {}
        for wi, nm in enumerate(["l1f_WrT", "l1f_WkT", "l1f_WvT"]):
            wb[nm] = lsb(f"wb{wi}", [128, 8 * DM], BF16)
            for half in range(2):
                t_wl = sc.dma("q", "qd", lambda q, nm=nm, half=half: [
                    q.dma_start(out=wf[:, :].rearrange("p (cc d) -> p cc d", cc=4),
                                in_=w[nm][half * 512:(half + 1) * 512, :]
                                .rearrange("(cc p) d -> p cc d", p=128))],
                    deps=[wslot.get("wf_c")])
                t_wc = sc.run("g", lambda q, nm=nm, half=half: q.tensor_copy(
                    wb[nm][:, half * 4 * DM:(half + 1) * 4 * DM], wf[:, :]),
                    deps=[t_wl])
                wslot["wf_c"] = t_wc
                t_wc_all.append(t_wc)

        muf = lsb("muf", [128, 8 * 2])
        mur = lsb("mur", [128, 8 * 4])
        t_mu = sc.dma("q", "qd", lambda q: [
            q.dma_start(out=muf[:, 0:8], in_=w["l1f_mur"].rearrange("(cc p) -> p cc", p=128),
                        allow_slow_non_contiguous=True),
            q.dma_start(out=muf[:, 8:16], in_=w["l1f_muk"].rearrange("(cc p) -> p cc", p=128),
                        allow_slow_non_contiguous=True)])

        def f_mu(q):
            q.tensor_copy(mur[:, 0:8], muf[:, 0:8])
            q.tensor_scalar(mur[:, 8:16], muf[:, 0:8], -1.0, 1.0, ALU.mult, ALU.add)
            q.tensor_copy(mur[:, 16:24], muf[:, 8:16])
            return q.tensor_scalar(mur[:, 24:32], muf[:, 8:16], -1.0, 1.0, ALU.mult, ALU.add)
        t_muc = sc.run("v", f_mu, deps=[t_mu])

        yfm = lsb("yfm", [128, 8 * (S + 1)], BF16)
        xt = lsb("xt", [128, 2 * DM])
        yb = lsb("yb", [128, DM], BF16)
        xr = lsb("xr", [128, 8 * TW], BF16)
        xk = lsb("xk", [128, 8 * TW], BF16)
        t1 = lsb("t1", [128, 2 * TW])
        rr = lsb("rr", [128, 8 * TW], BF16)
        ka = lsb("ka", [128, 8 * TW], BF16)
        of = lsb("of", [128, 8 * TW], BF16)
        xo = lsb("xo", [128, 2 * DM])
        pa = lps("pa", [128, 2 * DM])
        pt = lps("pt", [128, DM], BF16)

        slot = {}
        for b in range(BC):
            base = b * S
            t_z = sc.run("g", lambda q: q.memset(
                yfm[:, :].rearrange("p (cc t) -> p cc t", t=S + 1)[:, :, 0:1], 0.0),
                deps=[slot.get("yfm_c")])

            tr_last = None
            for ts_ in range(16):
                par = ts_ % 2
                xv = xt[:, par * DM:(par + 1) * DM]
                t_x = sc.dma("q", "qd", lambda q, ts_=ts_, xv=xv: [
                    q.dma_start(out=xv, in_=src[base + ts_ * 128:base + (ts_ + 1) * 128, :])],
                    deps=[slot.get(("x", par))])
                t_ln = emit_ln(sc, G, xv, [yb[:, :]], 1e-5,
                               deps=[t_x, slot.get("yb_c")])
                slot[("x", par)] = t_ln

                def f_tr(q):
                    ins = None
                    for cc in range(8):
                        ins = q.transpose(pt[:, cc * 128:(cc + 1) * 128],
                                          yb[:, cc * 128:(cc + 1) * 128],
                                          G["ident_b"][:, :])
                    return ins
                t_tr = sc.run("t", f_tr, deps=[t_ln, slot.get("pt_ev")])
                slot["yb_c"] = t_tr

                def f_te(q, ts_=ts_):
                    ov = yfm[:, :].rearrange("p (cc t) -> p cc t", t=S + 1)[
                        :, :, 1 + ts_ * 128:1 + (ts_ + 1) * 128]
                    return q.tensor_copy(ov, pt[:, :].rearrange("p (cc t) -> p cc t", t=128))
                t_te = sc.run("v", f_te, deps=[t_tr, t_z])
                slot["pt_ev"] = t_te
                tr_last = t_te

            out_done = []
            for hh in range(2):
                yv = yfm[:, :].rearrange("p (cc t) -> p cc t", t=S + 1)
                t_xr = None
                for cc in range(8):
                    ys = yv[:, cc, 1 + hh * TW:1 + (hh + 1) * TW]
                    yp = yv[:, cc, hh * TW:(hh + 1) * TW]

                    def f_a(q, ys=ys, cc=cc):
                        q.tensor_scalar(t1[:, 0:TW], ys, mur[:, cc:cc + 1],
                                        None, ALU.mult)
                        return q.tensor_scalar(t1[:, TW:2 * TW], ys,
                                               mur[:, 16 + cc:16 + cc + 1],
                                               None, ALU.mult)
                    t_a = sc.run("v", f_a,
                                 deps=[tr_last, t_muc, slot.get("xr_c")] if cc == 0
                                 else [t_a0])

                    def f_b(q, yp=yp, cc=cc):
                        q.scalar_tensor_tensor(xr[:, cc * TW:(cc + 1) * TW], yp,
                                               mur[:, 8 + cc:8 + cc + 1], t1[:, 0:TW],
                                               ALU.mult, ALU.add)
                        return q.scalar_tensor_tensor(xk[:, cc * TW:(cc + 1) * TW], yp,
                                                      mur[:, 24 + cc:24 + cc + 1],
                                                      t1[:, TW:2 * TW],
                                                      ALU.mult, ALU.add)
                    t_xr = sc.run("v", f_b, deps=[t_a])
                    t_a0 = t_xr

                rk_done = []
                for (xin_, wnm, obuf, act) in [(xr, "l1f_WrT", rr, "sig"),
                                               (xk, "l1f_WkT", ka, "relu2")]:
                    for dd in range(8):
                        par = dd % 2
                        pv = pa[:, par * DM:(par + 1) * DM]

                        def f_mm(q, xin_=xin_, wnm=wnm, dd=dd, pv=pv):
                            ins = None
                            for cc in range(8):
                                for nc2 in range(2):
                                    ins = q.matmul(
                                        pv[:, nc2 * 512:(nc2 + 1) * 512],
                                        wb[wnm][:, cc * DM + dd * 128:cc * DM + (dd + 1) * 128],
                                        xin_[:, cc * TW + nc2 * 512:cc * TW + (nc2 + 1) * 512],
                                        start=(cc == 0), stop=(cc == 7))
                            return ins
                        t_mm = sc.run("t", f_mm,
                                      deps=[t_xr] + t_wc_all + [slot.get(("pa", par))])

                        if act == "sig":
                            t_ev = sc.run("s", lambda q, dd=dd, pv=pv: q.activation(
                                rr[:, dd * TW:(dd + 1) * TW], pv, AF.Sigmoid),
                                deps=[t_mm, slot.get("rr_c")])
                        else:
                            t_rl = sc.run("s", lambda q, dd=dd, pv=pv: q.activation(
                                G["scr_b"][:, :], pv, AF.Relu),
                                deps=[t_mm, slot.get("ka_c")])
                            t_ev = sc.run("s", lambda q, dd=dd: q.activation(
                                ka[:, dd * TW:(dd + 1) * TW], G["scr_b"][:, :],
                                AF.Square), deps=[t_rl], serialize=True)
                        slot[("pa", par)] = t_ev
                        rk_done.append(t_ev)

                slot["xr_c"] = rk_done[-1]

                o_done = []
                for dd in range(8):
                    par = dd % 2
                    pv = pa[:, par * DM:(par + 1) * DM]

                    def f_vm(q, dd=dd, pv=pv):
                        ins = None
                        for cc in range(8):
                            for nc2 in range(2):
                                ins = q.matmul(
                                    pv[:, nc2 * 512:(nc2 + 1) * 512],
                                    wb["l1f_WvT"][:, cc * DM + dd * 128:cc * DM + (dd + 1) * 128],
                                    ka[:, cc * TW + nc2 * 512:cc * TW + (nc2 + 1) * 512],
                                    start=(cc == 0), stop=(cc == 7))
                        return ins
                    t_vm = sc.run("t", f_vm, deps=rk_done + [slot.get(("pa", par))])

                    t_om = sc.run("v", lambda q, dd=dd, pv=pv: q.tensor_mul(
                        of[:, dd * TW:(dd + 1) * TW], rr[:, dd * TW:(dd + 1) * TW], pv),
                        deps=[t_vm, slot.get("of_c")])
                    slot[("pa", par)] = t_om
                    o_done.append(t_om)
                slot["rr_c"] = o_done[-1]
                slot["ka_c"] = o_done[-1]

                for ts2 in range(8):
                    gt = hh * 8 + ts2
                    par = gt % 2
                    xv = xt[:, par * DM:(par + 1) * DM]
                    xov = xo[:, par * DM:(par + 1) * DM]
                    t_x2 = sc.dma("q", "qd", lambda q, gt=gt, xv=xv: [
                        q.dma_start(out=xv,
                                    in_=src[base + gt * 128:base + (gt + 1) * 128, :])],
                        deps=[slot.get(("x", par))])

                    def f_tb(q, ts2=ts2):
                        ins = None
                        for dd in range(8):
                            ins = q.transpose(
                                pt[:, dd * 128:(dd + 1) * 128],
                                of[:, dd * TW + ts2 * 128:dd * TW + (ts2 + 1) * 128],
                                G["ident_b"][:, :])
                        return ins
                    t_tb = sc.run("t", f_tb, deps=o_done + [slot.get("pt_ev")])

                    t_rs = sc.run("v", lambda q, xv=xv, xov=xov: q.scalar_tensor_tensor(
                        xov, pt[:, :], 0.5, xv, ALU.mult, ALU.add),
                        deps=[t_tb, t_x2, slot.get(("xo", par))])
                    slot["pt_ev"] = t_rs
                    slot[("x", par)] = t_rs
                    if ts2 == 7:
                        slot["of_c"] = t_rs

                    t_xs = sc.dma("s", "sd", lambda q, gt=gt, xov=xov: [
                        q.dma_start(out=dst[base + gt * 128:base + (gt + 1) * 128, :],
                                    in_=xov)], deps=[t_rs])
                    slot[("xo", par)] = t_xs
                    out_done.append(t_xs)
            slot["yfm_c"] = out_done[-1]

    sc.barrier()



def emit_l1attn(sc, G, src, dst):
    nc, w = G["nc"], G["w"]
    tag = G["tag"]
    NH, HD1 = 8, 128

    with ExitStack() as cx:
        def lsb(name, shape, dt=F32):
            return cx.enter_context(nc.sbuf_tensor(f"{tag}_{name}", shape, dt))

        def lps(name, shape, dt=F32):
            return cx.enter_context(nc.psum_tensor(f"{tag}_{name}", shape, dt))

        yfm = lsb("yfm", [128, 8 * S], BF16)
        outall = lsb("outall", [128, 16 * DM], BF16)
        wsl = lsb("wsl", [128, 8 * DM], BF16)
        wf = lsb("wf", [128, DM])
        xt = lsb("xt", [128, DM])
        yb = lsb("yb", [128, DM], BF16)
        rtm = lsb("rtm", [128, 2 * DM], BF16)
        qro = lsb("qro", [128, 2 * DM], BF16)
        l1tb = lsb("l1tb", [128, 4 * DM], BF16)
        qTs = lsb("qTs", [128, 2048], BF16)
        kTs = lsb("kTs", [128, 2048], BF16)
        vh = lsb("vh", [128, 2048], BF16)
        ptsb = lsb("ptsb", [128, 2 * 1024], BF16)
        osb = lsb("osb", [128, 1024], BF16)
        denr = lsb("denr", [1, 1024])
        denc = lsb("denc", [128, 8])
        rdn = lsb("rdn", [128, 8])
        pq = lps("pq", [128, 2048])
        pk = lps("pk", [128, 2048])

        t_tb = sc.dma("q", "qd", lambda q: [
            q.dma_start(out=l1tb[:, 0:DM], in_=w["l1_cA"][0:128, :]),
            q.dma_start(out=l1tb[:, DM:2 * DM], in_=w["l1_cA"][128:256, :]),
            q.dma_start(out=l1tb[:, 2 * DM:3 * DM], in_=w["l1_cB"][0:128, :]),
            q.dma_start(out=l1tb[:, 3 * DM:4 * DM], in_=w["l1_cB"][128:256, :])])

        slot = {}
        for b in range(BC):
            base = b * S

            def load_w(nm, dep_consumed):
                toks = []
                for cc in range(8):
                    t_l = sc.dma("q", "qd", lambda q, nm=nm, cc=cc: [
                        q.dma_start(out=wf[:, :],
                                    in_=w[nm][cc * 128:(cc + 1) * 128, :])],
                        deps=[dep_consumed, slot.get("wf_c")])
                    t_c = sc.run("g", lambda q, cc=cc: q.tensor_copy(
                        wsl[:, cc * DM:(cc + 1) * DM], wf[:, :]), deps=[t_l])
                    slot["wf_c"] = t_c
                    toks.append(t_c)
                return toks

            for ts_ in range(16):
                t_x = sc.dma("q", "qd", lambda q, ts_=ts_: [
                    q.dma_start(out=xt[:, :],
                                in_=src[base + ts_ * 128:base + (ts_ + 1) * 128, :])],
                    deps=[slot.get("x_c")])
                t_ln = emit_ln(sc, G, xt[:, :], [yb[:, :]], 1e-5,
                               deps=[t_x, slot.get("yb_c")])
                slot["x_c"] = t_ln

                tp = pk[:, 0:512].bitcast(BF16)

                def f_tr(q, tp=tp):
                    ins = None
                    for cc in range(8):
                        ins = q.transpose(tp[:, cc * 128:(cc + 1) * 128],
                                          yb[:, cc * 128:(cc + 1) * 128],
                                          G["ident_b"][:, :])
                    return ins
                t_tr = sc.run("t", f_tr, deps=[t_ln, slot.get("tp_ev")])
                slot["yb_c"] = t_tr

                def f_te(q, ts_=ts_, tp=tp):
                    ov = yfm[:, :].rearrange("p (cc t) -> p cc t", t=S)[
                        :, :, ts_ * 128:(ts_ + 1) * 128]
                    return q.tensor_copy(ov, tp.rearrange("p (cc t) -> p cc t", t=128))
                t_te = sc.run("v", f_te, deps=[t_tr, slot.get("yfm_c")])
                slot["tp_ev"] = t_te
            t_yfm = ("v", sc.count["v"])

            for wi, (wnm, qi, rope) in enumerate([("l1a_W1T", 0, True),
                                                  ("l1a_W2T", 1, True),
                                                  ("l1a_W3T", 2, False)]):
                wtoks = load_w(wnm, slot.get("wsl_c"))
                for ts_ in range(16):
                    par = ts_ % 2
                    pv = pq[:, par * 1024:(par + 1) * 1024]
                    qv = qro[:, par * DM:(par + 1) * DM]

                    def f_mm(q, ts_=ts_, pv=pv):
                        ins = None
                        for cc in range(8):
                            lh = yfm[:, cc * S + ts_ * 128:cc * S + (ts_ + 1) * 128]
                            for n2 in range(2):
                                ins = q.matmul(
                                    pv[:, n2 * 512:(n2 + 1) * 512], lh,
                                    wsl[:, cc * DM + n2 * 512:cc * DM + (n2 + 1) * 512],
                                    start=(cc == 0), stop=(cc == 7))
                        return ins
                    t_mm = sc.run("t", f_mm,
                                  deps=[t_yfm] + wtoks + [slot.get(("pp", par))])

                    if rope:
                        tb = ts_ % 2
                        cav = l1tb[:, tb * DM:(tb + 1) * DM]
                        cbv = l1tb[:, (2 + tb) * DM:(3 + tb) * DM]

                        def f_r1(q, pv=pv, cav=cav, cbv=cbv):
                            q.tensor_mul(rtm[:, 0:DM], pv, cav)
                            pvv = pv.rearrange("p (a c) -> p a c", c=2)
                            bvv = cbv.rearrange("p (a c) -> p a c", c=2)
                            mvv = rtm[:, DM:2 * DM].rearrange("p (a c) -> p a c", c=2)
                            q.tensor_mul(mvv[:, :, 0:1], pvv[:, :, 1:2], bvv[:, :, 0:1])
                            return q.tensor_mul(mvv[:, :, 1:2], pvv[:, :, 0:1],
                                                bvv[:, :, 1:2])
                        t_r1 = sc.run("v", f_r1, deps=[t_mm, t_tb, slot.get("rtm_c")])
                        t_r2 = sc.run("v", lambda q, qv=qv: q.tensor_add(
                            qv, rtm[:, 0:DM], rtm[:, DM:2 * DM]),
                            deps=[t_r1, slot.get(("qro", par))])
                        slot["rtm_c"] = t_r2
                        slot[("pp", par)] = t_r1
                        t_out = t_r2
                    else:
                        t_out = sc.run("v", lambda q, pv=pv, qv=qv: q.tensor_copy(qv, pv),
                                       deps=[t_mm, slot.get(("qro", par))])
                        slot[("pp", par)] = t_out

                    t_sto = sc.dma("s", "sd", lambda q, ts_=ts_, qi=qi, qv=qv: [
                        q.dma_start(out=G["qkv"][qi, ts_ * 128:(ts_ + 1) * 128, :],
                                    in_=qv)], deps=[t_out])
                    slot[("qro", par)] = t_sto
                slot["wsl_c"] = t_mm
            qkv_done = ("sd", sc.count["sd"])

            wo_toks = load_w("l1a_WoT", slot.get("wsl_c"))

            heads_done = None
            for h in range(NH):
                t_qT = sc.dma("q", "qd", lambda q, h=h: [
                    q.dma_start_transpose(
                        out=qTs[:, :],
                        in_=G["qkv"][0, 256 * h:256 * (h + 1), :]
                        .rearrange("u (w d) -> (u w) d", d=128))],
                    deps=[qkv_done, slot.get("qT_c")])
                t_kT = sc.dma("q", "qd", lambda q, h=h: [
                    q.dma_start_transpose(
                        out=kTs[:, :],
                        in_=G["qkv"][1, 256 * h:256 * (h + 1), :]
                        .rearrange("u (w d) -> (u w) d", d=128))],
                    deps=[qkv_done, slot.get("kT_c")])
                t_v = sc.dma("q", "qd", lambda q, h=h: [
                    q.dma_start(out=vh[:, :].rearrange("p (tt d) -> p tt d", tt=16),
                                in_=G["qkv"][2, 256 * h:256 * (h + 1), :]
                                .rearrange("(tt u2) (w d) -> (u2 w) tt d",
                                           tt=16, u2=16, w=8))],
                    deps=[qkv_done, slot.get("vh_c")])

                for qh in range(2):
                    vv_last = None
                    ex_tok = {}
                    for step in range(17):
                        if step < 16:
                            kt = step
                            par = kt % 2
                            stp = pq[:, par * 1024:(par + 1) * 1024]

                            def f_st(q, kt=kt, qh=qh, stp=stp):
                                lh = kTs[:, kt * 128:(kt + 1) * 128]
                                ins = None
                                for n2 in range(2):
                                    ins = q.matmul(
                                        stp[:, n2 * 512:(n2 + 1) * 512], lh,
                                        qTs[:, qh * 1024 + n2 * 512:qh * 1024 + (n2 + 1) * 512],
                                        start=True, stop=True)
                                return ins
                            dep_st = [t_qT, t_kT, slot.get(("stB", par))]
                            if kt < 2 and qh == 0 and h == 0:
                                dep_st += [slot.get("pq_free"), slot.get("tp_ev"),
                                           slot.get(("pp", 0)), slot.get(("pp", 1))]
                            t_st = sc.run("t", f_st, deps=dep_st)

                            t_ex = sc.run("s", lambda q, stp=stp, par=par: q.activation(
                                ptsb[:, par * 1024:(par + 1) * 1024], stp, AF.Exp,
                                bias=G["c20"][:, :], scale=float(1.0 / math.sqrt(8.0))),
                                deps=[t_st, slot.get(("ptsb", par))])
                            slot[("stB", par)] = t_ex
                            ex_tok[kt] = t_ex
                        if step >= 1:
                            kt = step - 1
                            par = kt % 2

                            def f_dn(q, kt=kt, par=par):
                                ins = None
                                for n2 in range(2):
                                    ins = q.matmul(
                                        pk[0:1, 1024 + n2 * 512:1024 + (n2 + 1) * 512],
                                        G["onec"][:, :],
                                        ptsb[:, par * 1024 + n2 * 512:par * 1024 + (n2 + 1) * 512],
                                        start=(kt == 0), stop=(kt == 15))
                                return ins
                            t_dn = sc.run("t", f_dn, deps=[ex_tok[kt], slot.get("dn_c")])

                            def f_vv(q, kt=kt, par=par):
                                ins = None
                                for n2 in range(2):
                                    ins = q.matmul(
                                        pk[:, n2 * 512:(n2 + 1) * 512],
                                        vh[:, kt * 128:(kt + 1) * 128],
                                        ptsb[:, par * 1024 + n2 * 512:par * 1024 + (n2 + 1) * 512],
                                        start=(kt == 0), stop=(kt == 15))
                                return ins
                            t_vv = sc.run("t", f_vv, deps=[t_v, ex_tok[kt], slot.get("ot_c")])
                            slot[("ptsb", par)] = t_vv
                            vv_last = t_vv

                    t_dc = sc.run("v", lambda q: q.tensor_copy(denr[:, :],
                                                               pk[0:1, 1024:2048]),
                                  deps=[t_dn, slot.get("denr_c")])
                    slot["dn_c"] = t_dc

                    def f_dt(q):
                        ins = None
                        for j in range(8):
                            ins = q.transpose(pq[:, 1536 + j:1536 + j + 1],
                                              denr[:, j * 128:(j + 1) * 128],
                                              G["ident_f"][0:1, 0:1])
                        return ins
                    t_dt = sc.run("t", f_dt, deps=[t_dc, vv_last])
                    t_dc2 = sc.run("v", lambda q: q.tensor_copy(denc[:, :],
                                                                pq[:, 1536:1544]),
                                   deps=[t_dt])
                    t_rd = sc.run("v", lambda q: q.reciprocal(rdn[:, :], denc[:, :]),
                                  deps=[t_dc2])
                    slot["denr_c"] = t_rd

                    t_oe = sc.run("v", lambda q: q.tensor_copy(osb[:, :],
                                                               pk[:, 0:1024]),
                                  deps=[vv_last, slot.get("osb_c")])
                    slot["ot_c"] = t_oe
                    if G.get("debug") and b == BC - 1 and h == 0 and qh == 0:
                        t_oe = sc.dma("g", "sd", lambda q: [
                            q.dma_start(out=G["osb_dbg"][:, :], in_=osb[:, :]),
                            q.dma_start(out=G["pt_dbg"][:, :], in_=ptsb[:, :])],
                            deps=[t_oe])
                        slot["ot_c"] = t_oe

                    tr_last = None
                    for grp in range(2):
                        base_c = 1024 if grp == 0 else 1792
                        tpo = pq[:, base_c:base_c + 256].bitcast(BF16)

                        def f_ot(q, grp=grp, tpo=tpo):
                            ins = None
                            for j in range(4):
                                i = grp * 4 + j
                                ins = q.transpose(tpo[:, j * 128:(j + 1) * 128],
                                                  osb[:, i * 128:(i + 1) * 128],
                                                  G["ident_b"][:, :])
                            return ins
                        t_ot = sc.run("t", f_ot, deps=[t_oe, slot.get(("otT", grp)),
                                                       t_dt, t_dc2])

                        def f_os(q, grp=grp, tpo=tpo, h=h, qh=qh):
                            ins = None
                            for j in range(4):
                                i = grp * 4 + j
                                tile = qh * 8 + i
                                ins = q.tensor_scalar(
                                    outall[:, tile * DM + h * HD1:tile * DM + (h + 1) * HD1],
                                    tpo[:, j * 128:(j + 1) * 128],
                                    rdn[:, i:i + 1], None, ALU.mult)
                            return ins
                        t_os = sc.run("v", f_os, deps=[t_ot, t_rd, slot.get("outall_c")])
                        slot[("otT", grp)] = t_os
                        tr_last = t_os
                    slot["osb_c"] = tr_last
                    slot[("stB", 1)] = tr_last
                    heads_done = tr_last
                slot["qT_c"] = vv_last
                slot["kT_c"] = vv_last
                slot["vh_c"] = vv_last
            slot["pq_free"] = heads_done

            if G.get("debug") and b == BC - 1:
                heads_done = sc.dma("g", "sd", lambda q: [
                    q.dma_start(out=G["oa_dbg"][:, :], in_=outall[:, :])],
                    deps=[heads_done])
            for sp in range(16):
                half = sp // 8
                spc = sp % 8
                par = sp % 2
                fp = pk[:, par * 1024:(par + 1) * 1024]

                def f_f(q, half=half, spc=spc, fp=fp):
                    ins = None
                    for jc in range(8):
                        tile = half * 8 + jc
                        for n2 in range(2):
                            ins = q.matmul(
                                fp[:, n2 * 512:(n2 + 1) * 512],
                                wsl[:, jc * DM + spc * 128:jc * DM + (spc + 1) * 128],
                                outall[:, tile * DM + n2 * 512:tile * DM + (n2 + 1) * 512],
                                start=(jc == 0), stop=(jc == 7))
                    return ins
                dep_f = [heads_done] + wo_toks + [slot.get(("fp", par))]
                if sp < 2:
                    dep_f += [slot.get("ot_c"), slot.get("dn_c")]
                t_f = sc.run("t", f_f, deps=dep_f)

                t_x2 = sc.dma("q", "qd", lambda q, sp=sp: [
                    q.dma_start(out=xt[:, :],
                                in_=src[base + sp * 128:base + (sp + 1) * 128, :])],
                    deps=[slot.get("x_c")])

                t_rs = sc.run("v", lambda q, fp=fp: q.tensor_tensor(
                    xt[:, :], fp, xt[:, :], ALU.add),
                    deps=[t_f, t_x2])
                slot[("fp", par)] = t_rs

                t_st2 = sc.dma("s", "sd", lambda q, sp=sp: [
                    q.dma_start(out=dst[base + sp * 128:base + (sp + 1) * 128, :],
                                in_=xt[:, :])], deps=[t_rs])
                slot["x_c"] = t_st2
            slot["yfm_c"] = slot.get("wsl_c")
            slot["outall_c"] = t_f
            slot["wsl_c"] = t_f
            slot["tp_ev"] = slot.get("tp_ev")

    sc.barrier()


def build_program(phases=(0, 1, 2, 3, 4, 5), debug=False):
    nc = bass.Bass("TRN2", target_bir_lowering=False)
    ap = lambda h: h.ap()

    xin = ap(nc.dram_tensor("xin", [T, DM], F32, kind="ExternalInput"))
    xout = ap(nc.dram_tensor("xout", [T, DM], F32, kind="ExternalOutput"))
    w = {}
    for name, shape in [
        ("l0f_Wr", [DM, 4]), ("l0f_W1", [NE, DM, I_FF]), ("l0f_b1", [NE, I_FF]),
        ("l0f_Wo", [I_FF, DM]), ("l0f_bo", [DM]), ("l0f_g", [DM]), ("l0f_b", [DM]),
        ("l0a_Wq", [DM, DM]), ("l0a_Wk", [DM, DM]), ("l0a_Wv", [DM, DM]),
        ("l0a_Wg", [DM, DM]),
        ("l1a_W1T", [DM, DM]), ("l1a_W2T", [DM, DM]), ("l1a_W3T", [DM, DM]),
        ("l1a_WoT", [DM, DM]),
        ("l1f_WrT", [DM, DM]), ("l1f_WkT", [DM, DM]), ("l1f_WvT", [DM, DM]),
        ("l1f_mur", [DM]), ("l1f_muk", [DM]),
    ]:
        w[name] = ap(nc.dram_tensor(name, shape, F32, kind="ExternalInput"))
    for name, arr in TABLES.items():
        w[name] = ap(nc.dram_tensor(name, list(arr.shape), BF16, kind="ExternalInput"))

    dk = "ExternalOutput" if debug else "Internal"
    xs = ap(nc.dram_tensor("xs", [T, DM], F32, kind="Internal"))
    yfm_hbm = ap(nc.dram_tensor("yfm_hbm", [NTT, 8, 128, 128], BF16, kind=dk))
    interm_hbm = ap(nc.dram_tensor("interm_hbm", [8, NTT, 128, 512], BF16,
                                   kind=dk))
    qkv_hbm = ap(nc.dram_tensor("qkv_hbm", [3, S, DM], BF16, kind=dk))
    masks_dbg = ap(nc.dram_tensor("masks_dbg", [128, NTT * 4], F32, kind=dk)) if debug else None
    lgs_dbg = ap(nc.dram_tensor("lgs_dbg", [NTT, 128, 4], F32, kind=dk)) if debug else None
    zsb_dbg = ap(nc.dram_tensor("zsb_dbg", [NTT, 128, DM], F32, kind=dk)) if debug else None
    oa_dbg = ap(nc.dram_tensor("oa_dbg", [128, 16 * DM], F32, kind=dk)) if debug else None
    osb_dbg = ap(nc.dram_tensor("osb_dbg", [128, 1024], F32, kind=dk)) if debug else None
    pt_dbg = ap(nc.dram_tensor("pt_dbg", [128, 2048], F32, kind=dk)) if debug else None
    yb_dbg = ap(nc.dram_tensor("yb_dbg", [4, 128, DM], F32, kind=dk)) if debug else None
    y2_dbg = ap(nc.dram_tensor("y2_dbg", [NTT, 128, DM], F32, kind=dk)) if debug else None

    phases = sorted(phases)

    with ExitStack() as ctx:
        block = ctx.enter_context(nc.Block())
        sc = Sched(nc, ctx, block)

        G = {"nc": nc, "w": w, "yfm": yfm_hbm, "interm": interm_hbm, "qkv": qkv_hbm,
             "lgs_dbg": lgs_dbg, "zsb_dbg": zsb_dbg, "y2_dbg": y2_dbg,
             "yb_dbg": yb_dbg, "oa_dbg": oa_dbg, "osb_dbg": osb_dbg,
             "pt_dbg": pt_dbg, "debug": debug}
        for nm, shape, dt in [
            ("ident_b", [128, 128], BF16), ("ident_f", [128, 128], F32),
            ("ones1", [1, 128], BF16), ("zrow", [1, 512], BF16),
            ("onec", [128, 1], BF16), ("c20", [128, 1], F32),
            ("masks", [128, NTT * 4], F32),
            ("tril_s", [128, 128], BF16),
            ("trilT_s", [128, 128], BF16), ("stat", [128, 8], F32),
            ("scr_b", [128, DM], BF16),
        ]:
            G[nm.replace("_s", "") if nm.endswith("_s") else nm] = ctx.enter_context(nc.sbuf_tensor(nm, shape, dt))

        t_tri = sc.dma("q", "qd", lambda q: [
            q.dma_start(out=G["tril"][:, :], in_=w["tril"][:, :]),
            q.dma_start(out=G["trilT"][:, :], in_=w["trilT"][:, :])])

        def f_init(q):
            q.memset(G["ones1"][:, :], 1.0)
            q.memset(G["zrow"][:, :], 0.0)
            q.memset(G["onec"][:, :], 1.0)
            q.memset(G["c20"][:, :], -20.0)
            q.memset(G["ident_b"][:, :], 0.0)
            q.affine_select(out=G["ident_b"][:, :], in_=G["ident_b"][:, :],
                            compare_op=ALU.not_equal, fill=1.0, base=0,
                            pattern=[[-1, 128]], channel_multiplier=1)
            q.memset(G["ident_f"][:, :], 0.0)
            return q.affine_select(out=G["ident_f"][:, :], in_=G["ident_f"][:, :],
                                   compare_op=ALU.not_equal, fill=1.0, base=0,
                                   pattern=[[-1, 128]], channel_multiplier=1)
        sc.run("g", f_init, deps=[t_tri])

        sc.barrier()

        emitters = {0: emit_l0ffn, 1: emit_l0attn, 2: emit_l0ffn,
                    3: emit_l1ffn, 4: emit_l1attn, 5: emit_l1ffn}
        PHASE_MARKS.clear()
        for ph in phases:
            src = xin if ph == phases[0] else xs
            dst = xout if ph == phases[-1] else xs
            G["tag"] = f"p{ph}"
            PHASE_MARKS.append((ph, nc.next_id()))
            emitters[ph](sc, G, src, dst)
        PHASE_MARKS.append((-1, nc.next_id()))

        if debug:
            sc.dma("s", "sd", lambda q: [
                q.dma_start(out=masks_dbg[:, :], in_=G["masks"][:, :])])

        @block.sync
        def _(q):
            q.wait_ge(sc.sem["sd"], sc.count["sd"])
            q.wait_ge(sc.sem["qd"], sc.count["qd"])
    return nc


def make_inmap(core, inputs):
    x = np.asarray(inputs["L0_attn_x"])
    xsh = np.ascontiguousarray(x.reshape(32, S, DM)[core * BC:(core + 1) * BC]
                               .reshape(T, DM))
    m = {"xin": xsh,
         "l0f_Wr": inputs["L0_ffn_W_router"], "l0f_W1": inputs["L0_ffn_W1"],
         "l0f_b1": inputs["L0_ffn_b1"], "l0f_Wo": inputs["L0_ffn_Wo"],
         "l0f_bo": inputs["L0_ffn_bo"], "l0f_g": inputs["L0_ffn_ln_gamma"],
         "l0f_b": inputs["L0_ffn_ln_beta"],
         "l0a_Wq": inputs["L0_attn_Wq"], "l0a_Wk": inputs["L0_attn_Wk"],
         "l0a_Wv": inputs["L0_attn_Wv"], "l0a_Wg": inputs["L0_attn_Wg"],
         "l1a_W1T": np.asarray(inputs["L1_attn_W1"]).T,
         "l1a_W2T": np.asarray(inputs["L1_attn_W2"]).T,
         "l1a_W3T": np.asarray(inputs["L1_attn_W3"]).T,
         "l1a_WoT": np.asarray(inputs["L1_attn_Wo"]).T,
         "l1f_WrT": np.asarray(inputs["L1_ffn_W_r"]).T,
         "l1f_WkT": np.asarray(inputs["L1_ffn_W_k"]).T,
         "l1f_WvT": np.asarray(inputs["L1_ffn_W_v"]).T,
         "l1f_mur": inputs["L1_ffn_mu_r"], "l1f_muk": inputs["L1_ffn_mu_k"]}
    for k, v in TABLES.items():
        m[k] = v
    return {k: np.ascontiguousarray(np.asarray(v)) for k, v in m.items()}


def kernel(**inputs):
    nc = build_program(phases=(0, 1, 2, 3, 4, 5))
    in_maps = [make_inmap(core, inputs) for core in range(8)]
    res = run_bass_kernel_spmd(nc, in_maps, core_ids=list(range(8)))
    out = np.concatenate([np.asarray(r["xout"]).reshape(BC, S, DM)
                          for r in res.results], axis=0)
    return out.astype(np.float32)


# version 05 (final)
# speedup vs baseline: 1.2094x; 1.0048x over previous
import math
from contextlib import ExitStack

import numpy as np
import ml_dtypes
import concourse.bass as bass
import concourse.mybir as mybir
from concourse.bass_utils import run_bass_kernel_spmd

F32 = mybir.dt.float32
BF16 = mybir.dt.bfloat16
AF = mybir.ActivationFunctionType
ALU = mybir.AluOpType
AX = mybir.AxisListType

BC = 4
S = 2048
T = BC * S
DM = 1024
NTT = T // 128
I_FF = 4096
NE = 4

BF = ml_dtypes.bfloat16


def host_tables():
    t = {}
    H, D = 16, 64
    a = 1.0 / 10000.0 ** np.linspace(0.0, 1.0, D // 2)
    angle = np.repeat(a, 2)
    decay = np.log(1.0 - 2.0 ** (-5.0 - np.arange(H, dtype=np.float64)))
    idx = np.arange(S, dtype=np.float64)
    sin = np.sin(idx[:, None] * angle[None, :])
    cos = np.cos(idx[:, None] * angle[None, :])
    sinp = sin.copy()
    sinp[:, 0::2] *= -1.0
    qdec = np.exp(idx[:, None] * decay[None, :])
    ed = np.exp(decay)[None, :]
    rowsum = (1.0 - np.exp((idx[:, None] + 1.0) * decay[None, :])) / (1.0 - ed)
    qdec = qdec / np.sqrt(rowsum)
    kdec = np.exp(-idx[:, None] * decay[None, :]) * (DM ** -0.5)
    cosT = np.tile(cos, (1, H))
    sinT = np.tile(sinp, (1, H))
    qdE = np.repeat(qdec, D, axis=1)
    kdE = np.repeat(kdec, D, axis=1)
    t["l0_qA"] = (cosT * qdE).astype(BF)
    t["l0_qB"] = (sinT * qdE).astype(BF)
    t["l0_kA"] = (cosT * kdE).astype(BF)
    t["l0_kB"] = (sinT * kdE).astype(BF)
    D1 = 128
    inv = 1.0 / 10000.0 ** (np.arange(0, D1, 2, dtype=np.float64) / D1)
    tt = np.arange(S, dtype=np.float64)
    freqs = np.repeat(tt[:, None] * inv[None, :], 2, axis=-1)
    c1 = np.cos(freqs)
    s1 = np.sin(freqs)
    s1[:, 0::2] *= -1.0
    t["l1_cA"] = np.ascontiguousarray(c1.reshape(256, 1024)).astype(BF)
    t["l1_cB"] = np.ascontiguousarray(s1.reshape(256, 1024)).astype(BF)
    tri = np.tril(np.ones((128, 128), np.float32))
    t["tril"] = tri.astype(BF)
    t["trilT"] = np.ascontiguousarray(tri.T).astype(BF)
    return t


TABLES = host_tables()
PHASE_MARKS = []

_SECTION = {"t": "tensor", "v": "vector", "s": "scalar", "g": "gpsimd", "q": "sync"}
_OWN_SEM = {"t": "t", "v": "v", "s": "s", "g": "g", "q": "q"}


class Sched:
    def __init__(self, nc, ctx, block):
        self.nc = nc
        self.block = block
        self.sem = {}
        self.count = {}
        self.waited = {}
        self.pending_barrier = {}
        for name in ["t", "v", "s", "g", "q", "qd", "sd"]:
            self.sem[name] = ctx.enter_context(nc.semaphore("sem_" + name))
            self.count[name] = 0

    def barrier(self):
        toks = [(n, self.count[n]) for n in self.sem if self.count[n] > 0]
        for eng in _SECTION:
            self.pending_barrier[eng] = toks

    def _emit_waits(self, q, eng, deps):
        alldeps = list(deps)
        if eng in self.pending_barrier:
            alldeps += self.pending_barrier.pop(eng)
        own = _OWN_SEM[eng]
        best = {}
        for d in alldeps:
            if d is None:
                continue
            sname, val = d
            if val <= 0:
                continue
            if sname == own:
                val = min(val, self.count[own])
            if val <= 0:
                continue
            best[sname] = max(best.get(sname, 0), val)
        for sname, val in best.items():
            key = (eng, sname)
            if self.waited.get(key, 0) >= val:
                continue
            q.wait_ge(self.sem[sname], val)
            self.waited[key] = val

    def run(self, eng, fn, deps=(), serialize=False):
        sec = getattr(self.block, _SECTION[eng])
        own = _OWN_SEM[eng]
        prev = self.count[own]

        @sec
        def _(q):
            if serialize and prev > 0:
                q.wait_ge(self.sem[own], prev)
            self._emit_waits(q, eng, deps)
            last = fn(q)
            last.then_inc(self.sem[own], 1)
        self.count[own] += 1
        return (own, self.count[own])

    def dma(self, queue_eng, sem_name, fn, deps=()):
        sec = getattr(self.block, _SECTION[queue_eng])

        @sec
        def _(q):
            self._emit_waits(q, queue_eng, deps)
            for ins in fn(q):
                ins.then_inc(self.sem[sem_name], 16)
                self.count[sem_name] += 16
        return (sem_name, self.count[sem_name])


def emit_ln(sc, G, xv, ybufs, eps, deps):
    stat, scr = G["stat"], G["scr_b"]

    def f1(q):
        q.tensor_scalar(scr[:, :], xv, 0.0, None, ALU.add, ALU.add,
                        accum_out=stat[:, 0:1])
        return q.scalar_tensor_tensor(scr[:, :], xv, 1.0, xv,
                                      ALU.mult, ALU.mult,
                                      accum_out=stat[:, 1:2])
    t1 = sc.run("v", f1, deps=deps)

    def f2(q):
        q.tensor_scalar_mul(stat[:, 2:3], stat[:, 0:1], 1.0 / DM)
        return q.tensor_scalar_mul(stat[:, 3:4], stat[:, 1:2], 1.0 / DM)
    t2 = sc.run("v", f2, deps=[t1])

    t3 = sc.run("s", lambda q: q.activation(stat[:, 4:5], stat[:, 2:3], AF.Square),
                deps=[t2])

    t4 = sc.run("v", lambda q: q.scalar_tensor_tensor(
        stat[:, 3:4], stat[:, 3:4], float(eps), stat[:, 4:5],
        ALU.add, ALU.subtract), deps=[t3, t2])

    t5 = sc.run("s", lambda q: q.activation(stat[:, 4:5], stat[:, 3:4], AF.Sqrt),
                deps=[t4])

    t6 = sc.run("v", lambda q: q.reciprocal(stat[:, 5:6], stat[:, 4:5]), deps=[t5])

    def f7(q):
        last = None
        for buf in ybufs:
            last = q.tensor_scalar(buf, xv, stat[:, 2:3], stat[:, 5:6],
                                   ALU.subtract, ALU.mult)
        return last
    return sc.run("v", f7, deps=[t6])


def emit_l0ffn(sc, G, src, dst):
    nc, w = G["nc"], G["w"]
    tag = G["tag"]

    def scope():
        return ExitStack()

    with scope() as cx:
        def lsb(name, shape, dt=F32):
            return cx.enter_context(nc.sbuf_tensor(f"{tag}_{name}", shape, dt))

        def lps(name, shape, dt=F32):
            return cx.enter_context(nc.psum_tensor(f"{tag}_{name}", shape, dt))

        xt = lsb("xt", [128, 2 * DM])
        yb = lsb("yb", [128, DM], BF16)
        yf32 = lsb("yf32", [128, DM])
        ylo = lsb("ylo", [128, DM], BF16)
        yfm = lsb("yfm", [128, 2 * DM], BF16)
        ylofm = lsb("ylofm", [128, DM], BF16)
        wrf = lsb("wrf", [128, 8 * 4])
        wrhi = lsb("wrhi", [128, 8 * 4], BF16)
        wrlo = lsb("wrlo", [128, 8 * 4], BF16)
        wrlof = lsb("wrlof", [128, 8 * 4])
        lgs = lsb("lgs", [128, 4])
        mtmp = lsb("mtmp", [128, 2])
        tp = lps("tp", [128, 2 * DM], BF16)
        lg = lps("lg", [128, 4])

        t_wr = sc.dma("q", "qd", lambda q: [
            q.dma_start(out=wrf[:, :].rearrange("p (cc e) -> p cc e", cc=8),
                        in_=w["l0f_Wr"].rearrange("(cc p) e -> p cc e", p=128))])

        t_wr1 = sc.run("g", lambda q: q.tensor_copy(wrhi[:, :], wrf[:, :]),
                       deps=[t_wr])
        t_wr2 = sc.run("g", lambda q: q.tensor_sub(wrlof[:, :], wrf[:, :], wrhi[:, :]),
                       deps=[t_wr1])
        t_wrc = sc.run("g", lambda q: q.tensor_copy(wrlo[:, :], wrlof[:, :]),
                       deps=[t_wr2])

        slot = {}
        t_st = None
        for tt in range(NTT):
            par = tt % 2
            xv = xt[:, par * DM:(par + 1) * DM]
            yfmv = yfm[:, par * DM:(par + 1) * DM]

            t_x = sc.dma("q", "qd", lambda q, tt=tt, xv=xv: [
                q.dma_start(out=xv, in_=src[tt * 128:(tt + 1) * 128, :])],
                deps=[slot.get(("x", par))])

            t_ln = emit_ln(sc, G, xv, [yb[:, :], yf32[:, :]], 1e-5,
                           deps=[t_x, slot.get("yb_c")])
            slot[("x", par)] = t_ln

            t_lo = sc.run("v", lambda q: q.tensor_sub(ylo[:, :], yf32[:, :], yb[:, :]),
                          deps=[t_ln])


            def f_tr(q):
                for cc in range(8):
                    q.transpose(tp[:, cc * 128:(cc + 1) * 128],
                                yb[:, cc * 128:(cc + 1) * 128], G["ident_b"][:, :])
                ins = None
                for cc in range(8):
                    ins = q.transpose(tp[:, DM + cc * 128:DM + (cc + 1) * 128],
                                      ylo[:, cc * 128:(cc + 1) * 128], G["ident_b"][:, :])
                return ins
            t_tr = sc.run("t", f_tr, deps=[t_lo, slot.get("tp_ev")])
            slot["yb_c"] = t_tr
            if G.get("debug") and tt == 47:
                t_tap = sc.dma("g", "sd", lambda q, xv=xv: [
                    q.dma_start(out=G["yb_dbg"][0], in_=yb[:, :]),
                    q.dma_start(out=G["yb_dbg"][1], in_=yf32[:, :]),
                    q.dma_start(out=G["yb_dbg"][2], in_=xv),
                    q.dma_start(out=G["yb_dbg"][3][:, 0:8], in_=G["stat"][:, :])], deps=[t_tr])
                slot["yb_c"] = t_tap
                slot[("x", tt % 2)] = t_tap

            def f_ev(q, yfmv=yfmv):
                q.tensor_copy(yfmv, tp[:, 0:DM])
                return q.tensor_copy(ylofm[:, :], tp[:, DM:2 * DM])
            t_ev = sc.run("v", f_ev,
                          deps=[t_tr, slot.get(("yfm", par)), slot.get("ylofm_c")])
            slot["tp_ev"] = t_ev

            def f_rt(q, yfmv=yfmv):
                prods = [(yfmv, wrhi), (yfmv, wrlo), (ylofm[:, :], wrhi)]
                ins = None
                n = 0
                for (lh, rh) in prods:
                    for cc in range(8):
                        n += 1
                        ins = q.matmul(lg[:, :], lh[:, cc * 128:(cc + 1) * 128],
                                       rh[:, cc * 4:(cc + 1) * 4],
                                       start=(n == 1), stop=(n == 24))
                return ins
            t_rt = sc.run("t", f_rt, deps=[t_ev, t_wrc, slot.get("lg_ev")])
            slot["ylofm_c"] = t_rt

            t_cp = sc.run("v", lambda q: q.tensor_copy(lgs[:, :], lg[:, :]),
                          deps=[t_rt])
            t_mx = sc.run("v", lambda q: q.reduce_max(
                G["stat"][:, 6:7], lgs[:, :], axis=AX.X),
                deps=[t_cp], serialize=True)

            def f_mk(q, tt=tt):
                ins = None
                for e in range(4):
                    ins = q.tensor_tensor(
                        G["masks"][:, tt * 4 + e:tt * 4 + e + 1],
                        lgs[:, e:e + 1], G["stat"][:, 6:7], ALU.is_ge)
                return ins
            t_mk = sc.run("v", f_mk, deps=[t_mx], serialize=True)
            slot["lg_ev"] = t_mk
            if G.get("debug"):
                slot["lg_ev"] = sc.dma("s", "sd", lambda q, tt=tt: [
                    q.dma_start(out=G["lgs_dbg"][tt], in_=lgs[:, :])], deps=[t_mk])

            t_st = sc.dma("s", "sd", lambda q, tt=tt, yfmv=yfmv: [
                q.dma_start(out=G["yfm"][tt].rearrange("cc p t -> p cc t"),
                            in_=yfmv.rearrange("p (cc t) -> p cc t", cc=8))],
                deps=[t_ev])
            slot[("yfm", par)] = t_st

    sc.barrier()

    with scope() as cx:
        def lsb(name, shape, dt=F32):
            return cx.enter_context(nc.sbuf_tensor(f"{tag}_{name}", shape, dt))

        def lps(name, shape, dt=F32):
            return cx.enter_context(nc.psum_tensor(f"{tag}_{name}", shape, dt))

        w1f = lsb("w1f", [128, 8 * 512])
        w1b = lsb("w1b", [128, 2 * NE * 8 * 512], BF16)
        b1f = lsb("b1f", [1, NE * 512])
        b1b = lsb("b1b", [1, 2 * NE * 512], BF16)
        yfs = lsb("yfs", [128, 2 * DM], BF16)
        hsb = lsb("hsb", [128, 2 * 2048], BF16)
        isb = lsb("isb", [128, 2 * 512], BF16)
        hp = lps("hp", [128, 2 * 2048])

        aslot = {}
        t_is = None

        def stream_w1(ic):
            wp = (ic % 2) * NE * 4096
            bp = (ic % 2) * NE * 512
            toks = []
            for e in range(NE):
                t_w = sc.dma("q", "qd", lambda q, e=e, ic=ic: [
                    q.dma_start(out=w1f[:, :].rearrange("p (cc i) -> p cc i", cc=8),
                                in_=w["l0f_W1"][e, :, ic * 512:(ic + 1) * 512]
                                .rearrange("(cc p) i -> p cc i", p=128))],
                    deps=[aslot.get("w1f_c")])

                def f_c(q, e=e, wp=wp):
                    return q.tensor_copy(w1b[:, wp + e * 4096:wp + (e + 1) * 4096],
                                         w1f[:, :])
                t_c = sc.run("g", f_c, deps=[t_w, aslot.get(("w1b_c", ic % 2))])
                aslot["w1f_c"] = t_c
                toks.append(t_c)
            t_b1 = sc.dma("q", "qd", lambda q, ic=ic: [
                q.dma_start(out=b1f[:, :].rearrange("o (e i) -> o e i", e=NE),
                            in_=w["l0f_b1"][:, ic * 512:(ic + 1) * 512].unsqueeze(0))],
                deps=[aslot.get(("b1_c", ic % 2)), aslot.get("b1f_c")])
            t_b1c = sc.run("g", lambda q, bp=bp: q.tensor_copy(
                b1b[:, bp:bp + NE * 512], b1f[:, :]), deps=[t_b1])
            aslot["b1f_c"] = t_b1c
            toks.append(t_b1c)
            return toks

        w_next = stream_w1(0)
        for ic in range(8):
            w_ready = w_next
            wp = (ic % 2) * NE * 4096
            bp = (ic % 2) * NE * 512
            if ic < 7:
                w_next = stream_w1(ic + 1)

            for tt in range(NTT):
                par = tt % 2
                yv = yfs[:, par * DM:(par + 1) * DM]
                hv = hp[:, par * 2048:(par + 1) * 2048]
                hbv = hsb[:, par * 2048:(par + 1) * 2048]
                iv = isb[:, par * 512:(par + 1) * 512]

                t_y = sc.dma("q", "qd", lambda q, tt=tt, yv=yv: [
                    q.dma_start(out=yv.rearrange("p (cc t) -> p cc t", cc=8),
                                in_=G["yfm"][tt].rearrange("cc p t -> p cc t"))],
                    deps=[aslot.get(("yfs", par))])

                def f_h(q, yv=yv, hv=hv, wp=wp, bp=bp):
                    ins = None
                    for e in range(NE):
                        ins = q.matmul(hv[:, e * 512:(e + 1) * 512], G["ones1"][:, :],
                                       b1b[:, bp + e * 512:bp + (e + 1) * 512],
                                       start=True, stop=False)
                        for cc in range(8):
                            ins = q.matmul(
                                hv[:, e * 512:(e + 1) * 512],
                                yv[:, cc * 128:(cc + 1) * 128],
                                w1b[:, wp + e * 4096 + cc * 512:wp + e * 4096 + (cc + 1) * 512],
                                start=False, stop=(cc == 7))
                    return ins
                t_h = sc.run("t", f_h, deps=[t_y] + w_ready + [aslot.get(("hp", par))])
                aslot[("yfs", par)] = t_h
                if tt == NTT - 1:
                    aslot[("w1b_c", ic % 2)] = t_h
                    aslot[("b1_c", ic % 2)] = t_h

                t_g = sc.run("s", lambda q, hv=hv, hbv=hbv:
                             q.activation(hbv, hv, AF.Gelu),
                             deps=[t_h, aslot.get(("hsb", par))])
                aslot[("hp", par)] = t_g

                def f_cb(q, tt=tt, hbv=hbv, iv=iv):
                    q.tensor_scalar(iv, hbv[:, 0:512],
                                    G["masks"][:, tt * 4:tt * 4 + 1], None, ALU.mult)
                    ins = None
                    for e in range(1, NE):
                        ins = q.scalar_tensor_tensor(
                            iv, hbv[:, e * 512:(e + 1) * 512],
                            G["masks"][:, tt * 4 + e:tt * 4 + e + 1],
                            iv, ALU.mult, ALU.add)
                    return ins
                t_cb = sc.run("v", f_cb, deps=[t_g, aslot.get(("isb", par))])
                aslot[("hsb", par)] = t_cb

                t_is = sc.dma("s", "sd", lambda q, ic=ic, tt=tt, iv=iv: [
                    q.dma_start(out=G["interm"][ic, tt], in_=iv)], deps=[t_cb])
                aslot[("isb", par)] = t_is

    sc.barrier()

    with scope() as cx:
        def lsb(name, shape, dt=F32):
            return cx.enter_context(nc.sbuf_tensor(f"{tag}_{name}", shape, dt))

        def lps(name, shape, dt=F32):
            return cx.enter_context(nc.psum_tensor(f"{tag}_{name}", shape, dt))

        wof = lsb("wof", [128, 4 * DM])
        wob = lsb("wob", [128, 32 * DM], BF16)
        bo_b = lsb("bo_b", [128, DM])
        gam_b = lsb("gam_b", [128, DM])
        bet_b = lsb("bet_b", [128, DM])
        row_f = lsb("row_f", [1, DM])
        row_b = lsb("row_b", [1, DM], BF16)
        ib = lsb("ib", [128, 2 * 4096], BF16)
        itf = lsb("itf", [128, 4096], BF16)
        zsb = lsb("zsb", [128, DM])
        ztmp = lsb("ztmp", [128, DM])
        xt2 = lsb("xt2", [128, 2 * DM])
        y2 = lsb("y2", [128, DM])
        xo = lsb("xo", [128, 2 * DM])
        tpb = lps("tpb", [128, 2048], BF16)
        zp = lps("zp", [128, 2 * DM])

        bslot = {}
        bc_ps_t = cx.enter_context(nc.psum_tensor(f"{tag}_bcps", [128, DM], F32))
        t_prev = None
        for name, dstbuf in [("l0f_bo", bo_b), ("l0f_g", gam_b), ("l0f_b", bet_b)]:
            t_r = sc.dma("q", "qd", lambda q, name=name: [
                q.dma_start(out=row_f[:, :], in_=w[name].unsqueeze(0))],
                deps=[t_prev])
            t_c = sc.run("v", lambda q: q.tensor_copy(row_b[:, :], row_f[:, :]),
                         deps=[t_r, t_prev])

            def f_bc(q):
                ins = None
                for nch in range(DM // 512):
                    ins = q.matmul(bc_ps_t[:, nch * 512:(nch + 1) * 512],
                                   G["ones1"][:, :],
                                   row_b[:, nch * 512:(nch + 1) * 512],
                                   start=True, stop=True)
                return ins
            t_bc = sc.run("t", f_bc, deps=[t_c, t_prev])
            t_prev = sc.run("v", lambda q, dstbuf=dstbuf: q.tensor_copy(
                dstbuf[:, :], bc_ps_t[:, :]), deps=[t_bc])
        t_bcast = t_prev
        wo_casts = [t_bcast]
        for gq in range(8):
            t_wo = sc.dma("q", "qd", lambda q, gq=gq: [
                q.dma_start(out=wof[:, :].rearrange("p (cc d) -> p cc d", cc=4),
                            in_=w["l0f_Wo"][gq * 512:(gq + 1) * 512, :]
                            .rearrange("(cc p) d -> p cc d", p=128))],
                deps=[bslot.get("wof_c")])
            t_woc = sc.run("g", lambda q, gq=gq: q.tensor_copy(
                wob[:, gq * 4 * DM:(gq + 1) * 4 * DM], wof[:, :]), deps=[t_wo])
            bslot["wof_c"] = t_woc
            wo_casts.append(t_woc)

        t_xs = None
        for tt in range(NTT):
            par = tt % 2
            ibv = ib[:, par * 4096:(par + 1) * 4096]
            xv = xt2[:, par * DM:(par + 1) * DM]
            zv = zp[:, par * DM:(par + 1) * DM]
            xov = xo[:, par * DM:(par + 1) * DM]

            t_i = sc.dma("q", "qd", lambda q, tt=tt, ibv=ibv, xv=xv: [
                q.dma_start(out=ibv.rearrange("p (ic i) -> p ic i", ic=8),
                            in_=G["interm"][:, tt].rearrange("ic p i -> p ic i")),
                q.dma_start(out=xv, in_=src[tt * 128:(tt + 1) * 128, :])],
                deps=[bslot.get(("ib", par)), bslot.get(("x2", par))])

            tr_toks = []
            for rnd in range(2):
                def f_tr(q, rnd=rnd, ibv=ibv):
                    ins = None
                    for j in range(16):
                        ci = rnd * 16 + j
                        ins = q.transpose(tpb[:, j * 128:(j + 1) * 128],
                                          ibv[:, ci * 128:(ci + 1) * 128],
                                          G["ident_b"][:, :])
                    return ins
                t_tr = sc.run("t", f_tr, deps=[t_i, bslot.get("tpb_ev")])

                def f_te(q, rnd=rnd):
                    return q.tensor_copy(itf[:, rnd * 2048:(rnd + 1) * 2048], tpb[:, :])
                t_te = sc.run("v", f_te, deps=[t_tr, bslot.get("itf_c")])
                bslot["tpb_ev"] = t_te
                tr_toks.append(t_te)

            def f_z(q, zv=zv):
                ins = None
                for ci in range(32):
                    for dch in range(2):
                        ins = q.matmul(
                            zv[:, dch * 512:(dch + 1) * 512],
                            itf[:, ci * 128:(ci + 1) * 128],
                            wob[:, ci * DM + dch * 512:ci * DM + (dch + 1) * 512],
                            start=(ci == 0), stop=(ci == 31))
                return ins
            t_z = sc.run("t", f_z, deps=tr_toks + wo_casts + [bslot.get(("zp", par))])
            bslot[("ib", par)] = t_z
            bslot["itf_c"] = t_z

            t_y2 = emit_ln(sc, G, xv, [y2[:, :]], 1e-5,
                           deps=[t_i, bslot.get("y2_c")])

            t_ze = sc.run("s", lambda q, zv=zv: q.copy(zsb[:, :], zv),
                          deps=[t_z, bslot.get("zsb_c")])
            bslot[("zp", par)] = t_ze

            def f_za(q):
                q.tensor_add(zsb[:, :], zsb[:, :], bo_b[:, :])
                return q.tensor_add(zsb[:, :], zsb[:, :], y2[:, :])
            t_za = sc.run("g", f_za, deps=[t_ze, t_y2])
            bslot["y2_c"] = t_za
            if G.get("debug"):
                t_za = sc.dma("s", "sd", lambda q, tt=tt: [
                    q.dma_start(out=G["zsb_dbg"][tt], in_=zsb[:, :]),
                    q.dma_start(out=G["y2_dbg"][tt], in_=y2[:, :])], deps=[t_za])

            t_iln = emit_ln(sc, G, zsb[:, :], [ztmp[:, :]], 1e-12,
                            deps=[t_za, bslot.get("ztmp_c")])
            bslot["zsb_c"] = t_iln

            def f_gb(q):
                q.tensor_mul(ztmp[:, :], ztmp[:, :], gam_b[:, :])
                return q.tensor_add(ztmp[:, :], ztmp[:, :], bet_b[:, :])
            t_gb = sc.run("g", f_gb, deps=[t_iln])

            t_res = sc.run("v", lambda q, xov=xov, xv=xv: q.scalar_tensor_tensor(
                xov, ztmp[:, :], 0.5, xv, ALU.mult, ALU.add),
                deps=[t_gb, bslot.get(("xo", par))])
            bslot["ztmp_c"] = t_res
            bslot[("x2", par)] = t_res

            t_xs = sc.dma("s", "sd", lambda q, tt=tt, xov=xov: [
                q.dma_start(out=dst[tt * 128:(tt + 1) * 128, :], in_=xov)],
                deps=[t_res])
            bslot[("xo", par)] = t_xs

    sc.barrier()



def emit_l0attn(sc, G, src, dst):
    nc, w = G["nc"], G["w"]
    tag = G["tag"]
    H, HD = 16, 64

    with ExitStack() as cx:
        def lsb(name, shape, dt=F32):
            return cx.enter_context(nc.sbuf_tensor(f"{tag}_{name}", shape, dt))

        def lps(name, shape, dt=F32):
            return cx.enter_context(nc.psum_tensor(f"{tag}_{name}", shape, dt))

        yfm = lsb("yfm", [128, 8 * S], BF16)
        qrf = lsb("qrf", [128, 8 * S], BF16)
        krf = lsb("krf", [128, 8 * S], BF16)
        vtm = lsb("vtm", [128, 16 * DM], BF16)
        outall = lsb("outall", [128, 16 * DM], BF16)
        wsl = lsb("wsl", [128, 8 * DM], BF16)
        wf = lsb("wf", [128, DM // 2])
        xt = lsb("xt", [128, DM])
        yb = lsb("yb", [128, DM], BF16)
        rtm = lsb("rtm", [128, 2 * DM], BF16)
        tabs = lsb("tabs", [128, 2 * DM], BF16)
        asb = lsb("asb", [128, 2048], BF16)
        dens = lsb("dens", [128, 16])
        rden = lsb("rden", [128, 16])
        dsb = lsb("dsb", [128, 256], BF16)
        dtmp = lsb("dtmp", [128, 16])
        otmp = lsb("otmp", [128, DM], BF16)
        pq = lps("pq", [128, 2048])
        pk = lps("pk", [128, 2048])

        def bank_chunks(lo, hi):
            out = []
            while lo < hi:
                nxt = min((lo // 512 + 1) * 512, hi)
                out.append((lo, nxt))
                lo = nxt
            return out

        slot = {}
        for b in range(BC):
            base = b * S
            PHASE_MARKS.append((f"p1s1_b{b}", nc.next_id()))

            def load_w(nm, dep_consumed):
                toks = []
                dep = dep_consumed
                for cc in range(16):
                    t_l = sc.dma("q", "qd", lambda q, nm=nm, cc=cc: [
                        q.dma_start(out=wf[:, :],
                                    in_=w[nm][(cc // 2) * 128:(cc // 2 + 1) * 128,
                                              (cc % 2) * 512:(cc % 2 + 1) * 512])],
                        deps=[dep, slot.get("wf_c")])
                    t_c = sc.run("g", lambda q, cc=cc: q.tensor_copy(
                        wsl[:, cc * 512:(cc + 1) * 512], wf[:, :]), deps=[t_l])
                    slot["wf_c"] = t_c
                    toks.append(t_c)
                return toks

            for ts_ in range(16):
                t_x = sc.dma("q", "qd", lambda q, ts_=ts_: [
                    q.dma_start(out=xt[:, :],
                                in_=src[base + ts_ * 128:base + (ts_ + 1) * 128, :])],
                    deps=[slot.get("x_c")])
                t_ln = emit_ln(sc, G, xt[:, :], [yb[:, :]], 1e-5,
                               deps=[t_x, slot.get("yb_c")])
                slot["x_c"] = t_ln

                tp = pk[:, 0:512].bitcast(BF16)

                def f_tr(q, tp=tp):
                    ins = None
                    for cc in range(8):
                        ins = q.transpose(tp[:, cc * 128:(cc + 1) * 128],
                                          yb[:, cc * 128:(cc + 1) * 128],
                                          G["ident_b"][:, :])
                    return ins
                t_tr = sc.run("t", f_tr, deps=[t_ln, slot.get("tp_ev")])
                slot["yb_c"] = t_tr

                def f_te(q, ts_=ts_, tp=tp):
                    ov = yfm[:, :].rearrange("p (cc t) -> p cc t", t=S)[
                        :, :, ts_ * 128:(ts_ + 1) * 128]
                    return q.tensor_copy(ov, tp.rearrange("p (cc t) -> p cc t", t=128))
                t_te = sc.run("v", f_te, deps=[t_tr, slot.get("yfm_c")])
                slot["tp_ev"] = t_te
            yfm_done = sc.count["v"]
            t_yfm = ("v", yfm_done)

            for (wnm, dstf, ta, tb) in [("l0a_Wq", qrf, "l0_qA", "l0_qB"),
                                        ("l0a_Wk", krf, "l0_kA", "l0_kB")]:
                wtoks = load_w(wnm, slot.get("wsl_c"))
                for ts_ in range(16):
                    par = ts_ % 2
                    pv = pq[:, par * 1024:(par + 1) * 1024]

                    def f_mm(q, ts_=ts_, pv=pv):
                        ins = None
                        for cc in range(8):
                            lh = yfm[:, cc * S + ts_ * 128:cc * S + (ts_ + 1) * 128]
                            for n2 in range(2):
                                ins = q.matmul(pv[:, n2 * 512:(n2 + 1) * 512], lh,
                                               wsl[:, cc * DM + n2 * 512:cc * DM + (n2 + 1) * 512],
                                               start=(cc == 0), stop=(cc == 7))
                        return ins
                    t_mm = sc.run("t", f_mm,
                                  deps=[t_yfm] + wtoks + [slot.get(("pp", par))])

                    t_tab = sc.dma("q", "qd", lambda q, ts_=ts_, ta=ta, tb=tb: [
                        q.dma_start(out=tabs[:, 0:DM],
                                    in_=w[ta][ts_ * 128:(ts_ + 1) * 128, :]),
                        q.dma_start(out=tabs[:, DM:2 * DM],
                                    in_=w[tb][ts_ * 128:(ts_ + 1) * 128, :])],
                        deps=[slot.get("tab_c")])

                    def f_r1(q, pv=pv):
                        q.tensor_mul(rtm[:, 0:DM], pv, tabs[:, 0:DM])
                        pvv = pv.rearrange("p (a c) -> p a c", c=2)
                        bvv = tabs[:, DM:2 * DM].rearrange("p (a c) -> p a c", c=2)
                        mvv = rtm[:, DM:2 * DM].rearrange("p (a c) -> p a c", c=2)
                        q.tensor_mul(mvv[:, :, 0:1], pvv[:, :, 1:2], bvv[:, :, 0:1])
                        return q.tensor_mul(mvv[:, :, 1:2], pvv[:, :, 0:1], bvv[:, :, 1:2])
                    t_r1 = sc.run("v", f_r1, deps=[t_mm, t_tab, slot.get("rtm_c")])
                    slot["tab_c"] = t_r1
                    slot[("pp", par)] = t_r1

                    t_r2 = sc.run("v", lambda q: q.tensor_add(
                        yb[:, :], rtm[:, 0:DM], rtm[:, DM:2 * DM]),
                        deps=[t_r1, slot.get("yb_c")])
                    slot["rtm_c"] = t_r2

                    tp = pk[:, 0:512].bitcast(BF16)

                    def f_tr2(q, tp=tp):
                        ins = None
                        for cc in range(8):
                            ins = q.transpose(tp[:, cc * 128:(cc + 1) * 128],
                                              yb[:, cc * 128:(cc + 1) * 128],
                                              G["ident_b"][:, :])
                        return ins
                    t_tr2 = sc.run("t", f_tr2, deps=[t_r2, slot.get("tp_ev")])
                    slot["yb_c"] = t_tr2

                    def f_te2(q, ts_=ts_, tp=tp, dstf=dstf):
                        ov = dstf[:, :].rearrange("p (cc t) -> p cc t", t=S)[
                            :, :, ts_ * 128:(ts_ + 1) * 128]
                        return q.tensor_copy(ov, tp.rearrange("p (cc t) -> p cc t", t=128))
                    t_te2 = sc.run("v", f_te2, deps=[t_tr2, slot.get(("qk_c", wnm))])
                    slot["tp_ev"] = t_te2
                slot["wsl_c"] = t_mm
            qk_done = ("v", sc.count["v"])

            wtoks = load_w("l0a_Wv", slot.get("wsl_c"))
            for ts_ in range(16):
                par = ts_ % 2
                pv = pq[:, par * 1024:(par + 1) * 1024]

                def f_mmv(q, ts_=ts_, pv=pv):
                    ins = None
                    for cc in range(8):
                        lh = yfm[:, cc * S + ts_ * 128:cc * S + (ts_ + 1) * 128]
                        for n2 in range(2):
                            ins = q.matmul(pv[:, n2 * 512:(n2 + 1) * 512], lh,
                                           wsl[:, cc * DM + n2 * 512:cc * DM + (n2 + 1) * 512],
                                           start=(cc == 0), stop=(cc == 7))
                    return ins
                t_mmv = sc.run("t", f_mmv,
                               deps=[t_yfm] + wtoks + [slot.get(("pp", par))])
                t_ev = sc.run("v", lambda q, ts_=ts_, pv=pv: q.tensor_copy(
                    vtm[:, ts_ * DM:(ts_ + 1) * DM], pv),
                    deps=[t_mmv, slot.get("vtm_c")])
                slot[("pp", par)] = t_ev
                slot["wsl_c2"] = t_mmv
            v_done = ("v", sc.count["v"])
            slot["wsl_c"] = slot["wsl_c2"]

            wg_toks = load_w("l0a_Wg", slot.get("wsl_c"))
            yv3 = yfm[:, :].rearrange("p (cc t) -> p cc t", t=S)
            for i in range(16):
                par = i % 2
                pv = pq[:, par * 1024:(par + 1) * 1024]

                def f_gm(q, i=i, pv=pv):
                    ins = None
                    for cc in range(8):
                        lh = yfm[:, cc * S + i * 128:cc * S + (i + 1) * 128]
                        for n2 in range(2):
                            ins = q.matmul(pv[:, n2 * 512:(n2 + 1) * 512], lh,
                                           wsl[:, cc * DM + n2 * 512:cc * DM + (n2 + 1) * 512],
                                           start=(cc == 0), stop=(cc == 7))
                    return ins
                t_gm = sc.run("t", f_gm,
                              deps=[t_yfm] + wg_toks + [slot.get(("pp", par))])

                def f_si(q, i=i, pv=pv):
                    ov = yv3[:, :, i * 128:(i + 1) * 128]
                    return q.activation(
                        ov.rearrange("p cc t -> p (cc t)") if False else ov,
                        pv.rearrange("p (cc t) -> p cc t", t=128), AF.Silu)
                t_si = sc.run("s", f_si, deps=[t_gm])
                slot[("pp", par)] = t_si
            gsilu_done = ("s", sc.count["s"])

            PHASE_MARKS.append((f"p1s2_b{b}", nc.next_id()))
            head_last = None
            for h in range(H):
                hh, hp = h // 2, (h % 2) * 64
                qrh = qrf[hp:hp + 64, hh * S:(hh + 1) * S]
                krh = krf[hp:hp + 64, hh * S:(hh + 1) * S]

                a_toks = []
                gp_toks = []
                for qt in range(16):
                    ps = pq if qt % 2 == 0 else pk
                    krange = (qt + 1) * 128
                    dpar = qt % 2

                    def f_a(q, qt=qt, ps=ps, krange=krange):
                        lh = qrh[:, qt * 128:(qt + 1) * 128]
                        ins = None
                        for (c0, c1) in bank_chunks(0, krange):
                            ins = q.matmul(ps[:, c0:c1], lh, krh[:, c0:c1],
                                           start=True, stop=True)
                        return ins
                    depk = [qk_done, slot.get(("psA", qt % 2)),
                            slot.get(("psAv", qt % 2))]
                    if qt < 2 and h == 0:
                        depk += [v_done, gsilu_done, slot.get(("pp", 0)),
                                 slot.get(("pp", 1)), slot.get("tp_ev")]
                    t_a = sc.run("t", f_a, deps=depk)

                    if qt > 0:
                        t_abv = sc.run("s", lambda q, qt=qt, ps=ps: q.activation(
                            asb[:, 0:qt * 128], ps[:, 0:qt * 128],
                            AF.Abs, accum_out=dens[:, qt:qt + 1]),
                            deps=[t_a, slot.get("asb_c"), slot.get("dens_c")])
                    else:
                        t_abv = None
                    t_ab = sc.run("s", lambda q, qt=qt, ps=ps: q.copy(
                        dsb[:, (qt % 2) * 128:(qt % 2) * 128 + 128],
                        ps[:, qt * 128:(qt + 1) * 128]),
                        deps=[t_a, slot.get(("dsb", dpar))])
                    slot[("psA", qt % 2)] = t_ab
                    slot[("psAv", qt % 2)] = t_abv
                    slot["asb_c"] = t_abv
                    a_toks.append(t_ab)
                    if t_abv is not None:
                        a_toks.append(t_abv)

                    dv = dsb[:, dpar * 128:dpar * 128 + 128]
                    t_g1 = sc.run("g", lambda q, dv=dv: q.tensor_mul(
                        dv, dv, G["tril"][:, :]), deps=[t_ab])
                    t_g2 = sc.run("v", lambda q, qt=qt, dv=dv: q.tensor_reduce(
                        dtmp[:, qt:qt + 1], dv, axis=AX.X, op=ALU.add,
                        apply_absolute_value=True), deps=[t_g1])
                    slot[("dsb", dpar)] = t_g2
                    if qt == 0:
                        t_g3 = sc.run("g", lambda q, qt=qt: q.tensor_copy(
                            dens[:, qt:qt + 1], dtmp[:, qt:qt + 1]), deps=[t_g2])
                    else:
                        t_g3 = sc.run("g", lambda q, qt=qt: q.tensor_tensor(
                            dens[:, qt:qt + 1], dens[:, qt:qt + 1],
                            dtmp[:, qt:qt + 1], ALU.add), deps=[t_g2, t_abv])
                    gp_toks.append(t_g3)

                t_cl = sc.run("v", lambda q: q.tensor_scalar(
                    rden[:, :], dens[:, :], 1.0, 50000.0, ALU.max, ALU.min),
                    deps=gp_toks[-2:] + a_toks[-1:])
                t_rc = sc.run("v", lambda q: q.reciprocal(rden[:, :], rden[:, :]),
                              deps=[t_cl])
                slot["dens_c"] = t_rc

                slot["atp0"] = slot.get(("psA", 0))
                slot["atp1"] = slot.get(("psA", 1))
                vv_last = None
                for qh in range(2):
                    nkt = (qh + 1) * 8
                    ae_tok = {}
                    gp_tok = {}
                    for step in range(nkt + 1):
                        if step < nkt:
                            kt = step
                            par = kt % 2
                            atp = pq[:, par * 1024:(par + 1) * 1024]
                            qlo = max(qh * 1024, kt * 128)
                            off = qlo - qh * 1024

                            def f_at(q, kt=kt, atp=atp, off=off, qh=qh):
                                lh = krh[:, kt * 128:(kt + 1) * 128]
                                ins = None
                                for (c0, c1) in bank_chunks(off, 1024):
                                    ins = q.matmul(atp[:, c0:c1], lh,
                                                   qrh[:, qh * 1024 + c0:qh * 1024 + c1],
                                                   start=True, stop=True)
                                return ins
                            dep_at = [slot.get(("atB", par)), slot.get("atp%d" % par)]
                            slot["atp%d" % par] = None
                            t_at = sc.run("t", f_at, deps=dep_at)

                            eng = "s" if kt % 2 == 0 else "v"
                            t_ae = sc.run(eng, lambda q, atp=atp, off=off, par=par:
                                          q.copy(asb[:, par * 1024 + off:(par + 1) * 1024],
                                                 atp[:, off:1024])
                                          if eng == "s" else
                                          q.tensor_copy(asb[:, par * 1024 + off:(par + 1) * 1024],
                                                        atp[:, off:1024]),
                                          deps=[t_at, slot.get(("asbB", par)),
                                                slot.get("asb_c")])
                            slot["asb_c"] = None
                            slot[("atB", par)] = t_ae
                            ae_tok[kt] = t_ae
                            if kt >= qh * 8:
                                dg0 = par * 1024 + kt * 128 - qh * 1024
                                t_gd = sc.run("g", lambda q, dg0=dg0: q.tensor_mul(
                                    asb[:, dg0:dg0 + 128], asb[:, dg0:dg0 + 128],
                                    G["trilT"][:, :]), deps=[t_ae])
                                gp_tok[kt] = t_gd
                        if step >= 1:
                            kt = step - 1
                            par = kt % 2

                            def f_vv(q, kt=kt, qh=qh, par=par, h=h):
                                ins = None
                                for i in range(max(kt + 1, qh * 8), qh * 8 + 8):
                                    ins = q.matmul(
                                        pk[0:64, i * 128:(i + 1) * 128],
                                        vtm[:, kt * DM + h * HD:kt * DM + (h + 1) * HD],
                                        asb[:, par * 1024 + i * 128 - qh * 1024:
                                            par * 1024 + (i + 1) * 128 - qh * 1024],
                                        start=False, stop=False, skip_group_check=True)
                                return ins
                            dep_vv = [ae_tok[kt]]
                            if kt == 0 and qh == 0:
                                def f_z(q):
                                    ins = None
                                    for bk in range(4):
                                        ins = q.matmul(pk[0:64, bk * 512:(bk + 1) * 512],
                                                       G["ones1"][:, 0:64],
                                                       G["zrow"][:, :],
                                                       start=True, stop=True)
                                    return ins
                                t_z = sc.run("t", f_z, deps=[slot.get("otB")] + a_toks[-2:])
                                dep_vv.append(t_z)
                            t_vv = None
                            if max(kt + 1, qh * 8) < qh * 8 + 8:
                                t_vv = sc.run("t", f_vv, deps=dep_vv)
                            if kt >= qh * 8:
                                def f_vd(q, kt=kt, qh=qh, par=par, h=h):
                                    return q.matmul(
                                        pk[0:64, kt * 128:(kt + 1) * 128],
                                        vtm[:, kt * DM + h * HD:kt * DM + (h + 1) * HD],
                                        asb[:, par * 1024 + kt * 128 - qh * 1024:
                                            par * 1024 + (kt + 1) * 128 - qh * 1024],
                                        start=False, stop=True, skip_group_check=True)
                                t_vv = sc.run("t", f_vd, deps=dep_vv + [gp_tok[kt]])
                            slot[("asbB", par)] = t_vv
                            vv_last = t_vv

                t_oe = sc.run("v", lambda q: q.tensor_copy(asb[0:64, :], pk[0:64, :]),
                              deps=[vv_last, slot.get("otsb_c")])
                slot["otB"] = t_oe

                tr_grp = None
                for grp in range(4):
                    tpo = pq[:, grp * 512:grp * 512 + 128].bitcast(BF16)

                    def f_ot(q, grp=grp, tpo=tpo):
                        ins = None
                        for j in range(4):
                            i = grp * 4 + j
                            ins = q.transpose(tpo[:, j * 64:(j + 1) * 64],
                                              asb[0:64, i * 128:(i + 1) * 128],
                                              G["ident_b"][0:64, 0:64])
                        return ins
                    t_ot = sc.run("t", f_ot, deps=[t_oe, slot.get(("otT", grp))])

                    def f_os(q, grp=grp, tpo=tpo, h=h):
                        ins = None
                        for j in range(4):
                            i = grp * 4 + j
                            ins = q.tensor_scalar(
                                outall[:, i * DM + h * HD:i * DM + (h + 1) * HD],
                                tpo[:, j * 64:(j + 1) * 64],
                                rden[:, i:i + 1], None, ALU.mult)
                        return ins
                    t_os = sc.run("v", f_os, deps=[t_ot, t_rc, slot.get("outall_c")])
                    slot[("otT", grp)] = t_os
                    tr_grp = t_os
                slot["otsb_c"] = tr_grp
                slot[("psA", 0)] = tr_grp
                slot[("psA", 1)] = t_oe
                slot["asb_c"] = tr_grp
                head_last = tr_grp
            heads_done = head_last

            PHASE_MARKS.append((f"p1s3_b{b}", nc.next_id()))
            gsumA = asb[:, 0:1024].bitcast(F32)
            ptm = tabs[:, :].bitcast(F32)
            sq_toks = []
            last_rd = None
            for i in range(16):
                sbuf_t = rtm[:, (i % 2) * DM:(i % 2 + 1) * DM]
                t_sq = sc.run("s", lambda q, i=i, sbuf_t=sbuf_t: q.activation(
                    sbuf_t, outall[:, i * DM:(i + 1) * DM], AF.Square),
                    deps=[heads_done, slot.get(("gnsq", i % 2)), slot.get("asb_gn"),
                          slot.get("rtm_c")])
                t_rd = sc.run("v", lambda q, i=i, sbuf_t=sbuf_t: q.tensor_reduce(
                    gsumA[:, i * 16:(i + 1) * 16],
                    sbuf_t.rearrange("p (hh d) -> p hh d", d=HD),
                    axis=AX.X, op=ALU.add), deps=[t_sq])
                slot[("gnsq", i % 2)] = t_rd
                last_rd = t_rd
            t_s1 = sc.run("v", lambda q: q.tensor_scalar(
                gsumA[:, 256:512], gsumA[:, 0:256], 1.0 / HD, 1e-6,
                ALU.mult, ALU.add), deps=[last_rd])
            t_s2 = sc.run("s", lambda q: q.activation(
                gsumA[:, 256:512], gsumA[:, 256:512], AF.Sqrt), deps=[t_s1])
            t_s3 = sc.run("v", lambda q: q.reciprocal(
                gsumA[:, 256:512], gsumA[:, 256:512]), deps=[t_s2])
            grstA = gsumA[:, 256:512]

            st_last = None
            for i in range(16):
                t_x = sc.dma("q", "qd", lambda q, i=i: [
                    q.dma_start(out=xt[:, :],
                                in_=src[base + i * 128:base + (i + 1) * 128, :])],
                    deps=[slot.get("x_c"), heads_done])

                def f_ap(q, i=i):
                    ins = None
                    for h2 in range(H):
                        ins = q.tensor_scalar(
                            otmp[:, h2 * HD:(h2 + 1) * HD],
                            outall[:, i * DM + h2 * HD:i * DM + (h2 + 1) * HD],
                            grstA[:, i * 16 + h2:i * 16 + h2 + 1], None, ALU.mult)
                    return ins
                t_ap = sc.run("v", f_ap, deps=[t_s3, slot.get("otmp_c")])

                gview = yv3[:, :, i * 128:(i + 1) * 128]
                t_mu2 = sc.run("v", lambda q, gview=gview: q.tensor_mul(
                    otmp[:, :].rearrange("p (cc t) -> p cc t", t=128),
                    otmp[:, :].rearrange("p (cc t) -> p cc t", t=128), gview),
                    deps=[t_ap, gsilu_done])

                t_fin = sc.run("v", lambda q: q.tensor_add(ptm, otmp[:, :],
                                                           xt[:, :]),
                               deps=[t_mu2, t_x, slot.get("ptm_st"),
                                     slot.get("tab_fin")])
                slot["otmp_c"] = t_fin
                slot["outall_c"] = t_ap
                slot["x_c"] = t_fin

                t_st = sc.dma("s", "sd", lambda q, i=i: [
                    q.dma_start(out=dst[base + i * 128:base + (i + 1) * 128, :],
                                in_=ptm)], deps=[t_fin])
                slot["ptm_st"] = t_st
                st_last = t_st
            slot["comb_ps"] = gsilu_done
            slot["asb_gn"] = t_s3
            slot["asb_c"] = slot.get("outall_c")
            slot["rtm_c"] = last_rd
            slot["tab_c"] = st_last
            slot["gps_c"] = gsilu_done
            slot["yfm_c"] = st_last
            slot["rtm_c"] = slot.get("rtm_c")
            slot["vtm_c"] = vv_last
            slot[("qk_c", "l0a_Wq")] = vv_last
            slot[("qk_c", "l0a_Wk")] = vv_last
            slot[("pp", 0)] = slot.get("gps_c")
            slot[("pp", 1)] = slot.get("gps_c")

    sc.barrier()



def emit_l1ffn(sc, G, src, dst):
    nc, w = G["nc"], G["w"]
    tag = G["tag"]
    TW = 1024

    with ExitStack() as cx:
        def lsb(name, shape, dt=F32):
            return cx.enter_context(nc.sbuf_tensor(f"{tag}_{name}", shape, dt))

        def lps(name, shape, dt=F32):
            return cx.enter_context(nc.psum_tensor(f"{tag}_{name}", shape, dt))

        wf = lsb("wf", [128, 4 * DM])
        wb = {}
        t_wc_all = []
        wslot = {}
        for wi, nm in enumerate(["l1f_WrT", "l1f_WkT", "l1f_WvT"]):
            wb[nm] = lsb(f"wb{wi}", [128, 8 * DM], BF16)
            for half in range(2):
                t_wl = sc.dma("q", "qd", lambda q, nm=nm, half=half: [
                    q.dma_start(out=wf[:, :].rearrange("p (cc d) -> p cc d", cc=4),
                                in_=w[nm][half * 512:(half + 1) * 512, :]
                                .rearrange("(cc p) d -> p cc d", p=128))],
                    deps=[wslot.get("wf_c")])
                t_wc = sc.run("g", lambda q, nm=nm, half=half: q.tensor_copy(
                    wb[nm][:, half * 4 * DM:(half + 1) * 4 * DM], wf[:, :]),
                    deps=[t_wl])
                wslot["wf_c"] = t_wc
                t_wc_all.append(t_wc)

        muf = lsb("muf", [128, 8 * 2])
        mur = lsb("mur", [128, 8 * 4])
        t_mu = sc.dma("q", "qd", lambda q: [
            q.dma_start(out=muf[:, 0:8], in_=w["l1f_mur"].rearrange("(cc p) -> p cc", p=128),
                        allow_slow_non_contiguous=True),
            q.dma_start(out=muf[:, 8:16], in_=w["l1f_muk"].rearrange("(cc p) -> p cc", p=128),
                        allow_slow_non_contiguous=True)])

        def f_mu(q):
            q.tensor_copy(mur[:, 0:8], muf[:, 0:8])
            q.tensor_scalar(mur[:, 8:16], muf[:, 0:8], -1.0, 1.0, ALU.mult, ALU.add)
            q.tensor_copy(mur[:, 16:24], muf[:, 8:16])
            return q.tensor_scalar(mur[:, 24:32], muf[:, 8:16], -1.0, 1.0, ALU.mult, ALU.add)
        t_muc = sc.run("v", f_mu, deps=[t_mu])

        yfm = lsb("yfm", [128, 8 * (S + 1)], BF16)
        xt = lsb("xt", [128, 2 * DM])
        yb = lsb("yb", [128, DM], BF16)
        xr = lsb("xr", [128, 8 * TW], BF16)
        xk = lsb("xk", [128, 8 * TW], BF16)
        t1 = lsb("t1", [128, 2 * TW])
        rr = lsb("rr", [128, 8 * TW], BF16)
        ka = lsb("ka", [128, 8 * TW], BF16)
        of = lsb("of", [128, 8 * TW], BF16)
        xo = lsb("xo", [128, 2 * DM])
        pa = lps("pa", [128, 2 * DM])
        pt = lps("pt", [128, DM], BF16)

        slot = {}
        for b in range(BC):
            base = b * S
            t_z = sc.run("g", lambda q: q.memset(
                yfm[:, :].rearrange("p (cc t) -> p cc t", t=S + 1)[:, :, 0:1], 0.0),
                deps=[slot.get("yfm_c")])

            tr_last = None
            for ts_ in range(16):
                par = ts_ % 2
                xv = xt[:, par * DM:(par + 1) * DM]
                t_x = sc.dma("q", "qd", lambda q, ts_=ts_, xv=xv: [
                    q.dma_start(out=xv, in_=src[base + ts_ * 128:base + (ts_ + 1) * 128, :])],
                    deps=[slot.get(("x", par))])
                t_ln = emit_ln(sc, G, xv, [yb[:, :]], 1e-5,
                               deps=[t_x, slot.get("yb_c")])
                slot[("x", par)] = t_ln

                def f_tr(q):
                    ins = None
                    for cc in range(8):
                        ins = q.transpose(pt[:, cc * 128:(cc + 1) * 128],
                                          yb[:, cc * 128:(cc + 1) * 128],
                                          G["ident_b"][:, :])
                    return ins
                t_tr = sc.run("t", f_tr, deps=[t_ln, slot.get("pt_ev")])
                slot["yb_c"] = t_tr

                def f_te(q, ts_=ts_):
                    ov = yfm[:, :].rearrange("p (cc t) -> p cc t", t=S + 1)[
                        :, :, 1 + ts_ * 128:1 + (ts_ + 1) * 128]
                    return q.tensor_copy(ov, pt[:, :].rearrange("p (cc t) -> p cc t", t=128))
                t_te = sc.run("v", f_te, deps=[t_tr, t_z])
                slot["pt_ev"] = t_te
                tr_last = t_te

            out_done = []
            for hh in range(2):
                yv = yfm[:, :].rearrange("p (cc t) -> p cc t", t=S + 1)
                t_xr = None
                for cc in range(8):
                    ys = yv[:, cc, 1 + hh * TW:1 + (hh + 1) * TW]
                    yp = yv[:, cc, hh * TW:(hh + 1) * TW]

                    def f_a(q, ys=ys, cc=cc):
                        q.tensor_scalar(t1[:, 0:TW], ys, mur[:, cc:cc + 1],
                                        None, ALU.mult)
                        return q.tensor_scalar(t1[:, TW:2 * TW], ys,
                                               mur[:, 16 + cc:16 + cc + 1],
                                               None, ALU.mult)
                    t_a = sc.run("v", f_a,
                                 deps=[tr_last, t_muc, slot.get("xr_c")] if cc == 0
                                 else [t_a0])

                    def f_b(q, yp=yp, cc=cc):
                        q.scalar_tensor_tensor(xr[:, cc * TW:(cc + 1) * TW], yp,
                                               mur[:, 8 + cc:8 + cc + 1], t1[:, 0:TW],
                                               ALU.mult, ALU.add)
                        return q.scalar_tensor_tensor(xk[:, cc * TW:(cc + 1) * TW], yp,
                                                      mur[:, 24 + cc:24 + cc + 1],
                                                      t1[:, TW:2 * TW],
                                                      ALU.mult, ALU.add)
                    t_xr = sc.run("v", f_b, deps=[t_a])
                    t_a0 = t_xr

                rk_done = []
                for (xin_, wnm, obuf, act) in [(xr, "l1f_WrT", rr, "sig"),
                                               (xk, "l1f_WkT", ka, "relu2")]:
                    for dd in range(8):
                        par = dd % 2
                        pv = pa[:, par * DM:(par + 1) * DM]

                        def f_mm(q, xin_=xin_, wnm=wnm, dd=dd, pv=pv):
                            ins = None
                            for cc in range(8):
                                for nc2 in range(2):
                                    ins = q.matmul(
                                        pv[:, nc2 * 512:(nc2 + 1) * 512],
                                        wb[wnm][:, cc * DM + dd * 128:cc * DM + (dd + 1) * 128],
                                        xin_[:, cc * TW + nc2 * 512:cc * TW + (nc2 + 1) * 512],
                                        start=(cc == 0), stop=(cc == 7))
                            return ins
                        t_mm = sc.run("t", f_mm,
                                      deps=[t_xr] + t_wc_all + [slot.get(("pa", par))])

                        if act == "sig":
                            t_ev = sc.run("s", lambda q, dd=dd, pv=pv: q.activation(
                                rr[:, dd * TW:(dd + 1) * TW], pv, AF.Sigmoid),
                                deps=[t_mm, slot.get("rr_c")])
                        else:
                            t_rl = sc.run("s", lambda q, dd=dd, pv=pv: q.activation(
                                G["scr_b"][:, :], pv, AF.Relu),
                                deps=[t_mm, slot.get("ka_c")])
                            t_ev = sc.run("s", lambda q, dd=dd: q.activation(
                                ka[:, dd * TW:(dd + 1) * TW], G["scr_b"][:, :],
                                AF.Square), deps=[t_rl], serialize=True)
                        slot[("pa", par)] = t_ev
                        rk_done.append(t_ev)

                slot["xr_c"] = rk_done[-1]

                o_done = []
                for dd in range(8):
                    par = dd % 2
                    pv = pa[:, par * DM:(par + 1) * DM]

                    def f_vm(q, dd=dd, pv=pv):
                        ins = None
                        for cc in range(8):
                            for nc2 in range(2):
                                ins = q.matmul(
                                    pv[:, nc2 * 512:(nc2 + 1) * 512],
                                    wb["l1f_WvT"][:, cc * DM + dd * 128:cc * DM + (dd + 1) * 128],
                                    ka[:, cc * TW + nc2 * 512:cc * TW + (nc2 + 1) * 512],
                                    start=(cc == 0), stop=(cc == 7))
                        return ins
                    t_vm = sc.run("t", f_vm, deps=rk_done + [slot.get(("pa", par))])

                    t_om = sc.run("v", lambda q, dd=dd, pv=pv: q.tensor_mul(
                        of[:, dd * TW:(dd + 1) * TW], rr[:, dd * TW:(dd + 1) * TW], pv),
                        deps=[t_vm, slot.get("of_c")])
                    slot[("pa", par)] = t_om
                    o_done.append(t_om)
                slot["rr_c"] = o_done[-1]
                slot["ka_c"] = o_done[-1]

                for ts2 in range(8):
                    gt = hh * 8 + ts2
                    par = gt % 2
                    xv = xt[:, par * DM:(par + 1) * DM]
                    xov = xo[:, par * DM:(par + 1) * DM]
                    t_x2 = sc.dma("q", "qd", lambda q, gt=gt, xv=xv: [
                        q.dma_start(out=xv,
                                    in_=src[base + gt * 128:base + (gt + 1) * 128, :])],
                        deps=[slot.get(("x", par))])

                    def f_tb(q, ts2=ts2):
                        ins = None
                        for dd in range(8):
                            ins = q.transpose(
                                pt[:, dd * 128:(dd + 1) * 128],
                                of[:, dd * TW + ts2 * 128:dd * TW + (ts2 + 1) * 128],
                                G["ident_b"][:, :])
                        return ins
                    t_tb = sc.run("t", f_tb, deps=o_done + [slot.get("pt_ev")])

                    t_rs = sc.run("v", lambda q, xv=xv, xov=xov: q.scalar_tensor_tensor(
                        xov, pt[:, :], 0.5, xv, ALU.mult, ALU.add),
                        deps=[t_tb, t_x2, slot.get(("xo", par))])
                    slot["pt_ev"] = t_rs
                    slot[("x", par)] = t_rs
                    if ts2 == 7:
                        slot["of_c"] = t_rs

                    t_xs = sc.dma("s", "sd", lambda q, gt=gt, xov=xov: [
                        q.dma_start(out=dst[base + gt * 128:base + (gt + 1) * 128, :],
                                    in_=xov)], deps=[t_rs])
                    slot[("xo", par)] = t_xs
                    out_done.append(t_xs)
            slot["yfm_c"] = out_done[-1]

    sc.barrier()



def emit_l1attn(sc, G, src, dst):
    nc, w = G["nc"], G["w"]
    tag = G["tag"]
    NH, HD1 = 8, 128

    with ExitStack() as cx:
        def lsb(name, shape, dt=F32):
            return cx.enter_context(nc.sbuf_tensor(f"{tag}_{name}", shape, dt))

        def lps(name, shape, dt=F32):
            return cx.enter_context(nc.psum_tensor(f"{tag}_{name}", shape, dt))

        yfm = lsb("yfm", [128, 8 * S], BF16)
        outall = lsb("outall", [128, 16 * DM], BF16)
        wsl = lsb("wsl", [128, 8 * DM], BF16)
        wf = lsb("wf", [128, DM])
        xt = lsb("xt", [128, DM])
        yb = lsb("yb", [128, DM], BF16)
        rtm = lsb("rtm", [128, 2 * DM], BF16)
        qro = lsb("qro", [128, 2 * DM], BF16)
        l1tb = lsb("l1tb", [128, 4 * DM], BF16)
        qTs = lsb("qTs", [128, 2 * 2048], BF16)
        kTs = lsb("kTs", [128, 2 * 2048], BF16)
        vh = lsb("vh", [128, 2 * 2048], BF16)
        ptsb = lsb("ptsb", [128, 2 * 1024], BF16)
        osb = lsb("osb", [128, 1024], BF16)
        denr = lsb("denr", [1, 1024])
        denc = lsb("denc", [128, 8])
        rdn = lsb("rdn", [128, 8])
        pq = lps("pq", [128, 2048])
        pk = lps("pk", [128, 2048])

        t_tb = sc.dma("q", "qd", lambda q: [
            q.dma_start(out=l1tb[:, 0:DM], in_=w["l1_cA"][0:128, :]),
            q.dma_start(out=l1tb[:, DM:2 * DM], in_=w["l1_cA"][128:256, :]),
            q.dma_start(out=l1tb[:, 2 * DM:3 * DM], in_=w["l1_cB"][0:128, :]),
            q.dma_start(out=l1tb[:, 3 * DM:4 * DM], in_=w["l1_cB"][128:256, :])])

        slot = {}
        for b in range(BC):
            base = b * S

            def load_w(nm, dep_consumed):
                toks = []
                for cc in range(8):
                    t_l = sc.dma("q", "qd", lambda q, nm=nm, cc=cc: [
                        q.dma_start(out=wf[:, :],
                                    in_=w[nm][cc * 128:(cc + 1) * 128, :])],
                        deps=[dep_consumed, slot.get("wf_c")])
                    t_c = sc.run("g", lambda q, cc=cc: q.tensor_copy(
                        wsl[:, cc * DM:(cc + 1) * DM], wf[:, :]), deps=[t_l])
                    slot["wf_c"] = t_c
                    toks.append(t_c)
                return toks

            for ts_ in range(16):
                t_x = sc.dma("q", "qd", lambda q, ts_=ts_: [
                    q.dma_start(out=xt[:, :],
                                in_=src[base + ts_ * 128:base + (ts_ + 1) * 128, :])],
                    deps=[slot.get("x_c")])
                t_ln = emit_ln(sc, G, xt[:, :], [yb[:, :]], 1e-5,
                               deps=[t_x, slot.get("yb_c")])
                slot["x_c"] = t_ln

                tp = pk[:, 0:512].bitcast(BF16)

                def f_tr(q, tp=tp):
                    ins = None
                    for cc in range(8):
                        ins = q.transpose(tp[:, cc * 128:(cc + 1) * 128],
                                          yb[:, cc * 128:(cc + 1) * 128],
                                          G["ident_b"][:, :])
                    return ins
                t_tr = sc.run("t", f_tr, deps=[t_ln, slot.get("tp_ev")])
                slot["yb_c"] = t_tr

                def f_te(q, ts_=ts_, tp=tp):
                    ov = yfm[:, :].rearrange("p (cc t) -> p cc t", t=S)[
                        :, :, ts_ * 128:(ts_ + 1) * 128]
                    return q.tensor_copy(ov, tp.rearrange("p (cc t) -> p cc t", t=128))
                t_te = sc.run("v", f_te, deps=[t_tr, slot.get("yfm_c")])
                slot["tp_ev"] = t_te
            t_yfm = ("v", sc.count["v"])

            for wi, (wnm, qi, rope) in enumerate([("l1a_W1T", 0, True),
                                                  ("l1a_W2T", 1, True),
                                                  ("l1a_W3T", 2, False)]):
                wtoks = load_w(wnm, slot.get("wsl_c"))
                for ts_ in range(16):
                    par = ts_ % 2
                    pv = pq[:, par * 1024:(par + 1) * 1024]
                    qv = qro[:, par * DM:(par + 1) * DM]

                    def f_mm(q, ts_=ts_, pv=pv):
                        ins = None
                        for cc in range(8):
                            lh = yfm[:, cc * S + ts_ * 128:cc * S + (ts_ + 1) * 128]
                            for n2 in range(2):
                                ins = q.matmul(
                                    pv[:, n2 * 512:(n2 + 1) * 512], lh,
                                    wsl[:, cc * DM + n2 * 512:cc * DM + (n2 + 1) * 512],
                                    start=(cc == 0), stop=(cc == 7))
                        return ins
                    t_mm = sc.run("t", f_mm,
                                  deps=[t_yfm] + wtoks + [slot.get(("pp", par))])

                    if rope:
                        tb = ts_ % 2
                        cav = l1tb[:, tb * DM:(tb + 1) * DM]
                        cbv = l1tb[:, (2 + tb) * DM:(3 + tb) * DM]

                        def f_r1(q, pv=pv, cav=cav, cbv=cbv):
                            q.tensor_mul(rtm[:, 0:DM], pv, cav)
                            pvv = pv.rearrange("p (a c) -> p a c", c=2)
                            bvv = cbv.rearrange("p (a c) -> p a c", c=2)
                            mvv = rtm[:, DM:2 * DM].rearrange("p (a c) -> p a c", c=2)
                            q.tensor_mul(mvv[:, :, 0:1], pvv[:, :, 1:2], bvv[:, :, 0:1])
                            return q.tensor_mul(mvv[:, :, 1:2], pvv[:, :, 0:1],
                                                bvv[:, :, 1:2])
                        t_r1 = sc.run("v", f_r1, deps=[t_mm, t_tb, slot.get("rtm_c")])
                        t_r2 = sc.run("v", lambda q, qv=qv: q.tensor_add(
                            qv, rtm[:, 0:DM], rtm[:, DM:2 * DM]),
                            deps=[t_r1, slot.get(("qro", par))])
                        slot["rtm_c"] = t_r2
                        slot[("pp", par)] = t_r1
                        t_out = t_r2
                    else:
                        t_out = sc.run("v", lambda q, pv=pv, qv=qv: q.tensor_copy(qv, pv),
                                       deps=[t_mm, slot.get(("qro", par))])
                        slot[("pp", par)] = t_out

                    t_sto = sc.dma("s", "sd", lambda q, ts_=ts_, qi=qi, qv=qv: [
                        q.dma_start(out=G["qkv"][qi, ts_ * 128:(ts_ + 1) * 128, :],
                                    in_=qv)], deps=[t_out])
                    slot[("qro", par)] = t_sto
                slot["wsl_c"] = t_mm
            qkv_done = ("sd", sc.count["sd"])

            wo_toks = load_w("l1a_WoT", slot.get("wsl_c"))

            heads_done = None

            def load_head(h):
                hp_ = (h % 2) * 2048
                t_qT = sc.dma("q", "qd", lambda q, h=h, hp_=hp_: [
                    q.dma_start_transpose(
                        out=qTs[:, hp_:hp_ + 2048],
                        in_=G["qkv"][0, 256 * h:256 * (h + 1), :]
                        .rearrange("u (w d) -> (u w) d", d=128))],
                    deps=[qkv_done, slot.get(("qT_c", h % 2))])
                t_kT = sc.dma("q", "qd", lambda q, h=h, hp_=hp_: [
                    q.dma_start_transpose(
                        out=kTs[:, hp_:hp_ + 2048],
                        in_=G["qkv"][1, 256 * h:256 * (h + 1), :]
                        .rearrange("u (w d) -> (u w) d", d=128))],
                    deps=[qkv_done, slot.get(("kT_c", h % 2))])
                t_v = sc.dma("q", "qd", lambda q, h=h, hp_=hp_: [
                    q.dma_start(out=vh[:, hp_:hp_ + 2048]
                                .rearrange("p (tt d) -> p tt d", tt=16),
                                in_=G["qkv"][2, 256 * h:256 * (h + 1), :]
                                .rearrange("(tt u2) (w d) -> (u2 w) tt d",
                                           tt=16, u2=16, w=8))],
                    deps=[qkv_done, slot.get(("vh_c", h % 2))])
                return t_qT, t_kT, t_v

            next_loads = load_head(0)
            for h in range(NH):
                t_qT, t_kT, t_v = next_loads
                hpar = (h % 2) * 2048
                qTh = qTs[:, hpar:hpar + 2048]
                kTh = kTs[:, hpar:hpar + 2048]
                vhh = vh[:, hpar:hpar + 2048]
                if h + 1 < NH:
                    next_loads = load_head(h + 1)

                for qh in range(2):
                    vv_last = None
                    ex_tok = {}
                    for step in range(17):
                        if step < 16:
                            kt = step
                            par = kt % 2
                            stp = pq[:, par * 1024:(par + 1) * 1024]

                            def f_st(q, kt=kt, qh=qh, stp=stp, qTh=qTh, kTh=kTh):
                                lh = kTh[:, kt * 128:(kt + 1) * 128]
                                ins = None
                                for n2 in range(2):
                                    ins = q.matmul(
                                        stp[:, n2 * 512:(n2 + 1) * 512], lh,
                                        qTh[:, qh * 1024 + n2 * 512:qh * 1024 + (n2 + 1) * 512],
                                        start=True, stop=True)
                                return ins
                            dep_st = [t_qT, t_kT, slot.get(("stB", par))]
                            if kt < 2 and qh == 0 and h == 0:
                                dep_st += [slot.get("pq_free"), slot.get("tp_ev"),
                                           slot.get(("pp", 0)), slot.get(("pp", 1))]
                            t_st = sc.run("t", f_st, deps=dep_st)

                            t_ex = sc.run("s", lambda q, stp=stp, par=par: q.activation(
                                ptsb[:, par * 1024:(par + 1) * 1024], stp, AF.Exp,
                                bias=G["c20"][:, :], scale=float(1.0 / math.sqrt(8.0))),
                                deps=[t_st, slot.get(("ptsb", par))])
                            slot[("stB", par)] = t_ex
                            ex_tok[kt] = t_ex
                        if step >= 1:
                            kt = step - 1
                            par = kt % 2

                            def f_dn(q, kt=kt, par=par):
                                ins = None
                                for n2 in range(2):
                                    ins = q.matmul(
                                        pk[0:1, 1024 + n2 * 512:1024 + (n2 + 1) * 512],
                                        G["onec"][:, :],
                                        ptsb[:, par * 1024 + n2 * 512:par * 1024 + (n2 + 1) * 512],
                                        start=(kt == 0), stop=(kt == 15))
                                return ins
                            t_dn = sc.run("t", f_dn, deps=[ex_tok[kt], slot.get("dn_c")])

                            def f_vv(q, kt=kt, par=par, vhh=vhh):
                                ins = None
                                for n2 in range(2):
                                    ins = q.matmul(
                                        pk[:, n2 * 512:(n2 + 1) * 512],
                                        vhh[:, kt * 128:(kt + 1) * 128],
                                        ptsb[:, par * 1024 + n2 * 512:par * 1024 + (n2 + 1) * 512],
                                        start=(kt == 0), stop=(kt == 15))
                                return ins
                            t_vv = sc.run("t", f_vv, deps=[t_v, ex_tok[kt], slot.get("ot_c")])
                            slot[("ptsb", par)] = t_vv
                            vv_last = t_vv

                    t_dc = sc.run("v", lambda q: q.tensor_copy(denr[:, :],
                                                               pk[0:1, 1024:2048]),
                                  deps=[t_dn, slot.get("denr_c")])
                    slot["dn_c"] = t_dc

                    def f_dt(q):
                        ins = None
                        for j in range(8):
                            ins = q.transpose(pq[:, 1536 + j:1536 + j + 1],
                                              denr[:, j * 128:(j + 1) * 128],
                                              G["ident_f"][0:1, 0:1])
                        return ins
                    t_dt = sc.run("t", f_dt, deps=[t_dc, vv_last])
                    t_dc2 = sc.run("v", lambda q: q.tensor_copy(denc[:, :],
                                                                pq[:, 1536:1544]),
                                   deps=[t_dt])
                    t_rd = sc.run("v", lambda q: q.reciprocal(rdn[:, :], denc[:, :]),
                                  deps=[t_dc2])
                    slot["denr_c"] = t_rd

                    t_oe = sc.run("v", lambda q: q.tensor_copy(osb[:, :],
                                                               pk[:, 0:1024]),
                                  deps=[vv_last, slot.get("osb_c")])
                    slot["ot_c"] = t_oe
                    if G.get("debug") and b == BC - 1 and h == 0 and qh == 0:
                        t_oe = sc.dma("g", "sd", lambda q: [
                            q.dma_start(out=G["osb_dbg"][:, :], in_=osb[:, :]),
                            q.dma_start(out=G["pt_dbg"][:, :], in_=ptsb[:, :])],
                            deps=[t_oe])
                        slot["ot_c"] = t_oe

                    tr_last = None
                    for grp in range(2):
                        base_c = 1024 if grp == 0 else 1792
                        tpo = pq[:, base_c:base_c + 256].bitcast(BF16)

                        def f_ot(q, grp=grp, tpo=tpo):
                            ins = None
                            for j in range(4):
                                i = grp * 4 + j
                                ins = q.transpose(tpo[:, j * 128:(j + 1) * 128],
                                                  osb[:, i * 128:(i + 1) * 128],
                                                  G["ident_b"][:, :])
                            return ins
                        t_ot = sc.run("t", f_ot, deps=[t_oe, slot.get(("otT", grp)),
                                                       t_dt, t_dc2])

                        def f_os(q, grp=grp, tpo=tpo, h=h, qh=qh):
                            ins = None
                            for j in range(4):
                                i = grp * 4 + j
                                tile = qh * 8 + i
                                ins = q.tensor_scalar(
                                    outall[:, tile * DM + h * HD1:tile * DM + (h + 1) * HD1],
                                    tpo[:, j * 128:(j + 1) * 128],
                                    rdn[:, i:i + 1], None, ALU.mult)
                            return ins
                        t_os = sc.run("v", f_os, deps=[t_ot, t_rd, slot.get("outall_c")])
                        slot[("otT", grp)] = t_os
                        tr_last = t_os
                    slot["osb_c"] = tr_last
                    slot[("stB", 1)] = tr_last
                    heads_done = tr_last
                slot[("qT_c", h % 2)] = vv_last
                slot[("kT_c", h % 2)] = vv_last
                slot[("vh_c", h % 2)] = vv_last
            slot["pq_free"] = heads_done

            if G.get("debug") and b == BC - 1:
                heads_done = sc.dma("g", "sd", lambda q: [
                    q.dma_start(out=G["oa_dbg"][:, :], in_=outall[:, :])],
                    deps=[heads_done])
            for sp in range(16):
                half = sp // 8
                spc = sp % 8
                par = sp % 2
                fp = pk[:, par * 1024:(par + 1) * 1024]

                def f_f(q, half=half, spc=spc, fp=fp):
                    ins = None
                    for jc in range(8):
                        tile = half * 8 + jc
                        for n2 in range(2):
                            ins = q.matmul(
                                fp[:, n2 * 512:(n2 + 1) * 512],
                                wsl[:, jc * DM + spc * 128:jc * DM + (spc + 1) * 128],
                                outall[:, tile * DM + n2 * 512:tile * DM + (n2 + 1) * 512],
                                start=(jc == 0), stop=(jc == 7))
                    return ins
                dep_f = [heads_done] + wo_toks + [slot.get(("fp", par))]
                if sp < 2:
                    dep_f += [slot.get("ot_c"), slot.get("dn_c")]
                t_f = sc.run("t", f_f, deps=dep_f)

                t_x2 = sc.dma("q", "qd", lambda q, sp=sp: [
                    q.dma_start(out=xt[:, :],
                                in_=src[base + sp * 128:base + (sp + 1) * 128, :])],
                    deps=[slot.get("x_c")])

                t_rs = sc.run("v", lambda q, fp=fp: q.tensor_tensor(
                    xt[:, :], fp, xt[:, :], ALU.add),
                    deps=[t_f, t_x2])
                slot[("fp", par)] = t_rs

                t_st2 = sc.dma("s", "sd", lambda q, sp=sp: [
                    q.dma_start(out=dst[base + sp * 128:base + (sp + 1) * 128, :],
                                in_=xt[:, :])], deps=[t_rs])
                slot["x_c"] = t_st2
            slot["yfm_c"] = slot.get("wsl_c")
            slot["outall_c"] = t_f
            slot["wsl_c"] = t_f
            slot["tp_ev"] = slot.get("tp_ev")

    sc.barrier()


def build_program(phases=(0, 1, 2, 3, 4, 5), debug=False):
    nc = bass.Bass("TRN2", target_bir_lowering=False)
    ap = lambda h: h.ap()

    xin = ap(nc.dram_tensor("xin", [T, DM], F32, kind="ExternalInput"))
    xout = ap(nc.dram_tensor("xout", [T, DM], F32, kind="ExternalOutput"))
    w = {}
    for name, shape in [
        ("l0f_Wr", [DM, 4]), ("l0f_W1", [NE, DM, I_FF]), ("l0f_b1", [NE, I_FF]),
        ("l0f_Wo", [I_FF, DM]), ("l0f_bo", [DM]), ("l0f_g", [DM]), ("l0f_b", [DM]),
        ("l0a_Wq", [DM, DM]), ("l0a_Wk", [DM, DM]), ("l0a_Wv", [DM, DM]),
        ("l0a_Wg", [DM, DM]),
        ("l1a_W1T", [DM, DM]), ("l1a_W2T", [DM, DM]), ("l1a_W3T", [DM, DM]),
        ("l1a_WoT", [DM, DM]),
        ("l1f_WrT", [DM, DM]), ("l1f_WkT", [DM, DM]), ("l1f_WvT", [DM, DM]),
        ("l1f_mur", [DM]), ("l1f_muk", [DM]),
    ]:
        w[name] = ap(nc.dram_tensor(name, shape, F32, kind="ExternalInput"))
    for name, arr in TABLES.items():
        w[name] = ap(nc.dram_tensor(name, list(arr.shape), BF16, kind="ExternalInput"))

    dk = "ExternalOutput" if debug else "Internal"
    xs = ap(nc.dram_tensor("xs", [T, DM], F32, kind="Internal"))
    yfm_hbm = ap(nc.dram_tensor("yfm_hbm", [NTT, 8, 128, 128], BF16, kind=dk))
    interm_hbm = ap(nc.dram_tensor("interm_hbm", [8, NTT, 128, 512], BF16,
                                   kind=dk))
    qkv_hbm = ap(nc.dram_tensor("qkv_hbm", [3, S, DM], BF16, kind=dk))
    masks_dbg = ap(nc.dram_tensor("masks_dbg", [128, NTT * 4], F32, kind=dk)) if debug else None
    lgs_dbg = ap(nc.dram_tensor("lgs_dbg", [NTT, 128, 4], F32, kind=dk)) if debug else None
    zsb_dbg = ap(nc.dram_tensor("zsb_dbg", [NTT, 128, DM], F32, kind=dk)) if debug else None
    oa_dbg = ap(nc.dram_tensor("oa_dbg", [128, 16 * DM], F32, kind=dk)) if debug else None
    osb_dbg = ap(nc.dram_tensor("osb_dbg", [128, 1024], F32, kind=dk)) if debug else None
    pt_dbg = ap(nc.dram_tensor("pt_dbg", [128, 2048], F32, kind=dk)) if debug else None
    yb_dbg = ap(nc.dram_tensor("yb_dbg", [4, 128, DM], F32, kind=dk)) if debug else None
    y2_dbg = ap(nc.dram_tensor("y2_dbg", [NTT, 128, DM], F32, kind=dk)) if debug else None

    phases = sorted(phases)

    with ExitStack() as ctx:
        block = ctx.enter_context(nc.Block())
        sc = Sched(nc, ctx, block)

        G = {"nc": nc, "w": w, "yfm": yfm_hbm, "interm": interm_hbm, "qkv": qkv_hbm,
             "lgs_dbg": lgs_dbg, "zsb_dbg": zsb_dbg, "y2_dbg": y2_dbg,
             "yb_dbg": yb_dbg, "oa_dbg": oa_dbg, "osb_dbg": osb_dbg,
             "pt_dbg": pt_dbg, "debug": debug}
        for nm, shape, dt in [
            ("ident_b", [128, 128], BF16), ("ident_f", [128, 128], F32),
            ("ones1", [1, 128], BF16), ("zrow", [1, 512], BF16),
            ("onec", [128, 1], BF16), ("c20", [128, 1], F32),
            ("masks", [128, NTT * 4], F32),
            ("tril_s", [128, 128], BF16),
            ("trilT_s", [128, 128], BF16), ("stat", [128, 8], F32),
            ("scr_b", [128, DM], BF16),
        ]:
            G[nm.replace("_s", "") if nm.endswith("_s") else nm] = ctx.enter_context(nc.sbuf_tensor(nm, shape, dt))

        t_tri = sc.dma("q", "qd", lambda q: [
            q.dma_start(out=G["tril"][:, :], in_=w["tril"][:, :]),
            q.dma_start(out=G["trilT"][:, :], in_=w["trilT"][:, :])])

        def f_init(q):
            q.memset(G["ones1"][:, :], 1.0)
            q.memset(G["zrow"][:, :], 0.0)
            q.memset(G["onec"][:, :], 1.0)
            q.memset(G["c20"][:, :], -20.0)
            q.memset(G["ident_b"][:, :], 0.0)
            q.affine_select(out=G["ident_b"][:, :], in_=G["ident_b"][:, :],
                            compare_op=ALU.not_equal, fill=1.0, base=0,
                            pattern=[[-1, 128]], channel_multiplier=1)
            q.memset(G["ident_f"][:, :], 0.0)
            return q.affine_select(out=G["ident_f"][:, :], in_=G["ident_f"][:, :],
                                   compare_op=ALU.not_equal, fill=1.0, base=0,
                                   pattern=[[-1, 128]], channel_multiplier=1)
        sc.run("g", f_init, deps=[t_tri])

        sc.barrier()

        emitters = {0: emit_l0ffn, 1: emit_l0attn, 2: emit_l0ffn,
                    3: emit_l1ffn, 4: emit_l1attn, 5: emit_l1ffn}
        PHASE_MARKS.clear()
        for ph in phases:
            src = xin if ph == phases[0] else xs
            dst = xout if ph == phases[-1] else xs
            G["tag"] = f"p{ph}"
            PHASE_MARKS.append((ph, nc.next_id()))
            emitters[ph](sc, G, src, dst)
        PHASE_MARKS.append((-1, nc.next_id()))

        if debug:
            sc.dma("s", "sd", lambda q: [
                q.dma_start(out=masks_dbg[:, :], in_=G["masks"][:, :])])

        @block.sync
        def _(q):
            q.wait_ge(sc.sem["sd"], sc.count["sd"])
            q.wait_ge(sc.sem["qd"], sc.count["qd"])
    return nc


def make_inmap(core, inputs):
    x = np.asarray(inputs["L0_attn_x"])
    xsh = np.ascontiguousarray(x.reshape(32, S, DM)[core * BC:(core + 1) * BC]
                               .reshape(T, DM))
    m = {"xin": xsh,
         "l0f_Wr": inputs["L0_ffn_W_router"], "l0f_W1": inputs["L0_ffn_W1"],
         "l0f_b1": inputs["L0_ffn_b1"], "l0f_Wo": inputs["L0_ffn_Wo"],
         "l0f_bo": inputs["L0_ffn_bo"], "l0f_g": inputs["L0_ffn_ln_gamma"],
         "l0f_b": inputs["L0_ffn_ln_beta"],
         "l0a_Wq": inputs["L0_attn_Wq"], "l0a_Wk": inputs["L0_attn_Wk"],
         "l0a_Wv": inputs["L0_attn_Wv"], "l0a_Wg": inputs["L0_attn_Wg"],
         "l1a_W1T": np.asarray(inputs["L1_attn_W1"]).T,
         "l1a_W2T": np.asarray(inputs["L1_attn_W2"]).T,
         "l1a_W3T": np.asarray(inputs["L1_attn_W3"]).T,
         "l1a_WoT": np.asarray(inputs["L1_attn_Wo"]).T,
         "l1f_WrT": np.asarray(inputs["L1_ffn_W_r"]).T,
         "l1f_WkT": np.asarray(inputs["L1_ffn_W_k"]).T,
         "l1f_WvT": np.asarray(inputs["L1_ffn_W_v"]).T,
         "l1f_mur": inputs["L1_ffn_mu_r"], "l1f_muk": inputs["L1_ffn_mu_k"]}
    for k, v in TABLES.items():
        m[k] = v
    return {k: np.ascontiguousarray(np.asarray(v)) for k, v in m.items()}


def kernel(**inputs):
    nc = build_program(phases=(0, 1, 2, 3, 4, 5))
    in_maps = [make_inmap(core, inputs) for core in range(8)]
    res = run_bass_kernel_spmd(nc, in_maps, core_ids=list(range(8)))
    out = np.concatenate([np.asarray(r["xout"]).reshape(BC, S, DM)
                          for r in res.results], axis=0)
    return out.astype(np.float32)
